# version 04
# baseline (speedup 1.0000x reference)
import numpy as np
import concourse.bass as bass
import concourse.mybir as mybir
from concourse.bass_utils import run_bass_kernel_spmd

F32 = mybir.dt.float32
BF16 = mybir.dt.bfloat16
AF = mybir.ActivationFunctionType
ALU = mybir.AluOpType
AX = mybir.AxisListType

D = 1024
B = 4
L = 4096
T_OWN = 2048

ENGINES = ("sync", "tensor", "vector", "scalar", "gpsimd")


class EM:

    NGS = 8

    def __init__(self, nc):
        self.nc = nc
        self.ops = {e: [] for e in ENGINES}
        self.count = {}
        self.last_w = {}
        self.readers = {}
        self.sems = {}
        self._exits = []
        self.q_groups = {}
        self.q_open = {}
        self.q_semcum = {}
        self.q_rot = {}

    def sem(self, st):
        if st not in self.sems:
            cm = self.nc.semaphore("sem_" + st.replace(":", "_"))
            self.sems[st] = cm.__enter__()
            self._exits.append(cm)
        return self.sems[st]

    def _dma_group_sem(self, st):
        if st not in self.q_open:
            rot = self.q_rot.get(st, 0)
            name = "%s:%d" % (st, rot % self.NGS)
            self.q_rot[st] = rot + 1
            self.q_open[st] = name
        return self.q_open[st]

    def _close_group(self, st, upto_seq):
        name = self.q_open.pop(st)
        target = self.q_semcum.get(name, 0)
        self.q_groups.setdefault(st, []).append((upto_seq, name, target))

    def _dep_wait(self, waits, my_stream, st2, c2):
        if st2.startswith("d:"):
            groups = self.q_groups.get(st2, [])
            grp = None
            for last_seq, name, target in groups:
                if c2 <= last_seq:
                    grp = (name, target)
                    break
            if grp is None:
                self._close_group(st2, self.count.get(st2, 0))
                last_seq, name, target = self.q_groups[st2][-1]
                grp = (name, target)
            name, target = grp
            if my_stream == st2:
                return
            waits[name] = max(waits.get(name, 0), target)
        else:
            waits[st2] = max(waits.get(st2, 0), c2)

    def emit(self, eng, fn, reads=(), writes=(), dma=False):
        my_stream = ("d:" if dma else "c:") + eng
        waits = {}
        for k in reads:
            lw = self.last_w.get(k)
            if lw:
                self._dep_wait(waits, my_stream, *lw)
        for k in writes:
            lw = self.last_w.get(k)
            if lw:
                self._dep_wait(waits, my_stream, *lw)
            for st2, c2 in self.readers.get(k, {}).items():
                self._dep_wait(waits, my_stream, st2, c2)
        self.count[my_stream] = self.count.get(my_stream, 0) + 1
        cnt = self.count[my_stream]
        gsem = None
        if dma:
            gsem = self._dma_group_sem(my_stream)
            self.q_semcum[gsem] = self.q_semcum.get(gsem, 0) + 16
        for k in writes:
            self.last_w[k] = (my_stream, cnt)
            self.readers[k] = {}
        for k in reads:
            self.readers.setdefault(k, {})[my_stream] = cnt
        self.ops[eng].append((waits, fn, gsem))

    def dma(self, eng, out, in_, reads=(), writes=(), **kw):
        self.emit(eng, lambda q: q.dma_start(out=out, in_=in_, **kw),
                  reads=reads, writes=writes, dma=True)

    def finalize(self):
        nc = self.nc
        with nc.Block() as block:
            for eng_name in ENGINES:
                ops = self.ops[eng_name]
                if not ops:
                    continue

                def make(eng_name=eng_name, ops=ops):
                    def body(q):
                        synced = {}
                        cs = "c:" + eng_name
                        nc_cnt = 0
                        for waits, fn, gsem in ops:
                            for st2, v2 in sorted(waits.items()):
                                if v2 > synced.get(st2, 0):
                                    q.wait_ge(self.sem(st2), v2)
                                    synced[st2] = v2
                            instr = fn(q)
                            if gsem is not None:
                                instr.then_inc(self.sem(gsem), 16)
                            else:
                                nc_cnt += 1
                                instr.then_inc(self.sem(cs), 1)
                        ds = "d:" + eng_name
                        for name, cum in sorted(self.q_semcum.items()):
                            if name.startswith(ds):
                                if cum > synced.get(name, 0):
                                    q.wait_ge(self.sem(name), cum)
                        if nc_cnt:
                            q.wait_ge(self.sem(cs), nc_cnt)
                    return body

                getattr(block, eng_name)(make())


def sbuf(nc, name, shape, dtype=F32):
    return nc.sbuf_tensor(name, shape, dtype).__enter__()


def psum_t(nc, name, shape, dtype=F32):
    return nc.psum_tensor(name, shape, dtype).__enter__()


def ts(i, n):
    return slice(i * n, (i + 1) * n)


def e_mm(em, out, lhsT, rhs, reads, write, start, stop):
    em.emit("tensor",
            lambda t: t.matmul(out=out, lhsT=lhsT, rhs=rhs, start=start,
                               stop=stop),
            reads=reads, writes=[write])


def e_bcast(em, ps, ones_s, row_ap, read, write, nrows=128):
    em.emit("tensor",
            lambda t: t.matmul(out=ps, lhsT=ones_s[0:1, 0:nrows], rhs=row_ap,
                               start=True, stop=True),
            reads=[read, "ones"], writes=[write])


def e_tt(em, out, in0, in1, op, reads, write):
    em.emit("vector",
            lambda v: v.tensor_tensor(out=out, in0=in0, in1=in1, op=op),
            reads=reads, writes=[write])


def e_ts(em, out, in0, s1, s2, op0, op1, reads, write, eng="vector"):
    if s2 is None:
        em.emit(eng,
                lambda v: v.tensor_scalar(out=out, in0=in0, scalar1=s1,
                                          scalar2=None, op0=op0),
                reads=reads, writes=[write])
    else:
        em.emit(eng,
                lambda v: v.tensor_scalar(out=out, in0=in0, scalar1=s1,
                                          scalar2=s2, op0=op0, op1=op1),
                reads=reads, writes=[write])


def e_stt(em, out, in0, scalar, in1, op0, op1, reads, write):
    em.emit("vector",
            lambda v: v.scalar_tensor_tensor(out=out, in0=in0, scalar=scalar,
                                             in1=in1, op0=op0, op1=op1),
            reads=reads, writes=[write])


def e_act(em, out, in_, func, reads, write, bias=None, scale=None, eng="scalar"):
    kw = {}
    if bias is not None:
        kw["bias"] = bias
    if scale is not None:
        kw["scale"] = scale
    em.emit(eng,
            lambda s: s.activation(out=out, in_=in_, func=func, **kw),
            reads=reads, writes=[write])


def e_copy(em, out, in_, reads, write, eng="scalar"):
    if eng == "scalar":
        em.emit(eng, lambda s: s.copy(out=out, in_=in_), reads=reads,
                writes=[write])
    else:
        em.emit(eng, lambda v: v.tensor_copy(out=out, in_=in_), reads=reads,
                writes=[write])


def e_red(em, out, in_, op, reads, write, negate=False):
    em.emit("vector",
            lambda v: v.tensor_reduce(out=out, in_=in_, axis=AX.X, op=op,
                                      negate=negate),
            reads=reads, writes=[write])


def ln_rows(em, st_ap, st2_ap, mu_ap, rs_ap, tmp_ap, eps, dimn,
            k_st, k_st2, k_mu, k_rs, k_tmp):
    e_ts(em, mu_ap, st_ap, 1.0 / dimn, None, ALU.mult, None, [k_st], k_mu)
    e_ts(em, tmp_ap, st2_ap, 1.0 / dimn, None, ALU.mult, None, [k_st2], k_tmp)
    e_tt(em, rs_ap, mu_ap, mu_ap, ALU.mult, [k_mu], k_rs)
    e_tt(em, tmp_ap, tmp_ap, rs_ap, ALU.subtract, [k_tmp, k_rs], k_tmp)
    e_ts(em, tmp_ap, tmp_ap, eps, None, ALU.add, None, [k_tmp], k_tmp)
    em.emit("scalar", lambda s: s.activation(out=tmp_ap, in_=tmp_ap, func=AF.Sqrt),
            reads=[k_tmp], writes=[k_tmp])
    em.emit("vector", lambda v: v.reciprocal(out=rs_ap, in_=tmp_ap),
            reads=[k_tmp], writes=[k_rs])


def build_l0():
    nc = bass.Bass("TRN2", target_bir_lowering=False)
    H, DX, QK = 8, 64, 512
    ND = D // 128
    NCH = L // 512
    OCH = T_OWN // 512

    xf = nc.dram_tensor("xf", [D, L], F32, kind="ExternalInput").ap()
    w_qkv = nc.dram_tensor("w_qkv", [D, 3 * QK], F32, kind="ExternalInput").ap()
    bqkv_row_d = nc.dram_tensor("bqkv_row", [1, 3 * QK], F32, kind="ExternalInput").ap()
    bqkv_col_d = nc.dram_tensor("bqkv_col", [128, 12], F32, kind="ExternalInput").ap()
    w_fc = nc.dram_tensor("w_fc", [QK, D], F32, kind="ExternalInput").ap()
    bfc_col_d = nc.dram_tensor("bfc_col", [128, ND], F32, kind="ExternalInput").ap()
    gam_col_d = nc.dram_tensor("gam_col", [128, ND], F32, kind="ExternalInput").ap()
    bet_col_d = nc.dram_tensor("bet_col", [128, ND], F32, kind="ExternalInput").ap()
    gate_wT = nc.dram_tensor("gate_wT", [D, 8], F32, kind="ExternalInput").ap()
    ew0T = nc.dram_tensor("ew0T", [D, D], F32, kind="ExternalInput").ap()
    ew1T = nc.dram_tensor("ew1T", [D, D], F32, kind="ExternalInput").ap()
    out_wT = nc.dram_tensor("out_wT", [D, D], F32, kind="ExternalInput").ap()
    iden_d = nc.dram_tensor("iden", [128, 128], F32, kind="ExternalInput").ap()
    x1_d = nc.dram_tensor("x1", [D, T_OWN], F32, kind="ExternalOutput").ap()
    xn1_d = nc.dram_tensor("xn1", [D, T_OWN], F32, kind="ExternalOutput").ap()

    em = EM(nc)

    wqkv_s = sbuf(nc, "wqkv_s", [128, ND * 3 * QK], BF16)
    wfc_s = sbuf(nc, "wfc_s", [128, 4 * D], BF16)
    gw_s = sbuf(nc, "gw_s", [128, ND * 8], BF16)
    ew0_s = sbuf(nc, "ew0_s", [128, ND * D], BF16)
    ew1_s = sbuf(nc, "ew1_s", [128, ND * D], BF16)
    ow_s = sbuf(nc, "ow_s", [128, ND * D], BF16)
    ones_s = sbuf(nc, "ones_s", [128, 128])
    eye_s = sbuf(nc, "eye_s", [128, 128])
    wtmp = sbuf(nc, "wtmp", [128, 512])
    bqkv_row = sbuf(nc, "bqkv_row_s", [1, 3 * QK])
    bqkvL_row = sbuf(nc, "bqkvL_row", [1, 3 * QK])
    bqkv_col = sbuf(nc, "bqkv_col_s", [128, 12])
    bfc_col = sbuf(nc, "bfc_col_s", [128, ND])
    gam_col = sbuf(nc, "gam_col_s", [128, ND])
    bet_col = sbuf(nc, "bet_col_s", [128, ND])

    xn0b = sbuf(nc, "xn0b", [128, ND * T_OWN], BF16)
    xnbt = sbuf(nc, "xnbt", [128, ND * 512], BF16)
    vfm = sbuf(nc, "vfm", [128, 4 * 512], BF16)
    s0 = sbuf(nc, "s0", [128, ND * 512])
    mu_r = sbuf(nc, "mu_r", [1, 512])
    rs_r = sbuf(nc, "rs_r", [1, 512])
    snx_c = sbuf(nc, "snx_c", [128, ND])
    snx_b = sbuf(nc, "snx_b", [128, ND], BF16)
    red_c = sbuf(nc, "red_c", [128, 1])
    sq_row = sbuf(nc, "sq_row", [1, QK])
    sk_row = sbuf(nc, "sk_row", [1, QK])
    scores = sbuf(nc, "scores", [128, 4 * DX])
    attn = sbuf(nc, "attn", [128, 4 * DX], BF16)
    qh_s = sbuf(nc, "qh_s", [128, QK], BF16)
    kh_s = sbuf(nc, "kh_s", [128, QK], BF16)
    xt_s = sbuf(nc, "xt_s", [128, 2 * 512])
    sq_s = sbuf(nc, "sq_s", [128, 512])
    tmp_a = sbuf(nc, "tmp_a", [128, 512])
    tmp_b = sbuf(nc, "tmp_b", [128, 512])
    rowtmp = sbuf(nc, "rowtmp", [1, 512])
    mu2_r = sbuf(nc, "mu2_r", [1, 512])
    rs2_r = sbuf(nc, "rs2_r", [1, 512])
    col_s = sbuf(nc, "col_s", [128, 16])
    pcols = sbuf(nc, "pcols", [128, 8])
    pv_s = sbuf(nc, "pv_s", [128, 4 * 512], BF16)
    h0_s = sbuf(nc, "h0_s", [128, 512], BF16)
    h1_s = sbuf(nc, "h1_s", [128, 512], BF16)
    gpre = sbuf(nc, "gpre", [128, ND * 512], BF16)
    ge_s = sbuf(nc, "ge_s", [128, 512])
    w01_r = sbuf(nc, "w01_r", [1, 2 * 512])
    x1c = sbuf(nc, "x1c", [128, 512])

    ps_a = psum_t(nc, "ps_a", [128, 512])
    ps_b = psum_t(nc, "ps_b", [128, 512])
    ps_c = psum_t(nc, "ps_c", [128, 512])
    ps_r = psum_t(nc, "ps_r", [128, 512])
    ps_r2 = psum_t(nc, "ps_r2", [128, 512])
    ps_sc = psum_t(nc, "ps_sc", [128, DX])
    ps_st = psum_t(nc, "ps_st", [1, 512])
    ps_st2 = psum_t(nc, "ps_st2", [1, 512])

    em.emit("gpsimd", lambda g: g.memset(ones_s[:, :], 1.0), writes=["ones"])
    em.dma("sync", eye_s[:, :], iden_d[:, :], writes=["eye"])
    em.dma("sync", bqkv_row[:, :], bqkv_row_d[:, :], writes=["bqr"])
    em.dma("sync", bqkv_col[:, :], bqkv_col_d[:, :], writes=["bqc"])
    em.dma("sync", bfc_col[:, :], bfc_col_d[:, :], writes=["bfc"])
    em.dma("sync", gam_col[:, :], gam_col_d[:, :], writes=["gam"])
    em.dma("sync", bet_col[:, :], bet_col_d[:, :], writes=["bet"])
    e_ts(em, bqkvL_row[:, :], bqkv_row[:, :], float(L), None, ALU.mult, None,
         ["bqr"], "bqL")
    em.emit("vector", lambda v: v.memset(scores[:, :], 0.0), writes=["scoresb"])

    def load_cast(dst_ap, src_ap, cols, tag):
        for o in range(0, cols, 512):
            w = min(512, cols - o)
            em.dma("sync", wtmp[:, 0:w], src_ap[:, o:o + w], writes=["wtmp"])
            e_copy(em, dst_ap[:, o:o + w], wtmp[:, 0:w], ["wtmp"], tag)

    for k in range(ND):
        load_cast(wqkv_s[:, ts(k, 3 * QK)], w_qkv[ts(k, 128), :], 3 * QK, "wqkv")
    for k in range(4):
        load_cast(wfc_s[:, ts(k, D)], w_fc[ts(k, 128), :], D, "wfc")
    for k in range(ND):
        load_cast(gw_s[:, ts(k, 8)], gate_wT[ts(k, 128), :], 8, "gw")
    for k in range(ND):
        load_cast(ew0_s[:, ts(k, D)], ew0T[ts(k, 128), :], D, "ew0")
    for k in range(ND):
        load_cast(ew1_s[:, ts(k, D)], ew1T[ts(k, 128), :], D, "ew1")
    for k in range(ND):
        load_cast(ow_s[:, ts(k, D)], out_wT[ts(k, 128), :], D, "ow")

    for c in range(NCH):
        own = c < OCH
        cs = slice(c * 512, (c + 1) * 512)
        rsl = slice(0, 512)
        for k in range(ND):
            em.dma("sync", xt_s[:, ts(k % 2, 512)], xf[ts(k, 128), cs],
                   writes=["xt%d" % (k % 2)])
            e_mm(em, ps_st[:, :], ones_s[:, 0:1], xt_s[:, ts(k % 2, 512)],
                 ["xt%d" % (k % 2), "ones"], "st", start=(k == 0), stop=(k == ND - 1))
            e_act(em, sq_s[:, :], xt_s[:, ts(k % 2, 512)], AF.Square,
                  ["xt%d" % (k % 2)], "sq")
            e_mm(em, ps_st2[:, :], ones_s[:, 0:1], sq_s[:, :],
                 ["sq", "ones"], "st2", start=(k == 0), stop=(k == ND - 1))
        ln_rows(em, ps_st[:, :], ps_st2[:, :], mu_r[:, rsl], rs_r[:, rsl],
                rowtmp[:, :], 1e-5, D, "st", "st2", "mu", "rs", "rt")
        e_bcast(em, ps_r[:, :], ones_s, mu_r[:, rsl], "mu", "psr")
        e_bcast(em, ps_r2[:, :], ones_s, rs_r[:, rsl], "rs", "psr2")
        for k in range(ND):
            em.dma("sync", xt_s[:, ts(k % 2, 512)], xf[ts(k, 128), cs],
                   writes=["xt%d" % (k % 2)])
            xnap = (xn0b[:, k * T_OWN + c * 512: k * T_OWN + (c + 1) * 512]
                    if own else xnbt[:, ts(k, 512)])
            xnkey = ("xn0b_%d" % k) if own else ("xnbt_%d" % k)
            e_tt(em, tmp_a[:, :], xt_s[:, ts(k % 2, 512)], ps_r[:, :], ALU.subtract,
                 ["xt%d" % (k % 2), "psr"], "tmpa")
            e_tt(em, xnap, tmp_a[:, :], ps_r2[:, :], ALU.mult,
                 ["tmpa", "psr2"], xnkey)
            e_red(em, red_c[:, 0:1], xnap, ALU.add, [xnkey], "redc")
            if c == 0:
                e_copy(em, snx_c[:, k:k + 1], red_c[:, 0:1], ["redc"],
                       "snx%d" % k, eng="vector")
            else:
                e_tt(em, snx_c[:, k:k + 1], snx_c[:, k:k + 1], red_c[:, 0:1],
                     ALU.add, ["snx%d" % k, "redc"], "snx%d" % k)
        for tt in range(4):
            gtok = c * 4 + tt
            for qk_i, dst, kdst in ((0, qh_s, "qh"), (1, kh_s, "kh")):
                for k in range(ND):
                    lhs_ap = (xn0b[:, k * T_OWN + c * 512 + tt * 128:
                                   k * T_OWN + c * 512 + (tt + 1) * 128]
                              if own else
                              xnbt[:, k * 512 + tt * 128: k * 512 + (tt + 1) * 128])
                    xnkey = ("xn0b_%d" % k) if own else ("xnbt_%d" % k)
                    e_mm(em, ps_a[:, :], lhs_ap,
                         wqkv_s[:, k * 3 * QK + qk_i * QK:
                                k * 3 * QK + (qk_i + 1) * QK],
                         [xnkey, "wqkv"], "psa", start=(k == 0),
                         stop=(k == ND - 1))
                e_copy(em, dst[:, :], ps_a[:, :], ["psa"], kdst)
            for hp in range(4):
                for sub in range(2):
                    h = 2 * hp + sub
                    e_mm(em, ps_sc[sub * 64:sub * 64 + 64, :],
                         qh_s[:, ts(h, DX)], kh_s[:, ts(h, DX)],
                         ["qh", "kh"], "psc", start=True, stop=True)
                e_tt(em, scores[:, ts(hp, DX)], scores[:, ts(hp, DX)],
                     ps_sc[:, :], ALU.add, ["scoresb", "psc"], "scoresb")
    e_copy(em, snx_b[:, :], snx_c[:, :],
           ["snx%d" % k2 for k2 in range(ND)], "snxb")
    for qk_i, dst, kdst in ((0, sq_row, "sqrow"), (1, sk_row, "skrow")):
        for k in range(ND):
            e_mm(em, ps_st[:, :], snx_b[:, k:k + 1],
                 wqkv_s[:, k * 3 * QK + qk_i * QK: k * 3 * QK + (qk_i + 1) * QK],
                 ["snxb", "wqkv"], "st", start=(k == 0), stop=(k == ND - 1))
        e_copy(em, dst[:, :], ps_st[:, :], ["st"], kdst)
    for hp in range(4):
        for sub in range(2):
            h = 2 * hp + sub
            out_ap = ps_sc[sub * 64:sub * 64 + 64, :]
            e_mm(em, out_ap, bqkv_row[:, ts(h, DX)], sk_row[:, ts(h, DX)],
                 ["bqr", "skrow"], "psc", start=True, stop=False)
            e_mm(em, out_ap, sq_row[:, ts(h, DX)], bqkv_row[:, QK + h * DX:
                                                            QK + (h + 1) * DX],
                 ["sqrow", "bqr"], "psc", start=False, stop=False)
            e_mm(em, out_ap, bqkvL_row[:, ts(h, DX)], bqkv_row[:, QK + h * DX:
                                                               QK + (h + 1) * DX],
                 ["bqL", "bqr"], "psc", start=False, stop=True)
        e_tt(em, scores[:, ts(hp, DX)], scores[:, ts(hp, DX)], ps_sc[:, :],
             ALU.add, ["scoresb", "psc"], "scoresb")


    for hp in range(4):
        sl = ts(hp, DX)
        e_red(em, col_s[:, hp:hp + 1], scores[:, sl], ALU.max, ["scoresb"],
              "cols")
        e_ts(em, col_s[:, 4 + hp:5 + hp], col_s[:, hp:hp + 1], -0.125, None,
             ALU.mult, None, ["cols"], "cols")
        em.emit("scalar",
                lambda s, hp=hp, sl=sl: s.activation(
                    out=scores[:, sl], in_=scores[:, sl], func=AF.Exp,
                    bias=col_s[:, 4 + hp:5 + hp], scale=0.125),
                reads=["scoresb", "cols"], writes=["scoresb"])
        e_red(em, col_s[:, 8 + hp:9 + hp], scores[:, sl], ALU.add,
              ["scoresb"], "cols")
        em.emit("vector",
                lambda v, hp=hp: v.reciprocal(out=col_s[:, 12 + hp:13 + hp],
                                              in_=col_s[:, 8 + hp:9 + hp]),
                reads=["cols"], writes=["cols"])
        e_ts(em, attn[:, sl], scores[:, sl], col_s[:, 12 + hp:13 + hp], None,
             ALU.mult, None, ["scoresb", "cols"], "attn")


    for c in range(OCH):
        for k in range(ND):
            em.dma("sync", xt_s[:, ts(k % 2, 512)], xf[ts(k, 128), ts(c, 512)],
                   writes=["xt%d" % (k % 2)])
            e_mm(em, ps_st[:, :], ones_s[:, 0:1], xt_s[:, ts(k % 2, 512)],
                 ["xt%d" % (k % 2), "ones"], "st", start=(k == 0),
                 stop=(k == ND - 1))
            e_act(em, sq_s[:, :], xt_s[:, ts(k % 2, 512)], AF.Square,
                  ["xt%d" % (k % 2)], "sq")
            e_mm(em, ps_st2[:, :], ones_s[:, 0:1], sq_s[:, :], ["sq", "ones"],
                 "st2", start=(k == 0), stop=(k == ND - 1))
        ln_rows(em, ps_st[:, :], ps_st2[:, :], mu_r[:, :], rs_r[:, :],
                rowtmp[:, :], 1e-5, D, "st", "st2", "mu", "rs", "rt")
        for jt in range(4):
            for k in range(ND):
                e_mm(em, ps_b[:, :],
                     wqkv_s[:, k * 3 * QK + 2 * QK + jt * 128:
                            k * 3 * QK + 2 * QK + (jt + 1) * 128],
                     xn0b[:, k * T_OWN + c * 512: k * T_OWN + (c + 1) * 512],
                     ["wqkv", "xn0b_%d" % k], "psb",
                     start=(k == 0), stop=(k == ND - 1))
            e_act(em, vfm[:, ts(jt, 512)], ps_b[:, :], AF.Identity,
                  ["psb", "bqc"], "vfm", bias=bqkv_col[:, 8 + jt:9 + jt])
        for hp in range(4):
            for sub in range(2):
                h = hp * 2 + sub
                e_mm(em, ps_b[sub * 64:sub * 64 + 64, :],
                     attn[(h % 2) * 64:(h % 2) * 64 + 64, ts(h // 2, DX)],
                     vfm[(h % 2) * 64:(h % 2) * 64 + 64, ts(h // 2, 512)],
                     ["attn", "vfm"], "psb", start=True, stop=True)
            e_copy(em, pv_s[:, ts(hp, 512)], ps_b[:, :], ["psb"], "pv")
        for k in range(ND):
            for j in range(4):
                e_mm(em, ps_c[:, :],
                     wfc_s[:, j * D + k * 128: j * D + (k + 1) * 128],
                     pv_s[:, ts(j, 512)], ["wfc", "pv"], "psc2",
                     start=(j == 0), stop=(j == 3))
            e_act(em, tmp_a[:, :], ps_c[:, :], AF.Identity, ["psc2", "bfc"],
                  "tmpa", bias=bfc_col[:, k:k + 1])
            em.dma("sync", xt_s[:, ts(k % 2, 512)], xf[ts(k, 128), ts(c, 512)],
                   writes=["xt%d" % (k % 2)])
            if k == 0:
                e_bcast(em, ps_r[:, :], ones_s, mu_r[:, :], "mu", "psr")
                e_bcast(em, ps_r2[:, :], ones_s, rs_r[:, :], "rs", "psr2")
            e_tt(em, tmp_b[:, :], xt_s[:, ts(k % 2, 512)], ps_r[:, :], ALU.subtract,
                 ["xt%d" % (k % 2), "psr"], "tmpb")
            e_tt(em, tmp_b[:, :], tmp_b[:, :], ps_r2[:, :], ALU.mult,
                 ["tmpb", "psr2"], "tmpb")
            e_tt(em, s0[:, ts(k, 512)],
                 tmp_a[:, :], tmp_b[:, :], ALU.add, ["tmpa", "tmpb"],
                 "s0_%d" % k)
            e_mm(em, ps_st[:, :], ones_s[:, 0:1],
                 s0[:, ts(k, 512)],
                 ["s0_%d" % k, "ones"], "st", start=(k == 0), stop=(k == ND - 1))
            e_act(em, sq_s[:, :],
                  s0[:, ts(k, 512)],
                  AF.Square, ["s0_%d" % k], "sq")
            e_mm(em, ps_st2[:, :], ones_s[:, 0:1], sq_s[:, :], ["sq", "ones"],
                 "st2", start=(k == 0), stop=(k == ND - 1))
        ln_rows(em, ps_st[:, :], ps_st2[:, :], mu2_r[:, :], rs2_r[:, :],
                rowtmp[:, :], 1e-6, D, "st", "st2", "mu2", "rs2", "rt")
        e_bcast(em, ps_r[:, :], ones_s, mu2_r[:, :], "mu2", "psr")
        e_bcast(em, ps_r2[:, :], ones_s, rs2_r[:, :], "rs2", "psr2")
        for k in range(ND):
            sl = ts(k, 512)
            e_tt(em, s0[:, sl], s0[:, sl], ps_r[:, :], ALU.subtract,
                 ["s0_%d" % k, "psr"], "s0_%d" % k)
            e_tt(em, s0[:, sl], s0[:, sl], ps_r2[:, :], ALU.mult,
                 ["s0_%d" % k, "psr2"], "s0_%d" % k)
            e_ts(em, s0[:, sl], s0[:, sl], gam_col[:, k:k + 1],
                 bet_col[:, k:k + 1], ALU.mult, ALU.add,
                 ["s0_%d" % k, "gam", "bet"], "s0_%d" % k)

        for tt in range(4):
            tok0 = c * 512 + tt * 128
            for k in range(ND):
                e_mm(em, ps_a[:, 0:8],
                     xn0b[:, k * T_OWN + tok0: k * T_OWN + tok0 + 128],
                     gw_s[:, ts(k, 8)], ["xn0b_%d" % k, "gw"], "psa",
                     start=(k == 0), stop=(k == ND - 1))
            e_red(em, col_s[:, 0:1], ps_a[:, 0:8], ALU.max, ["psa"], "cols",
                  negate=True)
            em.emit("scalar",
                    lambda s: s.activation(out=pcols[:, 0:8], in_=ps_a[:, 0:8],
                                           func=AF.Exp, bias=col_s[:, 0:1]),
                    reads=["psa", "cols"], writes=["pcols"])
            e_red(em, col_s[:, 1:2], pcols[:, 0:8], ALU.add, ["pcols"], "cols")
            em.emit("vector",
                    lambda v: v.reciprocal(out=col_s[:, 2:3], in_=col_s[:, 1:2]),
                    reads=["cols"], writes=["cols"])
            e_ts(em, pcols[:, 0:8], pcols[:, 0:8], col_s[:, 2:3], None,
                 ALU.mult, None, ["pcols", "cols"], "pcols")
            e_ts(em, col_s[:, 4:12], pcols[:, 0:8], 0.3, None, ALU.is_gt, None,
                 ["pcols"], "cols")
            e_red(em, col_s[:, 12:13], col_s[:, 4:12], ALU.add, ["cols"], "cols")
            e_ts(em, col_s[:, 13:14], col_s[:, 12:13], -1.0, 8.0, ALU.mult,
                 ALU.add, ["cols"], "cols")
            e_tt(em, col_s[:, 14:15], col_s[:, 13:14], pcols[:, 0:1], ALU.mult,
                 ["cols", "pcols"], "cols")
            e_tt(em, col_s[:, 15:16], col_s[:, 12:13], pcols[:, 1:2], ALU.mult,
                 ["cols", "pcols"], "cols")
            em.emit("tensor",
                    lambda t: t.transpose(out=ps_st[:, 0:128],
                                          in_=col_s[:, 14:15],
                                          identity=eye_s[:, :]),
                    reads=["cols", "eye"], writes=["st"])
            e_copy(em, w01_r[:, tt * 128:(tt + 1) * 128], ps_st[:, 0:128], ["st"], "w01")
            em.emit("tensor",
                    lambda t: t.transpose(out=ps_st[:, 0:128],
                                          in_=col_s[:, 15:16],
                                          identity=eye_s[:, :]),
                    reads=["cols", "eye"], writes=["st"])
            e_copy(em, w01_r[:, 512 + tt * 128: 512 + (tt + 1) * 128],
                   ps_st[:, 0:128], ["st"], "w01")
        e_bcast(em, ps_r[:, :], ones_s, w01_r[:, 0:512], "w01", "psr")
        e_bcast(em, ps_r2[:, :], ones_s, w01_r[:, 512:1024], "w01", "psr2")
        for jt in range(ND):
            for k in range(ND):
                e_mm(em, ps_b[:, :], ew0_s[:, k * D + jt * 128:
                                           k * D + (jt + 1) * 128],
                     xn0b[:, k * T_OWN + c * 512: k * T_OWN + (c + 1) * 512],
                     ["ew0", "xn0b_%d" % k], "psb", start=(k == 0),
                     stop=(k == ND - 1))
            e_copy(em, h0_s[:, :], ps_b[:, :], ["psb"], "h0")
            for k in range(ND):
                e_mm(em, ps_c[:, :], ew1_s[:, k * D + jt * 128:
                                           k * D + (jt + 1) * 128],
                     xn0b[:, k * T_OWN + c * 512: k * T_OWN + (c + 1) * 512],
                     ["ew1", "xn0b_%d" % k], "psc2", start=(k == 0),
                     stop=(k == ND - 1))
            e_copy(em, h1_s[:, :], ps_c[:, :], ["psc2"], "h1")
            e_tt(em, tmp_a[:, :], h0_s[:, :], ps_r[:, :], ALU.mult,
                 ["h0", "psr"], "tmpa")
            e_tt(em, tmp_b[:, :], h1_s[:, :], ps_r2[:, :], ALU.mult,
                 ["h1", "psr2"], "tmpb")
            e_tt(em, gpre[:, ts(jt, 512)], tmp_a[:, :], tmp_b[:, :], ALU.add,
                 ["tmpa", "tmpb"], "gpre")
        for k in range(ND):
            for j in range(ND):
                e_mm(em, ps_c[:, :], ow_s[:, j * D + k * 128:
                                          j * D + (k + 1) * 128],
                     gpre[:, ts(j, 512)], ["ow", "gpre"], "psc2",
                     start=(j == 0), stop=(j == ND - 1))
            e_act(em, ge_s[:, :], ps_c[:, :], AF.Silu, ["psc2"], "ge")
            em.dma("sync", xt_s[:, ts(k % 2, 512)], xf[ts(k, 128), ts(c, 512)],
                   writes=["xt%d" % (k % 2)])
            e_tt(em, tmp_a[:, :],
                 s0[:, ts(k, 512)],
                 ge_s[:, :], ALU.mult, ["s0_%d" % k, "ge"], "tmpa")
            e_tt(em, x1c[:, :], xt_s[:, ts(k % 2, 512)], tmp_a[:, :],
                 ALU.add, ["xt%d" % (k % 2), "tmpa"], "x1c")
            em.dma("gpsimd", x1_d[ts(k, 128), ts(c, 512)], x1c[:, :],
                   reads=["x1c"], writes=["x1out"])
            e_mm(em, ps_st[:, :], ones_s[:, 0:1], x1c[:, :],
                 ["x1c", "ones"], "st", start=(k == 0), stop=(k == ND - 1))
            e_act(em, sq_s[:, :], x1c[:, :], AF.Square,
                  ["x1c"], "sq")
            e_mm(em, ps_st2[:, :], ones_s[:, 0:1], sq_s[:, :], ["sq", "ones"],
                 "st2", start=(k == 0), stop=(k == ND - 1))
        ln_rows(em, ps_st[:, :], ps_st2[:, :], mu2_r[:, :], rs2_r[:, :],
                rowtmp[:, :], 1e-5, D, "st", "st2", "mu2", "rs2", "rt")
        e_bcast(em, ps_r[:, :], ones_s, mu2_r[:, :], "mu2", "psr")
        e_bcast(em, ps_r2[:, :], ones_s, rs2_r[:, :], "rs2", "psr2")
        for k in range(ND):
            em.dma("sync", x1c[:, :], x1_d[ts(k, 128), ts(c, 512)],
                   reads=["x1out"], writes=["x1c"])
            e_tt(em, tmp_a[:, :], x1c[:, :], ps_r[:, :], ALU.subtract,
                 ["x1c", "psr"], "tmpa")
            e_tt(em, tmp_a[:, :], tmp_a[:, :], ps_r2[:, :], ALU.mult,
                 ["tmpa", "psr2"], "tmpa")
            em.dma("gpsimd", xn1_d[ts(k, 128), ts(c, 512)], tmp_a[:, :],
                   reads=["tmpa"], writes=["xn1out"])

    em.finalize()
    return nc


def l0_in_maps(inputs):
    x = np.asarray(inputs["L0_attn_x"])
    w_qkv = np.ascontiguousarray(inputs["L0_attn_w_qkv"])
    b_qkv = np.asarray(inputs["L0_attn_b_qkv"])
    w_fc = np.ascontiguousarray(inputs["L0_attn_w_fc"])
    b_fc = np.asarray(inputs["L0_attn_b_fc"])
    gamma = np.asarray(inputs["L0_attn_gamma"])
    beta = np.asarray(inputs["L0_attn_beta"])
    gate_W = np.asarray(inputs["L0_ffn_gate_W"])
    expert_W = np.asarray(inputs["L0_ffn_expert_W"])
    out_W = np.asarray(inputs["L0_ffn_out_W"])

    common = {
        "w_qkv": w_qkv,
        "bqkv_row": b_qkv.reshape(1, -1),
        "bqkv_col": np.ascontiguousarray(b_qkv.reshape(12, 128).T),
        "w_fc": w_fc,
        "bfc_col": np.ascontiguousarray(b_fc.reshape(8, 128).T),
        "gam_col": np.ascontiguousarray(gamma.reshape(8, 128).T),
        "bet_col": np.ascontiguousarray(beta.reshape(8, 128).T),
        "gate_wT": np.ascontiguousarray(gate_W.T),
        "ew0T": np.ascontiguousarray(expert_W[0].T),
        "ew1T": np.ascontiguousarray(expert_W[1].T),
        "out_wT": np.ascontiguousarray(out_W.T),
        "iden": np.eye(128, dtype=np.float32),
    }
    maps = []
    for core in range(8):
        b, half = core // 2, core % 2
        xb = x[b]
        own = xb[half * T_OWN:(half + 1) * T_OWN]
        oth = xb[(1 - half) * T_OWN:(2 - half) * T_OWN]
        xf = np.ascontiguousarray(
            np.concatenate([own, oth], axis=0).T)
        m = dict(common)
        m["xf"] = xf
        maps.append(m)
    return maps


def run_l0(inputs):
    nc = build_l0()
    maps = l0_in_maps(inputs)
    res = run_bass_kernel_spmd(nc, maps, core_ids=list(range(8)))
    x1 = np.zeros((B, L, D), np.float32)
    xn1 = np.zeros((B, L, D), np.float32)
    for core in range(8):
        b, half = core // 2, core % 2
        sl = slice(half * T_OWN, (half + 1) * T_OWN)
        x1[b, sl] = res.results[core]["x1"].T
        xn1[b, sl] = res.results[core]["xn1"].T
    return x1, xn1


D_IN = 2048
DST = 16
TC = 512
NCT = D_IN // 128


def build_mamba():
    nc = bass.Bass("TRN2", target_bir_lowering=False)
    ND = D // 128

    NRT0 = 2 * D_IN // 128
    xn1f = nc.dram_tensor("xn1f", [D, L], F32, kind="ExternalInput").ap()
    in_wT = nc.dram_tensor("in_t", [128, NRT0 * ND * 128], F32,
                           kind="ExternalInput").ap()
    xprojT = nc.dram_tensor("xprojT", [D_IN, 96], F32, kind="ExternalInput").ap()
    dt_wT = nc.dram_tensor("dt_wT", [64, D_IN], F32, kind="ExternalInput").ap()
    conv_c_d = nc.dram_tensor("conv_c", [128, NCT * 4], F32, kind="ExternalInput").ap()
    convb_c_d = nc.dram_tensor("convb_c", [128, NCT], F32, kind="ExternalInput").ap()
    dtb_c_d = nc.dram_tensor("dtb_c", [128, NCT], F32, kind="ExternalInput").ap()
    alog_d = nc.dram_tensor("alog", [128, NCT * DST], F32, kind="ExternalInput").ap()
    dcol_d = nc.dram_tensor("dcol", [128, NCT], F32, kind="ExternalInput").ap()
    mout_wT = nc.dram_tensor("mout_t", [128, ND * NCT * 128], F32,
                             kind="ExternalInput").ap()
    outw_sT = nc.dram_tensor("outw_t", [128, ND * ND * 128], F32,
                             kind="ExternalInput").ap()
    gam_col_d = nc.dram_tensor("gam_col", [128, ND], F32, kind="ExternalInput").ap()
    bet_col_d = nc.dram_tensor("bet_col", [128, ND], F32, kind="ExternalInput").ap()
    esel_d = nc.dram_tensor("esel", [32, 32 * 128], F32, kind="ExternalInput").ap()
    part_d = nc.dram_tensor("partial", [D, L], F32, kind="ExternalOutput").ap()
    inw_b_d = nc.dram_tensor("inw_b", [128, NRT0 * ND * 128], BF16).ap()
    moutw_b_d = nc.dram_tensor("moutw_b", [128, ND * NCT * 128], BF16).ap()
    outw_b_d = nc.dram_tensor("outw_b", [128, ND * ND * 128], BF16).ap()

    em = EM(nc)

    ones_s = sbuf(nc, "ones_s", [128, 128])
    wtmp = sbuf(nc, "wtmp", [128, 512])
    wbtmp = sbuf(nc, "wbtmp", [128, 512], BF16)
    wstr = sbuf(nc, "wstr", [128, 2 * 2048], BF16)
    xproj_s = sbuf(nc, "xproj_s", [128, NCT * 96], BF16)
    dtw_s = sbuf(nc, "dtw_s", [64, D_IN], BF16)
    esel_s = sbuf(nc, "esel_s", [32, 32 * 128], BF16)
    conv_c = sbuf(nc, "conv_c_s", [128, NCT * 4])
    convb_c = sbuf(nc, "convb_c_s", [128, NCT])
    dtb_c = sbuf(nc, "dtb_c_s", [128, NCT])
    a_s = sbuf(nc, "a_s", [128, NCT * DST])
    dcol_s = sbuf(nc, "dcol_s", [128, NCT])
    gam_col = sbuf(nc, "gam_col_s", [128, ND])
    bet_col = sbuf(nc, "bet_col_s", [128, ND])
    xch = sbuf(nc, "xch", [128, ND * TC])
    hchb = sbuf(nc, "hchb", [128, ND * TC], BF16)
    xi_s = sbuf(nc, "xi_s", [128, NCT * (TC + 3)], BF16)
    z_s = sbuf(nc, "z_s", [128, NCT * TC], BF16)
    xcb_s = sbuf(nc, "xcb_s", [128, NCT * TC], BF16)
    ygb_s = sbuf(nc, "ygb_s", [128, NCT * TC], BF16)
    xdbl_s = sbuf(nc, "xdbl_s", [96, TC], BF16)
    xdbl_bc = sbuf(nc, "xdbl_bc", [32, TC], BF16)
    delta_s = sbuf(nc, "delta_s", [128, 2 * TC])
    du_s = sbuf(nc, "du_s", [128, 2 * TC])
    tsp_s = sbuf(nc, "tsp_s", [128, 2 * TC])
    av_s = sbuf(nc, "av_s", [128, 2 * TC])
    bv_s = sbuf(nc, "bv_s", [128, 2 * TC])
    hv_s = sbuf(nc, "hv_s", [128, 2 * TC])
    ty_s = sbuf(nc, "ty_s", [128, 2 * TC])
    yv_s = sbuf(nc, "yv_s", [128, TC])
    bc_c = sbuf(nc, "bc_c", [128, DST * TC], BF16)
    cc_c = sbuf(nc, "cc_c", [128, DST * TC], BF16)
    t1_s = sbuf(nc, "t1_s", [128, TC])
    t2_s = sbuf(nc, "t2_s", [128, TC])
    states = sbuf(nc, "states", [128, NCT * DST])
    p1_s = sbuf(nc, "p1_s", [128, ND * TC], BF16)
    mu_r = sbuf(nc, "mu_r", [1, TC])
    rs_r = sbuf(nc, "rs_r", [1, TC])
    rowtmp = sbuf(nc, "rowtmp", [1, TC])
    sq_s = sbuf(nc, "sq_s", [128, TC])

    ps_m = psum_t(nc, "ps_m", [128, 512])
    ps_x = psum_t(nc, "ps_x", [128, 512])
    ps_bc = psum_t(nc, "ps_bc", [128, 512])
    ps_cc = psum_t(nc, "ps_cc", [128, 512])
    ps_st = psum_t(nc, "ps_st", [1, 512])
    ps_st2 = psum_t(nc, "ps_st2", [1, 512])
    ps_r = psum_t(nc, "ps_r", [128, 512])
    ps_r2 = psum_t(nc, "ps_r2", [128, 512])

    em.emit("gpsimd", lambda g: g.memset(ones_s[:, :], 1.0), writes=["ones"])
    em.emit("gpsimd", lambda g: g.memset(xi_s[:, :], 0.0), writes=["xi"])
    em.dma("sync", conv_c[:, :], conv_c_d[:, :], writes=["convc"])
    em.dma("sync", convb_c[:, :], convb_c_d[:, :], writes=["convb"])
    em.dma("sync", dtb_c[:, :], dtb_c_d[:, :], writes=["dtb"])
    em.dma("sync", dcol_s[:, :], dcol_d[:, :], writes=["dcol"])
    em.dma("sync", gam_col[:, :], gam_col_d[:, :], writes=["gam"])
    em.dma("sync", bet_col[:, :], bet_col_d[:, :], writes=["bet"])
    em.dma("sync", a_s[:, :], alog_d[:, :], writes=["as"])
    em.emit("scalar", lambda s: s.activation(out=a_s[:, :], in_=a_s[:, :],
                                             func=AF.Exp),
            reads=["as"], writes=["as"])
    e_ts(em, a_s[:, :], a_s[:, :], -1.0, None, ALU.mult, None, ["as"], "as")
    for i in range(8):
        em.dma("sync", wtmp[0:32, 0:512], esel_d[:, ts(i, 512)], writes=["wtmp"])
        e_copy(em, esel_s[:, ts(i, 512)], wtmp[0:32, 0:512], ["wtmp"], "esel")
    for ct in range(NCT):
        em.dma("sync", wtmp[:, 0:96], xprojT[ts(ct, 128), :], writes=["wtmp"])
        e_copy(em, xproj_s[:, ts(ct, 96)], wtmp[:, 0:96], ["wtmp"], "xprj")
    for i in range(4):
        em.dma("sync", wtmp[0:64, 0:512], dt_wT[:, ts(i, 512)], writes=["wtmp"])
        e_copy(em, dtw_s[:, ts(i, 512)], wtmp[0:64, 0:512], ["wtmp"], "dtw")
    def cast_to_dram(src_ap, dst_ap, cols, tag):
        CW = 2048
        for o in range(0, cols, CW):
            par = (o // CW) % 2
            if par == 0:
                lbuf = xch[:, 0:CW]
                lkeys = ["xch0", "xch1", "xch2", "xch3"]
                stg, skey = xcb_s[:, 0:CW], "xcb"
            else:
                lbuf = xch[:, CW:2 * CW]
                lkeys = ["xch4", "xch5", "xch6", "xch7"]
                stg, skey = z_s[:, 0:CW], "z"
            em.dma(("sync", "scalar")[par], lbuf, src_ap[:, o:o + CW],
                   writes=lkeys)
            e_copy(em, stg, lbuf, lkeys, skey,
                   eng=("scalar" if par == 0 else "vector"))
            em.dma(("gpsimd", "sync")[par], dst_ap[:, o:o + CW],
                   stg, reads=[skey], writes=[tag])

    cast_to_dram(in_wT, inw_b_d, NRT0 * ND * 128, "inwb")
    cast_to_dram(mout_wT, moutw_b_d, ND * NCT * 128, "moutb")
    cast_to_dram(outw_sT, outw_b_d, ND * ND * 128, "outwb")

    NRT = 2 * D_IN // 128

    def ph_pre(c):
        csl = ts(c, TC)
        for k in range(ND):
            em.dma("sync", xch[:, ts(k, TC)], xn1f[ts(k, 128), csl],
                   writes=["xch%d" % k])
            e_mm(em, ps_st[:, :], ones_s[:, 0:1], xch[:, ts(k, TC)],
                 ["xch%d" % k, "ones"], "st", start=(k == 0), stop=(k == ND - 1))
            e_act(em, sq_s[:, :], xch[:, ts(k, TC)], AF.Square,
                  ["xch%d" % k], "sq")
            e_mm(em, ps_st2[:, :], ones_s[:, 0:1], sq_s[:, :], ["sq", "ones"],
                 "st2", start=(k == 0), stop=(k == ND - 1))
        ln_rows(em, ps_st[:, :], ps_st2[:, :], mu_r[:, :], rs_r[:, :],
                rowtmp[:, :], 1e-5, D, "st", "st2", "mu", "rs", "rt")
        e_bcast(em, ps_r[:, :], ones_s, mu_r[:, :], "mu", "psr")
        e_bcast(em, ps_r2[:, :], ones_s, rs_r[:, :], "rs", "psr2")
        for k in range(ND):
            e_tt(em, sq_s[:, :], xch[:, ts(k, TC)], ps_r[:, :], ALU.subtract,
                 ["xch%d" % k, "psr"], "sq")
            e_tt(em, sq_s[:, :], sq_s[:, :], ps_r2[:, :], ALU.mult,
                 ["sq", "psr2"], "sq")
            e_ts(em, hchb[:, ts(k, TC)], sq_s[:, :], gam_col[:, k:k + 1],
                 bet_col[:, k:k + 1], ALU.mult, ALU.add,
                 ["sq", "gam", "bet"], "hchb")
    def ph_mid(c):
        for rt in range(NRT):
            buf = rt % 2
            bank, bk = (ps_m, "psm") if buf == 0 else (ps_x, "psx")
            em.dma(("sync", "scalar")[buf],
                   wstr[:, buf * 2048: buf * 2048 + 1024],
                   inw_b_d[:, rt * 1024:(rt + 1) * 1024],
                   reads=["inwb"], writes=["wstr%d" % buf])
            for k in range(ND):
                e_mm(em, bank[:, :],
                     wstr[:, buf * 2048 + k * 128: buf * 2048 + (k + 1) * 128],
                     hchb[:, ts(k, TC)], ["wstr%d" % buf, "hchb"], bk,
                     start=(k == 0), stop=(k == ND - 1))
            if rt < NCT:
                ct = rt
                e_copy(em, xi_s[:, ct * (TC + 3): ct * (TC + 3) + 3],
                       xi_s[:, ct * (TC + 3) + TC: ct * (TC + 3) + TC + 3],
                       ["xi"], "xi", eng="vector")
                e_copy(em, xi_s[:, ct * (TC + 3) + 3: (ct + 1) * (TC + 3)],
                       bank[:, :], [bk, "xi"], "xi")
            else:
                ct = rt - NCT
                e_copy(em, z_s[:, ts(ct, TC)], bank[:, :], [bk], "z")
        for ct in range(NCT):
            base = ct * (TC + 3)
            e_ts(em, t1_s[:, :], xi_s[:, base: base + TC],
                 conv_c[:, ct * 4: ct * 4 + 1], None, ALU.mult, None,
                 ["xi", "convc"], "t1")
            for j in range(1, 4):
                e_stt(em, t1_s[:, :], xi_s[:, base + j: base + j + TC],
                      conv_c[:, ct * 4 + j: ct * 4 + j + 1], t1_s[:, :],
                      ALU.mult, ALU.add, ["xi", "convc", "t1"], "t1")
            e_act(em, xcb_s[:, ts(ct, TC)], t1_s[:, :], AF.Silu,
                  ["t1", "convb"], "xcb", bias=convb_c[:, ct:ct + 1])
        for ct in range(NCT):
            e_mm(em, ps_x[0:96, :], xproj_s[:, ts(ct, 96)], xcb_s[:, ts(ct, TC)],
                 ["xprj", "xcb"], "psx", start=(ct == 0), stop=(ct == NCT - 1))
        e_copy(em, xdbl_s[:, :], ps_x[0:96, :], ["psx"], "xdbl")
        em.dma("gpsimd", xdbl_bc[:, :], xdbl_s[64:96, :], reads=["xdbl"],
               writes=["xdblbc"])
        for s in range(DST):
            bank, bk = (ps_bc, "psbc") if s % 2 == 0 else (ps_cc, "pscc")
            e_mm(em, bank[:, :], esel_s[:, s * 128:(s + 1) * 128],
                 xdbl_bc[:, :], ["esel", "xdblbc"], bk, start=True, stop=True)
            e_copy(em, bc_c[:, ts(s, TC)], bank[:, :], [bk], "bcc",
                   eng=("scalar" if s % 2 == 0 else "vector"))
        for s in range(DST):
            bank, bk = (ps_bc, "psbc") if s % 2 == 0 else (ps_cc, "pscc")
            e_mm(em, bank[:, :], esel_s[:, (16 + s) * 128:(17 + s) * 128],
                 xdbl_bc[:, :], ["esel", "xdblbc"], bk, start=True, stop=True)
            e_copy(em, cc_c[:, ts(s, TC)], bank[:, :], [bk], "ccc",
                   eng=("scalar" if s % 2 == 0 else "vector"))
    def ph_scan(c):
        for ct in range(NCT):
            cp = ct % 2
            dsl = delta_s[:, ts(cp, TC)]
            dusl = du_s[:, ts(cp, TC)]
            dk, duk = "delta%d" % cp, "du%d" % cp
            bank, bk = (ps_m, "psm") if cp == 0 else (ps_x, "psx")
            tsa = tsp_s[:, ts(cp, TC)]
            tsk = "tsp%d" % cp
            e_mm(em, bank[:, :], dtw_s[:, ts(ct, 128)], xdbl_s[0:64, :],
                 ["dtw", "xdbl"], bk, start=True, stop=True)
            e_act(em, tsa, bank[:, :], AF.Abs, [bk, "dtb"], tsk,
                  bias=dtb_c[:, ct:ct + 1])
            e_act(em, tsa, tsa, AF.Exp, [tsk], tsk, scale=-1.0)
            e_act(em, tsa, tsa, AF.Ln, [tsk], tsk, bias=1.0)
            e_act(em, dsl, bank[:, :], AF.Relu, [bk, "dtb"], dk,
                  bias=dtb_c[:, ct:ct + 1])
            e_tt(em, dsl, dsl, tsa, ALU.add, [dk, tsk], dk)
            e_tt(em, dusl, dsl, xcb_s[:, ts(ct, TC)], ALU.mult,
                 [dk, "xcb"], duk)
            for s in range(DST):
                sp = s % 2
                av = av_s[:, ts(sp, TC)]
                bv = bv_s[:, ts(sp, TC)]
                hv = hv_s[:, ts(sp, TC)]
                avk, bvk, hvk = "av%d" % sp, "bv%d" % sp, "hv%d" % sp
                e_act(em, av, dsl, AF.Exp, [dk, "as"],
                      avk, scale=a_s[:, ct * DST + s: ct * DST + s + 1])
                e_tt(em, bv, dusl, bc_c[:, ts(s, TC)], ALU.mult,
                     [duk, "bcc"], bvk)
                stcol = states[:, ct * DST + s: ct * DST + s + 1]
                stkey = "stt%d_%d" % (ct, s)
                if c == 0:
                    em.emit("vector",
                            lambda v, av=av, bv=bv, hv=hv:
                            v.tensor_tensor_scan(
                                out=hv, data0=av, data1=bv,
                                initial=0.0, op0=ALU.mult, op1=ALU.add),
                            reads=[avk, bvk], writes=[hvk])
                else:
                    em.emit("vector",
                            lambda v, av=av, bv=bv, hv=hv, stcol=stcol:
                            v.tensor_tensor_scan(
                                out=hv, data0=av, data1=bv,
                                initial=stcol, op0=ALU.mult, op1=ALU.add),
                            reads=[avk, bvk, stkey], writes=[hvk])
                e_copy(em, stcol, hv_s[:, sp * TC + TC - 1: sp * TC + TC],
                       [hvk], stkey)
                if s == 0:
                    e_tt(em, yv_s[:, :], hv, cc_c[:, ts(s, TC)], ALU.mult,
                         [hvk, "ccc"], "yv")
                else:
                    tyv = ty_s[:, ts(sp, TC)]
                    e_tt(em, tyv, hv, cc_c[:, ts(s, TC)], ALU.mult,
                         [hvk, "ccc"], "ty%d" % sp)
                    e_tt(em, yv_s[:, :], yv_s[:, :], tyv, ALU.add,
                         ["yv", "ty%d" % sp], "yv")
            e_stt(em, yv_s[:, :], xcb_s[:, ts(ct, TC)], dcol_s[:, ct:ct + 1],
                  yv_s[:, :], ALU.mult, ALU.add, ["xcb", "dcol", "yv"], "yv")
            e_act(em, t1_s[:, :], z_s[:, ts(ct, TC)], AF.Silu, ["z"], "t1")
            e_tt(em, ygb_s[:, ts(ct, TC)], yv_s[:, :], t1_s[:, :], ALU.mult,
                 ["yv", "t1"], "ygb")
    def ph_post(c):
        csl = ts(c, TC)
        for dt in range(ND):
            sl2 = dt % 2
            bank, bk = (ps_m, "psm") if sl2 == 0 else (ps_x, "psx")
            em.dma("sync", wstr[:, sl2 * 2048: sl2 * 2048 + 1024],
                   moutw_b_d[:, dt * 2048: dt * 2048 + 1024],
                   reads=["moutb"], writes=["wstr%d" % sl2])
            em.dma("scalar", wstr[:, sl2 * 2048 + 1024: (sl2 + 1) * 2048],
                   moutw_b_d[:, dt * 2048 + 1024:(dt + 1) * 2048],
                   reads=["moutb"], writes=["wstr%d" % sl2])
            for ct in range(NCT):
                e_mm(em, bank[:, :],
                     wstr[:, sl2 * 2048 + ct * 128:
                          sl2 * 2048 + (ct + 1) * 128],
                     ygb_s[:, ts(ct, TC)], ["wstr%d" % sl2, "ygb"], bk,
                     start=(ct == 0), stop=(ct == NCT - 1))
            e_copy(em, p1_s[:, ts(dt, TC)], bank[:, :], [bk], "p1",
                   eng=("scalar" if sl2 == 0 else "vector"))
        for dt in range(ND):
            sl2 = dt % 2
            bank, bk = (ps_m, "psm") if sl2 == 0 else (ps_x, "psx")
            em.dma(("sync", "scalar")[sl2],
                   wstr[:, sl2 * 2048: sl2 * 2048 + 1024],
                   outw_b_d[:, dt * 1024:(dt + 1) * 1024],
                   reads=["outwb"], writes=["wstr%d" % sl2])
            for j in range(ND):
                e_mm(em, bank[:, :],
                     wstr[:, sl2 * 2048 + j * 128: sl2 * 2048 + (j + 1) * 128],
                     p1_s[:, ts(j, TC)], ["wstr%d" % sl2, "p1"], bk,
                     start=(j == 0), stop=(j == ND - 1))
            e_copy(em, t1_s[:, :], bank[:, :], [bk], "t1")
            em.dma(("gpsimd", "sync")[sl2], part_d[ts(dt, 128), csl],
                   t1_s[:, :], reads=["t1"], writes=["pout"])

    NCH = L // TC
    ph_pre(0)
    ph_mid(0)
    for c in range(NCH):
        if c + 1 < NCH:
            ph_pre(c + 1)
        ph_scan(c)
        if c + 1 < NCH:
            ph_mid(c + 1)
        ph_post(c)

    em.finalize()
    return nc


def _tilemaj(wmat, nr, nccols):
    return np.ascontiguousarray(
        np.float32(wmat).reshape(nr, 128, nccols, 128)
        .transpose(1, 2, 0, 3).reshape(128, -1))


def mamba_in_maps(inputs, xn1):
    esel = np.zeros((32, 32 * 128), np.float32)
    for i in range(32):
        esel[i, i * 128:(i + 1) * 128] = 1.0
    maps = []
    for core in range(8):
        b, dirb = core // 2, core % 2
        tag = "f" if dirb == 0 else "b"
        arr = xn1[b]
        if dirb == 1:
            arr = arr[::-1]
        m = {
            "xn1f": np.ascontiguousarray(arr.T),
            "in_t": _tilemaj(np.asarray(inputs["L1_attn_in_w_" + tag]).T,
                             D // 128, 2 * D_IN // 128),
            "xprojT": np.ascontiguousarray(
                np.asarray(inputs["L1_attn_xproj_w_" + tag]).T),
            "dt_wT": np.ascontiguousarray(
                np.asarray(inputs["L1_attn_dt_w_" + tag]).T),
            "conv_c": np.ascontiguousarray(
                np.asarray(inputs["L1_attn_conv_w_" + tag]).reshape(
                    D_IN, 4).reshape(NCT, 128, 4).transpose(1, 0, 2).reshape(
                        128, NCT * 4)),
            "convb_c": np.ascontiguousarray(
                np.asarray(inputs["L1_attn_conv_b_" + tag]).reshape(
                    NCT, 128).T),
            "dtb_c": np.ascontiguousarray(
                np.asarray(inputs["L1_attn_dt_b_" + tag]).reshape(NCT, 128).T),
            "alog": np.ascontiguousarray(
                np.asarray(inputs["L1_attn_A_log_" + tag]).reshape(
                    NCT, 128, DST).transpose(1, 0, 2).reshape(128, NCT * DST)),
            "dcol": np.ascontiguousarray(
                np.asarray(inputs["L1_attn_D_" + tag]).reshape(NCT, 128).T),
            "mout_t": _tilemaj(np.asarray(inputs["L1_attn_mout_w_" + tag]).T,
                               D_IN // 128, D // 128),
            "outw_t": _tilemaj(
                np.asarray(inputs["L1_attn_out_w"])[:,
                    dirb * D:(dirb + 1) * D].T, D // 128, D // 128),
            "gam_col": np.ascontiguousarray(
                np.asarray(inputs["L1_attn_gamma"]).reshape(8, 128).T),
            "bet_col": np.ascontiguousarray(
                np.asarray(inputs["L1_attn_beta"]).reshape(8, 128).T),
            "esel": esel,
        }
        maps.append(m)
    return maps


def run_mamba(inputs, xn1):
    nc = build_mamba()
    maps = mamba_in_maps(inputs, xn1)
    res = run_bass_kernel_spmd(nc, maps, core_ids=list(range(8)))
    pf = np.zeros((B, L, D), np.float32)
    pb = np.zeros((B, L, D), np.float32)
    for core in range(8):
        b, dirb = core // 2, core % 2
        p = res.results[core]["partial"].T
        if dirb == 1:
            p = p[::-1]
            pb[b] = p
        else:
            pf[b] = p
    return pf, pb


TS = 2560
NW = TS // 256
HEADS = 16
DH = 64
INNER = 1024
FFI = 2816


FFN_SKIP = set()


def build_ffn():
    nc = bass.Bass("TRN2", target_bir_lowering=False)
    ND = D // 128
    NI = INNER // 128
    NJC = FFI // 128

    xst_d = nc.dram_tensor("xst", [D, 1 + TS], F32, kind="ExternalInput").ap()
    wqk_d = nc.dram_tensor("wqk_t", [128, 2 * NI * ND * 128], F32,
                           kind="ExternalInput").ap()
    wv_d = nc.dram_tensor("wv_t", [128, 2 * ND * 512], F32,
                          kind="ExternalInput").ap()
    wo_d = nc.dram_tensor("wo_t", [128, ND * NI * 128], F32,
                          kind="ExternalInput").ap()
    w1a_d = nc.dram_tensor("w1a_t", [128, NJC * ND * 128], F32,
                           kind="ExternalInput").ap()
    w1g_d = nc.dram_tensor("w1g_t", [128, NJC * ND * 128], F32,
                           kind="ExternalInput").ap()
    w2_d = nc.dram_tensor("w2_t", [128, ND * NJC * 128], F32,
                          kind="ExternalInput").ap()
    b1a_d = nc.dram_tensor("b1a", [128, NJC], F32, kind="ExternalInput").ap()
    b1g_d = nc.dram_tensor("b1g", [128, NJC], F32, kind="ExternalInput").ap()
    b2_d = nc.dram_tensor("b2c", [128, ND], F32, kind="ExternalInput").ap()
    lng_d = nc.dram_tensor("lng", [128, ND], F32, kind="ExternalInput").ap()
    lnb_d = nc.dram_tensor("lnb", [128, ND], F32, kind="ExternalInput").ap()
    qs_d = nc.dram_tensor("qs", [128, 1], F32, kind="ExternalInput").ap()
    ks_d = nc.dram_tensor("ks", [128, 1], F32, kind="ExternalInput").ap()
    cs_d = nc.dram_tensor("cs_t", [128, TS], F32, kind="ExternalInput").ap()
    ssn_d = nc.dram_tensor("ssn_t", [128, TS], F32, kind="ExternalInput").ap()
    csk_d = nc.dram_tensor("csk_t", [128, TS], F32, kind="ExternalInput").ap()
    ssnk_d = nc.dram_tensor("ssnk_t", [128, TS], F32, kind="ExternalInput").ap()
    flag_d = nc.dram_tensor("flagrow", [1, NW * 2 * 384], F32,
                            kind="ExternalInput").ap()
    band_d = nc.dram_tensor("band", [128, 384], F32, kind="ExternalInput").ap()
    cmask_d = nc.dram_tensor("cmask", [1, TS], F32, kind="ExternalInput").ap()
    blk16_d = nc.dram_tensor("blk16", [128, NI * 16], F32,
                             kind="ExternalInput").ap()
    esel2_d = nc.dram_tensor("esel2", [16, NI * 128], F32,
                             kind="ExternalInput").ap()
    iden_d = nc.dram_tensor("iden", [128, 128], F32, kind="ExternalInput").ap()
    rperm_d = nc.dram_tensor("rperm", [128, 128], F32, kind="ExternalInput").ap()
    xout_d = nc.dram_tensor("xout", [D, T_OWN], F32, kind="ExternalOutput").ap()
    wqk_b = nc.dram_tensor("wqk_b", [128, 2 * NI * ND * 128], BF16).ap()
    wv_b = nc.dram_tensor("wv_b", [128, 2 * ND * 512], BF16).ap()
    wo_b = nc.dram_tensor("wo_b", [128, ND * NI * 128], BF16).ap()
    w1a_b = nc.dram_tensor("w1a_b", [128, NJC * ND * 128], BF16).ap()
    w1g_b = nc.dram_tensor("w1g_b", [128, NJC * ND * 128], BF16).ap()
    w2_b = nc.dram_tensor("w2_b", [128, ND * NJC * 128], BF16).ap()

    em = EM(nc)

    xs = sbuf(nc, "xs", [128, ND * (1 + TS)])
    wqkstr = sbuf(nc, "wqkstr", [128, 2 * 1024], BF16)
    wostr = sbuf(nc, "wostr", [128, 2 * NI * 128], BF16)
    wvstr = sbuf(nc, "wvstr", [128, 2 * 1024], BF16)
    ones_s = sbuf(nc, "ones_s", [128, 128])
    rperm_s = sbuf(nc, "rperm_s", [128, 128])
    eye_b = sbuf(nc, "eye_b", [128, 128], BF16)
    wtmp = sbuf(nc, "wtmp", [128, 512])
    w1str = sbuf(nc, "w1str", [128, 2048], BF16)
    tf1 = sbuf(nc, "tf1", [128, 2 * 512])
    tf2 = sbuf(nc, "tf2", [128, 2 * 512])
    xwb = sbuf(nc, "xwb", [128, ND * 256], BF16)
    qro = sbuf(nc, "qro", [128, NI * 256], BF16)
    kro = sbuf(nc, "kro", [128, NI * 512], BF16)
    vtm = sbuf(nc, "vtm", [128, 4 * INNER], BF16)
    q1_s = sbuf(nc, "q1_s", [128, 2 * 256])
    cs_s = sbuf(nc, "cs_s", [128, 256])
    ssn_s = sbuf(nc, "ssn_s", [128, 256])
    csk_s = sbuf(nc, "csk_s", [128, 256])
    ssnk_s = sbuf(nc, "ssnk_s", [128, 256])
    rr_s = sbuf(nc, "rr_s", [16, 512])
    rr_b = sbuf(nc, "rr_b", [16, 512], BF16)
    blk16_s = sbuf(nc, "blk16_s", [128, NI * 16])
    esel2_b = sbuf(nc, "esel2_b", [16, NI * 128], BF16)
    pexp_b = sbuf(nc, "pexp_b", [128, 4 * 384], BF16)
    pt_b = sbuf(nc, "pt_b", [128, 2 * 384], BF16)
    col_s = sbuf(nc, "col_s", [128, 16])
    attm = sbuf(nc, "attm", [128, INNER], BF16)
    attf = sbuf(nc, "attf", [128, NI * 256], BF16)
    band_b = sbuf(nc, "band_b", [128, 384], BF16)
    ones_bf = sbuf(nc, "ones_bf", [1, 128], BF16)
    flag_r = sbuf(nc, "flag_r", [1, 384])
    flag_b = sbuf(nc, "flag_b", [1, 384], BF16)
    cmask_r = sbuf(nc, "cmask_r", [1, 512])
    qs_c = sbuf(nc, "qs_c", [128, 2])
    b1a_c = sbuf(nc, "b1a_c", [128, NJC])
    b1g_c = sbuf(nc, "b1g_c", [128, NJC])
    b2_c = sbuf(nc, "b2_c", [128, ND])
    lng_c = sbuf(nc, "lng_c", [128, ND])
    lnb_c = sbuf(nc, "lnb_c", [128, ND])
    xnb = sbuf(nc, "xnb", [128, ND * 512], BF16)
    hgb = sbuf(nc, "hgb", [128, NJC * 512], BF16)
    fout = sbuf(nc, "fout", [128, ND * 512], BF16)
    mu_r = sbuf(nc, "mu_r", [1, 512])
    rs_r = sbuf(nc, "rs_r", [1, 512])
    rowtmp = sbuf(nc, "rowtmp", [1, 512])

    ps = [psum_t(nc, "psg%d" % i, [128, 512]) for i in range(6)]
    ps_r = psum_t(nc, "ps_r", [128, 512])
    ps_r2 = psum_t(nc, "ps_r2", [128, 512])
    ps_tb = ps[4][:, 256:320].bitcast(BF16)
    ps_tb2 = ps_r[:, 0:64].bitcast(BF16)
    PK = ["psg%d" % i for i in range(6)]

    em.emit("gpsimd", lambda g: g.memset(ones_s[:, :], 1.0), writes=["ones"])
    em.emit("gpsimd", lambda g: g.memset(kro[:, :], 0.0), writes=["kro"])
    em.emit("gpsimd", lambda g: g.memset(vtm[:, :], 0.0), writes=["vtm"])
    e_copy(em, ones_bf[:, :], ones_s[0:1, :], ["ones"], "onesb")
    em.dma("sync", wtmp[:, 0:384], band_d[:, :], writes=["wtmp0"])
    e_copy(em, band_b[:, :], wtmp[:, 0:384], ["wtmp0"], "bandb")
    em.dma("sync", blk16_s[:, :], blk16_d[:, :], writes=["blk16"])
    em.dma("sync", b1a_c[:, :], b1a_d[:, :], writes=["b1a"])
    em.dma("sync", b1g_c[:, :], b1g_d[:, :], writes=["b1g"])
    em.dma("sync", b2_c[:, :], b2_d[:, :], writes=["b2"])
    em.dma("sync", lng_c[:, :], lng_d[:, :], writes=["lng"])
    em.dma("sync", lnb_c[:, :], lnb_d[:, :], writes=["lnb"])
    for i in range(2):
        em.dma("sync", wtmp[0:16, 0:512], esel2_d[:, ts(i, 512)], writes=["wtmp0"])
        e_copy(em, esel2_b[:, ts(i, 512)], wtmp[0:16, 0:512], ["wtmp0"], "esel2")
    em.dma("sync", rperm_s[:, :], rperm_d[:, :], writes=["rperm"])
    em.dma("sync", wtmp[:, 0:128], iden_d[:, :], writes=["wtmp0"])
    e_copy(em, eye_b[:, :], wtmp[:, 0:128], ["wtmp0"], "eye")
    em.dma("sync", qs_c[:, 0:1], qs_d[:, :], writes=["qsc"])
    em.dma("sync", wtmp[:, 0:1], ks_d[:, :], writes=["wtmp0"])
    e_copy(em, qs_c[:, 1:2], wtmp[:, 0:1], ["wtmp0"], "qsc", eng="vector")
    e_ts(em, qs_c[:, 0:1], qs_c[:, 0:1], 8.0, None, ALU.mult, None,
         ["qsc"], "qsc")

    for k in range(ND):
        em.dma("sync", xs[:, k * (1 + TS):(k + 1) * (1 + TS)],
               xst_d[ts(k, 128), :], writes=["xs"])

    QKQ = ("sync", "scalar", "gpsimd", "sync")

    def cast_to_dram(src_ap, dst_ap, cols, tag):
        CW = 1024
        qi = 0
        for o in range(0, cols, CW):
            wd = min(CW, cols - o)
            par = qi % 2
            lbuf, lkeys = (tf1, ["tf10", "tf11"]) if par == 0 else \
                (tf2, ["tf20", "tf21"])
            stgb, skeys = (hgb, ["hgb0", "hgb1"]) if par == 0 else \
                (fout, ["fout"])
            stg = stgb[:, 0:wd]
            em.dma(QKQ[qi % 4], lbuf[:, 0:wd],
                   src_ap[:, o:o + wd], writes=lkeys)
            if qi % 2 == 0:
                em.emit("scalar", lambda s, stg=stg, lb=lbuf[:, 0:wd]:
                        s.copy(out=stg, in_=lb), reads=lkeys, writes=skeys)
            else:
                em.emit("vector", lambda v, stg=stg, lb=lbuf[:, 0:wd]:
                        v.tensor_copy(out=stg, in_=lb), reads=lkeys,
                        writes=skeys)
            em.dma(QKQ[(qi + 2) % 4], dst_ap[:, o:o + wd],
                   stg, reads=skeys, writes=[tag])
            qi += 1

    cast_to_dram(wqk_d, wqk_b, 2 * NI * ND * 128, "wqkb")
    cast_to_dram(wv_d, wv_b, 2 * ND * 512, "wvb")
    cast_to_dram(wo_d, wo_b, ND * NI * 128, "wob")
    cast_to_dram(w1a_d, w1a_b, NJC * ND * 128, "w1ab")
    cast_to_dram(w1g_d, w1g_b, NJC * ND * 128, "w1gb")
    cast_to_dram(w2_d, w2_b, ND * NJC * 128, "w2b")

    def xshift_ap(k, col0, width):
        base = k * (1 + TS)
        off = 1 + col0 - (1 if k >= ND // 2 else 0)
        return xs[:, base + off: base + off + width]

    def wo_residual(w):
        em.dma("sync", cmask_r[:, 0:256], cmask_d[:, w * 256:(w + 1) * 256],
               writes=["cmask"])
        e_bcast(em, ps_r2[:, 0:256], ones_s, cmask_r[:, 0:256], "cmask", "psr2")
        for dt in range(ND):
            pk = PK[dt % 2]
            wb3, wo3, wk3 = ((wostr, 0, ["wos0"]), (wostr, 1024, ["wos1"]),
                             (hgb, 0, ["hgb0", "hgb1"]))[dt % 3]
            em.dma(QKQ[dt % 3], wb3[:, wo3: wo3 + NI * 128],
                   wo_b[:, dt * NI * 128: (dt + 1) * NI * 128],
                   reads=["wob"], writes=wk3)
            for i in range(NI):
                e_mm(em, ps[dt % 2][:, 0:256],
                     wb3[:, wo3 + i * 128: wo3 + (i + 1) * 128],
                     attf[:, i * 256:(i + 1) * 256],
                     wk3 + ["attf"], pk,
                     start=(i == 0), stop=(i == NI - 1))
            par = dt % 2
            ta = tf1[:, par * 512: par * 512 + 256]
            e_copy(em, ta, ps[par][:, 0:256], [pk], "tf1%d" % par)
            e_tt(em, ta, ta, ps_r2[:, 0:256],
                 ALU.mult, ["tf1%d" % par, "psr2"], "tf1%d" % par)
            xsl = xs[:, dt * (1 + TS) + 1 + w * 256:
                     dt * (1 + TS) + 1 + (w + 1) * 256]
            e_tt(em, xsl, xsl, ta, ALU.add,
                 ["xs", "tf1%d" % par], "xs")

    for w in range(NW if "attn" not in FFN_SKIP else 0):
        for k in range(ND):
            e_copy(em, xwb[:, ts(k, 256)], xshift_ap(k, w * 256, 256),
                   ["xs"], "xwb", eng=("scalar" if k % 2 else "vector"))
        for jt in range(NI):
            e_copy(em, kro[:, jt * 512: jt * 512 + 256],
                   kro[:, jt * 512 + 256: (jt + 1) * 512], ["kro"], "kro",
                   eng="vector")
        for i in range(2):
            e_copy(em, vtm[:, ts(i, INNER)], vtm[:, ts(2 + i, INNER)],
                   ["vtm"], "vtm", eng="scalar")
        def vdma(idx):
            half, kg = idx // 4, idx % 4
            slot = idx % 2
            em.dma(QKQ[idx % 3], wvstr[:, ts(slot, 1024)],
                   wv_b[:, half * 4096 + kg * 1024:
                        half * 4096 + (kg + 1) * 1024],
                   reads=["wvb"], writes=["wvs%d" % slot])
        if "v" not in FFN_SKIP:
            vdma(0)
            vdma(1)
        for idx in range(8 if "v" not in FFN_SKIP else 0):
            half, kg = idx // 4, idx % 4
            slot = idx % 2
            vb = 2 - 2 * half
            for kk in range(2):
                k = kg * 2 + kk
                for tt in range(2):
                    e_mm(em, ps[vb + tt][:, :],
                         xwb[:, k * 256 + tt * 128: k * 256 + tt * 128 + 128],
                         wvstr[:, slot * 1024 + kk * 512:
                               slot * 1024 + kk * 512 + 512],
                         ["xwb", "wvs%d" % slot], PK[vb + tt],
                         start=(k == 0), stop=(k == ND - 1))
            if idx + 2 < 8:
                vdma(idx + 2)
            if kg == 3:
                for tt in range(2):
                    e_copy(em, vtm[:, (2 + tt) * INNER + half * 512:
                                   (2 + tt) * INNER + (half + 1) * 512],
                           ps[vb + tt][:, :], [PK[vb + tt]], "vtm",
                           eng=("scalar" if tt == 0 else "vector"))
        em.dma("sync", cs_s[:, :], cs_d[:, w * 256:(w + 1) * 256], writes=["cst"])
        em.dma("scalar", ssn_s[:, :], ssn_d[:, w * 256:(w + 1) * 256], writes=["sst"])
        em.dma("sync", csk_s[:, :], csk_d[:, w * 256:(w + 1) * 256], writes=["ckt"])
        em.dma("gpsimd", ssnk_s[:, :], ssnk_d[:, w * 256:(w + 1) * 256],
               writes=["skt"])
        for qk_i in range(2 if "qk" not in FFN_SKIP else 0):
            dkey = "qraw" if qk_i == 0 else "kro"
            def qk_tail(jt):
                par = jt % 2
                pk = PK[par]
                e_act(em, tf2[:, par * 512: par * 512 + 256],
                      ps[par][:, 0:256], AF.Square, [pk], "tf2%d" % par)
                e_mm(em, ps[4][0:16, 0:256], blk16_s[:, ts(jt, 16)],
                     tf2[:, par * 512: par * 512 + 256],
                     ["tf2%d" % par, "blk16"], "psg4",
                     start=(jt == 0), stop=(jt == NI - 1))
                dst = (qro[:, ts(jt, 256)] if qk_i == 0 else
                       kro[:, jt * 512 + 256:(jt + 1) * 512])
                e_copy(em, dst, ps[par][:, 0:256], [pk], dkey)
            pend = None
            for jt in range(NI):
                par = jt % 2
                pk = PK[par]
                qj = qk_i * NI + jt
                sl4 = (qk_i * NI + jt) % 4
                wb4, wo4, wk4 = ((wqkstr, 0, "wqks0"), (wqkstr, 1024, "wqks1"),
                                 (w1str, 0, "w1s0"), (w1str, 1024, "w1s1"))[sl4]
                em.dma(QKQ[jt % 3], wb4[:, wo4:wo4 + 1024],
                       wqk_b[:, qj * 1024:(qj + 1) * 1024],
                       reads=["wqkb"], writes=[wk4])
                for k in range(ND):
                    e_mm(em, ps[par][:, 0:256],
                         wb4[:, wo4 + k * 128: wo4 + (k + 1) * 128],
                         xwb[:, ts(k, 256)],
                         [wk4, "xwb"], pk,
                         start=(k == 0), stop=(k == ND - 1))
                if pend is not None:
                    qk_tail(pend)
                pend = jt
            qk_tail(pend)
            e_act(em, rr_s[:, ts(qk_i, 256)], ps[4][0:16, 0:256], AF.Sqrt,
                  ["psg4"], "rr")
            e_ts(em, rr_s[:, ts(qk_i, 256)], rr_s[:, ts(qk_i, 256)], 1e-12,
                 None, ALU.max, None, ["rr"], "rr")
            em.emit("vector",
                    lambda v, qk_i=qk_i: v.reciprocal(
                        out=rr_s[:, ts(qk_i, 256)],
                        in_=rr_s[:, ts(qk_i, 256)]),
                    reads=["rr"], writes=["rr"])
            e_copy(em, rr_b[:, ts(qk_i, 256)], rr_s[:, ts(qk_i, 256)],
                   ["rr"], "rrb")
        for qk_i in range(2 if "rot" not in FFN_SKIP else 0):
            srcbuf, dkey = (qro, "qraw") if qk_i == 0 else (kro, "kro")
            cst, ckey = (cs_s, "cst") if qk_i == 0 else (csk_s, "ckt")
            sst, skey2 = (ssn_s, "sst") if qk_i == 0 else (ssnk_s, "skt")
            for jt in range(NI):
                par = jt % 2
                po = par * 256
                ta = tf1[:, par * 512: par * 512 + 256]
                tb = tf2[:, par * 512 + 256: par * 512 + 512]
                sl = (ts(jt, 256) if qk_i == 0 else
                      slice(jt * 512 + 256, (jt + 1) * 512))
                e_mm(em, ps_r[:, po:po + 256], esel2_b[:, ts(jt, 128)],
                     rr_b[:, ts(qk_i, 256)], ["esel2", "rrb"], "psr%d" % par,
                     start=True, stop=True)
                e_stt(em, q1_s[:, po:po + 256], srcbuf[:, sl],
                      qs_c[:, qk_i:qk_i + 1], ps_r[:, po:po + 256],
                      ALU.mult, ALU.mult,
                      [dkey, "qsc", "psr%d" % par], "q1%d" % par)
                e_mm(em, ps[par][:, 256:512], rperm_s[:, :],
                     q1_s[:, po:po + 256], ["rperm", "q1%d" % par], PK[par],
                     start=True, stop=True)
                e_tt(em, ta, q1_s[:, po:po + 256], cst[:, :],
                     ALU.mult, ["q1%d" % par, ckey], "tf1%d" % par)
                e_tt(em, tb, ps[par][:, 256:512],
                     sst[:, :], ALU.mult, [PK[par], skey2], "tf2%d" % par)
                e_tt(em, srcbuf[:, sl], ta, tb, ALU.add,
                     ["tf1%d" % par, "tf2%d" % par], dkey)
        if w >= 1 and "wo" not in FFN_SKIP:
            wo_residual(w - 1)
        use_flag = w in (0, 2)
        for qc in range(2 if "core" not in FFN_SKIP else 0):
            if use_flag:
                em.dma("sync", flag_r[:, :],
                       flag_d[:, (w * 2 + qc) * 384:(w * 2 + qc + 1) * 384],
                       writes=["flag"])
                e_copy(em, flag_b[:, :], flag_r[:, :], ["flag"], "flagb")
            def stage_a(h):
                jt, sub = h // 2, h % 2
                pb4 = h % 4
                po4 = pb4 * 384
                rsl = slice(sub * 64, sub * 64 + 64)
                psim = ps[pb4]
                pks = PK[pb4]
                ck = "col%d" % pb4
                e_mm(em, psim[:, 0:384],
                     qro[rsl, jt * 256 + qc * 128: jt * 256 + qc * 128 + 128],
                     kro[rsl, jt * 512 + qc * 128: jt * 512 + qc * 128 + 384],
                     ["qraw", "kro"], pks, start=True, stop=False)
                if use_flag:
                    e_mm(em, psim[:, 0:384], ones_bf[0:1, :], flag_b[:, :],
                         ["onesb", "flagb"], pks, start=False, stop=False)
                e_mm(em, psim[:, 0:384], eye_b[:, :], band_b[:, :],
                     ["eye", "bandb"], pks, start=False, stop=True)
                e_red(em, col_s[:, pb4 * 4:pb4 * 4 + 1], psim[:, 0:384],
                      ALU.max, [pks], ck, negate=True)
                em.emit("scalar",
                        lambda s, po4=po4, pb4=pb4, psim=psim: s.activation(
                            out=pexp_b[:, po4:po4 + 384], in_=psim[:, 0:384],
                            func=AF.Exp, bias=col_s[:, pb4 * 4:pb4 * 4 + 1]),
                        reads=[pks, ck],
                        writes=["pexp%d" % pb4])
            def stage_b(h):
                par = h % 2
                pb4 = h % 4
                po4 = pb4 * 384
                po = par * 384
                ck = "col%d" % pb4
                e_red(em, col_s[:, pb4 * 4 + 1:pb4 * 4 + 2],
                      pexp_b[:, po4:po4 + 384], ALU.add, ["pexp%d" % pb4],
                      ck)
                em.emit("vector",
                        lambda v, pb4=pb4: v.reciprocal(
                            out=col_s[:, pb4 * 4 + 2:pb4 * 4 + 3],
                            in_=col_s[:, pb4 * 4 + 1:pb4 * 4 + 2]),
                        reads=[ck], writes=[ck])
                pvd = (ps[5], "psg5") if par == 0 else (ps_r2, "psr2")
                tpd0 = (ps_tb, ["pstb", "psg4"]) if par == 0 else \
                    (ps_tb2, ["pstb2", "psr0"])
                for i in range(3):
                    tpd = tpd0
                    em.emit("tensor",
                            lambda t, i=i, po4=po4, tpd=tpd: t.transpose(
                                out=tpd[0][:, :], in_=pexp_b[:, po4 + i * 128:
                                                            po4 + (i + 1) * 128],
                                identity=eye_b[:, :]),
                            reads=["pexp%d" % pb4, "eye"], writes=tpd[1])
                    e_copy(em, pt_b[:, po + i * 128: po + (i + 1) * 128],
                           tpd[0][:, :], tpd[1], "ptb%d" % par,
                           eng="vector")
                for i in range(3):
                    e_mm(em, pvd[0][:, 0:64],
                         pt_b[:, po + i * 128: po + (i + 1) * 128],
                         vtm[:, (qc + i) * INNER + h * 64:
                             (qc + i) * INNER + (h + 1) * 64],
                         ["ptb%d" % par, "vtm"], pvd[1],
                         start=(i == 0), stop=(i == 2))
                e_ts(em, attm[:, h * 64:(h + 1) * 64],
                     pvd[0][:, 0:64],
                     col_s[:, pb4 * 4 + 2:pb4 * 4 + 3], None, ALU.mult, None,
                     [pvd[1], ck], "attm_%d" % h)
            for h in range(4):
                stage_a(h)
            for h in range(HEADS):
                stage_b(h)
                if h + 4 < HEADS:
                    stage_a(h + 4)
            for i in range(NI):
                tpd = (ps_tb, ["pstb", "psg4"]) if i % 2 == 0 else \
                    (ps_tb2, ["pstb2", "psr0"])
                em.emit("tensor",
                        lambda t, i=i, tpd=tpd: t.transpose(
                            out=tpd[0][:, :], in_=attm[:, ts(i, 128)],
                            identity=eye_b[:, :]),
                        reads=["attm_%d" % (2 * i), "attm_%d" % (2 * i + 1),
                               "eye"], writes=tpd[1])
                e_copy(em, attf[:, i * 256 + qc * 128:
                               i * 256 + qc * 128 + 128],
                       tpd[0][:, :], tpd[1], "attf",
                       eng=("scalar" if i % 2 == 0 else "vector"))
    if "attn" not in FFN_SKIP and "wo" not in FFN_SKIP:
        wo_residual(NW - 1)

    CH = 512
    for cc in range(TS // CH if "ffn" not in FFN_SKIP else 0):
        c0 = cc * CH
        for k in range(ND):
            par = k % 2
            e_mm(em, ps[4][0:1, 0:CH], ones_s[:, 0:1],
                 xshift_ap(k, c0, CH),
                 ["xs", "ones"], "psg4", start=(k == 0),
                 stop=(k == ND - 1))
            e_act(em, tf2[:, par * CH:(par + 1) * CH],
                  xshift_ap(k, c0, CH), AF.Square,
                  ["xs"], "tf2%d" % par)
            e_mm(em, ps[5][0:1, 0:CH], ones_s[:, 0:1],
                 tf2[:, par * CH:(par + 1) * CH],
                 ["tf2%d" % par, "ones"], "psg5", start=(k == 0),
                 stop=(k == ND - 1))
        ln_rows(em, ps[4][0:1, 0:CH], ps[5][0:1, 0:CH], mu_r[:, :],
                rs_r[:, :], rowtmp[:, :], 1e-5, D, "psg4", "psg5", "mu", "rs",
                "rt")
        em.emit("tensor",
                lambda t: t.matmul(out=ps_r[:, 0:CH], lhsT=ones_s[0:1, :],
                                   rhs=mu_r[:, :], start=True, stop=True),
                reads=["mu", "ones"], writes=["psr0", "psr1"])
        e_bcast(em, ps_r2[:, 0:CH], ones_s, rs_r[:, :], "rs", "psr2")
        for k in range(ND):
            par = k % 2
            e_tt(em, tf1[:, par * CH:(par + 1) * CH],
                 xshift_ap(k, c0, CH), ps_r[:, 0:CH], ALU.subtract,
                 ["xs", "psr0", "psr1"], "tf1%d" % par)
            e_tt(em, tf1[:, par * CH:(par + 1) * CH],
                 tf1[:, par * CH:(par + 1) * CH], ps_r2[:, 0:CH],
                 ALU.mult, ["tf1%d" % par, "psr2"], "tf1%d" % par)
            e_ts(em, xnb[:, ts(k, CH)], tf1[:, par * CH:(par + 1) * CH],
                 lng_c[:, k:k + 1], lnb_c[:, k:k + 1], ALU.mult, ALU.add,
                 ["tf1%d" % par, "lng", "lnb"], "xnb")
        if cc >= 1:
            p0 = (cc - 1) * CH
            for dt in range(ND):
                xsl = xs[:, dt * (1 + TS) + 1 + p0:
                         dt * (1 + TS) + 1 + p0 + CH]
                e_tt(em, xsl, xsl, fout[:, ts(dt, CH)], ALU.add,
                     ["xs", "fout"], "xs")
        for jc in range(NJC if "w1" not in FFN_SKIP else 0):
            par = jc % 2
            wbuf = w1str if par == 0 else wqkstr
            ka = "w1s0" if par == 0 else "wqks0"
            kg2 = "w1s1" if par == 0 else "wqks1"
            em.dma(QKQ[jc % 3], wbuf[:, 0:1024],
                   w1a_b[:, jc * 1024:(jc + 1) * 1024],
                   reads=["w1ab"], writes=[ka])
            em.dma(QKQ[(jc + 1) % 3], wbuf[:, 1024:2048],
                   w1g_b[:, jc * 1024:(jc + 1) * 1024],
                   reads=["w1gb"], writes=[kg2])
            for k in range(ND):
                e_mm(em, ps[par][:, 0:CH],
                     wbuf[:, k * 128:(k + 1) * 128],
                     xnb[:, ts(k, CH)], [ka, "xnb"], PK[par],
                     start=(k == 0), stop=(k == ND - 1))
                e_mm(em, ps[2 + par][:, 0:CH],
                     wbuf[:, 1024 + k * 128: 1024 + (k + 1) * 128],
                     xnb[:, ts(k, CH)], [kg2, "xnb"], PK[2 + par],
                     start=(k == 0), stop=(k == ND - 1))
            e_act(em, tf1[:, par * CH:(par + 1) * CH], ps[par][:, 0:CH],
                  AF.Identity, [PK[par], "b1a"], "tf1%d" % par,
                  bias=b1a_c[:, jc:jc + 1])
            e_act(em, tf2[:, par * CH:(par + 1) * CH],
                  ps[2 + par][:, 0:CH], AF.Gelu, [PK[2 + par], "b1g"],
                  "tf2%d" % par, bias=b1g_c[:, jc:jc + 1])
            e_tt(em, hgb[:, ts(jc, CH)], tf1[:, par * CH:(par + 1) * CH],
                 tf2[:, par * CH:(par + 1) * CH], ALU.mult,
                 ["tf1%d" % par, "tf2%d" % par], "hgb%d" % par)
        em.dma("sync", cmask_r[:, 0:CH], cmask_d[:, c0:c0 + CH],
               writes=["cmask"])
        e_bcast(em, ps_r2[:, 0:CH], ones_s, cmask_r[:, 0:CH], "cmask", "psr2")
        for dt in range(ND if "w2" not in FFN_SKIP else 0):
            par = dt % 2
            wbuf, wkeys = (vtm, ["vtm"]) if par == 0 else (kro, ["kro"])
            em.dma(QKQ[dt % 3], wbuf[:, 0:NJC * 128],
                   w2_b[:, dt * NJC * 128:(dt + 1) * NJC * 128],
                   reads=["w2b"], writes=wkeys)
            for jc in range(NJC):
                e_mm(em, ps[4 + par][:, 0:CH],
                     wbuf[:, jc * 128:(jc + 1) * 128],
                     hgb[:, ts(jc, CH)],
                     wkeys + ["hgb0", "hgb1"], PK[4 + par],
                     start=(jc == 0), stop=(jc == NJC - 1))
            e_act(em, tf1[:, par * CH:(par + 1) * CH],
                  ps[4 + par][:, 0:CH], AF.Identity, [PK[4 + par], "b2"],
                  "tf1%d" % par, bias=b2_c[:, dt:dt + 1])
            e_tt(em, fout[:, ts(dt, CH)], tf1[:, par * CH:(par + 1) * CH],
                 ps_r2[:, 0:CH], ALU.mult, ["tf1%d" % par, "psr2"], "fout")
    p0 = TS - CH
    for dt in range(ND):
        xsl = xs[:, dt * (1 + TS) + 1 + p0: dt * (1 + TS) + 1 + p0 + CH]
        e_tt(em, xsl, xsl, fout[:, ts(dt, CH)], ALU.add, ["xs", "fout"], "xs")

    qi = 0
    for k in range(ND):
        for o in range(0, T_OWN, 512):
            em.dma(QKQ[qi % 3], xout_d[ts(k, 128), o:o + 512],
                   xs[:, k * (1 + TS) + 1 + 512 + o:
                      k * (1 + TS) + 1 + 512 + o + 512],
                   reads=["xs"], writes=["xo"])
            qi += 1

    em.finalize()
    return nc


def _rotary_tables():
    half = np.arange(0, DH, 2, dtype=np.float64)
    inv_freq = 1.0 / 10000.0 ** (half / DH)
    base_scale = (half + 0.4 * DH) / (1.4 * DH)

    def tables(positions):
        t = positions.astype(np.float64)
        freqs = t[:, None] * inv_freq[None, :]
        power = (t - L // 2) / (256 // 2)
        scale = base_scale[None, :] ** power[:, None]
        cos = np.cos(freqs)
        sin = np.sin(freqs)
        cs = (cos * scale).T
        ss = (sin * scale).T
        inv = 1.0 / scale
        ck = (cos * inv).T
        sk = (sin * inv).T
        def stack(c, s):
            blk = np.concatenate([c, c, c, c], axis=0)
            sgn = np.concatenate([-s, s, -s, s], axis=0)
            return np.float32(blk), np.float32(sgn)
        csq, ssnq = stack(cs, ss)
        csk2, ssnk2 = stack(ck, sk)
        return (np.ascontiguousarray(csq), np.ascontiguousarray(ssnq),
                np.ascontiguousarray(csk2), np.ascontiguousarray(ssnk2))

    return tables


_FFN_NC = None
_L0_NC = None
_MAMBA_NC = None
_COMB_NC = None


def _rperm128():
    r = np.zeros((128, 128), np.float32)
    for blk in range(2):
        b0 = blk * 64
        for i in range(32):
            r[b0 + 32 + i, b0 + i] = 1.0
            r[b0 + i, b0 + 32 + i] = 1.0
    return r


def ffn_in_maps(inputs, layer, xcur):
    NI = INNER // 128
    NJC = FFI // 128
    wqkv = np.ascontiguousarray(np.asarray(inputs["L1_ffn_Wqkv"])[layer])
    wo = np.ascontiguousarray(np.asarray(inputs["L1_ffn_Wo"])[layer])
    w1 = np.asarray(inputs["L1_ffn_W1"])[layer]
    w2 = np.asarray(inputs["L1_ffn_W2"])[layer]
    b1 = np.asarray(inputs["L1_ffn_b1"])[layer]
    b2 = np.asarray(inputs["L1_ffn_b2"])[layer]
    lng = np.asarray(inputs["L1_ffn_ln_g"])[layer]
    lnb = np.asarray(inputs["L1_ffn_ln_b"])[layer]
    qs = np.asarray(inputs["L1_ffn_q_scale"])[layer]
    ks = np.asarray(inputs["L1_ffn_k_scale"])[layer]
    FF0 = 2730
    ND = D // 128
    w1a = np.zeros((D, FFI), np.float32); w1a[:, :FF0] = w1[:, :FF0]
    w1g = np.zeros((D, FFI), np.float32); w1g[:, :FF0] = w1[:, FF0:]
    w2p = np.zeros((FFI, D), np.float32); w2p[:FF0] = w2
    b1a = np.zeros(FFI, np.float32); b1a[:FF0] = b1[:FF0]
    b1g = np.zeros(FFI, np.float32); b1g[:FF0] = b1[FF0:]
    def tilemaj(wmat, nr, nccols):
        return np.ascontiguousarray(
            np.float32(wmat).reshape(nr, 128, nccols, 128)
            .transpose(1, 2, 0, 3).reshape(128, -1))
    wqk_t = tilemaj(wqkv[:, :2 * INNER], ND, 2 * NI)
    wv_t = np.ascontiguousarray(
        np.float32(wqkv[:, 2 * INNER:]).reshape(ND, 128, 2, 512)
        .transpose(1, 2, 0, 3).reshape(128, -1))
    wo_t = tilemaj(wo, NI, ND)
    w1a_t = tilemaj(w1a, ND, NJC)
    w1g_t = tilemaj(w1g, ND, NJC)
    w2_t = tilemaj(w2p, NJC, ND)
    blk16 = np.zeros((128, NI * 16), np.float32)
    for jt in range(NI):
        blk16[0:64, jt * 16 + 2 * jt] = 1.0
        blk16[64:128, jt * 16 + 2 * jt + 1] = 1.0
    esel2 = np.zeros((16, NI * 128), np.float32)
    for jt in range(NI):
        esel2[2 * jt, jt * 128: jt * 128 + 64] = 1.0
        esel2[2 * jt + 1, jt * 128 + 64: (jt + 1) * 128] = 1.0
    band = np.full((128, 384), -1.0e38, np.float32)
    for i in range(128):
        band[i, i:i + 257] = 0.0
    tables = _rotary_tables()
    common = {
        "wqk_t": wqk_t, "wv_t": wv_t, "wo_t": wo_t,
        "w1a_t": w1a_t, "w1g_t": w1g_t, "w2_t": w2_t,
        "b1a": np.ascontiguousarray(b1a.reshape(NJC, 128).T),
        "b1g": np.ascontiguousarray(b1g.reshape(NJC, 128).T),
        "b2c": np.ascontiguousarray(b2.reshape(8, 128).T),
        "lng": np.ascontiguousarray(lng.reshape(8, 128).T),
        "lnb": np.ascontiguousarray(lnb.reshape(8, 128).T),
        "qs": np.ascontiguousarray(np.tile(qs, 2)[:, None]),
        "ks": np.ascontiguousarray(np.tile(ks, 2)[:, None]),
        "band": band, "blk16": blk16, "esel2": esel2,
        "iden": np.eye(128, dtype=np.float32),
        "rperm": _rperm128(),
    }
    maps = []
    for core in range(8):
        b, half = core // 2, core % 2
        start = half * T_OWN
        xb = xcur[b]
        xst = np.zeros((1 + TS, D), np.float32)
        lo = start - 512
        src_lo = max(lo, 0)
        xst[1 + (src_lo - lo): 1 + TS] = xb[src_lo: start + T_OWN]
        if lo - 1 >= 0:
            xst[0] = xb[lo - 1]
        pos = np.arange(lo, start + T_OWN)
        cs, ssn, csk, ssnk = tables(pos)
        flag = np.zeros((NW, 2, 384), np.float32)
        noprev = {0} | ({2} if half == 0 else set())
        for w in range(NW):
            if w in noprev:
                for qc in range(2):
                    n = 256 - qc * 128
                    flag[w, qc, 0:n] = -1.0e38
        cmask = np.ones((1, TS), np.float32)
        if half == 0:
            cmask[0, 0:512] = 0.0
        m = dict(common)
        m["xst"] = np.ascontiguousarray(xst.T)
        m["cs_t"] = cs
        m["ssn_t"] = ssn
        m["csk_t"] = csk
        m["ssnk_t"] = ssnk
        m["flagrow"] = flag.reshape(1, -1)
        m["cmask"] = cmask
        maps.append(m)
    return maps


def run_ffn_layer(inputs, layer, xcur):
    global _FFN_NC
    if _FFN_NC is None:
        _FFN_NC = build_ffn()
    maps = ffn_in_maps(inputs, layer, xcur)
    res = run_bass_kernel_spmd(_FFN_NC, maps, core_ids=list(range(8)))
    xnew = np.zeros((B, L, D), np.float32)
    for core in range(8):
        b, half = core // 2, core % 2
        xnew[b, half * T_OWN:(half + 1) * T_OWN] = res.results[core]["xout"].T
    return xnew


def build_combine():
    nc = bass.Bass("TRN2", target_bir_lowering=False)
    ND = D // 128
    x1_d = nc.dram_tensor("x1", [D, T_OWN], F32, kind="ExternalInput").ap()
    xn1_d = nc.dram_tensor("xn1", [D, T_OWN], F32, kind="ExternalInput").ap()
    pf_d = nc.dram_tensor("pf", [D, T_OWN], F32, kind="ExternalInput").ap()
    pb_d = nc.dram_tensor("pb", [D, T_OWN], F32, kind="ExternalInput").ap()
    g_d = nc.dram_tensor("g", [D, T_OWN], F32, kind="ExternalInput").ap()
    ob_d = nc.dram_tensor("obc", [128, ND], F32, kind="ExternalInput").ap()
    x2_d = nc.dram_tensor("x2", [D, T_OWN], F32, kind="ExternalOutput").ap()
    em = EM(nc)
    a_s = sbuf(nc, "a_s", [128, 512])
    b_s = sbuf(nc, "b_s", [128, 512])
    c_s = sbuf(nc, "c_s", [128, 512])
    ob_c = sbuf(nc, "ob_c", [128, ND])
    em.dma("sync", ob_c[:, :], ob_d[:, :], writes=["ob"])
    for k in range(ND):
        for o in range(0, T_OWN, 512):
            sl = (ts(k, 128), slice(o, o + 512))
            em.dma("sync", a_s[:, :], pf_d[sl[0], sl[1]], writes=["a"])
            em.dma("sync", b_s[:, :], pb_d[sl[0], sl[1]], writes=["b"])
            e_tt(em, a_s[:, :], a_s[:, :], b_s[:, :], ALU.add, ["a", "b"], "a")
            em.dma("sync", b_s[:, :], xn1_d[sl[0], sl[1]], writes=["b"])
            e_tt(em, a_s[:, :], a_s[:, :], b_s[:, :], ALU.add, ["a", "b"], "a")
            e_ts(em, a_s[:, :], a_s[:, :], ob_c[:, k:k + 1], None, ALU.add,
                 None, ["a", "ob"], "a")
            em.dma("sync", b_s[:, :], g_d[sl[0], sl[1]], writes=["b"])
            e_act(em, c_s[:, :], b_s[:, :], AF.Silu, ["b"], "c")
            e_tt(em, a_s[:, :], a_s[:, :], c_s[:, :], ALU.mult, ["a", "c"], "a")
            em.dma("sync", b_s[:, :], x1_d[sl[0], sl[1]], writes=["b"])
            e_tt(em, a_s[:, :], a_s[:, :], b_s[:, :], ALU.add, ["a", "b"], "a")
            em.dma("gpsimd", x2_d[sl[0], sl[1]], a_s[:, :], reads=["a"],
                   writes=["out"])
    em.finalize()
    return nc


def run_combine(inputs, x1, xn1, pf, pb, g):
    global _COMB_NC
    if _COMB_NC is None:
        _COMB_NC = build_combine()
    ob = np.asarray(inputs["L1_attn_out_b"])
    obc = np.ascontiguousarray(ob.reshape(8, 128).T)
    maps = []
    for core in range(8):
        b, half = core // 2, core % 2
        sl = slice(half * T_OWN, (half + 1) * T_OWN)
        maps.append({
            "x1": np.ascontiguousarray(x1[b, sl].T),
            "xn1": np.ascontiguousarray(xn1[b, sl].T),
            "pf": np.ascontiguousarray(pf[b, sl].T),
            "pb": np.ascontiguousarray(pb[b, sl].T),
            "g": np.ascontiguousarray(g[b, sl].T),
            "obc": obc,
        })
    res = run_bass_kernel_spmd(_COMB_NC, maps, core_ids=list(range(8)))
    x2 = np.zeros((B, L, D), np.float32)
    for core in range(8):
        b, half = core // 2, core % 2
        x2[b, half * T_OWN:(half + 1) * T_OWN] = res.results[core]["x2"].T
    return x2


def kernel(**inputs):
    x1, xn1 = run_l0(inputs)
    pf, pb = run_mamba(inputs, xn1)
    xcur = xn1
    for layer in range(6):
        xcur = run_ffn_layer(inputs, layer, xcur)
    x2 = run_combine(inputs, x1, xn1, pf, pb, xcur)
    return x2


# version 05
# speedup vs baseline: 1.1757x; 1.1757x over previous
import numpy as np
import concourse.bass as bass
import concourse.mybir as mybir
from concourse.bass_utils import run_bass_kernel_spmd

F32 = mybir.dt.float32
BF16 = mybir.dt.bfloat16
AF = mybir.ActivationFunctionType
ALU = mybir.AluOpType
AX = mybir.AxisListType

D = 1024
B = 4
L = 4096
T_OWN = 2048

ENGINES = ("sync", "tensor", "vector", "scalar", "gpsimd")


class EM:

    NGS = 8

    def __init__(self, nc):
        self.nc = nc
        self.ops = {e: [] for e in ENGINES}
        self.count = {}
        self.last_w = {}
        self.readers = {}
        self.sems = {}
        self._exits = []
        self.q_groups = {}
        self.q_open = {}
        self.q_semcum = {}
        self.q_rot = {}

    def sem(self, st):
        if st not in self.sems:
            cm = self.nc.semaphore("sem_" + st.replace(":", "_"))
            self.sems[st] = cm.__enter__()
            self._exits.append(cm)
        return self.sems[st]

    def _dma_group_sem(self, st):
        if st not in self.q_open:
            rot = self.q_rot.get(st, 0)
            name = "%s:%d" % (st, rot % self.NGS)
            self.q_rot[st] = rot + 1
            self.q_open[st] = name
        return self.q_open[st]

    def _close_group(self, st, upto_seq):
        name = self.q_open.pop(st)
        target = self.q_semcum.get(name, 0)
        self.q_groups.setdefault(st, []).append((upto_seq, name, target))

    def _dep_wait(self, waits, my_stream, st2, c2):
        if st2.startswith("d:"):
            groups = self.q_groups.get(st2, [])
            grp = None
            for last_seq, name, target in groups:
                if c2 <= last_seq:
                    grp = (name, target)
                    break
            if grp is None:
                self._close_group(st2, self.count.get(st2, 0))
                last_seq, name, target = self.q_groups[st2][-1]
                grp = (name, target)
            name, target = grp
            if my_stream == st2:
                return
            waits[name] = max(waits.get(name, 0), target)
        else:
            waits[st2] = max(waits.get(st2, 0), c2)

    def emit(self, eng, fn, reads=(), writes=(), dma=False):
        my_stream = ("d:" if dma else "c:") + eng
        waits = {}
        for k in reads:
            lw = self.last_w.get(k)
            if lw:
                self._dep_wait(waits, my_stream, *lw)
        for k in writes:
            lw = self.last_w.get(k)
            if lw:
                self._dep_wait(waits, my_stream, *lw)
            for st2, c2 in self.readers.get(k, {}).items():
                self._dep_wait(waits, my_stream, st2, c2)
        self.count[my_stream] = self.count.get(my_stream, 0) + 1
        cnt = self.count[my_stream]
        gsem = None
        if dma:
            gsem = self._dma_group_sem(my_stream)
            self.q_semcum[gsem] = self.q_semcum.get(gsem, 0) + 16
        for k in writes:
            self.last_w[k] = (my_stream, cnt)
            self.readers[k] = {}
        for k in reads:
            self.readers.setdefault(k, {})[my_stream] = cnt
        self.ops[eng].append((waits, fn, gsem))

    def dma(self, eng, out, in_, reads=(), writes=(), **kw):
        self.emit(eng, lambda q: q.dma_start(out=out, in_=in_, **kw),
                  reads=reads, writes=writes, dma=True)

    def finalize(self):
        nc = self.nc
        with nc.Block() as block:
            for eng_name in ENGINES:
                ops = self.ops[eng_name]
                if not ops:
                    continue

                def make(eng_name=eng_name, ops=ops):
                    def body(q):
                        synced = {}
                        cs = "c:" + eng_name
                        nc_cnt = 0
                        for waits, fn, gsem in ops:
                            for st2, v2 in sorted(waits.items()):
                                if v2 > synced.get(st2, 0):
                                    q.wait_ge(self.sem(st2), v2)
                                    synced[st2] = v2
                            instr = fn(q)
                            if gsem is not None:
                                instr.then_inc(self.sem(gsem), 16)
                            else:
                                nc_cnt += 1
                                instr.then_inc(self.sem(cs), 1)
                        ds = "d:" + eng_name
                        for name, cum in sorted(self.q_semcum.items()):
                            if name.startswith(ds):
                                if cum > synced.get(name, 0):
                                    q.wait_ge(self.sem(name), cum)
                        if nc_cnt:
                            q.wait_ge(self.sem(cs), nc_cnt)
                    return body

                getattr(block, eng_name)(make())


def sbuf(nc, name, shape, dtype=F32):
    return nc.sbuf_tensor(name, shape, dtype).__enter__()


def psum_t(nc, name, shape, dtype=F32):
    return nc.psum_tensor(name, shape, dtype).__enter__()


def ts(i, n):
    return slice(i * n, (i + 1) * n)


def e_mm(em, out, lhsT, rhs, reads, write, start, stop):
    em.emit("tensor",
            lambda t: t.matmul(out=out, lhsT=lhsT, rhs=rhs, start=start,
                               stop=stop),
            reads=reads, writes=[write])


def e_bcast(em, ps, ones_s, row_ap, read, write, nrows=128):
    em.emit("tensor",
            lambda t: t.matmul(out=ps, lhsT=ones_s[0:1, 0:nrows], rhs=row_ap,
                               start=True, stop=True),
            reads=[read, "ones"], writes=[write])


def e_tt(em, out, in0, in1, op, reads, write):
    em.emit("vector",
            lambda v: v.tensor_tensor(out=out, in0=in0, in1=in1, op=op),
            reads=reads, writes=[write])


def e_ts(em, out, in0, s1, s2, op0, op1, reads, write, eng="vector"):
    if s2 is None:
        em.emit(eng,
                lambda v: v.tensor_scalar(out=out, in0=in0, scalar1=s1,
                                          scalar2=None, op0=op0),
                reads=reads, writes=[write])
    else:
        em.emit(eng,
                lambda v: v.tensor_scalar(out=out, in0=in0, scalar1=s1,
                                          scalar2=s2, op0=op0, op1=op1),
                reads=reads, writes=[write])


def e_stt(em, out, in0, scalar, in1, op0, op1, reads, write):
    em.emit("vector",
            lambda v: v.scalar_tensor_tensor(out=out, in0=in0, scalar=scalar,
                                             in1=in1, op0=op0, op1=op1),
            reads=reads, writes=[write])


def e_act(em, out, in_, func, reads, write, bias=None, scale=None, eng="scalar"):
    kw = {}
    if bias is not None:
        kw["bias"] = bias
    if scale is not None:
        kw["scale"] = scale
    em.emit(eng,
            lambda s: s.activation(out=out, in_=in_, func=func, **kw),
            reads=reads, writes=[write])


def e_copy(em, out, in_, reads, write, eng="scalar"):
    if eng == "scalar":
        em.emit(eng, lambda s: s.copy(out=out, in_=in_), reads=reads,
                writes=[write])
    else:
        em.emit(eng, lambda v: v.tensor_copy(out=out, in_=in_), reads=reads,
                writes=[write])


def e_red(em, out, in_, op, reads, write, negate=False):
    em.emit("vector",
            lambda v: v.tensor_reduce(out=out, in_=in_, axis=AX.X, op=op,
                                      negate=negate),
            reads=reads, writes=[write])


def ln_rows(em, st_ap, st2_ap, mu_ap, rs_ap, tmp_ap, eps, dimn,
            k_st, k_st2, k_mu, k_rs, k_tmp):
    e_ts(em, mu_ap, st_ap, 1.0 / dimn, None, ALU.mult, None, [k_st], k_mu)
    e_ts(em, tmp_ap, st2_ap, 1.0 / dimn, None, ALU.mult, None, [k_st2], k_tmp)
    e_tt(em, rs_ap, mu_ap, mu_ap, ALU.mult, [k_mu], k_rs)
    e_tt(em, tmp_ap, tmp_ap, rs_ap, ALU.subtract, [k_tmp, k_rs], k_tmp)
    e_ts(em, tmp_ap, tmp_ap, eps, None, ALU.add, None, [k_tmp], k_tmp)
    em.emit("scalar", lambda s: s.activation(out=tmp_ap, in_=tmp_ap, func=AF.Sqrt),
            reads=[k_tmp], writes=[k_tmp])
    em.emit("vector", lambda v: v.reciprocal(out=rs_ap, in_=tmp_ap),
            reads=[k_tmp], writes=[k_rs])


def build_l0():
    nc = bass.Bass("TRN2", target_bir_lowering=False)
    H, DX, QK = 8, 64, 512
    ND = D // 128
    NCH = L // 512
    OCH = T_OWN // 512

    xf = nc.dram_tensor("xf", [D, L], F32, kind="ExternalInput").ap()
    w_qkv = nc.dram_tensor("w_qkv", [D, 3 * QK], F32, kind="ExternalInput").ap()
    bqkv_row_d = nc.dram_tensor("bqkv_row", [1, 3 * QK], F32, kind="ExternalInput").ap()
    bqkv_col_d = nc.dram_tensor("bqkv_col", [128, 12], F32, kind="ExternalInput").ap()
    w_fc = nc.dram_tensor("w_fc", [QK, D], F32, kind="ExternalInput").ap()
    bfc_col_d = nc.dram_tensor("bfc_col", [128, ND], F32, kind="ExternalInput").ap()
    gam_col_d = nc.dram_tensor("gam_col", [128, ND], F32, kind="ExternalInput").ap()
    bet_col_d = nc.dram_tensor("bet_col", [128, ND], F32, kind="ExternalInput").ap()
    gate_wT = nc.dram_tensor("gate_wT", [D, 8], F32, kind="ExternalInput").ap()
    ew0T = nc.dram_tensor("ew0T", [D, D], F32, kind="ExternalInput").ap()
    ew1T = nc.dram_tensor("ew1T", [D, D], F32, kind="ExternalInput").ap()
    out_wT = nc.dram_tensor("out_wT", [D, D], F32, kind="ExternalInput").ap()
    iden_d = nc.dram_tensor("iden", [128, 128], F32, kind="ExternalInput").ap()
    x1_d = nc.dram_tensor("x1", [D, T_OWN], F32, kind="ExternalOutput").ap()
    xn1_d = nc.dram_tensor("xn1", [D, T_OWN], F32, kind="ExternalOutput").ap()

    em = EM(nc)

    wqkv_s = sbuf(nc, "wqkv_s", [128, ND * 3 * QK], BF16)
    wfc_s = sbuf(nc, "wfc_s", [128, 4 * D], BF16)
    gw_s = sbuf(nc, "gw_s", [128, ND * 8], BF16)
    ew0_s = sbuf(nc, "ew0_s", [128, ND * D], BF16)
    ew1_s = sbuf(nc, "ew1_s", [128, ND * D], BF16)
    ow_s = sbuf(nc, "ow_s", [128, ND * D], BF16)
    ones_s = sbuf(nc, "ones_s", [128, 128])
    eye_s = sbuf(nc, "eye_s", [128, 128])
    wtmp = sbuf(nc, "wtmp", [128, 512])
    bqkv_row = sbuf(nc, "bqkv_row_s", [1, 3 * QK])
    bqkvL_row = sbuf(nc, "bqkvL_row", [1, 3 * QK])
    bqkv_col = sbuf(nc, "bqkv_col_s", [128, 12])
    bfc_col = sbuf(nc, "bfc_col_s", [128, ND])
    gam_col = sbuf(nc, "gam_col_s", [128, ND])
    bet_col = sbuf(nc, "bet_col_s", [128, ND])

    xn0b = sbuf(nc, "xn0b", [128, ND * T_OWN], BF16)
    xnbt = sbuf(nc, "xnbt", [128, ND * 512], BF16)
    vfm = sbuf(nc, "vfm", [128, 4 * 512], BF16)
    s0 = sbuf(nc, "s0", [128, ND * 512])
    mu_r = sbuf(nc, "mu_r", [1, 512])
    rs_r = sbuf(nc, "rs_r", [1, 512])
    snx_c = sbuf(nc, "snx_c", [128, ND])
    snx_b = sbuf(nc, "snx_b", [128, ND], BF16)
    red_c = sbuf(nc, "red_c", [128, 1])
    sq_row = sbuf(nc, "sq_row", [1, QK])
    sk_row = sbuf(nc, "sk_row", [1, QK])
    scores = sbuf(nc, "scores", [128, 4 * DX])
    attn = sbuf(nc, "attn", [128, 4 * DX], BF16)
    qh_s = sbuf(nc, "qh_s", [128, QK], BF16)
    kh_s = sbuf(nc, "kh_s", [128, QK], BF16)
    xt_s = sbuf(nc, "xt_s", [128, 2 * 512])
    sq_s = sbuf(nc, "sq_s", [128, 512])
    tmp_a = sbuf(nc, "tmp_a", [128, 512])
    tmp_b = sbuf(nc, "tmp_b", [128, 512])
    rowtmp = sbuf(nc, "rowtmp", [1, 512])
    mu2_r = sbuf(nc, "mu2_r", [1, 512])
    rs2_r = sbuf(nc, "rs2_r", [1, 512])
    col_s = sbuf(nc, "col_s", [128, 16])
    pcols = sbuf(nc, "pcols", [128, 8])
    pv_s = sbuf(nc, "pv_s", [128, 4 * 512], BF16)
    h0_s = sbuf(nc, "h0_s", [128, 512], BF16)
    h1_s = sbuf(nc, "h1_s", [128, 512], BF16)
    gpre = sbuf(nc, "gpre", [128, ND * 512], BF16)
    ge_s = sbuf(nc, "ge_s", [128, 512])
    w01_r = sbuf(nc, "w01_r", [1, 2 * 512])
    x1c = sbuf(nc, "x1c", [128, 512])

    ps_a = psum_t(nc, "ps_a", [128, 512])
    ps_b = psum_t(nc, "ps_b", [128, 512])
    ps_c = psum_t(nc, "ps_c", [128, 512])
    ps_r = psum_t(nc, "ps_r", [128, 512])
    ps_r2 = psum_t(nc, "ps_r2", [128, 512])
    ps_sc = psum_t(nc, "ps_sc", [128, DX])
    ps_st = psum_t(nc, "ps_st", [1, 512])
    ps_st2 = psum_t(nc, "ps_st2", [1, 512])

    em.emit("gpsimd", lambda g: g.memset(ones_s[:, :], 1.0), writes=["ones"])
    em.dma("sync", eye_s[:, :], iden_d[:, :], writes=["eye"])
    em.dma("sync", bqkv_row[:, :], bqkv_row_d[:, :], writes=["bqr"])
    em.dma("sync", bqkv_col[:, :], bqkv_col_d[:, :], writes=["bqc"])
    em.dma("sync", bfc_col[:, :], bfc_col_d[:, :], writes=["bfc"])
    em.dma("sync", gam_col[:, :], gam_col_d[:, :], writes=["gam"])
    em.dma("sync", bet_col[:, :], bet_col_d[:, :], writes=["bet"])
    e_ts(em, bqkvL_row[:, :], bqkv_row[:, :], float(L), None, ALU.mult, None,
         ["bqr"], "bqL")
    em.emit("vector", lambda v: v.memset(scores[:, :], 0.0), writes=["scoresb"])

    def load_cast(dst_ap, src_ap, cols, tag):
        for o in range(0, cols, 512):
            w = min(512, cols - o)
            em.dma("sync", wtmp[:, 0:w], src_ap[:, o:o + w], writes=["wtmp"])
            e_copy(em, dst_ap[:, o:o + w], wtmp[:, 0:w], ["wtmp"], tag)

    for k in range(ND):
        load_cast(wqkv_s[:, ts(k, 3 * QK)], w_qkv[ts(k, 128), :], 3 * QK, "wqkv")
    for k in range(4):
        load_cast(wfc_s[:, ts(k, D)], w_fc[ts(k, 128), :], D, "wfc")
    for k in range(ND):
        load_cast(gw_s[:, ts(k, 8)], gate_wT[ts(k, 128), :], 8, "gw")
    for k in range(ND):
        load_cast(ew0_s[:, ts(k, D)], ew0T[ts(k, 128), :], D, "ew0")
    for k in range(ND):
        load_cast(ew1_s[:, ts(k, D)], ew1T[ts(k, 128), :], D, "ew1")
    for k in range(ND):
        load_cast(ow_s[:, ts(k, D)], out_wT[ts(k, 128), :], D, "ow")

    for c in range(NCH):
        own = c < OCH
        cs = slice(c * 512, (c + 1) * 512)
        rsl = slice(0, 512)
        for k in range(ND):
            em.dma("sync", xt_s[:, ts(k % 2, 512)], xf[ts(k, 128), cs],
                   writes=["xt%d" % (k % 2)])
            e_mm(em, ps_st[:, :], ones_s[:, 0:1], xt_s[:, ts(k % 2, 512)],
                 ["xt%d" % (k % 2), "ones"], "st", start=(k == 0), stop=(k == ND - 1))
            e_act(em, sq_s[:, :], xt_s[:, ts(k % 2, 512)], AF.Square,
                  ["xt%d" % (k % 2)], "sq")
            e_mm(em, ps_st2[:, :], ones_s[:, 0:1], sq_s[:, :],
                 ["sq", "ones"], "st2", start=(k == 0), stop=(k == ND - 1))
        ln_rows(em, ps_st[:, :], ps_st2[:, :], mu_r[:, rsl], rs_r[:, rsl],
                rowtmp[:, :], 1e-5, D, "st", "st2", "mu", "rs", "rt")
        e_bcast(em, ps_r[:, :], ones_s, mu_r[:, rsl], "mu", "psr")
        e_bcast(em, ps_r2[:, :], ones_s, rs_r[:, rsl], "rs", "psr2")
        for k in range(ND):
            em.dma("sync", xt_s[:, ts(k % 2, 512)], xf[ts(k, 128), cs],
                   writes=["xt%d" % (k % 2)])
            xnap = (xn0b[:, k * T_OWN + c * 512: k * T_OWN + (c + 1) * 512]
                    if own else xnbt[:, ts(k, 512)])
            xnkey = ("xn0b_%d" % k) if own else ("xnbt_%d" % k)
            e_tt(em, tmp_a[:, :], xt_s[:, ts(k % 2, 512)], ps_r[:, :], ALU.subtract,
                 ["xt%d" % (k % 2), "psr"], "tmpa")
            e_tt(em, xnap, tmp_a[:, :], ps_r2[:, :], ALU.mult,
                 ["tmpa", "psr2"], xnkey)
            e_red(em, red_c[:, 0:1], xnap, ALU.add, [xnkey], "redc")
            if c == 0:
                e_copy(em, snx_c[:, k:k + 1], red_c[:, 0:1], ["redc"],
                       "snx%d" % k, eng="vector")
            else:
                e_tt(em, snx_c[:, k:k + 1], snx_c[:, k:k + 1], red_c[:, 0:1],
                     ALU.add, ["snx%d" % k, "redc"], "snx%d" % k)
        for tt in range(4):
            gtok = c * 4 + tt
            for qk_i, dst, kdst in ((0, qh_s, "qh"), (1, kh_s, "kh")):
                for k in range(ND):
                    lhs_ap = (xn0b[:, k * T_OWN + c * 512 + tt * 128:
                                   k * T_OWN + c * 512 + (tt + 1) * 128]
                              if own else
                              xnbt[:, k * 512 + tt * 128: k * 512 + (tt + 1) * 128])
                    xnkey = ("xn0b_%d" % k) if own else ("xnbt_%d" % k)
                    e_mm(em, ps_a[:, :], lhs_ap,
                         wqkv_s[:, k * 3 * QK + qk_i * QK:
                                k * 3 * QK + (qk_i + 1) * QK],
                         [xnkey, "wqkv"], "psa", start=(k == 0),
                         stop=(k == ND - 1))
                e_copy(em, dst[:, :], ps_a[:, :], ["psa"], kdst)
            for hp in range(4):
                for sub in range(2):
                    h = 2 * hp + sub
                    e_mm(em, ps_sc[sub * 64:sub * 64 + 64, :],
                         qh_s[:, ts(h, DX)], kh_s[:, ts(h, DX)],
                         ["qh", "kh"], "psc", start=True, stop=True)
                e_tt(em, scores[:, ts(hp, DX)], scores[:, ts(hp, DX)],
                     ps_sc[:, :], ALU.add, ["scoresb", "psc"], "scoresb")
    e_copy(em, snx_b[:, :], snx_c[:, :],
           ["snx%d" % k2 for k2 in range(ND)], "snxb")
    for qk_i, dst, kdst in ((0, sq_row, "sqrow"), (1, sk_row, "skrow")):
        for k in range(ND):
            e_mm(em, ps_st[:, :], snx_b[:, k:k + 1],
                 wqkv_s[:, k * 3 * QK + qk_i * QK: k * 3 * QK + (qk_i + 1) * QK],
                 ["snxb", "wqkv"], "st", start=(k == 0), stop=(k == ND - 1))
        e_copy(em, dst[:, :], ps_st[:, :], ["st"], kdst)
    for hp in range(4):
        for sub in range(2):
            h = 2 * hp + sub
            out_ap = ps_sc[sub * 64:sub * 64 + 64, :]
            e_mm(em, out_ap, bqkv_row[:, ts(h, DX)], sk_row[:, ts(h, DX)],
                 ["bqr", "skrow"], "psc", start=True, stop=False)
            e_mm(em, out_ap, sq_row[:, ts(h, DX)], bqkv_row[:, QK + h * DX:
                                                            QK + (h + 1) * DX],
                 ["sqrow", "bqr"], "psc", start=False, stop=False)
            e_mm(em, out_ap, bqkvL_row[:, ts(h, DX)], bqkv_row[:, QK + h * DX:
                                                               QK + (h + 1) * DX],
                 ["bqL", "bqr"], "psc", start=False, stop=True)
        e_tt(em, scores[:, ts(hp, DX)], scores[:, ts(hp, DX)], ps_sc[:, :],
             ALU.add, ["scoresb", "psc"], "scoresb")


    for hp in range(4):
        sl = ts(hp, DX)
        e_red(em, col_s[:, hp:hp + 1], scores[:, sl], ALU.max, ["scoresb"],
              "cols")
        e_ts(em, col_s[:, 4 + hp:5 + hp], col_s[:, hp:hp + 1], -0.125, None,
             ALU.mult, None, ["cols"], "cols")
        em.emit("scalar",
                lambda s, hp=hp, sl=sl: s.activation(
                    out=scores[:, sl], in_=scores[:, sl], func=AF.Exp,
                    bias=col_s[:, 4 + hp:5 + hp], scale=0.125),
                reads=["scoresb", "cols"], writes=["scoresb"])
        e_red(em, col_s[:, 8 + hp:9 + hp], scores[:, sl], ALU.add,
              ["scoresb"], "cols")
        em.emit("vector",
                lambda v, hp=hp: v.reciprocal(out=col_s[:, 12 + hp:13 + hp],
                                              in_=col_s[:, 8 + hp:9 + hp]),
                reads=["cols"], writes=["cols"])
        e_ts(em, attn[:, sl], scores[:, sl], col_s[:, 12 + hp:13 + hp], None,
             ALU.mult, None, ["scoresb", "cols"], "attn")


    for c in range(OCH):
        for k in range(ND):
            em.dma("sync", xt_s[:, ts(k % 2, 512)], xf[ts(k, 128), ts(c, 512)],
                   writes=["xt%d" % (k % 2)])
            e_mm(em, ps_st[:, :], ones_s[:, 0:1], xt_s[:, ts(k % 2, 512)],
                 ["xt%d" % (k % 2), "ones"], "st", start=(k == 0),
                 stop=(k == ND - 1))
            e_act(em, sq_s[:, :], xt_s[:, ts(k % 2, 512)], AF.Square,
                  ["xt%d" % (k % 2)], "sq")
            e_mm(em, ps_st2[:, :], ones_s[:, 0:1], sq_s[:, :], ["sq", "ones"],
                 "st2", start=(k == 0), stop=(k == ND - 1))
        ln_rows(em, ps_st[:, :], ps_st2[:, :], mu_r[:, :], rs_r[:, :],
                rowtmp[:, :], 1e-5, D, "st", "st2", "mu", "rs", "rt")
        for jt in range(4):
            for k in range(ND):
                e_mm(em, ps_b[:, :],
                     wqkv_s[:, k * 3 * QK + 2 * QK + jt * 128:
                            k * 3 * QK + 2 * QK + (jt + 1) * 128],
                     xn0b[:, k * T_OWN + c * 512: k * T_OWN + (c + 1) * 512],
                     ["wqkv", "xn0b_%d" % k], "psb",
                     start=(k == 0), stop=(k == ND - 1))
            e_act(em, vfm[:, ts(jt, 512)], ps_b[:, :], AF.Identity,
                  ["psb", "bqc"], "vfm", bias=bqkv_col[:, 8 + jt:9 + jt])
        for hp in range(4):
            for sub in range(2):
                h = hp * 2 + sub
                e_mm(em, ps_b[sub * 64:sub * 64 + 64, :],
                     attn[(h % 2) * 64:(h % 2) * 64 + 64, ts(h // 2, DX)],
                     vfm[(h % 2) * 64:(h % 2) * 64 + 64, ts(h // 2, 512)],
                     ["attn", "vfm"], "psb", start=True, stop=True)
            e_copy(em, pv_s[:, ts(hp, 512)], ps_b[:, :], ["psb"], "pv")
        for k in range(ND):
            for j in range(4):
                e_mm(em, ps_c[:, :],
                     wfc_s[:, j * D + k * 128: j * D + (k + 1) * 128],
                     pv_s[:, ts(j, 512)], ["wfc", "pv"], "psc2",
                     start=(j == 0), stop=(j == 3))
            e_act(em, tmp_a[:, :], ps_c[:, :], AF.Identity, ["psc2", "bfc"],
                  "tmpa", bias=bfc_col[:, k:k + 1])
            em.dma("sync", xt_s[:, ts(k % 2, 512)], xf[ts(k, 128), ts(c, 512)],
                   writes=["xt%d" % (k % 2)])
            if k == 0:
                e_bcast(em, ps_r[:, :], ones_s, mu_r[:, :], "mu", "psr")
                e_bcast(em, ps_r2[:, :], ones_s, rs_r[:, :], "rs", "psr2")
            e_tt(em, tmp_b[:, :], xt_s[:, ts(k % 2, 512)], ps_r[:, :], ALU.subtract,
                 ["xt%d" % (k % 2), "psr"], "tmpb")
            e_tt(em, tmp_b[:, :], tmp_b[:, :], ps_r2[:, :], ALU.mult,
                 ["tmpb", "psr2"], "tmpb")
            e_tt(em, s0[:, ts(k, 512)],
                 tmp_a[:, :], tmp_b[:, :], ALU.add, ["tmpa", "tmpb"],
                 "s0_%d" % k)
            e_mm(em, ps_st[:, :], ones_s[:, 0:1],
                 s0[:, ts(k, 512)],
                 ["s0_%d" % k, "ones"], "st", start=(k == 0), stop=(k == ND - 1))
            e_act(em, sq_s[:, :],
                  s0[:, ts(k, 512)],
                  AF.Square, ["s0_%d" % k], "sq")
            e_mm(em, ps_st2[:, :], ones_s[:, 0:1], sq_s[:, :], ["sq", "ones"],
                 "st2", start=(k == 0), stop=(k == ND - 1))
        ln_rows(em, ps_st[:, :], ps_st2[:, :], mu2_r[:, :], rs2_r[:, :],
                rowtmp[:, :], 1e-6, D, "st", "st2", "mu2", "rs2", "rt")
        e_bcast(em, ps_r[:, :], ones_s, mu2_r[:, :], "mu2", "psr")
        e_bcast(em, ps_r2[:, :], ones_s, rs2_r[:, :], "rs2", "psr2")
        for k in range(ND):
            sl = ts(k, 512)
            e_tt(em, s0[:, sl], s0[:, sl], ps_r[:, :], ALU.subtract,
                 ["s0_%d" % k, "psr"], "s0_%d" % k)
            e_tt(em, s0[:, sl], s0[:, sl], ps_r2[:, :], ALU.mult,
                 ["s0_%d" % k, "psr2"], "s0_%d" % k)
            e_ts(em, s0[:, sl], s0[:, sl], gam_col[:, k:k + 1],
                 bet_col[:, k:k + 1], ALU.mult, ALU.add,
                 ["s0_%d" % k, "gam", "bet"], "s0_%d" % k)

        for tt in range(4):
            tok0 = c * 512 + tt * 128
            for k in range(ND):
                e_mm(em, ps_a[:, 0:8],
                     xn0b[:, k * T_OWN + tok0: k * T_OWN + tok0 + 128],
                     gw_s[:, ts(k, 8)], ["xn0b_%d" % k, "gw"], "psa",
                     start=(k == 0), stop=(k == ND - 1))
            e_red(em, col_s[:, 0:1], ps_a[:, 0:8], ALU.max, ["psa"], "cols",
                  negate=True)
            em.emit("scalar",
                    lambda s: s.activation(out=pcols[:, 0:8], in_=ps_a[:, 0:8],
                                           func=AF.Exp, bias=col_s[:, 0:1]),
                    reads=["psa", "cols"], writes=["pcols"])
            e_red(em, col_s[:, 1:2], pcols[:, 0:8], ALU.add, ["pcols"], "cols")
            em.emit("vector",
                    lambda v: v.reciprocal(out=col_s[:, 2:3], in_=col_s[:, 1:2]),
                    reads=["cols"], writes=["cols"])
            e_ts(em, pcols[:, 0:8], pcols[:, 0:8], col_s[:, 2:3], None,
                 ALU.mult, None, ["pcols", "cols"], "pcols")
            e_ts(em, col_s[:, 4:12], pcols[:, 0:8], 0.3, None, ALU.is_gt, None,
                 ["pcols"], "cols")
            e_red(em, col_s[:, 12:13], col_s[:, 4:12], ALU.add, ["cols"], "cols")
            e_ts(em, col_s[:, 13:14], col_s[:, 12:13], -1.0, 8.0, ALU.mult,
                 ALU.add, ["cols"], "cols")
            e_tt(em, col_s[:, 14:15], col_s[:, 13:14], pcols[:, 0:1], ALU.mult,
                 ["cols", "pcols"], "cols")
            e_tt(em, col_s[:, 15:16], col_s[:, 12:13], pcols[:, 1:2], ALU.mult,
                 ["cols", "pcols"], "cols")
            em.emit("tensor",
                    lambda t: t.transpose(out=ps_st[:, 0:128],
                                          in_=col_s[:, 14:15],
                                          identity=eye_s[:, :]),
                    reads=["cols", "eye"], writes=["st"])
            e_copy(em, w01_r[:, tt * 128:(tt + 1) * 128], ps_st[:, 0:128], ["st"], "w01")
            em.emit("tensor",
                    lambda t: t.transpose(out=ps_st[:, 0:128],
                                          in_=col_s[:, 15:16],
                                          identity=eye_s[:, :]),
                    reads=["cols", "eye"], writes=["st"])
            e_copy(em, w01_r[:, 512 + tt * 128: 512 + (tt + 1) * 128],
                   ps_st[:, 0:128], ["st"], "w01")
        e_bcast(em, ps_r[:, :], ones_s, w01_r[:, 0:512], "w01", "psr")
        e_bcast(em, ps_r2[:, :], ones_s, w01_r[:, 512:1024], "w01", "psr2")
        for jt in range(ND):
            for k in range(ND):
                e_mm(em, ps_b[:, :], ew0_s[:, k * D + jt * 128:
                                           k * D + (jt + 1) * 128],
                     xn0b[:, k * T_OWN + c * 512: k * T_OWN + (c + 1) * 512],
                     ["ew0", "xn0b_%d" % k], "psb", start=(k == 0),
                     stop=(k == ND - 1))
            e_copy(em, h0_s[:, :], ps_b[:, :], ["psb"], "h0")
            for k in range(ND):
                e_mm(em, ps_c[:, :], ew1_s[:, k * D + jt * 128:
                                           k * D + (jt + 1) * 128],
                     xn0b[:, k * T_OWN + c * 512: k * T_OWN + (c + 1) * 512],
                     ["ew1", "xn0b_%d" % k], "psc2", start=(k == 0),
                     stop=(k == ND - 1))
            e_copy(em, h1_s[:, :], ps_c[:, :], ["psc2"], "h1")
            e_tt(em, tmp_a[:, :], h0_s[:, :], ps_r[:, :], ALU.mult,
                 ["h0", "psr"], "tmpa")
            e_tt(em, tmp_b[:, :], h1_s[:, :], ps_r2[:, :], ALU.mult,
                 ["h1", "psr2"], "tmpb")
            e_tt(em, gpre[:, ts(jt, 512)], tmp_a[:, :], tmp_b[:, :], ALU.add,
                 ["tmpa", "tmpb"], "gpre")
        for k in range(ND):
            for j in range(ND):
                e_mm(em, ps_c[:, :], ow_s[:, j * D + k * 128:
                                          j * D + (k + 1) * 128],
                     gpre[:, ts(j, 512)], ["ow", "gpre"], "psc2",
                     start=(j == 0), stop=(j == ND - 1))
            e_act(em, ge_s[:, :], ps_c[:, :], AF.Silu, ["psc2"], "ge")
            em.dma("sync", xt_s[:, ts(k % 2, 512)], xf[ts(k, 128), ts(c, 512)],
                   writes=["xt%d" % (k % 2)])
            e_tt(em, tmp_a[:, :],
                 s0[:, ts(k, 512)],
                 ge_s[:, :], ALU.mult, ["s0_%d" % k, "ge"], "tmpa")
            e_tt(em, x1c[:, :], xt_s[:, ts(k % 2, 512)], tmp_a[:, :],
                 ALU.add, ["xt%d" % (k % 2), "tmpa"], "x1c")
            em.dma("gpsimd", x1_d[ts(k, 128), ts(c, 512)], x1c[:, :],
                   reads=["x1c"], writes=["x1out"])
            e_mm(em, ps_st[:, :], ones_s[:, 0:1], x1c[:, :],
                 ["x1c", "ones"], "st", start=(k == 0), stop=(k == ND - 1))
            e_act(em, sq_s[:, :], x1c[:, :], AF.Square,
                  ["x1c"], "sq")
            e_mm(em, ps_st2[:, :], ones_s[:, 0:1], sq_s[:, :], ["sq", "ones"],
                 "st2", start=(k == 0), stop=(k == ND - 1))
        ln_rows(em, ps_st[:, :], ps_st2[:, :], mu2_r[:, :], rs2_r[:, :],
                rowtmp[:, :], 1e-5, D, "st", "st2", "mu2", "rs2", "rt")
        e_bcast(em, ps_r[:, :], ones_s, mu2_r[:, :], "mu2", "psr")
        e_bcast(em, ps_r2[:, :], ones_s, rs2_r[:, :], "rs2", "psr2")
        for k in range(ND):
            em.dma("sync", x1c[:, :], x1_d[ts(k, 128), ts(c, 512)],
                   reads=["x1out"], writes=["x1c"])
            e_tt(em, tmp_a[:, :], x1c[:, :], ps_r[:, :], ALU.subtract,
                 ["x1c", "psr"], "tmpa")
            e_tt(em, tmp_a[:, :], tmp_a[:, :], ps_r2[:, :], ALU.mult,
                 ["tmpa", "psr2"], "tmpa")
            em.dma("gpsimd", xn1_d[ts(k, 128), ts(c, 512)], tmp_a[:, :],
                   reads=["tmpa"], writes=["xn1out"])

    em.finalize()
    return nc


def l0_in_maps(inputs):
    x = np.asarray(inputs["L0_attn_x"])
    w_qkv = np.ascontiguousarray(inputs["L0_attn_w_qkv"])
    b_qkv = np.asarray(inputs["L0_attn_b_qkv"])
    w_fc = np.ascontiguousarray(inputs["L0_attn_w_fc"])
    b_fc = np.asarray(inputs["L0_attn_b_fc"])
    gamma = np.asarray(inputs["L0_attn_gamma"])
    beta = np.asarray(inputs["L0_attn_beta"])
    gate_W = np.asarray(inputs["L0_ffn_gate_W"])
    expert_W = np.asarray(inputs["L0_ffn_expert_W"])
    out_W = np.asarray(inputs["L0_ffn_out_W"])

    common = {
        "w_qkv": w_qkv,
        "bqkv_row": b_qkv.reshape(1, -1),
        "bqkv_col": np.ascontiguousarray(b_qkv.reshape(12, 128).T),
        "w_fc": w_fc,
        "bfc_col": np.ascontiguousarray(b_fc.reshape(8, 128).T),
        "gam_col": np.ascontiguousarray(gamma.reshape(8, 128).T),
        "bet_col": np.ascontiguousarray(beta.reshape(8, 128).T),
        "gate_wT": np.ascontiguousarray(gate_W.T),
        "ew0T": np.ascontiguousarray(expert_W[0].T),
        "ew1T": np.ascontiguousarray(expert_W[1].T),
        "out_wT": np.ascontiguousarray(out_W.T),
        "iden": np.eye(128, dtype=np.float32),
    }
    maps = []
    for core in range(8):
        b, half = core // 2, core % 2
        xb = x[b]
        own = xb[half * T_OWN:(half + 1) * T_OWN]
        oth = xb[(1 - half) * T_OWN:(2 - half) * T_OWN]
        xf = np.ascontiguousarray(
            np.concatenate([own, oth], axis=0).T)
        m = dict(common)
        m["xf"] = xf
        maps.append(m)
    return maps


def run_l0(inputs):
    nc = build_l0()
    maps = l0_in_maps(inputs)
    res = run_bass_kernel_spmd(nc, maps, core_ids=list(range(8)))
    x1 = np.zeros((B, L, D), np.float32)
    xn1 = np.zeros((B, L, D), np.float32)
    for core in range(8):
        b, half = core // 2, core % 2
        sl = slice(half * T_OWN, (half + 1) * T_OWN)
        x1[b, sl] = res.results[core]["x1"].T
        xn1[b, sl] = res.results[core]["xn1"].T
    return x1, xn1


D_IN = 2048
DST = 16
TC = 512
NCT = D_IN // 128


def build_mamba():
    nc = bass.Bass("TRN2", target_bir_lowering=False)
    ND = D // 128

    NRT0 = 2 * D_IN // 128
    xn1f = nc.dram_tensor("xn1f", [D, L], F32, kind="ExternalInput").ap()
    in_wT = nc.dram_tensor("in_t", [128, NRT0 * ND * 128], F32,
                           kind="ExternalInput").ap()
    xprojT = nc.dram_tensor("xprojT", [D_IN, 96], F32, kind="ExternalInput").ap()
    dt_wT = nc.dram_tensor("dt_wT", [64, D_IN], F32, kind="ExternalInput").ap()
    conv_c_d = nc.dram_tensor("conv_c", [128, NCT * 4], F32, kind="ExternalInput").ap()
    convb_c_d = nc.dram_tensor("convb_c", [128, NCT], F32, kind="ExternalInput").ap()
    dtb_c_d = nc.dram_tensor("dtb_c", [128, NCT], F32, kind="ExternalInput").ap()
    alog_d = nc.dram_tensor("alog", [128, NCT * DST], F32, kind="ExternalInput").ap()
    dcol_d = nc.dram_tensor("dcol", [128, NCT], F32, kind="ExternalInput").ap()
    mout_wT = nc.dram_tensor("mout_t", [128, ND * NCT * 128], F32,
                             kind="ExternalInput").ap()
    outw_sT = nc.dram_tensor("outw_t", [128, ND * ND * 128], F32,
                             kind="ExternalInput").ap()
    gam_col_d = nc.dram_tensor("gam_col", [128, ND], F32, kind="ExternalInput").ap()
    bet_col_d = nc.dram_tensor("bet_col", [128, ND], F32, kind="ExternalInput").ap()
    esel_d = nc.dram_tensor("esel", [32, 32 * 128], F32, kind="ExternalInput").ap()
    part_d = nc.dram_tensor("partial", [D, L], F32, kind="ExternalOutput").ap()
    inw_b_d = nc.dram_tensor("inw_b", [128, NRT0 * ND * 128], BF16).ap()
    moutw_b_d = nc.dram_tensor("moutw_b", [128, ND * NCT * 128], BF16).ap()
    outw_b_d = nc.dram_tensor("outw_b", [128, ND * ND * 128], BF16).ap()

    em = EM(nc)

    ones_s = sbuf(nc, "ones_s", [128, 128])
    wtmp = sbuf(nc, "wtmp", [128, 512])
    wbtmp = sbuf(nc, "wbtmp", [128, 512], BF16)
    wstr = sbuf(nc, "wstr", [128, 2 * 2048], BF16)
    xproj_s = sbuf(nc, "xproj_s", [128, NCT * 96], BF16)
    dtw_s = sbuf(nc, "dtw_s", [64, D_IN], BF16)
    esel_s = sbuf(nc, "esel_s", [32, 32 * 128], BF16)
    conv_c = sbuf(nc, "conv_c_s", [128, NCT * 4])
    convb_c = sbuf(nc, "convb_c_s", [128, NCT])
    dtb_c = sbuf(nc, "dtb_c_s", [128, NCT])
    a_s = sbuf(nc, "a_s", [128, NCT * DST])
    dcol_s = sbuf(nc, "dcol_s", [128, NCT])
    gam_col = sbuf(nc, "gam_col_s", [128, ND])
    bet_col = sbuf(nc, "bet_col_s", [128, ND])
    xch = sbuf(nc, "xch", [128, ND * TC])
    hchb = sbuf(nc, "hchb", [128, ND * TC], BF16)
    xi_s = sbuf(nc, "xi_s", [128, NCT * (TC + 3)], BF16)
    z_s = sbuf(nc, "z_s", [128, NCT * TC], BF16)
    xcb_s = sbuf(nc, "xcb_s", [128, NCT * TC], BF16)
    ygb_s = sbuf(nc, "ygb_s", [128, NCT * TC], BF16)
    xdbl_s = sbuf(nc, "xdbl_s", [96, TC], BF16)
    xdbl_bc = sbuf(nc, "xdbl_bc", [32, TC], BF16)
    delta_s = sbuf(nc, "delta_s", [128, 2 * TC])
    du_s = sbuf(nc, "du_s", [128, 2 * TC])
    tsp_s = sbuf(nc, "tsp_s", [128, 2 * TC])
    av_s = sbuf(nc, "av_s", [128, 2 * TC])
    bv_s = sbuf(nc, "bv_s", [128, 2 * TC])
    hv_s = sbuf(nc, "hv_s", [128, 2 * TC])
    ty_s = sbuf(nc, "ty_s", [128, 2 * TC])
    yv_s = sbuf(nc, "yv_s", [128, TC])
    bc_c = sbuf(nc, "bc_c", [128, DST * TC], BF16)
    cc_c = sbuf(nc, "cc_c", [128, DST * TC], BF16)
    t1_s = sbuf(nc, "t1_s", [128, TC])
    t2_s = sbuf(nc, "t2_s", [128, TC])
    states = sbuf(nc, "states", [128, NCT * DST])
    p1_s = sbuf(nc, "p1_s", [128, ND * TC], BF16)
    mu_r = sbuf(nc, "mu_r", [1, TC])
    rs_r = sbuf(nc, "rs_r", [1, TC])
    rowtmp = sbuf(nc, "rowtmp", [1, TC])
    sq_s = sbuf(nc, "sq_s", [128, TC])

    ps_m = psum_t(nc, "ps_m", [128, 512])
    ps_x = psum_t(nc, "ps_x", [128, 512])
    ps_bc = psum_t(nc, "ps_bc", [128, 512])
    ps_cc = psum_t(nc, "ps_cc", [128, 512])
    ps_st = psum_t(nc, "ps_st", [1, 512])
    ps_st2 = psum_t(nc, "ps_st2", [1, 512])
    ps_r = psum_t(nc, "ps_r", [128, 512])
    ps_r2 = psum_t(nc, "ps_r2", [128, 512])

    em.emit("gpsimd", lambda g: g.memset(ones_s[:, :], 1.0), writes=["ones"])
    em.emit("gpsimd", lambda g: g.memset(xi_s[:, :], 0.0), writes=["xi"])
    em.dma("sync", conv_c[:, :], conv_c_d[:, :], writes=["convc"])
    em.dma("sync", convb_c[:, :], convb_c_d[:, :], writes=["convb"])
    em.dma("sync", dtb_c[:, :], dtb_c_d[:, :], writes=["dtb"])
    em.dma("sync", dcol_s[:, :], dcol_d[:, :], writes=["dcol"])
    em.dma("sync", gam_col[:, :], gam_col_d[:, :], writes=["gam"])
    em.dma("sync", bet_col[:, :], bet_col_d[:, :], writes=["bet"])
    em.dma("sync", a_s[:, :], alog_d[:, :], writes=["as"])
    em.emit("scalar", lambda s: s.activation(out=a_s[:, :], in_=a_s[:, :],
                                             func=AF.Exp),
            reads=["as"], writes=["as"])
    e_ts(em, a_s[:, :], a_s[:, :], -1.0, None, ALU.mult, None, ["as"], "as")
    for i in range(8):
        em.dma("sync", wtmp[0:32, 0:512], esel_d[:, ts(i, 512)], writes=["wtmp"])
        e_copy(em, esel_s[:, ts(i, 512)], wtmp[0:32, 0:512], ["wtmp"], "esel")
    for ct in range(NCT):
        em.dma("sync", wtmp[:, 0:96], xprojT[ts(ct, 128), :], writes=["wtmp"])
        e_copy(em, xproj_s[:, ts(ct, 96)], wtmp[:, 0:96], ["wtmp"], "xprj")
    for i in range(4):
        em.dma("sync", wtmp[0:64, 0:512], dt_wT[:, ts(i, 512)], writes=["wtmp"])
        e_copy(em, dtw_s[:, ts(i, 512)], wtmp[0:64, 0:512], ["wtmp"], "dtw")
    def cast_to_dram(src_ap, dst_ap, cols, tag):
        CW = 2048
        for o in range(0, cols, CW):
            par = (o // CW) % 2
            if par == 0:
                lbuf = xch[:, 0:CW]
                lkeys = ["xch0", "xch1", "xch2", "xch3"]
                stg, skey = xcb_s[:, 0:CW], "xcb"
            else:
                lbuf = xch[:, CW:2 * CW]
                lkeys = ["xch4", "xch5", "xch6", "xch7"]
                stg, skey = z_s[:, 0:CW], "z"
            em.dma(("sync", "scalar")[par], lbuf, src_ap[:, o:o + CW],
                   writes=lkeys)
            e_copy(em, stg, lbuf, lkeys, skey,
                   eng=("scalar" if par == 0 else "vector"))
            em.dma(("gpsimd", "sync")[par], dst_ap[:, o:o + CW],
                   stg, reads=[skey], writes=[tag])

    cast_to_dram(in_wT, inw_b_d, NRT0 * ND * 128, "inwb")
    cast_to_dram(mout_wT, moutw_b_d, ND * NCT * 128, "moutb")
    cast_to_dram(outw_sT, outw_b_d, ND * ND * 128, "outwb")

    NRT = 2 * D_IN // 128

    def ph_pre(c):
        csl = ts(c, TC)
        for k in range(ND):
            em.dma("sync", xch[:, ts(k, TC)], xn1f[ts(k, 128), csl],
                   writes=["xch%d" % k])
            e_mm(em, ps_st[:, :], ones_s[:, 0:1], xch[:, ts(k, TC)],
                 ["xch%d" % k, "ones"], "st", start=(k == 0), stop=(k == ND - 1))
            e_act(em, sq_s[:, :], xch[:, ts(k, TC)], AF.Square,
                  ["xch%d" % k], "sq")
            e_mm(em, ps_st2[:, :], ones_s[:, 0:1], sq_s[:, :], ["sq", "ones"],
                 "st2", start=(k == 0), stop=(k == ND - 1))
        ln_rows(em, ps_st[:, :], ps_st2[:, :], mu_r[:, :], rs_r[:, :],
                rowtmp[:, :], 1e-5, D, "st", "st2", "mu", "rs", "rt")
        e_bcast(em, ps_r[:, :], ones_s, mu_r[:, :], "mu", "psr")
        e_bcast(em, ps_r2[:, :], ones_s, rs_r[:, :], "rs", "psr2")
        for k in range(ND):
            e_tt(em, sq_s[:, :], xch[:, ts(k, TC)], ps_r[:, :], ALU.subtract,
                 ["xch%d" % k, "psr"], "sq")
            e_tt(em, sq_s[:, :], sq_s[:, :], ps_r2[:, :], ALU.mult,
                 ["sq", "psr2"], "sq")
            e_ts(em, hchb[:, ts(k, TC)], sq_s[:, :], gam_col[:, k:k + 1],
                 bet_col[:, k:k + 1], ALU.mult, ALU.add,
                 ["sq", "gam", "bet"], "hchb")
    def ph_mid(c):
        for rt in range(NRT):
            buf = rt % 2
            bank, bk = (ps_m, "psm") if buf == 0 else (ps_x, "psx")
            em.dma(("sync", "scalar")[buf],
                   wstr[:, buf * 2048: buf * 2048 + 1024],
                   inw_b_d[:, rt * 1024:(rt + 1) * 1024],
                   reads=["inwb"], writes=["wstr%d" % buf])
            for k in range(ND):
                e_mm(em, bank[:, :],
                     wstr[:, buf * 2048 + k * 128: buf * 2048 + (k + 1) * 128],
                     hchb[:, ts(k, TC)], ["wstr%d" % buf, "hchb"], bk,
                     start=(k == 0), stop=(k == ND - 1))
            if rt < NCT:
                ct = rt
                e_copy(em, xi_s[:, ct * (TC + 3): ct * (TC + 3) + 3],
                       xi_s[:, ct * (TC + 3) + TC: ct * (TC + 3) + TC + 3],
                       ["xi"], "xi", eng="vector")
                e_copy(em, xi_s[:, ct * (TC + 3) + 3: (ct + 1) * (TC + 3)],
                       bank[:, :], [bk, "xi"], "xi")
            else:
                ct = rt - NCT
                e_copy(em, z_s[:, ts(ct, TC)], bank[:, :], [bk], "z")
        for ct in range(NCT):
            base = ct * (TC + 3)
            e_ts(em, t1_s[:, :], xi_s[:, base: base + TC],
                 conv_c[:, ct * 4: ct * 4 + 1], None, ALU.mult, None,
                 ["xi", "convc"], "t1")
            for j in range(1, 4):
                e_stt(em, t1_s[:, :], xi_s[:, base + j: base + j + TC],
                      conv_c[:, ct * 4 + j: ct * 4 + j + 1], t1_s[:, :],
                      ALU.mult, ALU.add, ["xi", "convc", "t1"], "t1")
            e_act(em, xcb_s[:, ts(ct, TC)], t1_s[:, :], AF.Silu,
                  ["t1", "convb"], "xcb", bias=convb_c[:, ct:ct + 1])
        for ct in range(NCT):
            e_mm(em, ps_x[0:96, :], xproj_s[:, ts(ct, 96)], xcb_s[:, ts(ct, TC)],
                 ["xprj", "xcb"], "psx", start=(ct == 0), stop=(ct == NCT - 1))
        e_copy(em, xdbl_s[:, :], ps_x[0:96, :], ["psx"], "xdbl")
        em.dma("gpsimd", xdbl_bc[:, :], xdbl_s[64:96, :], reads=["xdbl"],
               writes=["xdblbc"])
        for s in range(DST):
            bank, bk = (ps_bc, "psbc") if s % 2 == 0 else (ps_cc, "pscc")
            e_mm(em, bank[:, :], esel_s[:, s * 128:(s + 1) * 128],
                 xdbl_bc[:, :], ["esel", "xdblbc"], bk, start=True, stop=True)
            e_copy(em, bc_c[:, ts(s, TC)], bank[:, :], [bk], "bcc",
                   eng=("scalar" if s % 2 == 0 else "vector"))
        for s in range(DST):
            bank, bk = (ps_bc, "psbc") if s % 2 == 0 else (ps_cc, "pscc")
            e_mm(em, bank[:, :], esel_s[:, (16 + s) * 128:(17 + s) * 128],
                 xdbl_bc[:, :], ["esel", "xdblbc"], bk, start=True, stop=True)
            e_copy(em, cc_c[:, ts(s, TC)], bank[:, :], [bk], "ccc",
                   eng=("scalar" if s % 2 == 0 else "vector"))
    def ph_scan(c):
        for ct in range(NCT):
            cp = ct % 2
            dsl = delta_s[:, ts(cp, TC)]
            dusl = du_s[:, ts(cp, TC)]
            dk, duk = "delta%d" % cp, "du%d" % cp
            bank, bk = (ps_m, "psm") if cp == 0 else (ps_x, "psx")
            tsa = tsp_s[:, ts(cp, TC)]
            tsk = "tsp%d" % cp
            e_mm(em, bank[:, :], dtw_s[:, ts(ct, 128)], xdbl_s[0:64, :],
                 ["dtw", "xdbl"], bk, start=True, stop=True)
            e_act(em, tsa, bank[:, :], AF.Abs, [bk, "dtb"], tsk,
                  bias=dtb_c[:, ct:ct + 1])
            e_act(em, tsa, tsa, AF.Exp, [tsk], tsk, scale=-1.0)
            e_act(em, tsa, tsa, AF.Ln, [tsk], tsk, bias=1.0)
            e_act(em, dsl, bank[:, :], AF.Relu, [bk, "dtb"], dk,
                  bias=dtb_c[:, ct:ct + 1])
            e_tt(em, dsl, dsl, tsa, ALU.add, [dk, tsk], dk)
            e_tt(em, dusl, dsl, xcb_s[:, ts(ct, TC)], ALU.mult,
                 [dk, "xcb"], duk)
            for s in range(DST):
                sp = s % 2
                av = av_s[:, ts(sp, TC)]
                bv = bv_s[:, ts(sp, TC)]
                hv = hv_s[:, ts(sp, TC)]
                avk, bvk, hvk = "av%d" % sp, "bv%d" % sp, "hv%d" % sp
                e_act(em, av, dsl, AF.Exp, [dk, "as"],
                      avk, scale=a_s[:, ct * DST + s: ct * DST + s + 1])
                e_tt(em, bv, dusl, bc_c[:, ts(s, TC)], ALU.mult,
                     [duk, "bcc"], bvk)
                stcol = states[:, ct * DST + s: ct * DST + s + 1]
                stkey = "stt%d_%d" % (ct, s)
                if c == 0:
                    em.emit("vector",
                            lambda v, av=av, bv=bv, hv=hv:
                            v.tensor_tensor_scan(
                                out=hv, data0=av, data1=bv,
                                initial=0.0, op0=ALU.mult, op1=ALU.add),
                            reads=[avk, bvk], writes=[hvk])
                else:
                    em.emit("vector",
                            lambda v, av=av, bv=bv, hv=hv, stcol=stcol:
                            v.tensor_tensor_scan(
                                out=hv, data0=av, data1=bv,
                                initial=stcol, op0=ALU.mult, op1=ALU.add),
                            reads=[avk, bvk, stkey], writes=[hvk])
                e_copy(em, stcol, hv_s[:, sp * TC + TC - 1: sp * TC + TC],
                       [hvk], stkey)
                if s == 0:
                    e_tt(em, yv_s[:, :], hv, cc_c[:, ts(s, TC)], ALU.mult,
                         [hvk, "ccc"], "yv")
                else:
                    tyv = ty_s[:, ts(sp, TC)]
                    e_tt(em, tyv, hv, cc_c[:, ts(s, TC)], ALU.mult,
                         [hvk, "ccc"], "ty%d" % sp)
                    e_tt(em, yv_s[:, :], yv_s[:, :], tyv, ALU.add,
                         ["yv", "ty%d" % sp], "yv")
            e_stt(em, yv_s[:, :], xcb_s[:, ts(ct, TC)], dcol_s[:, ct:ct + 1],
                  yv_s[:, :], ALU.mult, ALU.add, ["xcb", "dcol", "yv"], "yv")
            e_act(em, t1_s[:, :], z_s[:, ts(ct, TC)], AF.Silu, ["z"], "t1")
            e_tt(em, ygb_s[:, ts(ct, TC)], yv_s[:, :], t1_s[:, :], ALU.mult,
                 ["yv", "t1"], "ygb")
    def ph_post(c):
        csl = ts(c, TC)
        for dt in range(ND):
            sl2 = dt % 2
            bank, bk = (ps_m, "psm") if sl2 == 0 else (ps_x, "psx")
            em.dma("sync", wstr[:, sl2 * 2048: sl2 * 2048 + 1024],
                   moutw_b_d[:, dt * 2048: dt * 2048 + 1024],
                   reads=["moutb"], writes=["wstr%d" % sl2])
            em.dma("scalar", wstr[:, sl2 * 2048 + 1024: (sl2 + 1) * 2048],
                   moutw_b_d[:, dt * 2048 + 1024:(dt + 1) * 2048],
                   reads=["moutb"], writes=["wstr%d" % sl2])
            for ct in range(NCT):
                e_mm(em, bank[:, :],
                     wstr[:, sl2 * 2048 + ct * 128:
                          sl2 * 2048 + (ct + 1) * 128],
                     ygb_s[:, ts(ct, TC)], ["wstr%d" % sl2, "ygb"], bk,
                     start=(ct == 0), stop=(ct == NCT - 1))
            e_copy(em, p1_s[:, ts(dt, TC)], bank[:, :], [bk], "p1",
                   eng=("scalar" if sl2 == 0 else "vector"))
        for dt in range(ND):
            sl2 = dt % 2
            bank, bk = (ps_m, "psm") if sl2 == 0 else (ps_x, "psx")
            em.dma(("sync", "scalar")[sl2],
                   wstr[:, sl2 * 2048: sl2 * 2048 + 1024],
                   outw_b_d[:, dt * 1024:(dt + 1) * 1024],
                   reads=["outwb"], writes=["wstr%d" % sl2])
            for j in range(ND):
                e_mm(em, bank[:, :],
                     wstr[:, sl2 * 2048 + j * 128: sl2 * 2048 + (j + 1) * 128],
                     p1_s[:, ts(j, TC)], ["wstr%d" % sl2, "p1"], bk,
                     start=(j == 0), stop=(j == ND - 1))
            e_copy(em, t1_s[:, :], bank[:, :], [bk], "t1")
            em.dma(("gpsimd", "sync")[sl2], part_d[ts(dt, 128), csl],
                   t1_s[:, :], reads=["t1"], writes=["pout"])

    NCH = L // TC
    ph_pre(0)
    ph_mid(0)
    for c in range(NCH):
        if c + 1 < NCH:
            ph_pre(c + 1)
        ph_scan(c)
        if c + 1 < NCH:
            ph_mid(c + 1)
        ph_post(c)

    em.finalize()
    return nc


def _tilemaj(wmat, nr, nccols):
    return np.ascontiguousarray(
        np.float32(wmat).reshape(nr, 128, nccols, 128)
        .transpose(1, 2, 0, 3).reshape(128, -1))


def mamba_in_maps(inputs, xn1):
    esel = np.zeros((32, 32 * 128), np.float32)
    for i in range(32):
        esel[i, i * 128:(i + 1) * 128] = 1.0
    maps = []
    for core in range(8):
        b, dirb = core // 2, core % 2
        tag = "f" if dirb == 0 else "b"
        arr = xn1[b]
        if dirb == 1:
            arr = arr[::-1]
        m = {
            "xn1f": np.ascontiguousarray(arr.T),
            "in_t": _tilemaj(np.asarray(inputs["L1_attn_in_w_" + tag]).T,
                             D // 128, 2 * D_IN // 128),
            "xprojT": np.ascontiguousarray(
                np.asarray(inputs["L1_attn_xproj_w_" + tag]).T),
            "dt_wT": np.ascontiguousarray(
                np.asarray(inputs["L1_attn_dt_w_" + tag]).T),
            "conv_c": np.ascontiguousarray(
                np.asarray(inputs["L1_attn_conv_w_" + tag]).reshape(
                    D_IN, 4).reshape(NCT, 128, 4).transpose(1, 0, 2).reshape(
                        128, NCT * 4)),
            "convb_c": np.ascontiguousarray(
                np.asarray(inputs["L1_attn_conv_b_" + tag]).reshape(
                    NCT, 128).T),
            "dtb_c": np.ascontiguousarray(
                np.asarray(inputs["L1_attn_dt_b_" + tag]).reshape(NCT, 128).T),
            "alog": np.ascontiguousarray(
                np.asarray(inputs["L1_attn_A_log_" + tag]).reshape(
                    NCT, 128, DST).transpose(1, 0, 2).reshape(128, NCT * DST)),
            "dcol": np.ascontiguousarray(
                np.asarray(inputs["L1_attn_D_" + tag]).reshape(NCT, 128).T),
            "mout_t": _tilemaj(np.asarray(inputs["L1_attn_mout_w_" + tag]).T,
                               D_IN // 128, D // 128),
            "outw_t": _tilemaj(
                np.asarray(inputs["L1_attn_out_w"])[:,
                    dirb * D:(dirb + 1) * D].T, D // 128, D // 128),
            "gam_col": np.ascontiguousarray(
                np.asarray(inputs["L1_attn_gamma"]).reshape(8, 128).T),
            "bet_col": np.ascontiguousarray(
                np.asarray(inputs["L1_attn_beta"]).reshape(8, 128).T),
            "esel": esel,
        }
        maps.append(m)
    return maps


def run_mamba(inputs, xn1):
    nc = build_mamba()
    maps = mamba_in_maps(inputs, xn1)
    res = run_bass_kernel_spmd(nc, maps, core_ids=list(range(8)))
    pf = np.zeros((B, L, D), np.float32)
    pb = np.zeros((B, L, D), np.float32)
    for core in range(8):
        b, dirb = core // 2, core % 2
        p = res.results[core]["partial"].T
        if dirb == 1:
            p = p[::-1]
            pb[b] = p
        else:
            pf[b] = p
    return pf, pb


TS = 2560
NW = TS // 256
HEADS = 16
DH = 64
INNER = 1024
FFI = 2816


FFN_SKIP = set()


def build_ffn():
    nc = bass.Bass("TRN2", target_bir_lowering=False)
    ND = D // 128
    NI = INNER // 128
    NJC = FFI // 128

    xst_d = nc.dram_tensor("xst", [D, 1 + TS], F32, kind="ExternalInput").ap()
    wqk_d = nc.dram_tensor("wqk_t", [128, 2 * NI * ND * 128], F32,
                           kind="ExternalInput").ap()
    wv_d = nc.dram_tensor("wv_t", [128, 2 * ND * 512], F32,
                          kind="ExternalInput").ap()
    wo_d = nc.dram_tensor("wo_t", [128, ND * NI * 128], F32,
                          kind="ExternalInput").ap()
    w1a_d = nc.dram_tensor("w1a_t", [128, NJC * ND * 128], F32,
                           kind="ExternalInput").ap()
    w1g_d = nc.dram_tensor("w1g_t", [128, NJC * ND * 128], F32,
                           kind="ExternalInput").ap()
    w2_d = nc.dram_tensor("w2_t", [128, ND * NJC * 128], F32,
                          kind="ExternalInput").ap()
    b1a_d = nc.dram_tensor("b1a", [128, NJC], F32, kind="ExternalInput").ap()
    b1g_d = nc.dram_tensor("b1g", [128, NJC], F32, kind="ExternalInput").ap()
    b2_d = nc.dram_tensor("b2c", [128, ND], F32, kind="ExternalInput").ap()
    lng_d = nc.dram_tensor("lng", [128, ND], F32, kind="ExternalInput").ap()
    lnb_d = nc.dram_tensor("lnb", [128, ND], F32, kind="ExternalInput").ap()
    qs_d = nc.dram_tensor("qs", [128, 1], F32, kind="ExternalInput").ap()
    ks_d = nc.dram_tensor("ks", [128, 1], F32, kind="ExternalInput").ap()
    cs_d = nc.dram_tensor("cs_t", [128, TS], F32, kind="ExternalInput").ap()
    ssn_d = nc.dram_tensor("ssn_t", [128, TS], F32, kind="ExternalInput").ap()
    csk_d = nc.dram_tensor("csk_t", [128, TS], F32, kind="ExternalInput").ap()
    ssnk_d = nc.dram_tensor("ssnk_t", [128, TS], F32, kind="ExternalInput").ap()
    flag_d = nc.dram_tensor("flagrow", [1, NW * 2 * 384], F32,
                            kind="ExternalInput").ap()
    band_d = nc.dram_tensor("band", [128, 384], F32, kind="ExternalInput").ap()
    cmask_d = nc.dram_tensor("cmask", [1, TS], F32, kind="ExternalInput").ap()
    blk16_d = nc.dram_tensor("blk16", [128, NI * 16], F32,
                             kind="ExternalInput").ap()
    esel2_d = nc.dram_tensor("esel2", [16, NI * 128], F32,
                             kind="ExternalInput").ap()
    iden_d = nc.dram_tensor("iden", [128, 128], F32, kind="ExternalInput").ap()
    rperm_d = nc.dram_tensor("rperm", [128, 128], F32, kind="ExternalInput").ap()
    xout_d = nc.dram_tensor("xout", [D, T_OWN], F32, kind="ExternalOutput").ap()
    wqk_b = nc.dram_tensor("wqk_b", [128, 2 * NI * ND * 128], BF16).ap()
    wv_b = nc.dram_tensor("wv_b", [128, 2 * ND * 512], BF16).ap()
    wo_b = nc.dram_tensor("wo_b", [128, ND * NI * 128], BF16).ap()
    w1a_b = nc.dram_tensor("w1a_b", [128, NJC * ND * 128], BF16).ap()
    w1g_b = nc.dram_tensor("w1g_b", [128, NJC * ND * 128], BF16).ap()
    w2_b = nc.dram_tensor("w2_b", [128, ND * NJC * 128], BF16).ap()

    em = EM(nc)

    xs = sbuf(nc, "xs", [128, ND * (1 + TS)])
    wqkstr = sbuf(nc, "wqkstr", [128, 2 * 1024], BF16)
    wostr = sbuf(nc, "wostr", [128, 2 * NI * 128], BF16)
    wvstr = sbuf(nc, "wvstr", [128, 2 * 1024], BF16)
    ones_s = sbuf(nc, "ones_s", [128, 128])
    rperm_s = sbuf(nc, "rperm_s", [128, 128])
    eye_b = sbuf(nc, "eye_b", [128, 128], BF16)
    wtmp = sbuf(nc, "wtmp", [128, 512])
    w1str = sbuf(nc, "w1str", [128, 2048], BF16)
    tf1 = sbuf(nc, "tf1", [128, 2 * 512])
    tf2 = sbuf(nc, "tf2", [128, 2 * 512])
    xwb = sbuf(nc, "xwb", [128, ND * 256], BF16)
    qro = sbuf(nc, "qro", [128, NI * 256], BF16)
    kro = sbuf(nc, "kro", [128, NI * 512], BF16)
    vtm = sbuf(nc, "vtm", [128, 4 * INNER], BF16)
    q1_s = sbuf(nc, "q1_s", [128, 2 * 256])
    cs_s = sbuf(nc, "cs_s", [128, 256])
    ssn_s = sbuf(nc, "ssn_s", [128, 256])
    csk_s = sbuf(nc, "csk_s", [128, 256])
    ssnk_s = sbuf(nc, "ssnk_s", [128, 256])
    rr_s = sbuf(nc, "rr_s", [16, 512])
    rr_b = sbuf(nc, "rr_b", [16, 512], BF16)
    blk16_s = sbuf(nc, "blk16_s", [128, NI * 16])
    esel2_b = sbuf(nc, "esel2_b", [16, NI * 128], BF16)
    pexp_b = sbuf(nc, "pexp_b", [128, 4 * 384], BF16)
    pt_b = sbuf(nc, "pt_b", [128, 2 * 384], BF16)
    col_s = sbuf(nc, "col_s", [128, 16])
    attm = sbuf(nc, "attm", [128, INNER], BF16)
    attf = sbuf(nc, "attf", [128, NI * 256], BF16)
    band_b = sbuf(nc, "band_b", [128, 384], BF16)
    ones_bf = sbuf(nc, "ones_bf", [1, 128], BF16)
    flag_r = sbuf(nc, "flag_r", [1, 384])
    flag_b = sbuf(nc, "flag_b", [1, 384], BF16)
    cmask_r = sbuf(nc, "cmask_r", [1, 512])
    qs_c = sbuf(nc, "qs_c", [128, 2])
    b1a_c = sbuf(nc, "b1a_c", [128, NJC])
    b1g_c = sbuf(nc, "b1g_c", [128, NJC])
    b2_c = sbuf(nc, "b2_c", [128, ND])
    lng_c = sbuf(nc, "lng_c", [128, ND])
    lnb_c = sbuf(nc, "lnb_c", [128, ND])
    xnb = sbuf(nc, "xnb", [128, ND * 512], BF16)
    hgb = sbuf(nc, "hgb", [128, NJC * 512], BF16)
    fout = sbuf(nc, "fout", [128, ND * 512], BF16)
    mu_r = sbuf(nc, "mu_r", [1, 512])
    rs_r = sbuf(nc, "rs_r", [1, 512])
    rowtmp = sbuf(nc, "rowtmp", [1, 512])

    ps = [psum_t(nc, "psg%d" % i, [128, 512]) for i in range(6)]
    ps_r = psum_t(nc, "ps_r", [128, 512])
    ps_r2 = psum_t(nc, "ps_r2", [128, 512])
    ps_tb = ps[4][:, 256:320].bitcast(BF16)
    ps_tb2 = ps_r[:, 0:64].bitcast(BF16)
    PK = ["psg%d" % i for i in range(6)]

    em.emit("gpsimd", lambda g: g.memset(ones_s[:, :], 1.0), writes=["ones"])
    em.emit("gpsimd", lambda g: g.memset(kro[:, :], 0.0), writes=["kro"])
    em.emit("gpsimd", lambda g: g.memset(vtm[:, :], 0.0), writes=["vtm"])
    e_copy(em, ones_bf[:, :], ones_s[0:1, :], ["ones"], "onesb")
    em.dma("sync", wtmp[:, 0:384], band_d[:, :], writes=["wtmp0"])
    e_copy(em, band_b[:, :], wtmp[:, 0:384], ["wtmp0"], "bandb")
    em.dma("sync", blk16_s[:, :], blk16_d[:, :], writes=["blk16"])
    em.dma("sync", b1a_c[:, :], b1a_d[:, :], writes=["b1a"])
    em.dma("sync", b1g_c[:, :], b1g_d[:, :], writes=["b1g"])
    em.dma("sync", b2_c[:, :], b2_d[:, :], writes=["b2"])
    em.dma("sync", lng_c[:, :], lng_d[:, :], writes=["lng"])
    em.dma("sync", lnb_c[:, :], lnb_d[:, :], writes=["lnb"])
    for i in range(2):
        em.dma("sync", wtmp[0:16, 0:512], esel2_d[:, ts(i, 512)], writes=["wtmp0"])
        e_copy(em, esel2_b[:, ts(i, 512)], wtmp[0:16, 0:512], ["wtmp0"], "esel2")
    em.dma("sync", rperm_s[:, :], rperm_d[:, :], writes=["rperm"])
    em.dma("sync", wtmp[:, 0:128], iden_d[:, :], writes=["wtmp0"])
    e_copy(em, eye_b[:, :], wtmp[:, 0:128], ["wtmp0"], "eye")
    em.dma("sync", qs_c[:, 0:1], qs_d[:, :], writes=["qsc"])
    em.dma("sync", wtmp[:, 0:1], ks_d[:, :], writes=["wtmp0"])
    e_copy(em, qs_c[:, 1:2], wtmp[:, 0:1], ["wtmp0"], "qsc", eng="vector")
    e_ts(em, qs_c[:, 0:1], qs_c[:, 0:1], 8.0, None, ALU.mult, None,
         ["qsc"], "qsc")

    for k in range(ND):
        em.dma("sync", xs[:, k * (1 + TS):(k + 1) * (1 + TS)],
               xst_d[ts(k, 128), :], writes=["xs"])

    QKQ = ("sync", "scalar", "gpsimd", "sync")

    def cast_to_dram(src_ap, dst_ap, cols, tag):
        CW = 1024
        qi = 0
        for o in range(0, cols, CW):
            wd = min(CW, cols - o)
            par = qi % 2
            lbuf, lkeys = (tf1, ["tf10", "tf11"]) if par == 0 else \
                (tf2, ["tf20", "tf21"])
            stgb, skeys = (hgb, ["hgb0", "hgb1"]) if par == 0 else \
                (fout, ["fout"])
            stg = stgb[:, 0:wd]
            em.dma(QKQ[qi % 4], lbuf[:, 0:wd],
                   src_ap[:, o:o + wd], writes=lkeys)
            if qi % 2 == 0:
                em.emit("scalar", lambda s, stg=stg, lb=lbuf[:, 0:wd]:
                        s.copy(out=stg, in_=lb), reads=lkeys, writes=skeys)
            else:
                em.emit("vector", lambda v, stg=stg, lb=lbuf[:, 0:wd]:
                        v.tensor_copy(out=stg, in_=lb), reads=lkeys,
                        writes=skeys)
            em.dma(QKQ[(qi + 2) % 4], dst_ap[:, o:o + wd],
                   stg, reads=skeys, writes=[tag])
            qi += 1

    cast_to_dram(wqk_d, wqk_b, 2 * NI * ND * 128, "wqkb")
    cast_to_dram(wv_d, wv_b, 2 * ND * 512, "wvb")
    cast_to_dram(wo_d, wo_b, ND * NI * 128, "wob")
    cast_to_dram(w1a_d, w1a_b, NJC * ND * 128, "w1ab")
    cast_to_dram(w1g_d, w1g_b, NJC * ND * 128, "w1gb")
    cast_to_dram(w2_d, w2_b, ND * NJC * 128, "w2b")

    def xshift_ap(k, col0, width):
        base = k * (1 + TS)
        off = 1 + col0 - (1 if k >= ND // 2 else 0)
        return xs[:, base + off: base + off + width]

    def wo_residual(w):
        em.dma("sync", cmask_r[:, 0:256], cmask_d[:, w * 256:(w + 1) * 256],
               writes=["cmask"])
        e_bcast(em, ps_r2[:, 0:256], ones_s, cmask_r[:, 0:256], "cmask", "psr2")
        for dt in range(ND):
            pk = PK[dt % 2]
            wb3, wo3, wk3 = ((wostr, 0, ["wos0"]), (wostr, 1024, ["wos1"]),
                             (hgb, 0, ["hgb0", "hgb1"]))[dt % 3]
            em.dma(QKQ[dt % 3], wb3[:, wo3: wo3 + NI * 128],
                   wo_b[:, dt * NI * 128: (dt + 1) * NI * 128],
                   reads=["wob"], writes=wk3)
            for i in range(NI):
                e_mm(em, ps[dt % 2][:, 0:256],
                     wb3[:, wo3 + i * 128: wo3 + (i + 1) * 128],
                     attf[:, i * 256:(i + 1) * 256],
                     wk3 + ["attf"], pk,
                     start=(i == 0), stop=(i == NI - 1))
            par = dt % 2
            ta = tf1[:, par * 512: par * 512 + 256]
            e_copy(em, ta, ps[par][:, 0:256], [pk], "tf1%d" % par)
            e_tt(em, ta, ta, ps_r2[:, 0:256],
                 ALU.mult, ["tf1%d" % par, "psr2"], "tf1%d" % par)
            xsl = xs[:, dt * (1 + TS) + 1 + w * 256:
                     dt * (1 + TS) + 1 + (w + 1) * 256]
            e_tt(em, xsl, xsl, ta, ALU.add,
                 ["xs", "tf1%d" % par], "xs")

    for w in range(NW if "attn" not in FFN_SKIP else 0):
        for k in range(ND):
            e_copy(em, xwb[:, ts(k, 256)], xshift_ap(k, w * 256, 256),
                   ["xs"], "xwb", eng=("scalar" if k % 2 else "vector"))
        for jt in range(NI):
            e_copy(em, kro[:, jt * 512: jt * 512 + 256],
                   kro[:, jt * 512 + 256: (jt + 1) * 512], ["kro"], "kro",
                   eng="vector")
        for i in range(2):
            e_copy(em, vtm[:, ts(i, INNER)], vtm[:, ts(2 + i, INNER)],
                   ["vtm"], "vtm", eng="scalar")
        def vdma(idx):
            half, kg = idx // 4, idx % 4
            slot = idx % 2
            em.dma(QKQ[idx % 3], wvstr[:, ts(slot, 1024)],
                   wv_b[:, half * 4096 + kg * 1024:
                        half * 4096 + (kg + 1) * 1024],
                   reads=["wvb"], writes=["wvs%d" % slot])
        if "v" not in FFN_SKIP:
            vdma(0)
            vdma(1)
        for idx in range(8 if "v" not in FFN_SKIP else 0):
            half, kg = idx // 4, idx % 4
            slot = idx % 2
            vb = 2 - 2 * half
            for kk in range(2):
                k = kg * 2 + kk
                for tt in range(2):
                    e_mm(em, ps[vb + tt][:, :],
                         xwb[:, k * 256 + tt * 128: k * 256 + tt * 128 + 128],
                         wvstr[:, slot * 1024 + kk * 512:
                               slot * 1024 + kk * 512 + 512],
                         ["xwb", "wvs%d" % slot], PK[vb + tt],
                         start=(k == 0), stop=(k == ND - 1))
            if idx + 2 < 8:
                vdma(idx + 2)
            if kg == 3:
                for tt in range(2):
                    e_copy(em, vtm[:, (2 + tt) * INNER + half * 512:
                                   (2 + tt) * INNER + (half + 1) * 512],
                           ps[vb + tt][:, :], [PK[vb + tt]], "vtm",
                           eng=("scalar" if tt == 0 else "vector"))
        em.dma("sync", cs_s[:, :], cs_d[:, w * 256:(w + 1) * 256], writes=["cst"])
        em.dma("scalar", ssn_s[:, :], ssn_d[:, w * 256:(w + 1) * 256], writes=["sst"])
        em.dma("sync", csk_s[:, :], csk_d[:, w * 256:(w + 1) * 256], writes=["ckt"])
        em.dma("gpsimd", ssnk_s[:, :], ssnk_d[:, w * 256:(w + 1) * 256],
               writes=["skt"])
        for qk_i in range(2 if "qk" not in FFN_SKIP else 0):
            dkey = "qraw" if qk_i == 0 else "kro"
            def qk_tail(jt):
                par = jt % 2
                pk = PK[par]
                e_act(em, tf2[:, par * 512: par * 512 + 256],
                      ps[par][:, 0:256], AF.Square, [pk], "tf2%d" % par)
                e_mm(em, ps[4][0:16, 0:256], blk16_s[:, ts(jt, 16)],
                     tf2[:, par * 512: par * 512 + 256],
                     ["tf2%d" % par, "blk16"], "psg4",
                     start=(jt == 0), stop=(jt == NI - 1))
                dst = (qro[:, ts(jt, 256)] if qk_i == 0 else
                       kro[:, jt * 512 + 256:(jt + 1) * 512])
                e_copy(em, dst, ps[par][:, 0:256], [pk], dkey)
            pend = None
            for jt in range(NI):
                par = jt % 2
                pk = PK[par]
                qj = qk_i * NI + jt
                sl4 = (qk_i * NI + jt) % 4
                wb4, wo4, wk4 = ((wqkstr, 0, "wqks0"), (wqkstr, 1024, "wqks1"),
                                 (w1str, 0, "w1s0"), (w1str, 1024, "w1s1"))[sl4]
                em.dma(QKQ[jt % 3], wb4[:, wo4:wo4 + 1024],
                       wqk_b[:, qj * 1024:(qj + 1) * 1024],
                       reads=["wqkb"], writes=[wk4])
                for k in range(ND):
                    e_mm(em, ps[par][:, 0:256],
                         wb4[:, wo4 + k * 128: wo4 + (k + 1) * 128],
                         xwb[:, ts(k, 256)],
                         [wk4, "xwb"], pk,
                         start=(k == 0), stop=(k == ND - 1))
                if pend is not None:
                    qk_tail(pend)
                pend = jt
            qk_tail(pend)
            e_act(em, rr_s[:, ts(qk_i, 256)], ps[4][0:16, 0:256], AF.Sqrt,
                  ["psg4"], "rr")
            e_ts(em, rr_s[:, ts(qk_i, 256)], rr_s[:, ts(qk_i, 256)], 1e-12,
                 None, ALU.max, None, ["rr"], "rr")
            em.emit("vector",
                    lambda v, qk_i=qk_i: v.reciprocal(
                        out=rr_s[:, ts(qk_i, 256)],
                        in_=rr_s[:, ts(qk_i, 256)]),
                    reads=["rr"], writes=["rr"])
            e_copy(em, rr_b[:, ts(qk_i, 256)], rr_s[:, ts(qk_i, 256)],
                   ["rr"], "rrb")
        for qk_i in range(2 if "rot" not in FFN_SKIP else 0):
            srcbuf, dkey = (qro, "qraw") if qk_i == 0 else (kro, "kro")
            cst, ckey = (cs_s, "cst") if qk_i == 0 else (csk_s, "ckt")
            sst, skey2 = (ssn_s, "sst") if qk_i == 0 else (ssnk_s, "skt")
            for jt in range(NI):
                par = jt % 2
                po = par * 256
                ta = tf1[:, par * 512: par * 512 + 256]
                tb = tf2[:, par * 512 + 256: par * 512 + 512]
                sl = (ts(jt, 256) if qk_i == 0 else
                      slice(jt * 512 + 256, (jt + 1) * 512))
                e_mm(em, ps_r[:, po:po + 256], esel2_b[:, ts(jt, 128)],
                     rr_b[:, ts(qk_i, 256)], ["esel2", "rrb"], "psr%d" % par,
                     start=True, stop=True)
                e_stt(em, q1_s[:, po:po + 256], srcbuf[:, sl],
                      qs_c[:, qk_i:qk_i + 1], ps_r[:, po:po + 256],
                      ALU.mult, ALU.mult,
                      [dkey, "qsc", "psr%d" % par], "q1%d" % par)
                e_mm(em, ps[par][:, 256:512], rperm_s[:, :],
                     q1_s[:, po:po + 256], ["rperm", "q1%d" % par], PK[par],
                     start=True, stop=True)
                e_tt(em, ta, q1_s[:, po:po + 256], cst[:, :],
                     ALU.mult, ["q1%d" % par, ckey], "tf1%d" % par)
                e_tt(em, tb, ps[par][:, 256:512],
                     sst[:, :], ALU.mult, [PK[par], skey2], "tf2%d" % par)
                e_tt(em, srcbuf[:, sl], ta, tb, ALU.add,
                     ["tf1%d" % par, "tf2%d" % par], dkey)
        if w >= 1 and "wo" not in FFN_SKIP:
            wo_residual(w - 1)
        use_flag = w in (0, 2)
        for qc in range(2 if "core" not in FFN_SKIP else 0):
            if use_flag:
                em.dma("sync", flag_r[:, :],
                       flag_d[:, (w * 2 + qc) * 384:(w * 2 + qc + 1) * 384],
                       writes=["flag"])
                e_copy(em, flag_b[:, :], flag_r[:, :], ["flag"], "flagb")
            def stage_a(h):
                jt, sub = h // 2, h % 2
                pb4 = h % 4
                po4 = pb4 * 384
                rsl = slice(sub * 64, sub * 64 + 64)
                psim = ps[pb4]
                pks = PK[pb4]
                ck = "col%d" % pb4
                e_mm(em, psim[:, 0:384],
                     qro[rsl, jt * 256 + qc * 128: jt * 256 + qc * 128 + 128],
                     kro[rsl, jt * 512 + qc * 128: jt * 512 + qc * 128 + 384],
                     ["qraw", "kro"], pks, start=True, stop=False)
                if use_flag:
                    e_mm(em, psim[:, 0:384], ones_bf[0:1, :], flag_b[:, :],
                         ["onesb", "flagb"], pks, start=False, stop=False)
                e_mm(em, psim[:, 0:384], eye_b[:, :], band_b[:, :],
                     ["eye", "bandb"], pks, start=False, stop=True)
                e_red(em, col_s[:, pb4 * 4:pb4 * 4 + 1], psim[:, 0:384],
                      ALU.max, [pks], ck, negate=True)
                em.emit("scalar",
                        lambda s, po4=po4, pb4=pb4, psim=psim: s.activation(
                            out=pexp_b[:, po4:po4 + 384], in_=psim[:, 0:384],
                            func=AF.Exp, bias=col_s[:, pb4 * 4:pb4 * 4 + 1]),
                        reads=[pks, ck],
                        writes=["pexp%d" % pb4])
            def stage_b(h):
                par = h % 2
                pb4 = h % 4
                po4 = pb4 * 384
                po = par * 384
                ck = "col%d" % pb4
                e_red(em, col_s[:, pb4 * 4 + 1:pb4 * 4 + 2],
                      pexp_b[:, po4:po4 + 384], ALU.add, ["pexp%d" % pb4],
                      ck)
                em.emit("vector",
                        lambda v, pb4=pb4: v.reciprocal(
                            out=col_s[:, pb4 * 4 + 2:pb4 * 4 + 3],
                            in_=col_s[:, pb4 * 4 + 1:pb4 * 4 + 2]),
                        reads=[ck], writes=[ck])
                pvd = (ps[5], "psg5") if par == 0 else (ps_r2, "psr2")
                tpd0 = (ps_tb, ["pstb", "psg4"]) if par == 0 else \
                    (ps_tb2, ["pstb2", "psr0"])
                for i in range(3):
                    tpd = tpd0
                    em.emit("tensor",
                            lambda t, i=i, po4=po4, tpd=tpd: t.transpose(
                                out=tpd[0][:, :], in_=pexp_b[:, po4 + i * 128:
                                                            po4 + (i + 1) * 128],
                                identity=eye_b[:, :]),
                            reads=["pexp%d" % pb4, "eye"], writes=tpd[1])
                    e_copy(em, pt_b[:, po + i * 128: po + (i + 1) * 128],
                           tpd[0][:, :], tpd[1], "ptb%d" % par,
                           eng="vector")
                for i in range(3):
                    e_mm(em, pvd[0][:, 0:64],
                         pt_b[:, po + i * 128: po + (i + 1) * 128],
                         vtm[:, (qc + i) * INNER + h * 64:
                             (qc + i) * INNER + (h + 1) * 64],
                         ["ptb%d" % par, "vtm"], pvd[1],
                         start=(i == 0), stop=(i == 2))
                e_ts(em, attm[:, h * 64:(h + 1) * 64],
                     pvd[0][:, 0:64],
                     col_s[:, pb4 * 4 + 2:pb4 * 4 + 3], None, ALU.mult, None,
                     [pvd[1], ck], "attm_%d" % h)
            for h in range(4):
                stage_a(h)
            for h in range(HEADS):
                stage_b(h)
                if h + 4 < HEADS:
                    stage_a(h + 4)
            for i in range(NI):
                tpd = (ps_tb, ["pstb", "psg4"]) if i % 2 == 0 else \
                    (ps_tb2, ["pstb2", "psr0"])
                em.emit("tensor",
                        lambda t, i=i, tpd=tpd: t.transpose(
                            out=tpd[0][:, :], in_=attm[:, ts(i, 128)],
                            identity=eye_b[:, :]),
                        reads=["attm_%d" % (2 * i), "attm_%d" % (2 * i + 1),
                               "eye"], writes=tpd[1])
                e_copy(em, attf[:, i * 256 + qc * 128:
                               i * 256 + qc * 128 + 128],
                       tpd[0][:, :], tpd[1], "attf",
                       eng=("scalar" if i % 2 == 0 else "vector"))
    if "attn" not in FFN_SKIP and "wo" not in FFN_SKIP:
        wo_residual(NW - 1)

    CH = 512
    for cc in range(TS // CH if "ffn" not in FFN_SKIP else 0):
        c0 = cc * CH
        for k in range(ND):
            par = k % 2
            e_mm(em, ps[4][0:1, 0:CH], ones_s[:, 0:1],
                 xshift_ap(k, c0, CH),
                 ["xs", "ones"], "psg4", start=(k == 0),
                 stop=(k == ND - 1))
            e_act(em, tf2[:, par * CH:(par + 1) * CH],
                  xshift_ap(k, c0, CH), AF.Square,
                  ["xs"], "tf2%d" % par)
            e_mm(em, ps[5][0:1, 0:CH], ones_s[:, 0:1],
                 tf2[:, par * CH:(par + 1) * CH],
                 ["tf2%d" % par, "ones"], "psg5", start=(k == 0),
                 stop=(k == ND - 1))
        ln_rows(em, ps[4][0:1, 0:CH], ps[5][0:1, 0:CH], mu_r[:, :],
                rs_r[:, :], rowtmp[:, :], 1e-5, D, "psg4", "psg5", "mu", "rs",
                "rt")
        em.emit("tensor",
                lambda t: t.matmul(out=ps_r[:, 0:CH], lhsT=ones_s[0:1, :],
                                   rhs=mu_r[:, :], start=True, stop=True),
                reads=["mu", "ones"], writes=["psr0", "psr1"])
        e_bcast(em, ps_r2[:, 0:CH], ones_s, rs_r[:, :], "rs", "psr2")
        for k in range(ND):
            par = k % 2
            e_tt(em, tf1[:, par * CH:(par + 1) * CH],
                 xshift_ap(k, c0, CH), ps_r[:, 0:CH], ALU.subtract,
                 ["xs", "psr0", "psr1"], "tf1%d" % par)
            e_tt(em, tf1[:, par * CH:(par + 1) * CH],
                 tf1[:, par * CH:(par + 1) * CH], ps_r2[:, 0:CH],
                 ALU.mult, ["tf1%d" % par, "psr2"], "tf1%d" % par)
            e_ts(em, xnb[:, ts(k, CH)], tf1[:, par * CH:(par + 1) * CH],
                 lng_c[:, k:k + 1], lnb_c[:, k:k + 1], ALU.mult, ALU.add,
                 ["tf1%d" % par, "lng", "lnb"], "xnb")
        if cc >= 1:
            p0 = (cc - 1) * CH
            for dt in range(ND):
                xsl = xs[:, dt * (1 + TS) + 1 + p0:
                         dt * (1 + TS) + 1 + p0 + CH]
                e_tt(em, xsl, xsl, fout[:, ts(dt, CH)], ALU.add,
                     ["xs", "fout"], "xs")
        for jc in range(NJC if "w1" not in FFN_SKIP else 0):
            par = jc % 2
            wbuf = w1str if par == 0 else wqkstr
            ka = "w1s0" if par == 0 else "wqks0"
            kg2 = "w1s1" if par == 0 else "wqks1"
            em.dma(QKQ[jc % 3], wbuf[:, 0:1024],
                   w1a_b[:, jc * 1024:(jc + 1) * 1024],
                   reads=["w1ab"], writes=[ka])
            em.dma(QKQ[(jc + 1) % 3], wbuf[:, 1024:2048],
                   w1g_b[:, jc * 1024:(jc + 1) * 1024],
                   reads=["w1gb"], writes=[kg2])
            for k in range(ND):
                e_mm(em, ps[par][:, 0:CH],
                     wbuf[:, k * 128:(k + 1) * 128],
                     xnb[:, ts(k, CH)], [ka, "xnb"], PK[par],
                     start=(k == 0), stop=(k == ND - 1))
                e_mm(em, ps[2 + par][:, 0:CH],
                     wbuf[:, 1024 + k * 128: 1024 + (k + 1) * 128],
                     xnb[:, ts(k, CH)], [kg2, "xnb"], PK[2 + par],
                     start=(k == 0), stop=(k == ND - 1))
            e_act(em, tf1[:, par * CH:(par + 1) * CH], ps[par][:, 0:CH],
                  AF.Identity, [PK[par], "b1a"], "tf1%d" % par,
                  bias=b1a_c[:, jc:jc + 1])
            e_act(em, tf2[:, par * CH:(par + 1) * CH],
                  ps[2 + par][:, 0:CH], AF.Gelu, [PK[2 + par], "b1g"],
                  "tf2%d" % par, bias=b1g_c[:, jc:jc + 1])
            e_tt(em, hgb[:, ts(jc, CH)], tf1[:, par * CH:(par + 1) * CH],
                 tf2[:, par * CH:(par + 1) * CH], ALU.mult,
                 ["tf1%d" % par, "tf2%d" % par], "hgb%d" % par)
        em.dma("sync", cmask_r[:, 0:CH], cmask_d[:, c0:c0 + CH],
               writes=["cmask"])
        e_bcast(em, ps_r2[:, 0:CH], ones_s, cmask_r[:, 0:CH], "cmask", "psr2")
        for dt in range(ND if "w2" not in FFN_SKIP else 0):
            par = dt % 2
            wbuf, wkeys = (vtm, ["vtm"]) if par == 0 else (kro, ["kro"])
            em.dma(QKQ[dt % 3], wbuf[:, 0:NJC * 128],
                   w2_b[:, dt * NJC * 128:(dt + 1) * NJC * 128],
                   reads=["w2b"], writes=wkeys)
            for jc in range(NJC):
                e_mm(em, ps[4 + par][:, 0:CH],
                     wbuf[:, jc * 128:(jc + 1) * 128],
                     hgb[:, ts(jc, CH)],
                     wkeys + ["hgb0", "hgb1"], PK[4 + par],
                     start=(jc == 0), stop=(jc == NJC - 1))
            e_act(em, tf1[:, par * CH:(par + 1) * CH],
                  ps[4 + par][:, 0:CH], AF.Identity, [PK[4 + par], "b2"],
                  "tf1%d" % par, bias=b2_c[:, dt:dt + 1])
            e_tt(em, fout[:, ts(dt, CH)], tf1[:, par * CH:(par + 1) * CH],
                 ps_r2[:, 0:CH], ALU.mult, ["tf1%d" % par, "psr2"], "fout")
    p0 = TS - CH
    for dt in range(ND):
        xsl = xs[:, dt * (1 + TS) + 1 + p0: dt * (1 + TS) + 1 + p0 + CH]
        e_tt(em, xsl, xsl, fout[:, ts(dt, CH)], ALU.add, ["xs", "fout"], "xs")

    qi = 0
    for k in range(ND):
        for o in range(0, T_OWN, 512):
            em.dma(QKQ[qi % 3], xout_d[ts(k, 128), o:o + 512],
                   xs[:, k * (1 + TS) + 1 + 512 + o:
                      k * (1 + TS) + 1 + 512 + o + 512],
                   reads=["xs"], writes=["xo"])
            qi += 1

    em.finalize()
    return nc


def _rotary_tables():
    half = np.arange(0, DH, 2, dtype=np.float64)
    inv_freq = 1.0 / 10000.0 ** (half / DH)
    base_scale = (half + 0.4 * DH) / (1.4 * DH)

    def tables(positions):
        t = positions.astype(np.float64)
        freqs = t[:, None] * inv_freq[None, :]
        power = (t - L // 2) / (256 // 2)
        scale = base_scale[None, :] ** power[:, None]
        cos = np.cos(freqs)
        sin = np.sin(freqs)
        cs = (cos * scale).T
        ss = (sin * scale).T
        inv = 1.0 / scale
        ck = (cos * inv).T
        sk = (sin * inv).T
        def stack(c, s):
            blk = np.concatenate([c, c, c, c], axis=0)
            sgn = np.concatenate([-s, s, -s, s], axis=0)
            return np.float32(blk), np.float32(sgn)
        csq, ssnq = stack(cs, ss)
        csk2, ssnk2 = stack(ck, sk)
        return (np.ascontiguousarray(csq), np.ascontiguousarray(ssnq),
                np.ascontiguousarray(csk2), np.ascontiguousarray(ssnk2))

    return tables


_FFN_NC = None
_L0_NC = None
_MAMBA_NC = None
_COMB_NC = None


def _rperm128():
    r = np.zeros((128, 128), np.float32)
    for blk in range(2):
        b0 = blk * 64
        for i in range(32):
            r[b0 + 32 + i, b0 + i] = 1.0
            r[b0 + i, b0 + 32 + i] = 1.0
    return r


def ffn_in_maps(inputs, layer, xcur):
    NI = INNER // 128
    NJC = FFI // 128
    wqkv = np.ascontiguousarray(np.asarray(inputs["L1_ffn_Wqkv"])[layer])
    wo = np.ascontiguousarray(np.asarray(inputs["L1_ffn_Wo"])[layer])
    w1 = np.asarray(inputs["L1_ffn_W1"])[layer]
    w2 = np.asarray(inputs["L1_ffn_W2"])[layer]
    b1 = np.asarray(inputs["L1_ffn_b1"])[layer]
    b2 = np.asarray(inputs["L1_ffn_b2"])[layer]
    lng = np.asarray(inputs["L1_ffn_ln_g"])[layer]
    lnb = np.asarray(inputs["L1_ffn_ln_b"])[layer]
    qs = np.asarray(inputs["L1_ffn_q_scale"])[layer]
    ks = np.asarray(inputs["L1_ffn_k_scale"])[layer]
    FF0 = 2730
    ND = D // 128
    w1a = np.zeros((D, FFI), np.float32); w1a[:, :FF0] = w1[:, :FF0]
    w1g = np.zeros((D, FFI), np.float32); w1g[:, :FF0] = w1[:, FF0:]
    w2p = np.zeros((FFI, D), np.float32); w2p[:FF0] = w2
    b1a = np.zeros(FFI, np.float32); b1a[:FF0] = b1[:FF0]
    b1g = np.zeros(FFI, np.float32); b1g[:FF0] = b1[FF0:]
    def tilemaj(wmat, nr, nccols):
        return np.ascontiguousarray(
            np.float32(wmat).reshape(nr, 128, nccols, 128)
            .transpose(1, 2, 0, 3).reshape(128, -1))
    wqk_t = tilemaj(wqkv[:, :2 * INNER], ND, 2 * NI)
    wv_t = np.ascontiguousarray(
        np.float32(wqkv[:, 2 * INNER:]).reshape(ND, 128, 2, 512)
        .transpose(1, 2, 0, 3).reshape(128, -1))
    wo_t = tilemaj(wo, NI, ND)
    w1a_t = tilemaj(w1a, ND, NJC)
    w1g_t = tilemaj(w1g, ND, NJC)
    w2_t = tilemaj(w2p, NJC, ND)
    blk16 = np.zeros((128, NI * 16), np.float32)
    for jt in range(NI):
        blk16[0:64, jt * 16 + 2 * jt] = 1.0
        blk16[64:128, jt * 16 + 2 * jt + 1] = 1.0
    esel2 = np.zeros((16, NI * 128), np.float32)
    for jt in range(NI):
        esel2[2 * jt, jt * 128: jt * 128 + 64] = 1.0
        esel2[2 * jt + 1, jt * 128 + 64: (jt + 1) * 128] = 1.0
    band = np.full((128, 384), -1.0e38, np.float32)
    for i in range(128):
        band[i, i:i + 257] = 0.0
    tables = _rotary_tables()
    common = {
        "wqk_t": wqk_t, "wv_t": wv_t, "wo_t": wo_t,
        "w1a_t": w1a_t, "w1g_t": w1g_t, "w2_t": w2_t,
        "b1a": np.ascontiguousarray(b1a.reshape(NJC, 128).T),
        "b1g": np.ascontiguousarray(b1g.reshape(NJC, 128).T),
        "b2c": np.ascontiguousarray(b2.reshape(8, 128).T),
        "lng": np.ascontiguousarray(lng.reshape(8, 128).T),
        "lnb": np.ascontiguousarray(lnb.reshape(8, 128).T),
        "qs": np.ascontiguousarray(np.tile(qs, 2)[:, None]),
        "ks": np.ascontiguousarray(np.tile(ks, 2)[:, None]),
        "band": band, "blk16": blk16, "esel2": esel2,
        "iden": np.eye(128, dtype=np.float32),
        "rperm": _rperm128(),
    }
    maps = []
    for core in range(8):
        b, half = core // 2, core % 2
        start = half * T_OWN
        xb = xcur[b]
        xst = np.zeros((1 + TS, D), np.float32)
        lo = start - 512
        src_lo = max(lo, 0)
        xst[1 + (src_lo - lo): 1 + TS] = xb[src_lo: start + T_OWN]
        if lo - 1 >= 0:
            xst[0] = xb[lo - 1]
        pos = np.arange(lo, start + T_OWN)
        cs, ssn, csk, ssnk = tables(pos)
        flag = np.zeros((NW, 2, 384), np.float32)
        noprev = {0} | ({2} if half == 0 else set())
        for w in range(NW):
            if w in noprev:
                for qc in range(2):
                    n = 256 - qc * 128
                    flag[w, qc, 0:n] = -1.0e38
        cmask = np.ones((1, TS), np.float32)
        if half == 0:
            cmask[0, 0:512] = 0.0
        m = dict(common)
        m["xst"] = np.ascontiguousarray(xst.T)
        m["cs_t"] = cs
        m["ssn_t"] = ssn
        m["csk_t"] = csk
        m["ssnk_t"] = ssnk
        m["flagrow"] = flag.reshape(1, -1)
        m["cmask"] = cmask
        maps.append(m)
    return maps


def run_ffn_layer(inputs, layer, xcur):
    global _FFN_NC
    if _FFN_NC is None:
        _FFN_NC = build_ffn()
    maps = ffn_in_maps(inputs, layer, xcur)
    res = run_bass_kernel_spmd(_FFN_NC, maps, core_ids=list(range(8)))
    xnew = np.zeros((B, L, D), np.float32)
    for core in range(8):
        b, half = core // 2, core % 2
        xnew[b, half * T_OWN:(half + 1) * T_OWN] = res.results[core]["xout"].T
    return xnew


def build_combine():
    nc = bass.Bass("TRN2", target_bir_lowering=False)
    ND = D // 128
    x1_d = nc.dram_tensor("x1", [D, T_OWN], F32, kind="ExternalInput").ap()
    xn1_d = nc.dram_tensor("xn1", [D, T_OWN], F32, kind="ExternalInput").ap()
    pf_d = nc.dram_tensor("pf", [D, T_OWN], F32, kind="ExternalInput").ap()
    pb_d = nc.dram_tensor("pb", [D, T_OWN], F32, kind="ExternalInput").ap()
    g_d = nc.dram_tensor("g", [D, T_OWN], F32, kind="ExternalInput").ap()
    ob_d = nc.dram_tensor("obc", [128, ND], F32, kind="ExternalInput").ap()
    x2_d = nc.dram_tensor("x2", [D, T_OWN], F32, kind="ExternalOutput").ap()
    em = EM(nc)
    a_s = sbuf(nc, "a_s", [128, 2 * 512])
    p_s = sbuf(nc, "p_s", [128, 2 * 512])
    n_s = sbuf(nc, "n_s", [128, 2 * 512])
    g_s = sbuf(nc, "g_s", [128, 2 * 512])
    x_s = sbuf(nc, "x_s", [128, 2 * 512])
    c_s = sbuf(nc, "c_s", [128, 2 * 512])
    ob_c = sbuf(nc, "ob_c", [128, ND])
    em.dma("sync", ob_c[:, :], ob_d[:, :], writes=["ob"])
    qi = 0
    for k in range(ND):
        for o in range(0, T_OWN, 512):
            sl = (ts(k, 128), slice(o, o + 512))
            p = qi % 2
            qs3 = ("sync", "scalar", "gpsimd")
            aa = a_s[:, ts(p, 512)]
            em.dma(qs3[qi % 3], aa, pf_d[sl[0], sl[1]], writes=["a%d" % p])
            em.dma(qs3[(qi + 1) % 3], p_s[:, ts(p, 512)], pb_d[sl[0], sl[1]],
                   writes=["p%d" % p])
            em.dma(qs3[(qi + 2) % 3], n_s[:, ts(p, 512)], xn1_d[sl[0], sl[1]],
                   writes=["n%d" % p])
            em.dma(qs3[qi % 3], g_s[:, ts(p, 512)], g_d[sl[0], sl[1]],
                   writes=["g%d" % p])
            em.dma(qs3[(qi + 1) % 3], x_s[:, ts(p, 512)], x1_d[sl[0], sl[1]],
                   writes=["x%d" % p])
            e_tt(em, aa, aa, p_s[:, ts(p, 512)], ALU.add,
                 ["a%d" % p, "p%d" % p], "a%d" % p)
            e_tt(em, aa, aa, n_s[:, ts(p, 512)], ALU.add,
                 ["a%d" % p, "n%d" % p], "a%d" % p)
            e_ts(em, aa, aa, ob_c[:, k:k + 1], None, ALU.add,
                 None, ["a%d" % p, "ob"], "a%d" % p)
            e_act(em, c_s[:, ts(p, 512)], g_s[:, ts(p, 512)], AF.Silu,
                  ["g%d" % p], "c%d" % p)
            e_tt(em, aa, aa, c_s[:, ts(p, 512)], ALU.mult,
                 ["a%d" % p, "c%d" % p], "a%d" % p)
            e_tt(em, aa, aa, x_s[:, ts(p, 512)], ALU.add,
                 ["a%d" % p, "x%d" % p], "a%d" % p)
            em.dma(qs3[(qi + 2) % 3], x2_d[sl[0], sl[1]], aa, reads=["a%d" % p],
                   writes=["out"])
            qi += 1
    em.finalize()
    return nc


def run_combine(inputs, x1, xn1, pf, pb, g):
    global _COMB_NC
    if _COMB_NC is None:
        _COMB_NC = build_combine()
    ob = np.asarray(inputs["L1_attn_out_b"])
    obc = np.ascontiguousarray(ob.reshape(8, 128).T)
    maps = []
    for core in range(8):
        b, half = core // 2, core % 2
        sl = slice(half * T_OWN, (half + 1) * T_OWN)
        maps.append({
            "x1": np.ascontiguousarray(x1[b, sl].T),
            "xn1": np.ascontiguousarray(xn1[b, sl].T),
            "pf": np.ascontiguousarray(pf[b, sl].T),
            "pb": np.ascontiguousarray(pb[b, sl].T),
            "g": np.ascontiguousarray(g[b, sl].T),
            "obc": obc,
        })
    res = run_bass_kernel_spmd(_COMB_NC, maps, core_ids=list(range(8)))
    x2 = np.zeros((B, L, D), np.float32)
    for core in range(8):
        b, half = core // 2, core % 2
        x2[b, half * T_OWN:(half + 1) * T_OWN] = res.results[core]["x2"].T
    return x2


def kernel(**inputs):
    x1, xn1 = run_l0(inputs)
    pf, pb = run_mamba(inputs, xn1)
    xcur = xn1
    for layer in range(6):
        xcur = run_ffn_layer(inputs, layer, xcur)
    x2 = run_combine(inputs, x1, xn1, pf, pb, xcur)
    return x2


# version 06
# speedup vs baseline: 1.1879x; 1.0104x over previous
import numpy as np
import concourse.bass as bass
import concourse.mybir as mybir
from concourse.bass_utils import run_bass_kernel_spmd

F32 = mybir.dt.float32
BF16 = mybir.dt.bfloat16
AF = mybir.ActivationFunctionType
ALU = mybir.AluOpType
AX = mybir.AxisListType

D = 1024
B = 4
L = 4096
T_OWN = 2048

ENGINES = ("sync", "tensor", "vector", "scalar", "gpsimd")


class EM:

    NGS = 8

    def __init__(self, nc):
        self.nc = nc
        self.ops = {e: [] for e in ENGINES}
        self.count = {}
        self.last_w = {}
        self.readers = {}
        self.sems = {}
        self._exits = []
        self.q_groups = {}
        self.q_open = {}
        self.q_semcum = {}
        self.q_rot = {}

    def sem(self, st):
        if st not in self.sems:
            cm = self.nc.semaphore("sem_" + st.replace(":", "_"))
            self.sems[st] = cm.__enter__()
            self._exits.append(cm)
        return self.sems[st]

    def _dma_group_sem(self, st):
        if st not in self.q_open:
            rot = self.q_rot.get(st, 0)
            name = "%s:%d" % (st, rot % self.NGS)
            self.q_rot[st] = rot + 1
            self.q_open[st] = name
        return self.q_open[st]

    def _close_group(self, st, upto_seq):
        name = self.q_open.pop(st)
        target = self.q_semcum.get(name, 0)
        self.q_groups.setdefault(st, []).append((upto_seq, name, target))

    def _dep_wait(self, waits, my_stream, st2, c2):
        if st2.startswith("d:"):
            groups = self.q_groups.get(st2, [])
            grp = None
            for last_seq, name, target in groups:
                if c2 <= last_seq:
                    grp = (name, target)
                    break
            if grp is None:
                self._close_group(st2, self.count.get(st2, 0))
                last_seq, name, target = self.q_groups[st2][-1]
                grp = (name, target)
            name, target = grp
            if my_stream == st2:
                return
            waits[name] = max(waits.get(name, 0), target)
        else:
            waits[st2] = max(waits.get(st2, 0), c2)

    def emit(self, eng, fn, reads=(), writes=(), dma=False):
        my_stream = ("d:" if dma else "c:") + eng
        waits = {}
        for k in reads:
            lw = self.last_w.get(k)
            if lw:
                self._dep_wait(waits, my_stream, *lw)
        for k in writes:
            lw = self.last_w.get(k)
            if lw:
                self._dep_wait(waits, my_stream, *lw)
            for st2, c2 in self.readers.get(k, {}).items():
                self._dep_wait(waits, my_stream, st2, c2)
        self.count[my_stream] = self.count.get(my_stream, 0) + 1
        cnt = self.count[my_stream]
        gsem = None
        if dma:
            gsem = self._dma_group_sem(my_stream)
            self.q_semcum[gsem] = self.q_semcum.get(gsem, 0) + 16
        for k in writes:
            self.last_w[k] = (my_stream, cnt)
            self.readers[k] = {}
        for k in reads:
            self.readers.setdefault(k, {})[my_stream] = cnt
        self.ops[eng].append((waits, fn, gsem))

    def dma(self, eng, out, in_, reads=(), writes=(), **kw):
        self.emit(eng, lambda q: q.dma_start(out=out, in_=in_, **kw),
                  reads=reads, writes=writes, dma=True)

    def finalize(self):
        nc = self.nc
        with nc.Block() as block:
            for eng_name in ENGINES:
                ops = self.ops[eng_name]
                if not ops:
                    continue

                def make(eng_name=eng_name, ops=ops):
                    def body(q):
                        synced = {}
                        cs = "c:" + eng_name
                        nc_cnt = 0
                        for waits, fn, gsem in ops:
                            for st2, v2 in sorted(waits.items()):
                                if v2 > synced.get(st2, 0):
                                    q.wait_ge(self.sem(st2), v2)
                                    synced[st2] = v2
                            instr = fn(q)
                            if gsem is not None:
                                instr.then_inc(self.sem(gsem), 16)
                            else:
                                nc_cnt += 1
                                instr.then_inc(self.sem(cs), 1)
                        ds = "d:" + eng_name
                        for name, cum in sorted(self.q_semcum.items()):
                            if name.startswith(ds):
                                if cum > synced.get(name, 0):
                                    q.wait_ge(self.sem(name), cum)
                        if nc_cnt:
                            q.wait_ge(self.sem(cs), nc_cnt)
                    return body

                getattr(block, eng_name)(make())


def sbuf(nc, name, shape, dtype=F32):
    return nc.sbuf_tensor(name, shape, dtype).__enter__()


def psum_t(nc, name, shape, dtype=F32):
    return nc.psum_tensor(name, shape, dtype).__enter__()


def ts(i, n):
    return slice(i * n, (i + 1) * n)


def e_mm(em, out, lhsT, rhs, reads, write, start, stop):
    em.emit("tensor",
            lambda t: t.matmul(out=out, lhsT=lhsT, rhs=rhs, start=start,
                               stop=stop),
            reads=reads, writes=[write])


def e_bcast(em, ps, ones_s, row_ap, read, write, nrows=128):
    em.emit("tensor",
            lambda t: t.matmul(out=ps, lhsT=ones_s[0:1, 0:nrows], rhs=row_ap,
                               start=True, stop=True),
            reads=[read, "ones"], writes=[write])


def e_tt(em, out, in0, in1, op, reads, write):
    em.emit("vector",
            lambda v: v.tensor_tensor(out=out, in0=in0, in1=in1, op=op),
            reads=reads, writes=[write])


def e_ts(em, out, in0, s1, s2, op0, op1, reads, write, eng="vector"):
    if s2 is None:
        em.emit(eng,
                lambda v: v.tensor_scalar(out=out, in0=in0, scalar1=s1,
                                          scalar2=None, op0=op0),
                reads=reads, writes=[write])
    else:
        em.emit(eng,
                lambda v: v.tensor_scalar(out=out, in0=in0, scalar1=s1,
                                          scalar2=s2, op0=op0, op1=op1),
                reads=reads, writes=[write])


def e_stt(em, out, in0, scalar, in1, op0, op1, reads, write):
    em.emit("vector",
            lambda v: v.scalar_tensor_tensor(out=out, in0=in0, scalar=scalar,
                                             in1=in1, op0=op0, op1=op1),
            reads=reads, writes=[write])


def e_act(em, out, in_, func, reads, write, bias=None, scale=None, eng="scalar"):
    kw = {}
    if bias is not None:
        kw["bias"] = bias
    if scale is not None:
        kw["scale"] = scale
    em.emit(eng,
            lambda s: s.activation(out=out, in_=in_, func=func, **kw),
            reads=reads, writes=[write])


def e_copy(em, out, in_, reads, write, eng="scalar"):
    if eng == "scalar":
        em.emit(eng, lambda s: s.copy(out=out, in_=in_), reads=reads,
                writes=[write])
    else:
        em.emit(eng, lambda v: v.tensor_copy(out=out, in_=in_), reads=reads,
                writes=[write])


def e_red(em, out, in_, op, reads, write, negate=False):
    em.emit("vector",
            lambda v: v.tensor_reduce(out=out, in_=in_, axis=AX.X, op=op,
                                      negate=negate),
            reads=reads, writes=[write])


def ln_rows(em, st_ap, st2_ap, mu_ap, rs_ap, tmp_ap, eps, dimn,
            k_st, k_st2, k_mu, k_rs, k_tmp):
    e_ts(em, mu_ap, st_ap, 1.0 / dimn, None, ALU.mult, None, [k_st], k_mu)
    e_ts(em, tmp_ap, st2_ap, 1.0 / dimn, None, ALU.mult, None, [k_st2], k_tmp)
    e_tt(em, rs_ap, mu_ap, mu_ap, ALU.mult, [k_mu], k_rs)
    e_tt(em, tmp_ap, tmp_ap, rs_ap, ALU.subtract, [k_tmp, k_rs], k_tmp)
    e_ts(em, tmp_ap, tmp_ap, eps, None, ALU.add, None, [k_tmp], k_tmp)
    em.emit("scalar", lambda s: s.activation(out=tmp_ap, in_=tmp_ap, func=AF.Sqrt),
            reads=[k_tmp], writes=[k_tmp])
    em.emit("vector", lambda v: v.reciprocal(out=rs_ap, in_=tmp_ap),
            reads=[k_tmp], writes=[k_rs])


def build_l0():
    nc = bass.Bass("TRN2", target_bir_lowering=False)
    H, DX, QK = 8, 64, 512
    ND = D // 128
    NCH = L // 512
    OCH = T_OWN // 512

    xf = nc.dram_tensor("xf", [D, L], F32, kind="ExternalInput").ap()
    w_qkv = nc.dram_tensor("w_qkv", [D, 3 * QK], F32, kind="ExternalInput").ap()
    bqkv_row_d = nc.dram_tensor("bqkv_row", [1, 3 * QK], F32, kind="ExternalInput").ap()
    bqkv_col_d = nc.dram_tensor("bqkv_col", [128, 12], F32, kind="ExternalInput").ap()
    w_fc = nc.dram_tensor("w_fc", [QK, D], F32, kind="ExternalInput").ap()
    bfc_col_d = nc.dram_tensor("bfc_col", [128, ND], F32, kind="ExternalInput").ap()
    gam_col_d = nc.dram_tensor("gam_col", [128, ND], F32, kind="ExternalInput").ap()
    bet_col_d = nc.dram_tensor("bet_col", [128, ND], F32, kind="ExternalInput").ap()
    gate_wT = nc.dram_tensor("gate_wT", [D, 8], F32, kind="ExternalInput").ap()
    ew0T = nc.dram_tensor("ew0T", [D, D], F32, kind="ExternalInput").ap()
    ew1T = nc.dram_tensor("ew1T", [D, D], F32, kind="ExternalInput").ap()
    out_wT = nc.dram_tensor("out_wT", [D, D], F32, kind="ExternalInput").ap()
    iden_d = nc.dram_tensor("iden", [128, 128], F32, kind="ExternalInput").ap()
    x1_d = nc.dram_tensor("x1", [D, T_OWN], F32, kind="ExternalOutput").ap()
    xn1_d = nc.dram_tensor("xn1", [D, T_OWN], F32, kind="ExternalOutput").ap()

    em = EM(nc)

    wqkv_s = sbuf(nc, "wqkv_s", [128, ND * 3 * QK], BF16)
    wfc_s = sbuf(nc, "wfc_s", [128, 4 * D], BF16)
    gw_s = sbuf(nc, "gw_s", [128, ND * 8], BF16)
    ew0_s = sbuf(nc, "ew0_s", [128, ND * D], BF16)
    ew1_s = sbuf(nc, "ew1_s", [128, ND * D], BF16)
    ow_s = sbuf(nc, "ow_s", [128, ND * D], BF16)
    ones_s = sbuf(nc, "ones_s", [128, 128])
    eye_s = sbuf(nc, "eye_s", [128, 128])
    wtmp = sbuf(nc, "wtmp", [128, 512])
    bqkv_row = sbuf(nc, "bqkv_row_s", [1, 3 * QK])
    bqkvL_row = sbuf(nc, "bqkvL_row", [1, 3 * QK])
    bqkv_col = sbuf(nc, "bqkv_col_s", [128, 12])
    bfc_col = sbuf(nc, "bfc_col_s", [128, ND])
    gam_col = sbuf(nc, "gam_col_s", [128, ND])
    bet_col = sbuf(nc, "bet_col_s", [128, ND])

    xn0b = sbuf(nc, "xn0b", [128, ND * T_OWN], BF16)
    xnbt = sbuf(nc, "xnbt", [128, ND * 512], BF16)
    vfm = sbuf(nc, "vfm", [128, 4 * 512], BF16)
    s0 = sbuf(nc, "s0", [128, ND * 512])
    mu_r = sbuf(nc, "mu_r", [1, 512])
    rs_r = sbuf(nc, "rs_r", [1, 512])
    snx_c = sbuf(nc, "snx_c", [128, ND])
    snx_b = sbuf(nc, "snx_b", [128, ND], BF16)
    red_c = sbuf(nc, "red_c", [128, 1])
    sq_row = sbuf(nc, "sq_row", [1, QK])
    sk_row = sbuf(nc, "sk_row", [1, QK])
    scores = sbuf(nc, "scores", [128, 4 * DX])
    attn = sbuf(nc, "attn", [128, 4 * DX], BF16)
    qh_s = sbuf(nc, "qh_s", [128, QK], BF16)
    kh_s = sbuf(nc, "kh_s", [128, QK], BF16)
    xt_s = sbuf(nc, "xt_s", [128, 2 * 512])
    sq_s = sbuf(nc, "sq_s", [128, 512])
    tmp_a = sbuf(nc, "tmp_a", [128, 512])
    tmp_b = sbuf(nc, "tmp_b", [128, 512])
    rowtmp = sbuf(nc, "rowtmp", [1, 512])
    mu2_r = sbuf(nc, "mu2_r", [1, 512])
    rs2_r = sbuf(nc, "rs2_r", [1, 512])
    col_s = sbuf(nc, "col_s", [128, 16])
    pcols = sbuf(nc, "pcols", [128, 8])
    pv_s = sbuf(nc, "pv_s", [128, 4 * 512], BF16)
    h0_s = sbuf(nc, "h0_s", [128, 512], BF16)
    h1_s = sbuf(nc, "h1_s", [128, 512], BF16)
    gpre = sbuf(nc, "gpre", [128, ND * 512], BF16)
    ge_s = sbuf(nc, "ge_s", [128, 512])
    w01_r = sbuf(nc, "w01_r", [1, 2 * 512])
    x1c = sbuf(nc, "x1c", [128, 512])

    ps_a = psum_t(nc, "ps_a", [128, 512])
    ps_b = psum_t(nc, "ps_b", [128, 512])
    ps_c = psum_t(nc, "ps_c", [128, 512])
    ps_r = psum_t(nc, "ps_r", [128, 512])
    ps_r2 = psum_t(nc, "ps_r2", [128, 512])
    ps_sc = psum_t(nc, "ps_sc", [128, DX])
    ps_st = psum_t(nc, "ps_st", [1, 512])
    ps_st2 = psum_t(nc, "ps_st2", [1, 512])

    em.emit("gpsimd", lambda g: g.memset(ones_s[:, :], 1.0), writes=["ones"])
    em.dma("sync", eye_s[:, :], iden_d[:, :], writes=["eye"])
    em.dma("sync", bqkv_row[:, :], bqkv_row_d[:, :], writes=["bqr"])
    em.dma("sync", bqkv_col[:, :], bqkv_col_d[:, :], writes=["bqc"])
    em.dma("sync", bfc_col[:, :], bfc_col_d[:, :], writes=["bfc"])
    em.dma("sync", gam_col[:, :], gam_col_d[:, :], writes=["gam"])
    em.dma("sync", bet_col[:, :], bet_col_d[:, :], writes=["bet"])
    e_ts(em, bqkvL_row[:, :], bqkv_row[:, :], float(L), None, ALU.mult, None,
         ["bqr"], "bqL")
    em.emit("vector", lambda v: v.memset(scores[:, :], 0.0), writes=["scoresb"])

    def load_cast(dst_ap, src_ap, cols, tag):
        for o in range(0, cols, 512):
            w = min(512, cols - o)
            em.dma("sync", wtmp[:, 0:w], src_ap[:, o:o + w], writes=["wtmp"])
            e_copy(em, dst_ap[:, o:o + w], wtmp[:, 0:w], ["wtmp"], tag)

    for k in range(ND):
        load_cast(wqkv_s[:, ts(k, 3 * QK)], w_qkv[ts(k, 128), :], 3 * QK, "wqkv")
    for k in range(4):
        load_cast(wfc_s[:, ts(k, D)], w_fc[ts(k, 128), :], D, "wfc")
    for k in range(ND):
        load_cast(gw_s[:, ts(k, 8)], gate_wT[ts(k, 128), :], 8, "gw")
    for k in range(ND):
        load_cast(ew0_s[:, ts(k, D)], ew0T[ts(k, 128), :], D, "ew0")
    for k in range(ND):
        load_cast(ew1_s[:, ts(k, D)], ew1T[ts(k, 128), :], D, "ew1")
    for k in range(ND):
        load_cast(ow_s[:, ts(k, D)], out_wT[ts(k, 128), :], D, "ow")

    for c in range(NCH):
        own = c < OCH
        cs = slice(c * 512, (c + 1) * 512)
        rsl = slice(0, 512)
        for k in range(ND):
            em.dma("sync", xt_s[:, ts(k % 2, 512)], xf[ts(k, 128), cs],
                   writes=["xt%d" % (k % 2)])
            e_mm(em, ps_st[:, :], ones_s[:, 0:1], xt_s[:, ts(k % 2, 512)],
                 ["xt%d" % (k % 2), "ones"], "st", start=(k == 0), stop=(k == ND - 1))
            e_act(em, sq_s[:, :], xt_s[:, ts(k % 2, 512)], AF.Square,
                  ["xt%d" % (k % 2)], "sq")
            e_mm(em, ps_st2[:, :], ones_s[:, 0:1], sq_s[:, :],
                 ["sq", "ones"], "st2", start=(k == 0), stop=(k == ND - 1))
        ln_rows(em, ps_st[:, :], ps_st2[:, :], mu_r[:, rsl], rs_r[:, rsl],
                rowtmp[:, :], 1e-5, D, "st", "st2", "mu", "rs", "rt")
        e_bcast(em, ps_r[:, :], ones_s, mu_r[:, rsl], "mu", "psr")
        e_bcast(em, ps_r2[:, :], ones_s, rs_r[:, rsl], "rs", "psr2")
        for k in range(ND):
            em.dma("sync", xt_s[:, ts(k % 2, 512)], xf[ts(k, 128), cs],
                   writes=["xt%d" % (k % 2)])
            xnap = (xn0b[:, k * T_OWN + c * 512: k * T_OWN + (c + 1) * 512]
                    if own else xnbt[:, ts(k, 512)])
            xnkey = ("xn0b_%d" % k) if own else ("xnbt_%d" % k)
            e_tt(em, tmp_a[:, :], xt_s[:, ts(k % 2, 512)], ps_r[:, :], ALU.subtract,
                 ["xt%d" % (k % 2), "psr"], "tmpa")
            e_tt(em, xnap, tmp_a[:, :], ps_r2[:, :], ALU.mult,
                 ["tmpa", "psr2"], xnkey)
            e_red(em, red_c[:, 0:1], xnap, ALU.add, [xnkey], "redc")
            if c == 0:
                e_copy(em, snx_c[:, k:k + 1], red_c[:, 0:1], ["redc"],
                       "snx%d" % k, eng="vector")
            else:
                e_tt(em, snx_c[:, k:k + 1], snx_c[:, k:k + 1], red_c[:, 0:1],
                     ALU.add, ["snx%d" % k, "redc"], "snx%d" % k)
        for tt in range(4):
            gtok = c * 4 + tt
            for qk_i, dst, kdst in ((0, qh_s, "qh"), (1, kh_s, "kh")):
                for k in range(ND):
                    lhs_ap = (xn0b[:, k * T_OWN + c * 512 + tt * 128:
                                   k * T_OWN + c * 512 + (tt + 1) * 128]
                              if own else
                              xnbt[:, k * 512 + tt * 128: k * 512 + (tt + 1) * 128])
                    xnkey = ("xn0b_%d" % k) if own else ("xnbt_%d" % k)
                    e_mm(em, ps_a[:, :], lhs_ap,
                         wqkv_s[:, k * 3 * QK + qk_i * QK:
                                k * 3 * QK + (qk_i + 1) * QK],
                         [xnkey, "wqkv"], "psa", start=(k == 0),
                         stop=(k == ND - 1))
                e_copy(em, dst[:, :], ps_a[:, :], ["psa"], kdst)
            for hp in range(4):
                for sub in range(2):
                    h = 2 * hp + sub
                    e_mm(em, ps_sc[sub * 64:sub * 64 + 64, :],
                         qh_s[:, ts(h, DX)], kh_s[:, ts(h, DX)],
                         ["qh", "kh"], "psc", start=True, stop=True)
                e_tt(em, scores[:, ts(hp, DX)], scores[:, ts(hp, DX)],
                     ps_sc[:, :], ALU.add, ["scoresb", "psc"], "scoresb")
    e_copy(em, snx_b[:, :], snx_c[:, :],
           ["snx%d" % k2 for k2 in range(ND)], "snxb")
    for qk_i, dst, kdst in ((0, sq_row, "sqrow"), (1, sk_row, "skrow")):
        for k in range(ND):
            e_mm(em, ps_st[:, :], snx_b[:, k:k + 1],
                 wqkv_s[:, k * 3 * QK + qk_i * QK: k * 3 * QK + (qk_i + 1) * QK],
                 ["snxb", "wqkv"], "st", start=(k == 0), stop=(k == ND - 1))
        e_copy(em, dst[:, :], ps_st[:, :], ["st"], kdst)
    for hp in range(4):
        for sub in range(2):
            h = 2 * hp + sub
            out_ap = ps_sc[sub * 64:sub * 64 + 64, :]
            e_mm(em, out_ap, bqkv_row[:, ts(h, DX)], sk_row[:, ts(h, DX)],
                 ["bqr", "skrow"], "psc", start=True, stop=False)
            e_mm(em, out_ap, sq_row[:, ts(h, DX)], bqkv_row[:, QK + h * DX:
                                                            QK + (h + 1) * DX],
                 ["sqrow", "bqr"], "psc", start=False, stop=False)
            e_mm(em, out_ap, bqkvL_row[:, ts(h, DX)], bqkv_row[:, QK + h * DX:
                                                               QK + (h + 1) * DX],
                 ["bqL", "bqr"], "psc", start=False, stop=True)
        e_tt(em, scores[:, ts(hp, DX)], scores[:, ts(hp, DX)], ps_sc[:, :],
             ALU.add, ["scoresb", "psc"], "scoresb")


    for hp in range(4):
        sl = ts(hp, DX)
        e_red(em, col_s[:, hp:hp + 1], scores[:, sl], ALU.max, ["scoresb"],
              "cols")
        e_ts(em, col_s[:, 4 + hp:5 + hp], col_s[:, hp:hp + 1], -0.125, None,
             ALU.mult, None, ["cols"], "cols")
        em.emit("scalar",
                lambda s, hp=hp, sl=sl: s.activation(
                    out=scores[:, sl], in_=scores[:, sl], func=AF.Exp,
                    bias=col_s[:, 4 + hp:5 + hp], scale=0.125),
                reads=["scoresb", "cols"], writes=["scoresb"])
        e_red(em, col_s[:, 8 + hp:9 + hp], scores[:, sl], ALU.add,
              ["scoresb"], "cols")
        em.emit("vector",
                lambda v, hp=hp: v.reciprocal(out=col_s[:, 12 + hp:13 + hp],
                                              in_=col_s[:, 8 + hp:9 + hp]),
                reads=["cols"], writes=["cols"])
        e_ts(em, attn[:, sl], scores[:, sl], col_s[:, 12 + hp:13 + hp], None,
             ALU.mult, None, ["scoresb", "cols"], "attn")


    for c in range(OCH):
        for k in range(ND):
            em.dma("sync", xt_s[:, ts(k % 2, 512)], xf[ts(k, 128), ts(c, 512)],
                   writes=["xt%d" % (k % 2)])
            e_mm(em, ps_st[:, :], ones_s[:, 0:1], xt_s[:, ts(k % 2, 512)],
                 ["xt%d" % (k % 2), "ones"], "st", start=(k == 0),
                 stop=(k == ND - 1))
            e_act(em, sq_s[:, :], xt_s[:, ts(k % 2, 512)], AF.Square,
                  ["xt%d" % (k % 2)], "sq")
            e_mm(em, ps_st2[:, :], ones_s[:, 0:1], sq_s[:, :], ["sq", "ones"],
                 "st2", start=(k == 0), stop=(k == ND - 1))
        ln_rows(em, ps_st[:, :], ps_st2[:, :], mu_r[:, :], rs_r[:, :],
                rowtmp[:, :], 1e-5, D, "st", "st2", "mu", "rs", "rt")
        for jt in range(4):
            for k in range(ND):
                e_mm(em, ps_b[:, :],
                     wqkv_s[:, k * 3 * QK + 2 * QK + jt * 128:
                            k * 3 * QK + 2 * QK + (jt + 1) * 128],
                     xn0b[:, k * T_OWN + c * 512: k * T_OWN + (c + 1) * 512],
                     ["wqkv", "xn0b_%d" % k], "psb",
                     start=(k == 0), stop=(k == ND - 1))
            e_act(em, vfm[:, ts(jt, 512)], ps_b[:, :], AF.Identity,
                  ["psb", "bqc"], "vfm", bias=bqkv_col[:, 8 + jt:9 + jt])
        for hp in range(4):
            for sub in range(2):
                h = hp * 2 + sub
                e_mm(em, ps_b[sub * 64:sub * 64 + 64, :],
                     attn[(h % 2) * 64:(h % 2) * 64 + 64, ts(h // 2, DX)],
                     vfm[(h % 2) * 64:(h % 2) * 64 + 64, ts(h // 2, 512)],
                     ["attn", "vfm"], "psb", start=True, stop=True)
            e_copy(em, pv_s[:, ts(hp, 512)], ps_b[:, :], ["psb"], "pv")
        for k in range(ND):
            for j in range(4):
                e_mm(em, ps_c[:, :],
                     wfc_s[:, j * D + k * 128: j * D + (k + 1) * 128],
                     pv_s[:, ts(j, 512)], ["wfc", "pv"], "psc2",
                     start=(j == 0), stop=(j == 3))
            e_act(em, tmp_a[:, :], ps_c[:, :], AF.Identity, ["psc2", "bfc"],
                  "tmpa", bias=bfc_col[:, k:k + 1])
            em.dma("sync", xt_s[:, ts(k % 2, 512)], xf[ts(k, 128), ts(c, 512)],
                   writes=["xt%d" % (k % 2)])
            if k == 0:
                e_bcast(em, ps_r[:, :], ones_s, mu_r[:, :], "mu", "psr")
                e_bcast(em, ps_r2[:, :], ones_s, rs_r[:, :], "rs", "psr2")
            e_tt(em, tmp_b[:, :], xt_s[:, ts(k % 2, 512)], ps_r[:, :], ALU.subtract,
                 ["xt%d" % (k % 2), "psr"], "tmpb")
            e_tt(em, tmp_b[:, :], tmp_b[:, :], ps_r2[:, :], ALU.mult,
                 ["tmpb", "psr2"], "tmpb")
            e_tt(em, s0[:, ts(k, 512)],
                 tmp_a[:, :], tmp_b[:, :], ALU.add, ["tmpa", "tmpb"],
                 "s0_%d" % k)
            e_mm(em, ps_st[:, :], ones_s[:, 0:1],
                 s0[:, ts(k, 512)],
                 ["s0_%d" % k, "ones"], "st", start=(k == 0), stop=(k == ND - 1))
            e_act(em, sq_s[:, :],
                  s0[:, ts(k, 512)],
                  AF.Square, ["s0_%d" % k], "sq")
            e_mm(em, ps_st2[:, :], ones_s[:, 0:1], sq_s[:, :], ["sq", "ones"],
                 "st2", start=(k == 0), stop=(k == ND - 1))
        ln_rows(em, ps_st[:, :], ps_st2[:, :], mu2_r[:, :], rs2_r[:, :],
                rowtmp[:, :], 1e-6, D, "st", "st2", "mu2", "rs2", "rt")
        e_bcast(em, ps_r[:, :], ones_s, mu2_r[:, :], "mu2", "psr")
        e_bcast(em, ps_r2[:, :], ones_s, rs2_r[:, :], "rs2", "psr2")
        for k in range(ND):
            sl = ts(k, 512)
            e_tt(em, s0[:, sl], s0[:, sl], ps_r[:, :], ALU.subtract,
                 ["s0_%d" % k, "psr"], "s0_%d" % k)
            e_tt(em, s0[:, sl], s0[:, sl], ps_r2[:, :], ALU.mult,
                 ["s0_%d" % k, "psr2"], "s0_%d" % k)
            e_ts(em, s0[:, sl], s0[:, sl], gam_col[:, k:k + 1],
                 bet_col[:, k:k + 1], ALU.mult, ALU.add,
                 ["s0_%d" % k, "gam", "bet"], "s0_%d" % k)

        for tt in range(4):
            tok0 = c * 512 + tt * 128
            for k in range(ND):
                e_mm(em, ps_a[:, 0:8],
                     xn0b[:, k * T_OWN + tok0: k * T_OWN + tok0 + 128],
                     gw_s[:, ts(k, 8)], ["xn0b_%d" % k, "gw"], "psa",
                     start=(k == 0), stop=(k == ND - 1))
            e_red(em, col_s[:, 0:1], ps_a[:, 0:8], ALU.max, ["psa"], "cols",
                  negate=True)
            em.emit("scalar",
                    lambda s: s.activation(out=pcols[:, 0:8], in_=ps_a[:, 0:8],
                                           func=AF.Exp, bias=col_s[:, 0:1]),
                    reads=["psa", "cols"], writes=["pcols"])
            e_red(em, col_s[:, 1:2], pcols[:, 0:8], ALU.add, ["pcols"], "cols")
            em.emit("vector",
                    lambda v: v.reciprocal(out=col_s[:, 2:3], in_=col_s[:, 1:2]),
                    reads=["cols"], writes=["cols"])
            e_ts(em, pcols[:, 0:8], pcols[:, 0:8], col_s[:, 2:3], None,
                 ALU.mult, None, ["pcols", "cols"], "pcols")
            e_ts(em, col_s[:, 4:12], pcols[:, 0:8], 0.3, None, ALU.is_gt, None,
                 ["pcols"], "cols")
            e_red(em, col_s[:, 12:13], col_s[:, 4:12], ALU.add, ["cols"], "cols")
            e_ts(em, col_s[:, 13:14], col_s[:, 12:13], -1.0, 8.0, ALU.mult,
                 ALU.add, ["cols"], "cols")
            e_tt(em, col_s[:, 14:15], col_s[:, 13:14], pcols[:, 0:1], ALU.mult,
                 ["cols", "pcols"], "cols")
            e_tt(em, col_s[:, 15:16], col_s[:, 12:13], pcols[:, 1:2], ALU.mult,
                 ["cols", "pcols"], "cols")
            em.emit("tensor",
                    lambda t: t.transpose(out=ps_st[:, 0:128],
                                          in_=col_s[:, 14:15],
                                          identity=eye_s[:, :]),
                    reads=["cols", "eye"], writes=["st"])
            e_copy(em, w01_r[:, tt * 128:(tt + 1) * 128], ps_st[:, 0:128], ["st"], "w01")
            em.emit("tensor",
                    lambda t: t.transpose(out=ps_st[:, 0:128],
                                          in_=col_s[:, 15:16],
                                          identity=eye_s[:, :]),
                    reads=["cols", "eye"], writes=["st"])
            e_copy(em, w01_r[:, 512 + tt * 128: 512 + (tt + 1) * 128],
                   ps_st[:, 0:128], ["st"], "w01")
        e_bcast(em, ps_r[:, :], ones_s, w01_r[:, 0:512], "w01", "psr")
        e_bcast(em, ps_r2[:, :], ones_s, w01_r[:, 512:1024], "w01", "psr2")
        for jt in range(ND):
            for k in range(ND):
                e_mm(em, ps_b[:, :], ew0_s[:, k * D + jt * 128:
                                           k * D + (jt + 1) * 128],
                     xn0b[:, k * T_OWN + c * 512: k * T_OWN + (c + 1) * 512],
                     ["ew0", "xn0b_%d" % k], "psb", start=(k == 0),
                     stop=(k == ND - 1))
            e_copy(em, h0_s[:, :], ps_b[:, :], ["psb"], "h0")
            for k in range(ND):
                e_mm(em, ps_c[:, :], ew1_s[:, k * D + jt * 128:
                                           k * D + (jt + 1) * 128],
                     xn0b[:, k * T_OWN + c * 512: k * T_OWN + (c + 1) * 512],
                     ["ew1", "xn0b_%d" % k], "psc2", start=(k == 0),
                     stop=(k == ND - 1))
            e_copy(em, h1_s[:, :], ps_c[:, :], ["psc2"], "h1")
            e_tt(em, tmp_a[:, :], h0_s[:, :], ps_r[:, :], ALU.mult,
                 ["h0", "psr"], "tmpa")
            e_tt(em, tmp_b[:, :], h1_s[:, :], ps_r2[:, :], ALU.mult,
                 ["h1", "psr2"], "tmpb")
            e_tt(em, gpre[:, ts(jt, 512)], tmp_a[:, :], tmp_b[:, :], ALU.add,
                 ["tmpa", "tmpb"], "gpre")
        for k in range(ND):
            for j in range(ND):
                e_mm(em, ps_c[:, :], ow_s[:, j * D + k * 128:
                                          j * D + (k + 1) * 128],
                     gpre[:, ts(j, 512)], ["ow", "gpre"], "psc2",
                     start=(j == 0), stop=(j == ND - 1))
            e_act(em, ge_s[:, :], ps_c[:, :], AF.Silu, ["psc2"], "ge")
            em.dma("sync", xt_s[:, ts(k % 2, 512)], xf[ts(k, 128), ts(c, 512)],
                   writes=["xt%d" % (k % 2)])
            e_tt(em, tmp_a[:, :],
                 s0[:, ts(k, 512)],
                 ge_s[:, :], ALU.mult, ["s0_%d" % k, "ge"], "tmpa")
            e_tt(em, x1c[:, :], xt_s[:, ts(k % 2, 512)], tmp_a[:, :],
                 ALU.add, ["xt%d" % (k % 2), "tmpa"], "x1c")
            em.dma("gpsimd", x1_d[ts(k, 128), ts(c, 512)], x1c[:, :],
                   reads=["x1c"], writes=["x1out"])
            e_mm(em, ps_st[:, :], ones_s[:, 0:1], x1c[:, :],
                 ["x1c", "ones"], "st", start=(k == 0), stop=(k == ND - 1))
            e_act(em, sq_s[:, :], x1c[:, :], AF.Square,
                  ["x1c"], "sq")
            e_mm(em, ps_st2[:, :], ones_s[:, 0:1], sq_s[:, :], ["sq", "ones"],
                 "st2", start=(k == 0), stop=(k == ND - 1))
        ln_rows(em, ps_st[:, :], ps_st2[:, :], mu2_r[:, :], rs2_r[:, :],
                rowtmp[:, :], 1e-5, D, "st", "st2", "mu2", "rs2", "rt")
        e_bcast(em, ps_r[:, :], ones_s, mu2_r[:, :], "mu2", "psr")
        e_bcast(em, ps_r2[:, :], ones_s, rs2_r[:, :], "rs2", "psr2")
        for k in range(ND):
            em.dma("sync", x1c[:, :], x1_d[ts(k, 128), ts(c, 512)],
                   reads=["x1out"], writes=["x1c"])
            e_tt(em, tmp_a[:, :], x1c[:, :], ps_r[:, :], ALU.subtract,
                 ["x1c", "psr"], "tmpa")
            e_tt(em, tmp_a[:, :], tmp_a[:, :], ps_r2[:, :], ALU.mult,
                 ["tmpa", "psr2"], "tmpa")
            em.dma("gpsimd", xn1_d[ts(k, 128), ts(c, 512)], tmp_a[:, :],
                   reads=["tmpa"], writes=["xn1out"])

    em.finalize()
    return nc


def l0_in_maps(inputs):
    x = np.asarray(inputs["L0_attn_x"])
    w_qkv = np.ascontiguousarray(inputs["L0_attn_w_qkv"])
    b_qkv = np.asarray(inputs["L0_attn_b_qkv"])
    w_fc = np.ascontiguousarray(inputs["L0_attn_w_fc"])
    b_fc = np.asarray(inputs["L0_attn_b_fc"])
    gamma = np.asarray(inputs["L0_attn_gamma"])
    beta = np.asarray(inputs["L0_attn_beta"])
    gate_W = np.asarray(inputs["L0_ffn_gate_W"])
    expert_W = np.asarray(inputs["L0_ffn_expert_W"])
    out_W = np.asarray(inputs["L0_ffn_out_W"])

    common = {
        "w_qkv": w_qkv,
        "bqkv_row": b_qkv.reshape(1, -1),
        "bqkv_col": np.ascontiguousarray(b_qkv.reshape(12, 128).T),
        "w_fc": w_fc,
        "bfc_col": np.ascontiguousarray(b_fc.reshape(8, 128).T),
        "gam_col": np.ascontiguousarray(gamma.reshape(8, 128).T),
        "bet_col": np.ascontiguousarray(beta.reshape(8, 128).T),
        "gate_wT": np.ascontiguousarray(gate_W.T),
        "ew0T": np.ascontiguousarray(expert_W[0].T),
        "ew1T": np.ascontiguousarray(expert_W[1].T),
        "out_wT": np.ascontiguousarray(out_W.T),
        "iden": np.eye(128, dtype=np.float32),
    }
    maps = []
    for core in range(8):
        b, half = core // 2, core % 2
        xb = x[b]
        own = xb[half * T_OWN:(half + 1) * T_OWN]
        oth = xb[(1 - half) * T_OWN:(2 - half) * T_OWN]
        xf = np.ascontiguousarray(
            np.concatenate([own, oth], axis=0).T)
        m = dict(common)
        m["xf"] = xf
        maps.append(m)
    return maps


def run_l0(inputs):
    nc = build_l0()
    maps = l0_in_maps(inputs)
    res = run_bass_kernel_spmd(nc, maps, core_ids=list(range(8)))
    x1 = np.zeros((B, L, D), np.float32)
    xn1 = np.zeros((B, L, D), np.float32)
    for core in range(8):
        b, half = core // 2, core % 2
        sl = slice(half * T_OWN, (half + 1) * T_OWN)
        x1[b, sl] = res.results[core]["x1"].T
        xn1[b, sl] = res.results[core]["xn1"].T
    return x1, xn1


D_IN = 2048
DST = 16
TC = 512
NCT = D_IN // 128


def build_mamba():
    nc = bass.Bass("TRN2", target_bir_lowering=False)
    ND = D // 128

    NRT0 = 2 * D_IN // 128
    xn1f = nc.dram_tensor("xn1f", [D, L], F32, kind="ExternalInput").ap()
    in_wT = nc.dram_tensor("in_t", [128, NRT0 * ND * 128], F32,
                           kind="ExternalInput").ap()
    xprojT = nc.dram_tensor("xprojT", [D_IN, 96], F32, kind="ExternalInput").ap()
    dt_wT = nc.dram_tensor("dt_wT", [64, D_IN], F32, kind="ExternalInput").ap()
    conv_c_d = nc.dram_tensor("conv_c", [128, NCT * 4], F32, kind="ExternalInput").ap()
    convb_c_d = nc.dram_tensor("convb_c", [128, NCT], F32, kind="ExternalInput").ap()
    dtb_c_d = nc.dram_tensor("dtb_c", [128, NCT], F32, kind="ExternalInput").ap()
    alog_d = nc.dram_tensor("alog", [128, NCT * DST], F32, kind="ExternalInput").ap()
    dcol_d = nc.dram_tensor("dcol", [128, NCT], F32, kind="ExternalInput").ap()
    mout_wT = nc.dram_tensor("mout_t", [128, ND * NCT * 128], F32,
                             kind="ExternalInput").ap()
    outw_sT = nc.dram_tensor("outw_t", [128, ND * ND * 128], F32,
                             kind="ExternalInput").ap()
    gam_col_d = nc.dram_tensor("gam_col", [128, ND], F32, kind="ExternalInput").ap()
    bet_col_d = nc.dram_tensor("bet_col", [128, ND], F32, kind="ExternalInput").ap()
    esel_d = nc.dram_tensor("esel", [32, 32 * 128], F32, kind="ExternalInput").ap()
    part_d = nc.dram_tensor("partial", [D, L], F32, kind="ExternalOutput").ap()
    inw_b_d = nc.dram_tensor("inw_b", [128, NRT0 * ND * 128], BF16).ap()
    moutw_b_d = nc.dram_tensor("moutw_b", [128, ND * NCT * 128], BF16).ap()
    outw_b_d = nc.dram_tensor("outw_b", [128, ND * ND * 128], BF16).ap()

    em = EM(nc)

    ones_s = sbuf(nc, "ones_s", [128, 128])
    wtmp = sbuf(nc, "wtmp", [128, 512])
    wbtmp = sbuf(nc, "wbtmp", [128, 512], BF16)
    wstr = sbuf(nc, "wstr", [128, 2 * 2048], BF16)
    xproj_s = sbuf(nc, "xproj_s", [128, NCT * 96], BF16)
    dtw_s = sbuf(nc, "dtw_s", [64, D_IN], BF16)
    esel_s = sbuf(nc, "esel_s", [32, 32 * 128], BF16)
    conv_c = sbuf(nc, "conv_c_s", [128, NCT * 4])
    convb_c = sbuf(nc, "convb_c_s", [128, NCT])
    dtb_c = sbuf(nc, "dtb_c_s", [128, NCT])
    a_s = sbuf(nc, "a_s", [128, NCT * DST])
    dcol_s = sbuf(nc, "dcol_s", [128, NCT])
    gam_col = sbuf(nc, "gam_col_s", [128, ND])
    bet_col = sbuf(nc, "bet_col_s", [128, ND])
    xch = sbuf(nc, "xch", [128, ND * TC])
    hchb = sbuf(nc, "hchb", [128, ND * TC], BF16)
    xi_s = sbuf(nc, "xi_s", [128, NCT * (TC + 3)], BF16)
    z_s = sbuf(nc, "z_s", [128, NCT * TC], BF16)
    xcb_s = sbuf(nc, "xcb_s", [128, NCT * TC], BF16)
    ygb_s = sbuf(nc, "ygb_s", [128, NCT * TC], BF16)
    xdbl_s = sbuf(nc, "xdbl_s", [96, TC], BF16)
    xdbl_bc = sbuf(nc, "xdbl_bc", [32, TC], BF16)
    delta_s = sbuf(nc, "delta_s", [128, 2 * TC])
    du_s = sbuf(nc, "du_s", [128, 2 * TC])
    tsp_s = sbuf(nc, "tsp_s", [128, 2 * TC])
    av_s = sbuf(nc, "av_s", [128, 2 * TC])
    bv_s = sbuf(nc, "bv_s", [128, 2 * TC])
    hv_s = sbuf(nc, "hv_s", [128, 2 * TC])
    ty_s = sbuf(nc, "ty_s", [128, 2 * TC])
    yv_s = sbuf(nc, "yv_s", [128, TC])
    bc_c = sbuf(nc, "bc_c", [128, DST * TC], BF16)
    cc_c = sbuf(nc, "cc_c", [128, DST * TC], BF16)
    t1_s = sbuf(nc, "t1_s", [128, TC])
    t2_s = sbuf(nc, "t2_s", [128, TC])
    states = sbuf(nc, "states", [128, NCT * DST])
    p1_s = sbuf(nc, "p1_s", [128, ND * TC], BF16)
    mu_r = sbuf(nc, "mu_r", [1, TC])
    rs_r = sbuf(nc, "rs_r", [1, TC])
    rowtmp = sbuf(nc, "rowtmp", [1, TC])
    sq_s = sbuf(nc, "sq_s", [128, TC])

    ps_m = psum_t(nc, "ps_m", [128, 512])
    ps_x = psum_t(nc, "ps_x", [128, 512])
    ps_bc = psum_t(nc, "ps_bc", [128, 512])
    ps_cc = psum_t(nc, "ps_cc", [128, 512])
    ps_st = psum_t(nc, "ps_st", [1, 512])
    ps_st2 = psum_t(nc, "ps_st2", [1, 512])
    ps_r = psum_t(nc, "ps_r", [128, 512])
    ps_r2 = psum_t(nc, "ps_r2", [128, 512])

    em.emit("gpsimd", lambda g: g.memset(ones_s[:, :], 1.0), writes=["ones"])
    em.emit("gpsimd", lambda g: g.memset(xi_s[:, :], 0.0), writes=["xi"])
    em.dma("sync", conv_c[:, :], conv_c_d[:, :], writes=["convc"])
    em.dma("sync", convb_c[:, :], convb_c_d[:, :], writes=["convb"])
    em.dma("sync", dtb_c[:, :], dtb_c_d[:, :], writes=["dtb"])
    em.dma("sync", dcol_s[:, :], dcol_d[:, :], writes=["dcol"])
    em.dma("sync", gam_col[:, :], gam_col_d[:, :], writes=["gam"])
    em.dma("sync", bet_col[:, :], bet_col_d[:, :], writes=["bet"])
    em.dma("sync", a_s[:, :], alog_d[:, :], writes=["as"])
    em.emit("scalar", lambda s: s.activation(out=a_s[:, :], in_=a_s[:, :],
                                             func=AF.Exp),
            reads=["as"], writes=["as"])
    e_ts(em, a_s[:, :], a_s[:, :], -1.0, None, ALU.mult, None, ["as"], "as")
    for i in range(8):
        em.dma("sync", wtmp[0:32, 0:512], esel_d[:, ts(i, 512)], writes=["wtmp"])
        e_copy(em, esel_s[:, ts(i, 512)], wtmp[0:32, 0:512], ["wtmp"], "esel")
    for ct in range(NCT):
        em.dma("sync", wtmp[:, 0:96], xprojT[ts(ct, 128), :], writes=["wtmp"])
        e_copy(em, xproj_s[:, ts(ct, 96)], wtmp[:, 0:96], ["wtmp"], "xprj")
    for i in range(4):
        em.dma("sync", wtmp[0:64, 0:512], dt_wT[:, ts(i, 512)], writes=["wtmp"])
        e_copy(em, dtw_s[:, ts(i, 512)], wtmp[0:64, 0:512], ["wtmp"], "dtw")
    def cast_to_dram(src_ap, dst_ap, cols, tag):
        CW = 2048
        for o in range(0, cols, CW):
            par = (o // CW) % 2
            if par == 0:
                lbuf = xch[:, 0:CW]
                lkeys = ["xch0", "xch1", "xch2", "xch3"]
                stg, skey = xcb_s[:, 0:CW], "xcb"
            else:
                lbuf = xch[:, CW:2 * CW]
                lkeys = ["xch4", "xch5", "xch6", "xch7"]
                stg, skey = z_s[:, 0:CW], "z"
            em.dma(("sync", "scalar")[par], lbuf, src_ap[:, o:o + CW],
                   writes=lkeys)
            e_copy(em, stg, lbuf, lkeys, skey,
                   eng=("scalar" if par == 0 else "vector"))
            em.dma(("gpsimd", "sync")[par], dst_ap[:, o:o + CW],
                   stg, reads=[skey], writes=[tag])

    cast_to_dram(in_wT, inw_b_d, NRT0 * ND * 128, "inwb")
    cast_to_dram(mout_wT, moutw_b_d, ND * NCT * 128, "moutb")
    cast_to_dram(outw_sT, outw_b_d, ND * ND * 128, "outwb")

    NRT = 2 * D_IN // 128

    def ph_pre(c):
        csl = ts(c, TC)
        for k in range(ND):
            em.dma("sync", xch[:, ts(k, TC)], xn1f[ts(k, 128), csl],
                   writes=["xch%d" % k])
            e_mm(em, ps_st[:, :], ones_s[:, 0:1], xch[:, ts(k, TC)],
                 ["xch%d" % k, "ones"], "st", start=(k == 0), stop=(k == ND - 1))
            e_act(em, sq_s[:, :], xch[:, ts(k, TC)], AF.Square,
                  ["xch%d" % k], "sq")
            e_mm(em, ps_st2[:, :], ones_s[:, 0:1], sq_s[:, :], ["sq", "ones"],
                 "st2", start=(k == 0), stop=(k == ND - 1))
        ln_rows(em, ps_st[:, :], ps_st2[:, :], mu_r[:, :], rs_r[:, :],
                rowtmp[:, :], 1e-5, D, "st", "st2", "mu", "rs", "rt")
        e_bcast(em, ps_r[:, :], ones_s, mu_r[:, :], "mu", "psr")
        e_bcast(em, ps_r2[:, :], ones_s, rs_r[:, :], "rs", "psr2")
        for k in range(ND):
            e_tt(em, sq_s[:, :], xch[:, ts(k, TC)], ps_r[:, :], ALU.subtract,
                 ["xch%d" % k, "psr"], "sq")
            e_tt(em, sq_s[:, :], sq_s[:, :], ps_r2[:, :], ALU.mult,
                 ["sq", "psr2"], "sq")
            e_ts(em, hchb[:, ts(k, TC)], sq_s[:, :], gam_col[:, k:k + 1],
                 bet_col[:, k:k + 1], ALU.mult, ALU.add,
                 ["sq", "gam", "bet"], "hchb")
    def ph_mid(c):
        for rt in range(NRT):
            buf = rt % 2
            bank, bk = (ps_m, "psm") if buf == 0 else (ps_x, "psx")
            em.dma(("sync", "scalar")[buf],
                   wstr[:, buf * 2048: buf * 2048 + 1024],
                   inw_b_d[:, rt * 1024:(rt + 1) * 1024],
                   reads=["inwb"], writes=["wstr%d" % buf])
            for k in range(ND):
                e_mm(em, bank[:, :],
                     wstr[:, buf * 2048 + k * 128: buf * 2048 + (k + 1) * 128],
                     hchb[:, ts(k, TC)], ["wstr%d" % buf, "hchb"], bk,
                     start=(k == 0), stop=(k == ND - 1))
            if rt < NCT:
                ct = rt
                e_copy(em, xi_s[:, ct * (TC + 3): ct * (TC + 3) + 3],
                       xi_s[:, ct * (TC + 3) + TC: ct * (TC + 3) + TC + 3],
                       ["xi"], "xi", eng="vector")
                e_copy(em, xi_s[:, ct * (TC + 3) + 3: (ct + 1) * (TC + 3)],
                       bank[:, :], [bk, "xi"], "xi")
            else:
                ct = rt - NCT
                e_copy(em, z_s[:, ts(ct, TC)], bank[:, :], [bk], "z")
        for ct in range(NCT):
            base = ct * (TC + 3)
            e_ts(em, t1_s[:, :], xi_s[:, base: base + TC],
                 conv_c[:, ct * 4: ct * 4 + 1], None, ALU.mult, None,
                 ["xi", "convc"], "t1")
            for j in range(1, 4):
                e_stt(em, t1_s[:, :], xi_s[:, base + j: base + j + TC],
                      conv_c[:, ct * 4 + j: ct * 4 + j + 1], t1_s[:, :],
                      ALU.mult, ALU.add, ["xi", "convc", "t1"], "t1")
            e_act(em, xcb_s[:, ts(ct, TC)], t1_s[:, :], AF.Silu,
                  ["t1", "convb"], "xcb", bias=convb_c[:, ct:ct + 1])
        for ct in range(NCT):
            e_mm(em, ps_x[0:96, :], xproj_s[:, ts(ct, 96)], xcb_s[:, ts(ct, TC)],
                 ["xprj", "xcb"], "psx", start=(ct == 0), stop=(ct == NCT - 1))
        e_copy(em, xdbl_s[:, :], ps_x[0:96, :], ["psx"], "xdbl")
        em.dma("gpsimd", xdbl_bc[:, :], xdbl_s[64:96, :], reads=["xdbl"],
               writes=["xdblbc"])
        for s in range(DST):
            bank, bk = (ps_bc, "psbc") if s % 2 == 0 else (ps_cc, "pscc")
            e_mm(em, bank[:, :], esel_s[:, s * 128:(s + 1) * 128],
                 xdbl_bc[:, :], ["esel", "xdblbc"], bk, start=True, stop=True)
            e_copy(em, bc_c[:, ts(s, TC)], bank[:, :], [bk], "bcc",
                   eng=("scalar" if s % 2 == 0 else "vector"))
        for s in range(DST):
            bank, bk = (ps_bc, "psbc") if s % 2 == 0 else (ps_cc, "pscc")
            e_mm(em, bank[:, :], esel_s[:, (16 + s) * 128:(17 + s) * 128],
                 xdbl_bc[:, :], ["esel", "xdblbc"], bk, start=True, stop=True)
            e_copy(em, cc_c[:, ts(s, TC)], bank[:, :], [bk], "ccc",
                   eng=("scalar" if s % 2 == 0 else "vector"))
    def ph_scan(c):
        for ct in range(NCT):
            cp = ct % 2
            dsl = delta_s[:, ts(cp, TC)]
            dusl = du_s[:, ts(cp, TC)]
            dk, duk = "delta%d" % cp, "du%d" % cp
            bank, bk = (ps_m, "psm") if cp == 0 else (ps_x, "psx")
            tsa = tsp_s[:, ts(cp, TC)]
            tsk = "tsp%d" % cp
            e_mm(em, bank[:, :], dtw_s[:, ts(ct, 128)], xdbl_s[0:64, :],
                 ["dtw", "xdbl"], bk, start=True, stop=True)
            e_act(em, tsa, bank[:, :], AF.Abs, [bk, "dtb"], tsk,
                  bias=dtb_c[:, ct:ct + 1])
            e_act(em, tsa, tsa, AF.Exp, [tsk], tsk, scale=-1.0)
            e_act(em, tsa, tsa, AF.Ln, [tsk], tsk, bias=1.0)
            e_act(em, dsl, bank[:, :], AF.Relu, [bk, "dtb"], dk,
                  bias=dtb_c[:, ct:ct + 1])
            e_tt(em, dsl, dsl, tsa, ALU.add, [dk, tsk], dk)
            e_tt(em, dusl, dsl, xcb_s[:, ts(ct, TC)], ALU.mult,
                 [dk, "xcb"], duk)
            def emit_ab(s):
                sp = s % 2
                e_act(em, av_s[:, ts(sp, TC)], dsl, AF.Exp, [dk, "as"],
                      "av%d" % sp,
                      scale=a_s[:, ct * DST + s: ct * DST + s + 1])
                e_tt(em, bv_s[:, ts(sp, TC)], dusl, bc_c[:, ts(s, TC)],
                     ALU.mult, [duk, "bcc"], "bv%d" % sp)
            emit_ab(0)
            for s in range(DST):
                sp = s % 2
                av = av_s[:, ts(sp, TC)]
                bv = bv_s[:, ts(sp, TC)]
                hv = hv_s[:, ts(sp, TC)]
                avk, bvk, hvk = "av%d" % sp, "bv%d" % sp, "hv%d" % sp
                stcol = states[:, ct * DST + s: ct * DST + s + 1]
                stkey = "stt%d_%d" % (ct, s)
                if c == 0:
                    em.emit("vector",
                            lambda v, av=av, bv=bv, hv=hv:
                            v.tensor_tensor_scan(
                                out=hv, data0=av, data1=bv,
                                initial=0.0, op0=ALU.mult, op1=ALU.add),
                            reads=[avk, bvk], writes=[hvk])
                else:
                    em.emit("vector",
                            lambda v, av=av, bv=bv, hv=hv, stcol=stcol:
                            v.tensor_tensor_scan(
                                out=hv, data0=av, data1=bv,
                                initial=stcol, op0=ALU.mult, op1=ALU.add),
                            reads=[avk, bvk, stkey], writes=[hvk])
                e_copy(em, stcol, hv_s[:, sp * TC + TC - 1: sp * TC + TC],
                       [hvk], stkey)
                if s + 1 < DST:
                    emit_ab(s + 1)
                if s == 0:
                    e_tt(em, yv_s[:, :], hv, cc_c[:, ts(s, TC)], ALU.mult,
                         [hvk, "ccc"], "yv")
                else:
                    tyv = ty_s[:, ts(sp, TC)]
                    e_tt(em, tyv, hv, cc_c[:, ts(s, TC)], ALU.mult,
                         [hvk, "ccc"], "ty%d" % sp)
                    e_tt(em, yv_s[:, :], yv_s[:, :], tyv, ALU.add,
                         ["yv", "ty%d" % sp], "yv")
            e_stt(em, yv_s[:, :], xcb_s[:, ts(ct, TC)], dcol_s[:, ct:ct + 1],
                  yv_s[:, :], ALU.mult, ALU.add, ["xcb", "dcol", "yv"], "yv")
            e_act(em, t1_s[:, :], z_s[:, ts(ct, TC)], AF.Silu, ["z"], "t1")
            e_tt(em, ygb_s[:, ts(ct, TC)], yv_s[:, :], t1_s[:, :], ALU.mult,
                 ["yv", "t1"], "ygb")
    def ph_post(c):
        csl = ts(c, TC)
        for dt in range(ND):
            sl2 = dt % 2
            bank, bk = (ps_m, "psm") if sl2 == 0 else (ps_x, "psx")
            em.dma("sync", wstr[:, sl2 * 2048: sl2 * 2048 + 1024],
                   moutw_b_d[:, dt * 2048: dt * 2048 + 1024],
                   reads=["moutb"], writes=["wstr%d" % sl2])
            em.dma("scalar", wstr[:, sl2 * 2048 + 1024: (sl2 + 1) * 2048],
                   moutw_b_d[:, dt * 2048 + 1024:(dt + 1) * 2048],
                   reads=["moutb"], writes=["wstr%d" % sl2])
            for ct in range(NCT):
                e_mm(em, bank[:, :],
                     wstr[:, sl2 * 2048 + ct * 128:
                          sl2 * 2048 + (ct + 1) * 128],
                     ygb_s[:, ts(ct, TC)], ["wstr%d" % sl2, "ygb"], bk,
                     start=(ct == 0), stop=(ct == NCT - 1))
            e_copy(em, p1_s[:, ts(dt, TC)], bank[:, :], [bk], "p1",
                   eng=("scalar" if sl2 == 0 else "vector"))
        for dt in range(ND):
            sl2 = dt % 2
            bank, bk = (ps_m, "psm") if sl2 == 0 else (ps_x, "psx")
            em.dma(("sync", "scalar")[sl2],
                   wstr[:, sl2 * 2048: sl2 * 2048 + 1024],
                   outw_b_d[:, dt * 1024:(dt + 1) * 1024],
                   reads=["outwb"], writes=["wstr%d" % sl2])
            for j in range(ND):
                e_mm(em, bank[:, :],
                     wstr[:, sl2 * 2048 + j * 128: sl2 * 2048 + (j + 1) * 128],
                     p1_s[:, ts(j, TC)], ["wstr%d" % sl2, "p1"], bk,
                     start=(j == 0), stop=(j == ND - 1))
            e_copy(em, t1_s[:, :], bank[:, :], [bk], "t1")
            em.dma(("gpsimd", "sync")[sl2], part_d[ts(dt, 128), csl],
                   t1_s[:, :], reads=["t1"], writes=["pout"])

    NCH = L // TC
    ph_pre(0)
    ph_mid(0)
    for c in range(NCH):
        if c + 1 < NCH:
            ph_pre(c + 1)
        ph_scan(c)
        if c + 1 < NCH:
            ph_mid(c + 1)
        ph_post(c)

    em.finalize()
    return nc


def _tilemaj(wmat, nr, nccols):
    return np.ascontiguousarray(
        np.float32(wmat).reshape(nr, 128, nccols, 128)
        .transpose(1, 2, 0, 3).reshape(128, -1))


def mamba_in_maps(inputs, xn1):
    esel = np.zeros((32, 32 * 128), np.float32)
    for i in range(32):
        esel[i, i * 128:(i + 1) * 128] = 1.0
    maps = []
    for core in range(8):
        b, dirb = core // 2, core % 2
        tag = "f" if dirb == 0 else "b"
        arr = xn1[b]
        if dirb == 1:
            arr = arr[::-1]
        m = {
            "xn1f": np.ascontiguousarray(arr.T),
            "in_t": _tilemaj(np.asarray(inputs["L1_attn_in_w_" + tag]).T,
                             D // 128, 2 * D_IN // 128),
            "xprojT": np.ascontiguousarray(
                np.asarray(inputs["L1_attn_xproj_w_" + tag]).T),
            "dt_wT": np.ascontiguousarray(
                np.asarray(inputs["L1_attn_dt_w_" + tag]).T),
            "conv_c": np.ascontiguousarray(
                np.asarray(inputs["L1_attn_conv_w_" + tag]).reshape(
                    D_IN, 4).reshape(NCT, 128, 4).transpose(1, 0, 2).reshape(
                        128, NCT * 4)),
            "convb_c": np.ascontiguousarray(
                np.asarray(inputs["L1_attn_conv_b_" + tag]).reshape(
                    NCT, 128).T),
            "dtb_c": np.ascontiguousarray(
                np.asarray(inputs["L1_attn_dt_b_" + tag]).reshape(NCT, 128).T),
            "alog": np.ascontiguousarray(
                np.asarray(inputs["L1_attn_A_log_" + tag]).reshape(
                    NCT, 128, DST).transpose(1, 0, 2).reshape(128, NCT * DST)),
            "dcol": np.ascontiguousarray(
                np.asarray(inputs["L1_attn_D_" + tag]).reshape(NCT, 128).T),
            "mout_t": _tilemaj(np.asarray(inputs["L1_attn_mout_w_" + tag]).T,
                               D_IN // 128, D // 128),
            "outw_t": _tilemaj(
                np.asarray(inputs["L1_attn_out_w"])[:,
                    dirb * D:(dirb + 1) * D].T, D // 128, D // 128),
            "gam_col": np.ascontiguousarray(
                np.asarray(inputs["L1_attn_gamma"]).reshape(8, 128).T),
            "bet_col": np.ascontiguousarray(
                np.asarray(inputs["L1_attn_beta"]).reshape(8, 128).T),
            "esel": esel,
        }
        maps.append(m)
    return maps


def run_mamba(inputs, xn1):
    nc = build_mamba()
    maps = mamba_in_maps(inputs, xn1)
    res = run_bass_kernel_spmd(nc, maps, core_ids=list(range(8)))
    pf = np.zeros((B, L, D), np.float32)
    pb = np.zeros((B, L, D), np.float32)
    for core in range(8):
        b, dirb = core // 2, core % 2
        p = res.results[core]["partial"].T
        if dirb == 1:
            p = p[::-1]
            pb[b] = p
        else:
            pf[b] = p
    return pf, pb


TS = 2560
NW = TS // 256
HEADS = 16
DH = 64
INNER = 1024
FFI = 2816


FFN_SKIP = set()


def build_ffn():
    nc = bass.Bass("TRN2", target_bir_lowering=False)
    ND = D // 128
    NI = INNER // 128
    NJC = FFI // 128

    xst_d = nc.dram_tensor("xst", [D, 1 + TS], F32, kind="ExternalInput").ap()
    wqk_d = nc.dram_tensor("wqk_t", [128, 2 * NI * ND * 128], F32,
                           kind="ExternalInput").ap()
    wv_d = nc.dram_tensor("wv_t", [128, 2 * ND * 512], F32,
                          kind="ExternalInput").ap()
    wo_d = nc.dram_tensor("wo_t", [128, ND * NI * 128], F32,
                          kind="ExternalInput").ap()
    w1a_d = nc.dram_tensor("w1a_t", [128, NJC * ND * 128], F32,
                           kind="ExternalInput").ap()
    w1g_d = nc.dram_tensor("w1g_t", [128, NJC * ND * 128], F32,
                           kind="ExternalInput").ap()
    w2_d = nc.dram_tensor("w2_t", [128, ND * NJC * 128], F32,
                          kind="ExternalInput").ap()
    b1a_d = nc.dram_tensor("b1a", [128, NJC], F32, kind="ExternalInput").ap()
    b1g_d = nc.dram_tensor("b1g", [128, NJC], F32, kind="ExternalInput").ap()
    b2_d = nc.dram_tensor("b2c", [128, ND], F32, kind="ExternalInput").ap()
    lng_d = nc.dram_tensor("lng", [128, ND], F32, kind="ExternalInput").ap()
    lnb_d = nc.dram_tensor("lnb", [128, ND], F32, kind="ExternalInput").ap()
    qs_d = nc.dram_tensor("qs", [128, 1], F32, kind="ExternalInput").ap()
    ks_d = nc.dram_tensor("ks", [128, 1], F32, kind="ExternalInput").ap()
    cs_d = nc.dram_tensor("cs_t", [128, TS], F32, kind="ExternalInput").ap()
    ssn_d = nc.dram_tensor("ssn_t", [128, TS], F32, kind="ExternalInput").ap()
    csk_d = nc.dram_tensor("csk_t", [128, TS], F32, kind="ExternalInput").ap()
    ssnk_d = nc.dram_tensor("ssnk_t", [128, TS], F32, kind="ExternalInput").ap()
    flag_d = nc.dram_tensor("flagrow", [1, NW * 2 * 384], F32,
                            kind="ExternalInput").ap()
    band_d = nc.dram_tensor("band", [128, 384], F32, kind="ExternalInput").ap()
    cmask_d = nc.dram_tensor("cmask", [1, TS], F32, kind="ExternalInput").ap()
    blk16_d = nc.dram_tensor("blk16", [128, NI * 16], F32,
                             kind="ExternalInput").ap()
    esel2_d = nc.dram_tensor("esel2", [16, NI * 128], F32,
                             kind="ExternalInput").ap()
    iden_d = nc.dram_tensor("iden", [128, 128], F32, kind="ExternalInput").ap()
    rperm_d = nc.dram_tensor("rperm", [128, 128], F32, kind="ExternalInput").ap()
    xout_d = nc.dram_tensor("xout", [D, T_OWN], F32, kind="ExternalOutput").ap()
    wqk_b = nc.dram_tensor("wqk_b", [128, 2 * NI * ND * 128], BF16).ap()
    wv_b = nc.dram_tensor("wv_b", [128, 2 * ND * 512], BF16).ap()
    wo_b = nc.dram_tensor("wo_b", [128, ND * NI * 128], BF16).ap()
    w1a_b = nc.dram_tensor("w1a_b", [128, NJC * ND * 128], BF16).ap()
    w1g_b = nc.dram_tensor("w1g_b", [128, NJC * ND * 128], BF16).ap()
    w2_b = nc.dram_tensor("w2_b", [128, ND * NJC * 128], BF16).ap()

    em = EM(nc)

    xs = sbuf(nc, "xs", [128, ND * (1 + TS)])
    wqkstr = sbuf(nc, "wqkstr", [128, 2 * 1024], BF16)
    wostr = sbuf(nc, "wostr", [128, 2 * NI * 128], BF16)
    wvstr = sbuf(nc, "wvstr", [128, 2 * 1024], BF16)
    ones_s = sbuf(nc, "ones_s", [128, 128])
    rperm_s = sbuf(nc, "rperm_s", [128, 128])
    eye_b = sbuf(nc, "eye_b", [128, 128], BF16)
    wtmp = sbuf(nc, "wtmp", [128, 512])
    w1str = sbuf(nc, "w1str", [128, 2048], BF16)
    tf1 = sbuf(nc, "tf1", [128, 2 * 512])
    tf2 = sbuf(nc, "tf2", [128, 2 * 512])
    xwb = sbuf(nc, "xwb", [128, ND * 256], BF16)
    qro = sbuf(nc, "qro", [128, NI * 256], BF16)
    kro = sbuf(nc, "kro", [128, NI * 512], BF16)
    vtm = sbuf(nc, "vtm", [128, 4 * INNER], BF16)
    q1_s = sbuf(nc, "q1_s", [128, 2 * 256])
    cs_s = sbuf(nc, "cs_s", [128, 256])
    ssn_s = sbuf(nc, "ssn_s", [128, 256])
    csk_s = sbuf(nc, "csk_s", [128, 256])
    ssnk_s = sbuf(nc, "ssnk_s", [128, 256])
    rr_s = sbuf(nc, "rr_s", [16, 512])
    rr_b = sbuf(nc, "rr_b", [16, 512], BF16)
    blk16_s = sbuf(nc, "blk16_s", [128, NI * 16])
    esel2_b = sbuf(nc, "esel2_b", [16, NI * 128], BF16)
    pexp_b = sbuf(nc, "pexp_b", [128, 4 * 384], BF16)
    pt_b = sbuf(nc, "pt_b", [128, 2 * 384], BF16)
    col_s = sbuf(nc, "col_s", [128, 16])
    attm = sbuf(nc, "attm", [128, INNER], BF16)
    attf = sbuf(nc, "attf", [128, NI * 256], BF16)
    band_b = sbuf(nc, "band_b", [128, 384], BF16)
    ones_bf = sbuf(nc, "ones_bf", [1, 128], BF16)
    flag_r = sbuf(nc, "flag_r", [1, 384])
    flag_b = sbuf(nc, "flag_b", [1, 384], BF16)
    cmask_r = sbuf(nc, "cmask_r", [1, 512])
    qs_c = sbuf(nc, "qs_c", [128, 2])
    b1a_c = sbuf(nc, "b1a_c", [128, NJC])
    b1g_c = sbuf(nc, "b1g_c", [128, NJC])
    b2_c = sbuf(nc, "b2_c", [128, ND])
    lng_c = sbuf(nc, "lng_c", [128, ND])
    lnb_c = sbuf(nc, "lnb_c", [128, ND])
    xnb = sbuf(nc, "xnb", [128, ND * 512], BF16)
    hgb = sbuf(nc, "hgb", [128, NJC * 512], BF16)
    fout = sbuf(nc, "fout", [128, ND * 512], BF16)
    mu_r = sbuf(nc, "mu_r", [1, 512])
    rs_r = sbuf(nc, "rs_r", [1, 512])
    rowtmp = sbuf(nc, "rowtmp", [1, 512])

    ps = [psum_t(nc, "psg%d" % i, [128, 512]) for i in range(6)]
    ps_r = psum_t(nc, "ps_r", [128, 512])
    ps_r2 = psum_t(nc, "ps_r2", [128, 512])
    ps_tb = ps[4][:, 256:320].bitcast(BF16)
    ps_tb2 = ps_r[:, 0:64].bitcast(BF16)
    PK = ["psg%d" % i for i in range(6)]

    em.emit("gpsimd", lambda g: g.memset(ones_s[:, :], 1.0), writes=["ones"])
    em.emit("gpsimd", lambda g: g.memset(kro[:, :], 0.0), writes=["kro"])
    em.emit("gpsimd", lambda g: g.memset(vtm[:, :], 0.0), writes=["vtm"])
    e_copy(em, ones_bf[:, :], ones_s[0:1, :], ["ones"], "onesb")
    em.dma("sync", wtmp[:, 0:384], band_d[:, :], writes=["wtmp0"])
    e_copy(em, band_b[:, :], wtmp[:, 0:384], ["wtmp0"], "bandb")
    em.dma("sync", blk16_s[:, :], blk16_d[:, :], writes=["blk16"])
    em.dma("sync", b1a_c[:, :], b1a_d[:, :], writes=["b1a"])
    em.dma("sync", b1g_c[:, :], b1g_d[:, :], writes=["b1g"])
    em.dma("sync", b2_c[:, :], b2_d[:, :], writes=["b2"])
    em.dma("sync", lng_c[:, :], lng_d[:, :], writes=["lng"])
    em.dma("sync", lnb_c[:, :], lnb_d[:, :], writes=["lnb"])
    for i in range(2):
        em.dma("sync", wtmp[0:16, 0:512], esel2_d[:, ts(i, 512)], writes=["wtmp0"])
        e_copy(em, esel2_b[:, ts(i, 512)], wtmp[0:16, 0:512], ["wtmp0"], "esel2")
    em.dma("sync", rperm_s[:, :], rperm_d[:, :], writes=["rperm"])
    em.dma("sync", wtmp[:, 0:128], iden_d[:, :], writes=["wtmp0"])
    e_copy(em, eye_b[:, :], wtmp[:, 0:128], ["wtmp0"], "eye")
    em.dma("sync", qs_c[:, 0:1], qs_d[:, :], writes=["qsc"])
    em.dma("sync", wtmp[:, 0:1], ks_d[:, :], writes=["wtmp0"])
    e_copy(em, qs_c[:, 1:2], wtmp[:, 0:1], ["wtmp0"], "qsc", eng="vector")
    e_ts(em, qs_c[:, 0:1], qs_c[:, 0:1], 8.0, None, ALU.mult, None,
         ["qsc"], "qsc")

    for k in range(ND):
        em.dma("sync", xs[:, k * (1 + TS):(k + 1) * (1 + TS)],
               xst_d[ts(k, 128), :], writes=["xs"])

    QKQ = ("sync", "scalar", "gpsimd", "sync")

    def cast_to_dram(src_ap, dst_ap, cols, tag):
        CW = 1024
        qi = 0
        for o in range(0, cols, CW):
            wd = min(CW, cols - o)
            par = qi % 2
            lbuf, lkeys = (tf1, ["tf10", "tf11"]) if par == 0 else \
                (tf2, ["tf20", "tf21"])
            stgb, skeys = (hgb, ["hgb0", "hgb1"]) if par == 0 else \
                (fout, ["fout"])
            stg = stgb[:, 0:wd]
            em.dma(QKQ[qi % 4], lbuf[:, 0:wd],
                   src_ap[:, o:o + wd], writes=lkeys)
            if qi % 2 == 0:
                em.emit("scalar", lambda s, stg=stg, lb=lbuf[:, 0:wd]:
                        s.copy(out=stg, in_=lb), reads=lkeys, writes=skeys)
            else:
                em.emit("vector", lambda v, stg=stg, lb=lbuf[:, 0:wd]:
                        v.tensor_copy(out=stg, in_=lb), reads=lkeys,
                        writes=skeys)
            em.dma(QKQ[(qi + 2) % 4], dst_ap[:, o:o + wd],
                   stg, reads=skeys, writes=[tag])
            qi += 1

    cast_to_dram(wqk_d, wqk_b, 2 * NI * ND * 128, "wqkb")
    cast_to_dram(wv_d, wv_b, 2 * ND * 512, "wvb")
    cast_to_dram(wo_d, wo_b, ND * NI * 128, "wob")
    cast_to_dram(w1a_d, w1a_b, NJC * ND * 128, "w1ab")
    cast_to_dram(w1g_d, w1g_b, NJC * ND * 128, "w1gb")
    cast_to_dram(w2_d, w2_b, ND * NJC * 128, "w2b")

    def xshift_ap(k, col0, width):
        base = k * (1 + TS)
        off = 1 + col0 - (1 if k >= ND // 2 else 0)
        return xs[:, base + off: base + off + width]

    def wo_residual(w):
        em.dma("sync", cmask_r[:, 0:256], cmask_d[:, w * 256:(w + 1) * 256],
               writes=["cmask"])
        e_bcast(em, ps_r2[:, 0:256], ones_s, cmask_r[:, 0:256], "cmask", "psr2")
        for dt in range(ND):
            pk = PK[dt % 2]
            wb3, wo3, wk3 = ((wostr, 0, ["wos0"]), (wostr, 1024, ["wos1"]),
                             (hgb, 0, ["hgb0", "hgb1"]))[dt % 3]
            em.dma(QKQ[dt % 3], wb3[:, wo3: wo3 + NI * 128],
                   wo_b[:, dt * NI * 128: (dt + 1) * NI * 128],
                   reads=["wob"], writes=wk3)
            for i in range(NI):
                e_mm(em, ps[dt % 2][:, 0:256],
                     wb3[:, wo3 + i * 128: wo3 + (i + 1) * 128],
                     attf[:, i * 256:(i + 1) * 256],
                     wk3 + ["attf"], pk,
                     start=(i == 0), stop=(i == NI - 1))
            par = dt % 2
            ta = tf1[:, par * 512: par * 512 + 256]
            e_copy(em, ta, ps[par][:, 0:256], [pk], "tf1%d" % par)
            e_tt(em, ta, ta, ps_r2[:, 0:256],
                 ALU.mult, ["tf1%d" % par, "psr2"], "tf1%d" % par)
            xsl = xs[:, dt * (1 + TS) + 1 + w * 256:
                     dt * (1 + TS) + 1 + (w + 1) * 256]
            e_tt(em, xsl, xsl, ta, ALU.add,
                 ["xs", "tf1%d" % par], "xs")

    for w in range(NW if "attn" not in FFN_SKIP else 0):
        for k in range(ND):
            e_copy(em, xwb[:, ts(k, 256)], xshift_ap(k, w * 256, 256),
                   ["xs"], "xwb", eng=("scalar" if k % 2 else "vector"))
        for jt in range(NI):
            e_copy(em, kro[:, jt * 512: jt * 512 + 256],
                   kro[:, jt * 512 + 256: (jt + 1) * 512], ["kro"], "kro",
                   eng="vector")
        for i in range(2):
            e_copy(em, vtm[:, ts(i, INNER)], vtm[:, ts(2 + i, INNER)],
                   ["vtm"], "vtm", eng="scalar")
        def vdma(idx):
            half, kg = idx // 4, idx % 4
            slot = idx % 2
            em.dma(QKQ[idx % 3], wvstr[:, ts(slot, 1024)],
                   wv_b[:, half * 4096 + kg * 1024:
                        half * 4096 + (kg + 1) * 1024],
                   reads=["wvb"], writes=["wvs%d" % slot])
        if "v" not in FFN_SKIP:
            vdma(0)
            vdma(1)
        for idx in range(8 if "v" not in FFN_SKIP else 0):
            half, kg = idx // 4, idx % 4
            slot = idx % 2
            vb = 2 - 2 * half
            for kk in range(2):
                k = kg * 2 + kk
                for tt in range(2):
                    e_mm(em, ps[vb + tt][:, :],
                         xwb[:, k * 256 + tt * 128: k * 256 + tt * 128 + 128],
                         wvstr[:, slot * 1024 + kk * 512:
                               slot * 1024 + kk * 512 + 512],
                         ["xwb", "wvs%d" % slot], PK[vb + tt],
                         start=(k == 0), stop=(k == ND - 1))
            if idx + 2 < 8:
                vdma(idx + 2)
            if kg == 3:
                for tt in range(2):
                    e_copy(em, vtm[:, (2 + tt) * INNER + half * 512:
                                   (2 + tt) * INNER + (half + 1) * 512],
                           ps[vb + tt][:, :], [PK[vb + tt]], "vtm",
                           eng=("scalar" if tt == 0 else "vector"))
        em.dma("sync", cs_s[:, :], cs_d[:, w * 256:(w + 1) * 256], writes=["cst"])
        em.dma("scalar", ssn_s[:, :], ssn_d[:, w * 256:(w + 1) * 256], writes=["sst"])
        em.dma("sync", csk_s[:, :], csk_d[:, w * 256:(w + 1) * 256], writes=["ckt"])
        em.dma("gpsimd", ssnk_s[:, :], ssnk_d[:, w * 256:(w + 1) * 256],
               writes=["skt"])
        for qk_i in range(2 if "qk" not in FFN_SKIP else 0):
            dkey = "qraw" if qk_i == 0 else "kro"
            def qk_tail(jt):
                par = jt % 2
                pk = PK[par]
                e_act(em, tf2[:, par * 512: par * 512 + 256],
                      ps[par][:, 0:256], AF.Square, [pk], "tf2%d" % par)
                e_mm(em, ps[4][0:16, 0:256], blk16_s[:, ts(jt, 16)],
                     tf2[:, par * 512: par * 512 + 256],
                     ["tf2%d" % par, "blk16"], "psg4",
                     start=(jt == 0), stop=(jt == NI - 1))
                dst = (qro[:, ts(jt, 256)] if qk_i == 0 else
                       kro[:, jt * 512 + 256:(jt + 1) * 512])
                e_copy(em, dst, ps[par][:, 0:256], [pk], dkey)
            pend = None
            for jt in range(NI):
                par = jt % 2
                pk = PK[par]
                qj = qk_i * NI + jt
                sl4 = (qk_i * NI + jt) % 4
                wb4, wo4, wk4 = ((wqkstr, 0, "wqks0"), (wqkstr, 1024, "wqks1"),
                                 (w1str, 0, "w1s0"), (w1str, 1024, "w1s1"))[sl4]
                em.dma(QKQ[jt % 3], wb4[:, wo4:wo4 + 1024],
                       wqk_b[:, qj * 1024:(qj + 1) * 1024],
                       reads=["wqkb"], writes=[wk4])
                for k in range(ND):
                    e_mm(em, ps[par][:, 0:256],
                         wb4[:, wo4 + k * 128: wo4 + (k + 1) * 128],
                         xwb[:, ts(k, 256)],
                         [wk4, "xwb"], pk,
                         start=(k == 0), stop=(k == ND - 1))
                if pend is not None:
                    qk_tail(pend)
                pend = jt
            qk_tail(pend)
            e_act(em, rr_s[:, ts(qk_i, 256)], ps[4][0:16, 0:256], AF.Sqrt,
                  ["psg4"], "rr")
            e_ts(em, rr_s[:, ts(qk_i, 256)], rr_s[:, ts(qk_i, 256)], 1e-12,
                 None, ALU.max, None, ["rr"], "rr")
            em.emit("vector",
                    lambda v, qk_i=qk_i: v.reciprocal(
                        out=rr_s[:, ts(qk_i, 256)],
                        in_=rr_s[:, ts(qk_i, 256)]),
                    reads=["rr"], writes=["rr"])
            e_copy(em, rr_b[:, ts(qk_i, 256)], rr_s[:, ts(qk_i, 256)],
                   ["rr"], "rrb")
        for qk_i in range(2 if "rot" not in FFN_SKIP else 0):
            srcbuf, dkey = (qro, "qraw") if qk_i == 0 else (kro, "kro")
            cst, ckey = (cs_s, "cst") if qk_i == 0 else (csk_s, "ckt")
            sst, skey2 = (ssn_s, "sst") if qk_i == 0 else (ssnk_s, "skt")
            for jt in range(NI):
                par = jt % 2
                po = par * 256
                ta = tf1[:, par * 512: par * 512 + 256]
                tb = tf2[:, par * 512 + 256: par * 512 + 512]
                sl = (ts(jt, 256) if qk_i == 0 else
                      slice(jt * 512 + 256, (jt + 1) * 512))
                e_mm(em, ps_r[:, po:po + 256], esel2_b[:, ts(jt, 128)],
                     rr_b[:, ts(qk_i, 256)], ["esel2", "rrb"], "psr%d" % par,
                     start=True, stop=True)
                e_stt(em, q1_s[:, po:po + 256], srcbuf[:, sl],
                      qs_c[:, qk_i:qk_i + 1], ps_r[:, po:po + 256],
                      ALU.mult, ALU.mult,
                      [dkey, "qsc", "psr%d" % par], "q1%d" % par)
                e_mm(em, ps[par][:, 256:512], rperm_s[:, :],
                     q1_s[:, po:po + 256], ["rperm", "q1%d" % par], PK[par],
                     start=True, stop=True)
                e_tt(em, ta, q1_s[:, po:po + 256], cst[:, :],
                     ALU.mult, ["q1%d" % par, ckey], "tf1%d" % par)
                e_tt(em, tb, ps[par][:, 256:512],
                     sst[:, :], ALU.mult, [PK[par], skey2], "tf2%d" % par)
                e_tt(em, srcbuf[:, sl], ta, tb, ALU.add,
                     ["tf1%d" % par, "tf2%d" % par], dkey)
        if w >= 1 and "wo" not in FFN_SKIP:
            wo_residual(w - 1)
        use_flag = w in (0, 2)
        for qc in range(2 if "core" not in FFN_SKIP else 0):
            if use_flag:
                em.dma("sync", flag_r[:, :],
                       flag_d[:, (w * 2 + qc) * 384:(w * 2 + qc + 1) * 384],
                       writes=["flag"])
                e_copy(em, flag_b[:, :], flag_r[:, :], ["flag"], "flagb")
            def stage_a(h):
                jt, sub = h // 2, h % 2
                pb4 = h % 4
                po4 = pb4 * 384
                rsl = slice(sub * 64, sub * 64 + 64)
                psim = ps[pb4]
                pks = PK[pb4]
                ck = "col%d" % pb4
                e_mm(em, psim[:, 0:384],
                     qro[rsl, jt * 256 + qc * 128: jt * 256 + qc * 128 + 128],
                     kro[rsl, jt * 512 + qc * 128: jt * 512 + qc * 128 + 384],
                     ["qraw", "kro"], pks, start=True, stop=False)
                if use_flag:
                    e_mm(em, psim[:, 0:384], ones_bf[0:1, :], flag_b[:, :],
                         ["onesb", "flagb"], pks, start=False, stop=False)
                e_mm(em, psim[:, 0:384], eye_b[:, :], band_b[:, :],
                     ["eye", "bandb"], pks, start=False, stop=True)
                e_red(em, col_s[:, pb4 * 4:pb4 * 4 + 1], psim[:, 0:384],
                      ALU.max, [pks], ck, negate=True)
                em.emit("scalar",
                        lambda s, po4=po4, pb4=pb4, psim=psim: s.activation(
                            out=pexp_b[:, po4:po4 + 384], in_=psim[:, 0:384],
                            func=AF.Exp, bias=col_s[:, pb4 * 4:pb4 * 4 + 1]),
                        reads=[pks, ck],
                        writes=["pexp%d" % pb4])
            def stage_b(h):
                par = h % 2
                pb4 = h % 4
                po4 = pb4 * 384
                po = par * 384
                ck = "col%d" % pb4
                e_red(em, col_s[:, pb4 * 4 + 1:pb4 * 4 + 2],
                      pexp_b[:, po4:po4 + 384], ALU.add, ["pexp%d" % pb4],
                      ck)
                em.emit("vector",
                        lambda v, pb4=pb4: v.reciprocal(
                            out=col_s[:, pb4 * 4 + 2:pb4 * 4 + 3],
                            in_=col_s[:, pb4 * 4 + 1:pb4 * 4 + 2]),
                        reads=[ck], writes=[ck])
                pvd = (ps[5], "psg5") if par == 0 else (ps_r2, "psr2")
                tpd0 = (ps_tb, ["pstb", "psg4"]) if par == 0 else \
                    (ps_tb2, ["pstb2", "psr0"])
                for i in range(3):
                    tpd = tpd0
                    em.emit("tensor",
                            lambda t, i=i, po4=po4, tpd=tpd: t.transpose(
                                out=tpd[0][:, :], in_=pexp_b[:, po4 + i * 128:
                                                            po4 + (i + 1) * 128],
                                identity=eye_b[:, :]),
                            reads=["pexp%d" % pb4, "eye"], writes=tpd[1])
                    e_copy(em, pt_b[:, po + i * 128: po + (i + 1) * 128],
                           tpd[0][:, :], tpd[1], "ptb%d" % par,
                           eng="vector")
                for i in range(3):
                    e_mm(em, pvd[0][:, 0:64],
                         pt_b[:, po + i * 128: po + (i + 1) * 128],
                         vtm[:, (qc + i) * INNER + h * 64:
                             (qc + i) * INNER + (h + 1) * 64],
                         ["ptb%d" % par, "vtm"], pvd[1],
                         start=(i == 0), stop=(i == 2))
                e_ts(em, attm[:, h * 64:(h + 1) * 64],
                     pvd[0][:, 0:64],
                     col_s[:, pb4 * 4 + 2:pb4 * 4 + 3], None, ALU.mult, None,
                     [pvd[1], ck], "attm_%d" % h)
            for h in range(4):
                stage_a(h)
            for h in range(HEADS):
                stage_b(h)
                if h + 4 < HEADS:
                    stage_a(h + 4)
            for i in range(NI):
                tpd = (ps_tb, ["pstb", "psg4"]) if i % 2 == 0 else \
                    (ps_tb2, ["pstb2", "psr0"])
                em.emit("tensor",
                        lambda t, i=i, tpd=tpd: t.transpose(
                            out=tpd[0][:, :], in_=attm[:, ts(i, 128)],
                            identity=eye_b[:, :]),
                        reads=["attm_%d" % (2 * i), "attm_%d" % (2 * i + 1),
                               "eye"], writes=tpd[1])
                e_copy(em, attf[:, i * 256 + qc * 128:
                               i * 256 + qc * 128 + 128],
                       tpd[0][:, :], tpd[1], "attf",
                       eng=("scalar" if i % 2 == 0 else "vector"))
    if "attn" not in FFN_SKIP and "wo" not in FFN_SKIP:
        wo_residual(NW - 1)

    CH = 512
    for cc in range(TS // CH if "ffn" not in FFN_SKIP else 0):
        c0 = cc * CH
        for k in range(ND):
            par = k % 2
            e_mm(em, ps[4][0:1, 0:CH], ones_s[:, 0:1],
                 xshift_ap(k, c0, CH),
                 ["xs", "ones"], "psg4", start=(k == 0),
                 stop=(k == ND - 1))
            e_act(em, tf2[:, par * CH:(par + 1) * CH],
                  xshift_ap(k, c0, CH), AF.Square,
                  ["xs"], "tf2%d" % par)
            e_mm(em, ps[5][0:1, 0:CH], ones_s[:, 0:1],
                 tf2[:, par * CH:(par + 1) * CH],
                 ["tf2%d" % par, "ones"], "psg5", start=(k == 0),
                 stop=(k == ND - 1))
        ln_rows(em, ps[4][0:1, 0:CH], ps[5][0:1, 0:CH], mu_r[:, :],
                rs_r[:, :], rowtmp[:, :], 1e-5, D, "psg4", "psg5", "mu", "rs",
                "rt")
        em.emit("tensor",
                lambda t: t.matmul(out=ps_r[:, 0:CH], lhsT=ones_s[0:1, :],
                                   rhs=mu_r[:, :], start=True, stop=True),
                reads=["mu", "ones"], writes=["psr0", "psr1"])
        e_bcast(em, ps_r2[:, 0:CH], ones_s, rs_r[:, :], "rs", "psr2")
        for k in range(ND):
            par = k % 2
            e_tt(em, tf1[:, par * CH:(par + 1) * CH],
                 xshift_ap(k, c0, CH), ps_r[:, 0:CH], ALU.subtract,
                 ["xs", "psr0", "psr1"], "tf1%d" % par)
            e_tt(em, tf1[:, par * CH:(par + 1) * CH],
                 tf1[:, par * CH:(par + 1) * CH], ps_r2[:, 0:CH],
                 ALU.mult, ["tf1%d" % par, "psr2"], "tf1%d" % par)
            e_ts(em, xnb[:, ts(k, CH)], tf1[:, par * CH:(par + 1) * CH],
                 lng_c[:, k:k + 1], lnb_c[:, k:k + 1], ALU.mult, ALU.add,
                 ["tf1%d" % par, "lng", "lnb"], "xnb")
        if cc >= 1:
            p0 = (cc - 1) * CH
            for dt in range(ND):
                xsl = xs[:, dt * (1 + TS) + 1 + p0:
                         dt * (1 + TS) + 1 + p0 + CH]
                e_tt(em, xsl, xsl, fout[:, ts(dt, CH)], ALU.add,
                     ["xs", "fout"], "xs")
        for jc in range(NJC if "w1" not in FFN_SKIP else 0):
            par = jc % 2
            wbuf = w1str if par == 0 else wqkstr
            ka = "w1s0" if par == 0 else "wqks0"
            kg2 = "w1s1" if par == 0 else "wqks1"
            em.dma(QKQ[jc % 3], wbuf[:, 0:1024],
                   w1a_b[:, jc * 1024:(jc + 1) * 1024],
                   reads=["w1ab"], writes=[ka])
            em.dma(QKQ[(jc + 1) % 3], wbuf[:, 1024:2048],
                   w1g_b[:, jc * 1024:(jc + 1) * 1024],
                   reads=["w1gb"], writes=[kg2])
            for k in range(ND):
                e_mm(em, ps[par][:, 0:CH],
                     wbuf[:, k * 128:(k + 1) * 128],
                     xnb[:, ts(k, CH)], [ka, "xnb"], PK[par],
                     start=(k == 0), stop=(k == ND - 1))
                e_mm(em, ps[2 + par][:, 0:CH],
                     wbuf[:, 1024 + k * 128: 1024 + (k + 1) * 128],
                     xnb[:, ts(k, CH)], [kg2, "xnb"], PK[2 + par],
                     start=(k == 0), stop=(k == ND - 1))
            e_act(em, tf1[:, par * CH:(par + 1) * CH], ps[par][:, 0:CH],
                  AF.Identity, [PK[par], "b1a"], "tf1%d" % par,
                  bias=b1a_c[:, jc:jc + 1])
            e_act(em, tf2[:, par * CH:(par + 1) * CH],
                  ps[2 + par][:, 0:CH], AF.Gelu, [PK[2 + par], "b1g"],
                  "tf2%d" % par, bias=b1g_c[:, jc:jc + 1])
            e_tt(em, hgb[:, ts(jc, CH)], tf1[:, par * CH:(par + 1) * CH],
                 tf2[:, par * CH:(par + 1) * CH], ALU.mult,
                 ["tf1%d" % par, "tf2%d" % par], "hgb%d" % par)
        em.dma("sync", cmask_r[:, 0:CH], cmask_d[:, c0:c0 + CH],
               writes=["cmask"])
        e_bcast(em, ps_r2[:, 0:CH], ones_s, cmask_r[:, 0:CH], "cmask", "psr2")
        for dt in range(ND if "w2" not in FFN_SKIP else 0):
            par = dt % 2
            wbuf, wkeys = (vtm, ["vtm"]) if par == 0 else (kro, ["kro"])
            em.dma(QKQ[dt % 3], wbuf[:, 0:NJC * 128],
                   w2_b[:, dt * NJC * 128:(dt + 1) * NJC * 128],
                   reads=["w2b"], writes=wkeys)
            for jc in range(NJC):
                e_mm(em, ps[4 + par][:, 0:CH],
                     wbuf[:, jc * 128:(jc + 1) * 128],
                     hgb[:, ts(jc, CH)],
                     wkeys + ["hgb0", "hgb1"], PK[4 + par],
                     start=(jc == 0), stop=(jc == NJC - 1))
            e_act(em, tf1[:, par * CH:(par + 1) * CH],
                  ps[4 + par][:, 0:CH], AF.Identity, [PK[4 + par], "b2"],
                  "tf1%d" % par, bias=b2_c[:, dt:dt + 1])
            e_tt(em, fout[:, ts(dt, CH)], tf1[:, par * CH:(par + 1) * CH],
                 ps_r2[:, 0:CH], ALU.mult, ["tf1%d" % par, "psr2"], "fout")
    p0 = TS - CH
    for dt in range(ND):
        xsl = xs[:, dt * (1 + TS) + 1 + p0: dt * (1 + TS) + 1 + p0 + CH]
        e_tt(em, xsl, xsl, fout[:, ts(dt, CH)], ALU.add, ["xs", "fout"], "xs")

    qi = 0
    for k in range(ND):
        for o in range(0, T_OWN, 512):
            em.dma(QKQ[qi % 3], xout_d[ts(k, 128), o:o + 512],
                   xs[:, k * (1 + TS) + 1 + 512 + o:
                      k * (1 + TS) + 1 + 512 + o + 512],
                   reads=["xs"], writes=["xo"])
            qi += 1

    em.finalize()
    return nc


def _rotary_tables():
    half = np.arange(0, DH, 2, dtype=np.float64)
    inv_freq = 1.0 / 10000.0 ** (half / DH)
    base_scale = (half + 0.4 * DH) / (1.4 * DH)

    def tables(positions):
        t = positions.astype(np.float64)
        freqs = t[:, None] * inv_freq[None, :]
        power = (t - L // 2) / (256 // 2)
        scale = base_scale[None, :] ** power[:, None]
        cos = np.cos(freqs)
        sin = np.sin(freqs)
        cs = (cos * scale).T
        ss = (sin * scale).T
        inv = 1.0 / scale
        ck = (cos * inv).T
        sk = (sin * inv).T
        def stack(c, s):
            blk = np.concatenate([c, c, c, c], axis=0)
            sgn = np.concatenate([-s, s, -s, s], axis=0)
            return np.float32(blk), np.float32(sgn)
        csq, ssnq = stack(cs, ss)
        csk2, ssnk2 = stack(ck, sk)
        return (np.ascontiguousarray(csq), np.ascontiguousarray(ssnq),
                np.ascontiguousarray(csk2), np.ascontiguousarray(ssnk2))

    return tables


_FFN_NC = None
_L0_NC = None
_MAMBA_NC = None
_COMB_NC = None


def _rperm128():
    r = np.zeros((128, 128), np.float32)
    for blk in range(2):
        b0 = blk * 64
        for i in range(32):
            r[b0 + 32 + i, b0 + i] = 1.0
            r[b0 + i, b0 + 32 + i] = 1.0
    return r


def ffn_in_maps(inputs, layer, xcur):
    NI = INNER // 128
    NJC = FFI // 128
    wqkv = np.ascontiguousarray(np.asarray(inputs["L1_ffn_Wqkv"])[layer])
    wo = np.ascontiguousarray(np.asarray(inputs["L1_ffn_Wo"])[layer])
    w1 = np.asarray(inputs["L1_ffn_W1"])[layer]
    w2 = np.asarray(inputs["L1_ffn_W2"])[layer]
    b1 = np.asarray(inputs["L1_ffn_b1"])[layer]
    b2 = np.asarray(inputs["L1_ffn_b2"])[layer]
    lng = np.asarray(inputs["L1_ffn_ln_g"])[layer]
    lnb = np.asarray(inputs["L1_ffn_ln_b"])[layer]
    qs = np.asarray(inputs["L1_ffn_q_scale"])[layer]
    ks = np.asarray(inputs["L1_ffn_k_scale"])[layer]
    FF0 = 2730
    ND = D // 128
    w1a = np.zeros((D, FFI), np.float32); w1a[:, :FF0] = w1[:, :FF0]
    w1g = np.zeros((D, FFI), np.float32); w1g[:, :FF0] = w1[:, FF0:]
    w2p = np.zeros((FFI, D), np.float32); w2p[:FF0] = w2
    b1a = np.zeros(FFI, np.float32); b1a[:FF0] = b1[:FF0]
    b1g = np.zeros(FFI, np.float32); b1g[:FF0] = b1[FF0:]
    def tilemaj(wmat, nr, nccols):
        return np.ascontiguousarray(
            np.float32(wmat).reshape(nr, 128, nccols, 128)
            .transpose(1, 2, 0, 3).reshape(128, -1))
    wqk_t = tilemaj(wqkv[:, :2 * INNER], ND, 2 * NI)
    wv_t = np.ascontiguousarray(
        np.float32(wqkv[:, 2 * INNER:]).reshape(ND, 128, 2, 512)
        .transpose(1, 2, 0, 3).reshape(128, -1))
    wo_t = tilemaj(wo, NI, ND)
    w1a_t = tilemaj(w1a, ND, NJC)
    w1g_t = tilemaj(w1g, ND, NJC)
    w2_t = tilemaj(w2p, NJC, ND)
    blk16 = np.zeros((128, NI * 16), np.float32)
    for jt in range(NI):
        blk16[0:64, jt * 16 + 2 * jt] = 1.0
        blk16[64:128, jt * 16 + 2 * jt + 1] = 1.0
    esel2 = np.zeros((16, NI * 128), np.float32)
    for jt in range(NI):
        esel2[2 * jt, jt * 128: jt * 128 + 64] = 1.0
        esel2[2 * jt + 1, jt * 128 + 64: (jt + 1) * 128] = 1.0
    band = np.full((128, 384), -1.0e38, np.float32)
    for i in range(128):
        band[i, i:i + 257] = 0.0
    tables = _rotary_tables()
    common = {
        "wqk_t": wqk_t, "wv_t": wv_t, "wo_t": wo_t,
        "w1a_t": w1a_t, "w1g_t": w1g_t, "w2_t": w2_t,
        "b1a": np.ascontiguousarray(b1a.reshape(NJC, 128).T),
        "b1g": np.ascontiguousarray(b1g.reshape(NJC, 128).T),
        "b2c": np.ascontiguousarray(b2.reshape(8, 128).T),
        "lng": np.ascontiguousarray(lng.reshape(8, 128).T),
        "lnb": np.ascontiguousarray(lnb.reshape(8, 128).T),
        "qs": np.ascontiguousarray(np.tile(qs, 2)[:, None]),
        "ks": np.ascontiguousarray(np.tile(ks, 2)[:, None]),
        "band": band, "blk16": blk16, "esel2": esel2,
        "iden": np.eye(128, dtype=np.float32),
        "rperm": _rperm128(),
    }
    maps = []
    for core in range(8):
        b, half = core // 2, core % 2
        start = half * T_OWN
        xb = xcur[b]
        xst = np.zeros((1 + TS, D), np.float32)
        lo = start - 512
        src_lo = max(lo, 0)
        xst[1 + (src_lo - lo): 1 + TS] = xb[src_lo: start + T_OWN]
        if lo - 1 >= 0:
            xst[0] = xb[lo - 1]
        pos = np.arange(lo, start + T_OWN)
        cs, ssn, csk, ssnk = tables(pos)
        flag = np.zeros((NW, 2, 384), np.float32)
        noprev = {0} | ({2} if half == 0 else set())
        for w in range(NW):
            if w in noprev:
                for qc in range(2):
                    n = 256 - qc * 128
                    flag[w, qc, 0:n] = -1.0e38
        cmask = np.ones((1, TS), np.float32)
        if half == 0:
            cmask[0, 0:512] = 0.0
        m = dict(common)
        m["xst"] = np.ascontiguousarray(xst.T)
        m["cs_t"] = cs
        m["ssn_t"] = ssn
        m["csk_t"] = csk
        m["ssnk_t"] = ssnk
        m["flagrow"] = flag.reshape(1, -1)
        m["cmask"] = cmask
        maps.append(m)
    return maps


def run_ffn_layer(inputs, layer, xcur):
    global _FFN_NC
    if _FFN_NC is None:
        _FFN_NC = build_ffn()
    maps = ffn_in_maps(inputs, layer, xcur)
    res = run_bass_kernel_spmd(_FFN_NC, maps, core_ids=list(range(8)))
    xnew = np.zeros((B, L, D), np.float32)
    for core in range(8):
        b, half = core // 2, core % 2
        xnew[b, half * T_OWN:(half + 1) * T_OWN] = res.results[core]["xout"].T
    return xnew


def build_combine():
    nc = bass.Bass("TRN2", target_bir_lowering=False)
    ND = D // 128
    x1_d = nc.dram_tensor("x1", [D, T_OWN], F32, kind="ExternalInput").ap()
    xn1_d = nc.dram_tensor("xn1", [D, T_OWN], F32, kind="ExternalInput").ap()
    pf_d = nc.dram_tensor("pf", [D, T_OWN], F32, kind="ExternalInput").ap()
    pb_d = nc.dram_tensor("pb", [D, T_OWN], F32, kind="ExternalInput").ap()
    g_d = nc.dram_tensor("g", [D, T_OWN], F32, kind="ExternalInput").ap()
    ob_d = nc.dram_tensor("obc", [128, ND], F32, kind="ExternalInput").ap()
    x2_d = nc.dram_tensor("x2", [D, T_OWN], F32, kind="ExternalOutput").ap()
    em = EM(nc)
    a_s = sbuf(nc, "a_s", [128, 2 * 512])
    p_s = sbuf(nc, "p_s", [128, 2 * 512])
    n_s = sbuf(nc, "n_s", [128, 2 * 512])
    g_s = sbuf(nc, "g_s", [128, 2 * 512])
    x_s = sbuf(nc, "x_s", [128, 2 * 512])
    c_s = sbuf(nc, "c_s", [128, 2 * 512])
    ob_c = sbuf(nc, "ob_c", [128, ND])
    em.dma("sync", ob_c[:, :], ob_d[:, :], writes=["ob"])
    qi = 0
    for k in range(ND):
        for o in range(0, T_OWN, 512):
            sl = (ts(k, 128), slice(o, o + 512))
            p = qi % 2
            qs3 = ("sync", "scalar", "gpsimd")
            aa = a_s[:, ts(p, 512)]
            em.dma(qs3[qi % 3], aa, pf_d[sl[0], sl[1]], writes=["a%d" % p])
            em.dma(qs3[(qi + 1) % 3], p_s[:, ts(p, 512)], pb_d[sl[0], sl[1]],
                   writes=["p%d" % p])
            em.dma(qs3[(qi + 2) % 3], n_s[:, ts(p, 512)], xn1_d[sl[0], sl[1]],
                   writes=["n%d" % p])
            em.dma(qs3[qi % 3], g_s[:, ts(p, 512)], g_d[sl[0], sl[1]],
                   writes=["g%d" % p])
            em.dma(qs3[(qi + 1) % 3], x_s[:, ts(p, 512)], x1_d[sl[0], sl[1]],
                   writes=["x%d" % p])
            e_tt(em, aa, aa, p_s[:, ts(p, 512)], ALU.add,
                 ["a%d" % p, "p%d" % p], "a%d" % p)
            e_tt(em, aa, aa, n_s[:, ts(p, 512)], ALU.add,
                 ["a%d" % p, "n%d" % p], "a%d" % p)
            e_ts(em, aa, aa, ob_c[:, k:k + 1], None, ALU.add,
                 None, ["a%d" % p, "ob"], "a%d" % p)
            e_act(em, c_s[:, ts(p, 512)], g_s[:, ts(p, 512)], AF.Silu,
                  ["g%d" % p], "c%d" % p)
            e_tt(em, aa, aa, c_s[:, ts(p, 512)], ALU.mult,
                 ["a%d" % p, "c%d" % p], "a%d" % p)
            e_tt(em, aa, aa, x_s[:, ts(p, 512)], ALU.add,
                 ["a%d" % p, "x%d" % p], "a%d" % p)
            em.dma(qs3[(qi + 2) % 3], x2_d[sl[0], sl[1]], aa, reads=["a%d" % p],
                   writes=["out"])
            qi += 1
    em.finalize()
    return nc


def run_combine(inputs, x1, xn1, pf, pb, g):
    global _COMB_NC
    if _COMB_NC is None:
        _COMB_NC = build_combine()
    ob = np.asarray(inputs["L1_attn_out_b"])
    obc = np.ascontiguousarray(ob.reshape(8, 128).T)
    maps = []
    for core in range(8):
        b, half = core // 2, core % 2
        sl = slice(half * T_OWN, (half + 1) * T_OWN)
        maps.append({
            "x1": np.ascontiguousarray(x1[b, sl].T),
            "xn1": np.ascontiguousarray(xn1[b, sl].T),
            "pf": np.ascontiguousarray(pf[b, sl].T),
            "pb": np.ascontiguousarray(pb[b, sl].T),
            "g": np.ascontiguousarray(g[b, sl].T),
            "obc": obc,
        })
    res = run_bass_kernel_spmd(_COMB_NC, maps, core_ids=list(range(8)))
    x2 = np.zeros((B, L, D), np.float32)
    for core in range(8):
        b, half = core // 2, core % 2
        x2[b, half * T_OWN:(half + 1) * T_OWN] = res.results[core]["x2"].T
    return x2


def kernel(**inputs):
    x1, xn1 = run_l0(inputs)
    pf, pb = run_mamba(inputs, xn1)
    xcur = xn1
    for layer in range(6):
        xcur = run_ffn_layer(inputs, layer, xcur)
    x2 = run_combine(inputs, x1, xn1, pf, pb, xcur)
    return x2


# version 07
# speedup vs baseline: 1.1902x; 1.0019x over previous
import numpy as np
import concourse.bass as bass
import concourse.mybir as mybir
from concourse.bass_utils import run_bass_kernel_spmd

F32 = mybir.dt.float32
BF16 = mybir.dt.bfloat16
AF = mybir.ActivationFunctionType
ALU = mybir.AluOpType
AX = mybir.AxisListType

D = 1024
B = 4
L = 4096
T_OWN = 2048

ENGINES = ("sync", "tensor", "vector", "scalar", "gpsimd")


class EM:

    NGS = 8

    def __init__(self, nc):
        self.nc = nc
        self.ops = {e: [] for e in ENGINES}
        self.count = {}
        self.last_w = {}
        self.readers = {}
        self.sems = {}
        self._exits = []
        self.q_groups = {}
        self.q_open = {}
        self.q_semcum = {}
        self.q_rot = {}

    def sem(self, st):
        if st not in self.sems:
            cm = self.nc.semaphore("sem_" + st.replace(":", "_"))
            self.sems[st] = cm.__enter__()
            self._exits.append(cm)
        return self.sems[st]

    def _dma_group_sem(self, st):
        if st not in self.q_open:
            rot = self.q_rot.get(st, 0)
            name = "%s:%d" % (st, rot % self.NGS)
            self.q_rot[st] = rot + 1
            self.q_open[st] = name
        return self.q_open[st]

    def _close_group(self, st, upto_seq):
        name = self.q_open.pop(st)
        target = self.q_semcum.get(name, 0)
        self.q_groups.setdefault(st, []).append((upto_seq, name, target))

    def _dep_wait(self, waits, my_stream, st2, c2):
        if st2.startswith("d:"):
            groups = self.q_groups.get(st2, [])
            grp = None
            for last_seq, name, target in groups:
                if c2 <= last_seq:
                    grp = (name, target)
                    break
            if grp is None:
                self._close_group(st2, self.count.get(st2, 0))
                last_seq, name, target = self.q_groups[st2][-1]
                grp = (name, target)
            name, target = grp
            if my_stream == st2:
                return
            waits[name] = max(waits.get(name, 0), target)
        else:
            waits[st2] = max(waits.get(st2, 0), c2)

    def emit(self, eng, fn, reads=(), writes=(), dma=False):
        my_stream = ("d:" if dma else "c:") + eng
        waits = {}
        for k in reads:
            lw = self.last_w.get(k)
            if lw:
                self._dep_wait(waits, my_stream, *lw)
        for k in writes:
            lw = self.last_w.get(k)
            if lw:
                self._dep_wait(waits, my_stream, *lw)
            for st2, c2 in self.readers.get(k, {}).items():
                self._dep_wait(waits, my_stream, st2, c2)
        self.count[my_stream] = self.count.get(my_stream, 0) + 1
        cnt = self.count[my_stream]
        gsem = None
        if dma:
            gsem = self._dma_group_sem(my_stream)
            self.q_semcum[gsem] = self.q_semcum.get(gsem, 0) + 16
        for k in writes:
            self.last_w[k] = (my_stream, cnt)
            self.readers[k] = {}
        for k in reads:
            self.readers.setdefault(k, {})[my_stream] = cnt
        self.ops[eng].append((waits, fn, gsem))

    def dma(self, eng, out, in_, reads=(), writes=(), **kw):
        self.emit(eng, lambda q: q.dma_start(out=out, in_=in_, **kw),
                  reads=reads, writes=writes, dma=True)

    def finalize(self):
        nc = self.nc
        with nc.Block() as block:
            for eng_name in ENGINES:
                ops = self.ops[eng_name]
                if not ops:
                    continue

                def make(eng_name=eng_name, ops=ops):
                    def body(q):
                        synced = {}
                        cs = "c:" + eng_name
                        nc_cnt = 0
                        for waits, fn, gsem in ops:
                            for st2, v2 in sorted(waits.items()):
                                if v2 > synced.get(st2, 0):
                                    q.wait_ge(self.sem(st2), v2)
                                    synced[st2] = v2
                            instr = fn(q)
                            if gsem is not None:
                                instr.then_inc(self.sem(gsem), 16)
                            else:
                                nc_cnt += 1
                                instr.then_inc(self.sem(cs), 1)
                        ds = "d:" + eng_name
                        for name, cum in sorted(self.q_semcum.items()):
                            if name.startswith(ds):
                                if cum > synced.get(name, 0):
                                    q.wait_ge(self.sem(name), cum)
                        if nc_cnt:
                            q.wait_ge(self.sem(cs), nc_cnt)
                    return body

                getattr(block, eng_name)(make())


def sbuf(nc, name, shape, dtype=F32):
    return nc.sbuf_tensor(name, shape, dtype).__enter__()


def psum_t(nc, name, shape, dtype=F32):
    return nc.psum_tensor(name, shape, dtype).__enter__()


def ts(i, n):
    return slice(i * n, (i + 1) * n)


def e_mm(em, out, lhsT, rhs, reads, write, start, stop):
    em.emit("tensor",
            lambda t: t.matmul(out=out, lhsT=lhsT, rhs=rhs, start=start,
                               stop=stop),
            reads=reads, writes=[write])


def e_bcast(em, ps, ones_s, row_ap, read, write, nrows=128):
    em.emit("tensor",
            lambda t: t.matmul(out=ps, lhsT=ones_s[0:1, 0:nrows], rhs=row_ap,
                               start=True, stop=True),
            reads=[read, "ones"], writes=[write])


def e_tt(em, out, in0, in1, op, reads, write):
    em.emit("vector",
            lambda v: v.tensor_tensor(out=out, in0=in0, in1=in1, op=op),
            reads=reads, writes=[write])


def e_ts(em, out, in0, s1, s2, op0, op1, reads, write, eng="vector"):
    if s2 is None:
        em.emit(eng,
                lambda v: v.tensor_scalar(out=out, in0=in0, scalar1=s1,
                                          scalar2=None, op0=op0),
                reads=reads, writes=[write])
    else:
        em.emit(eng,
                lambda v: v.tensor_scalar(out=out, in0=in0, scalar1=s1,
                                          scalar2=s2, op0=op0, op1=op1),
                reads=reads, writes=[write])


def e_stt(em, out, in0, scalar, in1, op0, op1, reads, write):
    em.emit("vector",
            lambda v: v.scalar_tensor_tensor(out=out, in0=in0, scalar=scalar,
                                             in1=in1, op0=op0, op1=op1),
            reads=reads, writes=[write])


def e_act(em, out, in_, func, reads, write, bias=None, scale=None, eng="scalar"):
    kw = {}
    if bias is not None:
        kw["bias"] = bias
    if scale is not None:
        kw["scale"] = scale
    em.emit(eng,
            lambda s: s.activation(out=out, in_=in_, func=func, **kw),
            reads=reads, writes=[write])


def e_copy(em, out, in_, reads, write, eng="scalar"):
    if eng == "scalar":
        em.emit(eng, lambda s: s.copy(out=out, in_=in_), reads=reads,
                writes=[write])
    else:
        em.emit(eng, lambda v: v.tensor_copy(out=out, in_=in_), reads=reads,
                writes=[write])


def e_red(em, out, in_, op, reads, write, negate=False):
    em.emit("vector",
            lambda v: v.tensor_reduce(out=out, in_=in_, axis=AX.X, op=op,
                                      negate=negate),
            reads=reads, writes=[write])


def ln_rows(em, st_ap, st2_ap, mu_ap, rs_ap, tmp_ap, eps, dimn,
            k_st, k_st2, k_mu, k_rs, k_tmp):
    e_ts(em, mu_ap, st_ap, 1.0 / dimn, None, ALU.mult, None, [k_st], k_mu)
    e_ts(em, tmp_ap, st2_ap, 1.0 / dimn, None, ALU.mult, None, [k_st2], k_tmp)
    e_tt(em, rs_ap, mu_ap, mu_ap, ALU.mult, [k_mu], k_rs)
    e_tt(em, tmp_ap, tmp_ap, rs_ap, ALU.subtract, [k_tmp, k_rs], k_tmp)
    e_ts(em, tmp_ap, tmp_ap, eps, None, ALU.add, None, [k_tmp], k_tmp)
    em.emit("scalar", lambda s: s.activation(out=tmp_ap, in_=tmp_ap, func=AF.Sqrt),
            reads=[k_tmp], writes=[k_tmp])
    em.emit("vector", lambda v: v.reciprocal(out=rs_ap, in_=tmp_ap),
            reads=[k_tmp], writes=[k_rs])


def build_l0():
    nc = bass.Bass("TRN2", target_bir_lowering=False)
    H, DX, QK = 8, 64, 512
    ND = D // 128
    NCH = L // 512
    OCH = T_OWN // 512

    xf = nc.dram_tensor("xf", [D, L], F32, kind="ExternalInput").ap()
    w_qkv = nc.dram_tensor("w_qkv", [D, 3 * QK], F32, kind="ExternalInput").ap()
    bqkv_row_d = nc.dram_tensor("bqkv_row", [1, 3 * QK], F32, kind="ExternalInput").ap()
    bqkv_col_d = nc.dram_tensor("bqkv_col", [128, 12], F32, kind="ExternalInput").ap()
    w_fc = nc.dram_tensor("w_fc", [QK, D], F32, kind="ExternalInput").ap()
    bfc_col_d = nc.dram_tensor("bfc_col", [128, ND], F32, kind="ExternalInput").ap()
    gam_col_d = nc.dram_tensor("gam_col", [128, ND], F32, kind="ExternalInput").ap()
    bet_col_d = nc.dram_tensor("bet_col", [128, ND], F32, kind="ExternalInput").ap()
    gate_wT = nc.dram_tensor("gate_wT", [D, 8], F32, kind="ExternalInput").ap()
    ew0T = nc.dram_tensor("ew0T", [D, D], F32, kind="ExternalInput").ap()
    ew1T = nc.dram_tensor("ew1T", [D, D], F32, kind="ExternalInput").ap()
    out_wT = nc.dram_tensor("out_wT", [D, D], F32, kind="ExternalInput").ap()
    iden_d = nc.dram_tensor("iden", [128, 128], F32, kind="ExternalInput").ap()
    x1_d = nc.dram_tensor("x1", [D, T_OWN], F32, kind="ExternalOutput").ap()
    xn1_d = nc.dram_tensor("xn1", [D, T_OWN], F32, kind="ExternalOutput").ap()

    em = EM(nc)

    wqkv_s = sbuf(nc, "wqkv_s", [128, ND * 3 * QK], BF16)
    wfc_s = sbuf(nc, "wfc_s", [128, 4 * D], BF16)
    gw_s = sbuf(nc, "gw_s", [128, ND * 8], BF16)
    ew0_s = sbuf(nc, "ew0_s", [128, ND * D], BF16)
    ew1_s = sbuf(nc, "ew1_s", [128, ND * D], BF16)
    ow_s = sbuf(nc, "ow_s", [128, ND * D], BF16)
    ones_s = sbuf(nc, "ones_s", [128, 128])
    eye_s = sbuf(nc, "eye_s", [128, 128])
    wtmp = sbuf(nc, "wtmp", [128, 512])
    bqkv_row = sbuf(nc, "bqkv_row_s", [1, 3 * QK])
    bqkvL_row = sbuf(nc, "bqkvL_row", [1, 3 * QK])
    bqkv_col = sbuf(nc, "bqkv_col_s", [128, 12])
    bfc_col = sbuf(nc, "bfc_col_s", [128, ND])
    gam_col = sbuf(nc, "gam_col_s", [128, ND])
    bet_col = sbuf(nc, "bet_col_s", [128, ND])

    xn0b = sbuf(nc, "xn0b", [128, ND * T_OWN], BF16)
    xnbt = sbuf(nc, "xnbt", [128, ND * 512], BF16)
    vfm = sbuf(nc, "vfm", [128, 4 * 512], BF16)
    s0 = sbuf(nc, "s0", [128, ND * 512])
    mu_r = sbuf(nc, "mu_r", [1, 512])
    rs_r = sbuf(nc, "rs_r", [1, 512])
    snx_c = sbuf(nc, "snx_c", [128, ND])
    snx_b = sbuf(nc, "snx_b", [128, ND], BF16)
    red_c = sbuf(nc, "red_c", [128, 1])
    sq_row = sbuf(nc, "sq_row", [1, QK])
    sk_row = sbuf(nc, "sk_row", [1, QK])
    scores = sbuf(nc, "scores", [128, 4 * DX])
    attn = sbuf(nc, "attn", [128, 4 * DX], BF16)
    qh_s = sbuf(nc, "qh_s", [128, QK], BF16)
    kh_s = sbuf(nc, "kh_s", [128, QK], BF16)
    xt_s = sbuf(nc, "xt_s", [128, 2 * 512])
    sq_s = sbuf(nc, "sq_s", [128, 512])
    tmp_a = sbuf(nc, "tmp_a", [128, 512])
    tmp_b = sbuf(nc, "tmp_b", [128, 512])
    rowtmp = sbuf(nc, "rowtmp", [1, 512])
    mu2_r = sbuf(nc, "mu2_r", [1, 512])
    rs2_r = sbuf(nc, "rs2_r", [1, 512])
    col_s = sbuf(nc, "col_s", [128, 16])
    pcols = sbuf(nc, "pcols", [128, 8])
    pv_s = sbuf(nc, "pv_s", [128, 4 * 512], BF16)
    h0_s = sbuf(nc, "h0_s", [128, 512], BF16)
    h1_s = sbuf(nc, "h1_s", [128, 512], BF16)
    gpre = sbuf(nc, "gpre", [128, ND * 512], BF16)
    ge_s = sbuf(nc, "ge_s", [128, 512])
    w01_r = sbuf(nc, "w01_r", [1, 2 * 512])
    x1c = sbuf(nc, "x1c", [128, 512])

    ps_a = psum_t(nc, "ps_a", [128, 512])
    ps_b = psum_t(nc, "ps_b", [128, 512])
    ps_c = psum_t(nc, "ps_c", [128, 512])
    ps_r = psum_t(nc, "ps_r", [128, 512])
    ps_r2 = psum_t(nc, "ps_r2", [128, 512])
    ps_sc = psum_t(nc, "ps_sc", [128, DX])
    ps_st = psum_t(nc, "ps_st", [1, 512])
    ps_st2 = psum_t(nc, "ps_st2", [1, 512])

    em.emit("gpsimd", lambda g: g.memset(ones_s[:, :], 1.0), writes=["ones"])
    em.dma("sync", eye_s[:, :], iden_d[:, :], writes=["eye"])
    em.dma("sync", bqkv_row[:, :], bqkv_row_d[:, :], writes=["bqr"])
    em.dma("sync", bqkv_col[:, :], bqkv_col_d[:, :], writes=["bqc"])
    em.dma("sync", bfc_col[:, :], bfc_col_d[:, :], writes=["bfc"])
    em.dma("sync", gam_col[:, :], gam_col_d[:, :], writes=["gam"])
    em.dma("sync", bet_col[:, :], bet_col_d[:, :], writes=["bet"])
    e_ts(em, bqkvL_row[:, :], bqkv_row[:, :], float(L), None, ALU.mult, None,
         ["bqr"], "bqL")
    em.emit("vector", lambda v: v.memset(scores[:, :], 0.0), writes=["scoresb"])

    def load_cast(dst_ap, src_ap, cols, tag):
        for o in range(0, cols, 512):
            w = min(512, cols - o)
            em.dma("sync", wtmp[:, 0:w], src_ap[:, o:o + w], writes=["wtmp"])
            e_copy(em, dst_ap[:, o:o + w], wtmp[:, 0:w], ["wtmp"], tag)

    for k in range(ND):
        load_cast(wqkv_s[:, ts(k, 3 * QK)], w_qkv[ts(k, 128), :], 3 * QK, "wqkv")
    for k in range(4):
        load_cast(wfc_s[:, ts(k, D)], w_fc[ts(k, 128), :], D, "wfc")
    for k in range(ND):
        load_cast(gw_s[:, ts(k, 8)], gate_wT[ts(k, 128), :], 8, "gw")
    for k in range(ND):
        load_cast(ew0_s[:, ts(k, D)], ew0T[ts(k, 128), :], D, "ew0")
    for k in range(ND):
        load_cast(ew1_s[:, ts(k, D)], ew1T[ts(k, 128), :], D, "ew1")
    for k in range(ND):
        load_cast(ow_s[:, ts(k, D)], out_wT[ts(k, 128), :], D, "ow")

    for c in range(NCH):
        own = c < OCH
        cs = slice(c * 512, (c + 1) * 512)
        rsl = slice(0, 512)
        for k in range(ND):
            em.dma("sync", xt_s[:, ts(k % 2, 512)], xf[ts(k, 128), cs],
                   writes=["xt%d" % (k % 2)])
            e_mm(em, ps_st[:, :], ones_s[:, 0:1], xt_s[:, ts(k % 2, 512)],
                 ["xt%d" % (k % 2), "ones"], "st", start=(k == 0), stop=(k == ND - 1))
            e_act(em, sq_s[:, :], xt_s[:, ts(k % 2, 512)], AF.Square,
                  ["xt%d" % (k % 2)], "sq")
            e_mm(em, ps_st2[:, :], ones_s[:, 0:1], sq_s[:, :],
                 ["sq", "ones"], "st2", start=(k == 0), stop=(k == ND - 1))
        ln_rows(em, ps_st[:, :], ps_st2[:, :], mu_r[:, rsl], rs_r[:, rsl],
                rowtmp[:, :], 1e-5, D, "st", "st2", "mu", "rs", "rt")
        e_bcast(em, ps_r[:, :], ones_s, mu_r[:, rsl], "mu", "psr")
        e_bcast(em, ps_r2[:, :], ones_s, rs_r[:, rsl], "rs", "psr2")
        for k in range(ND):
            em.dma("sync", xt_s[:, ts(k % 2, 512)], xf[ts(k, 128), cs],
                   writes=["xt%d" % (k % 2)])
            xnap = (xn0b[:, k * T_OWN + c * 512: k * T_OWN + (c + 1) * 512]
                    if own else xnbt[:, ts(k, 512)])
            xnkey = ("xn0b_%d" % k) if own else ("xnbt_%d" % k)
            e_tt(em, tmp_a[:, :], xt_s[:, ts(k % 2, 512)], ps_r[:, :], ALU.subtract,
                 ["xt%d" % (k % 2), "psr"], "tmpa")
            e_tt(em, xnap, tmp_a[:, :], ps_r2[:, :], ALU.mult,
                 ["tmpa", "psr2"], xnkey)
            e_red(em, red_c[:, 0:1], xnap, ALU.add, [xnkey], "redc")
            if c == 0:
                e_copy(em, snx_c[:, k:k + 1], red_c[:, 0:1], ["redc"],
                       "snx%d" % k, eng="vector")
            else:
                e_tt(em, snx_c[:, k:k + 1], snx_c[:, k:k + 1], red_c[:, 0:1],
                     ALU.add, ["snx%d" % k, "redc"], "snx%d" % k)
        for tt in range(4):
            gtok = c * 4 + tt
            for qk_i, dst, kdst in ((0, qh_s, "qh"), (1, kh_s, "kh")):
                for k in range(ND):
                    lhs_ap = (xn0b[:, k * T_OWN + c * 512 + tt * 128:
                                   k * T_OWN + c * 512 + (tt + 1) * 128]
                              if own else
                              xnbt[:, k * 512 + tt * 128: k * 512 + (tt + 1) * 128])
                    xnkey = ("xn0b_%d" % k) if own else ("xnbt_%d" % k)
                    e_mm(em, ps_a[:, :], lhs_ap,
                         wqkv_s[:, k * 3 * QK + qk_i * QK:
                                k * 3 * QK + (qk_i + 1) * QK],
                         [xnkey, "wqkv"], "psa", start=(k == 0),
                         stop=(k == ND - 1))
                e_copy(em, dst[:, :], ps_a[:, :], ["psa"], kdst)
            for hp in range(4):
                for sub in range(2):
                    h = 2 * hp + sub
                    e_mm(em, ps_sc[sub * 64:sub * 64 + 64, :],
                         qh_s[:, ts(h, DX)], kh_s[:, ts(h, DX)],
                         ["qh", "kh"], "psc", start=True, stop=True)
                e_tt(em, scores[:, ts(hp, DX)], scores[:, ts(hp, DX)],
                     ps_sc[:, :], ALU.add, ["scoresb", "psc"], "scoresb")
    e_copy(em, snx_b[:, :], snx_c[:, :],
           ["snx%d" % k2 for k2 in range(ND)], "snxb")
    for qk_i, dst, kdst in ((0, sq_row, "sqrow"), (1, sk_row, "skrow")):
        for k in range(ND):
            e_mm(em, ps_st[:, :], snx_b[:, k:k + 1],
                 wqkv_s[:, k * 3 * QK + qk_i * QK: k * 3 * QK + (qk_i + 1) * QK],
                 ["snxb", "wqkv"], "st", start=(k == 0), stop=(k == ND - 1))
        e_copy(em, dst[:, :], ps_st[:, :], ["st"], kdst)
    for hp in range(4):
        for sub in range(2):
            h = 2 * hp + sub
            out_ap = ps_sc[sub * 64:sub * 64 + 64, :]
            e_mm(em, out_ap, bqkv_row[:, ts(h, DX)], sk_row[:, ts(h, DX)],
                 ["bqr", "skrow"], "psc", start=True, stop=False)
            e_mm(em, out_ap, sq_row[:, ts(h, DX)], bqkv_row[:, QK + h * DX:
                                                            QK + (h + 1) * DX],
                 ["sqrow", "bqr"], "psc", start=False, stop=False)
            e_mm(em, out_ap, bqkvL_row[:, ts(h, DX)], bqkv_row[:, QK + h * DX:
                                                               QK + (h + 1) * DX],
                 ["bqL", "bqr"], "psc", start=False, stop=True)
        e_tt(em, scores[:, ts(hp, DX)], scores[:, ts(hp, DX)], ps_sc[:, :],
             ALU.add, ["scoresb", "psc"], "scoresb")


    for hp in range(4):
        sl = ts(hp, DX)
        e_red(em, col_s[:, hp:hp + 1], scores[:, sl], ALU.max, ["scoresb"],
              "cols")
        e_ts(em, col_s[:, 4 + hp:5 + hp], col_s[:, hp:hp + 1], -0.125, None,
             ALU.mult, None, ["cols"], "cols")
        em.emit("scalar",
                lambda s, hp=hp, sl=sl: s.activation(
                    out=scores[:, sl], in_=scores[:, sl], func=AF.Exp,
                    bias=col_s[:, 4 + hp:5 + hp], scale=0.125),
                reads=["scoresb", "cols"], writes=["scoresb"])
        e_red(em, col_s[:, 8 + hp:9 + hp], scores[:, sl], ALU.add,
              ["scoresb"], "cols")
        em.emit("vector",
                lambda v, hp=hp: v.reciprocal(out=col_s[:, 12 + hp:13 + hp],
                                              in_=col_s[:, 8 + hp:9 + hp]),
                reads=["cols"], writes=["cols"])
        e_ts(em, attn[:, sl], scores[:, sl], col_s[:, 12 + hp:13 + hp], None,
             ALU.mult, None, ["scoresb", "cols"], "attn")


    for c in range(OCH):
        for k in range(ND):
            em.dma("sync", xt_s[:, ts(k % 2, 512)], xf[ts(k, 128), ts(c, 512)],
                   writes=["xt%d" % (k % 2)])
            e_mm(em, ps_st[:, :], ones_s[:, 0:1], xt_s[:, ts(k % 2, 512)],
                 ["xt%d" % (k % 2), "ones"], "st", start=(k == 0),
                 stop=(k == ND - 1))
            e_act(em, sq_s[:, :], xt_s[:, ts(k % 2, 512)], AF.Square,
                  ["xt%d" % (k % 2)], "sq")
            e_mm(em, ps_st2[:, :], ones_s[:, 0:1], sq_s[:, :], ["sq", "ones"],
                 "st2", start=(k == 0), stop=(k == ND - 1))
        ln_rows(em, ps_st[:, :], ps_st2[:, :], mu_r[:, :], rs_r[:, :],
                rowtmp[:, :], 1e-5, D, "st", "st2", "mu", "rs", "rt")
        for jt in range(4):
            for k in range(ND):
                e_mm(em, ps_b[:, :],
                     wqkv_s[:, k * 3 * QK + 2 * QK + jt * 128:
                            k * 3 * QK + 2 * QK + (jt + 1) * 128],
                     xn0b[:, k * T_OWN + c * 512: k * T_OWN + (c + 1) * 512],
                     ["wqkv", "xn0b_%d" % k], "psb",
                     start=(k == 0), stop=(k == ND - 1))
            e_act(em, vfm[:, ts(jt, 512)], ps_b[:, :], AF.Identity,
                  ["psb", "bqc"], "vfm", bias=bqkv_col[:, 8 + jt:9 + jt])
        for hp in range(4):
            for sub in range(2):
                h = hp * 2 + sub
                e_mm(em, ps_b[sub * 64:sub * 64 + 64, :],
                     attn[(h % 2) * 64:(h % 2) * 64 + 64, ts(h // 2, DX)],
                     vfm[(h % 2) * 64:(h % 2) * 64 + 64, ts(h // 2, 512)],
                     ["attn", "vfm"], "psb", start=True, stop=True)
            e_copy(em, pv_s[:, ts(hp, 512)], ps_b[:, :], ["psb"], "pv")
        for k in range(ND):
            for j in range(4):
                e_mm(em, ps_c[:, :],
                     wfc_s[:, j * D + k * 128: j * D + (k + 1) * 128],
                     pv_s[:, ts(j, 512)], ["wfc", "pv"], "psc2",
                     start=(j == 0), stop=(j == 3))
            e_act(em, tmp_a[:, :], ps_c[:, :], AF.Identity, ["psc2", "bfc"],
                  "tmpa", bias=bfc_col[:, k:k + 1])
            em.dma("sync", xt_s[:, ts(k % 2, 512)], xf[ts(k, 128), ts(c, 512)],
                   writes=["xt%d" % (k % 2)])
            if k == 0:
                e_bcast(em, ps_r[:, :], ones_s, mu_r[:, :], "mu", "psr")
                e_bcast(em, ps_r2[:, :], ones_s, rs_r[:, :], "rs", "psr2")
            e_tt(em, tmp_b[:, :], xt_s[:, ts(k % 2, 512)], ps_r[:, :], ALU.subtract,
                 ["xt%d" % (k % 2), "psr"], "tmpb")
            e_tt(em, tmp_b[:, :], tmp_b[:, :], ps_r2[:, :], ALU.mult,
                 ["tmpb", "psr2"], "tmpb")
            e_tt(em, s0[:, ts(k, 512)],
                 tmp_a[:, :], tmp_b[:, :], ALU.add, ["tmpa", "tmpb"],
                 "s0_%d" % k)
            e_mm(em, ps_st[:, :], ones_s[:, 0:1],
                 s0[:, ts(k, 512)],
                 ["s0_%d" % k, "ones"], "st", start=(k == 0), stop=(k == ND - 1))
            e_act(em, sq_s[:, :],
                  s0[:, ts(k, 512)],
                  AF.Square, ["s0_%d" % k], "sq")
            e_mm(em, ps_st2[:, :], ones_s[:, 0:1], sq_s[:, :], ["sq", "ones"],
                 "st2", start=(k == 0), stop=(k == ND - 1))
        ln_rows(em, ps_st[:, :], ps_st2[:, :], mu2_r[:, :], rs2_r[:, :],
                rowtmp[:, :], 1e-6, D, "st", "st2", "mu2", "rs2", "rt")
        e_bcast(em, ps_r[:, :], ones_s, mu2_r[:, :], "mu2", "psr")
        e_bcast(em, ps_r2[:, :], ones_s, rs2_r[:, :], "rs2", "psr2")
        for k in range(ND):
            sl = ts(k, 512)
            e_tt(em, s0[:, sl], s0[:, sl], ps_r[:, :], ALU.subtract,
                 ["s0_%d" % k, "psr"], "s0_%d" % k)
            e_tt(em, s0[:, sl], s0[:, sl], ps_r2[:, :], ALU.mult,
                 ["s0_%d" % k, "psr2"], "s0_%d" % k)
            e_ts(em, s0[:, sl], s0[:, sl], gam_col[:, k:k + 1],
                 bet_col[:, k:k + 1], ALU.mult, ALU.add,
                 ["s0_%d" % k, "gam", "bet"], "s0_%d" % k)

        for tt in range(4):
            tok0 = c * 512 + tt * 128
            for k in range(ND):
                e_mm(em, ps_a[:, 0:8],
                     xn0b[:, k * T_OWN + tok0: k * T_OWN + tok0 + 128],
                     gw_s[:, ts(k, 8)], ["xn0b_%d" % k, "gw"], "psa",
                     start=(k == 0), stop=(k == ND - 1))
            e_red(em, col_s[:, 0:1], ps_a[:, 0:8], ALU.max, ["psa"], "cols",
                  negate=True)
            em.emit("scalar",
                    lambda s: s.activation(out=pcols[:, 0:8], in_=ps_a[:, 0:8],
                                           func=AF.Exp, bias=col_s[:, 0:1]),
                    reads=["psa", "cols"], writes=["pcols"])
            e_red(em, col_s[:, 1:2], pcols[:, 0:8], ALU.add, ["pcols"], "cols")
            em.emit("vector",
                    lambda v: v.reciprocal(out=col_s[:, 2:3], in_=col_s[:, 1:2]),
                    reads=["cols"], writes=["cols"])
            e_ts(em, pcols[:, 0:8], pcols[:, 0:8], col_s[:, 2:3], None,
                 ALU.mult, None, ["pcols", "cols"], "pcols")
            e_ts(em, col_s[:, 4:12], pcols[:, 0:8], 0.3, None, ALU.is_gt, None,
                 ["pcols"], "cols")
            e_red(em, col_s[:, 12:13], col_s[:, 4:12], ALU.add, ["cols"], "cols")
            e_ts(em, col_s[:, 13:14], col_s[:, 12:13], -1.0, 8.0, ALU.mult,
                 ALU.add, ["cols"], "cols")
            e_tt(em, col_s[:, 14:15], col_s[:, 13:14], pcols[:, 0:1], ALU.mult,
                 ["cols", "pcols"], "cols")
            e_tt(em, col_s[:, 15:16], col_s[:, 12:13], pcols[:, 1:2], ALU.mult,
                 ["cols", "pcols"], "cols")
            em.emit("tensor",
                    lambda t: t.transpose(out=ps_st[:, 0:128],
                                          in_=col_s[:, 14:15],
                                          identity=eye_s[:, :]),
                    reads=["cols", "eye"], writes=["st"])
            e_copy(em, w01_r[:, tt * 128:(tt + 1) * 128], ps_st[:, 0:128], ["st"], "w01")
            em.emit("tensor",
                    lambda t: t.transpose(out=ps_st[:, 0:128],
                                          in_=col_s[:, 15:16],
                                          identity=eye_s[:, :]),
                    reads=["cols", "eye"], writes=["st"])
            e_copy(em, w01_r[:, 512 + tt * 128: 512 + (tt + 1) * 128],
                   ps_st[:, 0:128], ["st"], "w01")
        e_bcast(em, ps_r[:, :], ones_s, w01_r[:, 0:512], "w01", "psr")
        e_bcast(em, ps_r2[:, :], ones_s, w01_r[:, 512:1024], "w01", "psr2")
        for jt in range(ND):
            for k in range(ND):
                e_mm(em, ps_b[:, :], ew0_s[:, k * D + jt * 128:
                                           k * D + (jt + 1) * 128],
                     xn0b[:, k * T_OWN + c * 512: k * T_OWN + (c + 1) * 512],
                     ["ew0", "xn0b_%d" % k], "psb", start=(k == 0),
                     stop=(k == ND - 1))
            e_copy(em, h0_s[:, :], ps_b[:, :], ["psb"], "h0")
            for k in range(ND):
                e_mm(em, ps_c[:, :], ew1_s[:, k * D + jt * 128:
                                           k * D + (jt + 1) * 128],
                     xn0b[:, k * T_OWN + c * 512: k * T_OWN + (c + 1) * 512],
                     ["ew1", "xn0b_%d" % k], "psc2", start=(k == 0),
                     stop=(k == ND - 1))
            e_copy(em, h1_s[:, :], ps_c[:, :], ["psc2"], "h1")
            e_tt(em, tmp_a[:, :], h0_s[:, :], ps_r[:, :], ALU.mult,
                 ["h0", "psr"], "tmpa")
            e_tt(em, tmp_b[:, :], h1_s[:, :], ps_r2[:, :], ALU.mult,
                 ["h1", "psr2"], "tmpb")
            e_tt(em, gpre[:, ts(jt, 512)], tmp_a[:, :], tmp_b[:, :], ALU.add,
                 ["tmpa", "tmpb"], "gpre")
        for k in range(ND):
            for j in range(ND):
                e_mm(em, ps_c[:, :], ow_s[:, j * D + k * 128:
                                          j * D + (k + 1) * 128],
                     gpre[:, ts(j, 512)], ["ow", "gpre"], "psc2",
                     start=(j == 0), stop=(j == ND - 1))
            e_act(em, ge_s[:, :], ps_c[:, :], AF.Silu, ["psc2"], "ge")
            em.dma("sync", xt_s[:, ts(k % 2, 512)], xf[ts(k, 128), ts(c, 512)],
                   writes=["xt%d" % (k % 2)])
            e_tt(em, tmp_a[:, :],
                 s0[:, ts(k, 512)],
                 ge_s[:, :], ALU.mult, ["s0_%d" % k, "ge"], "tmpa")
            e_tt(em, x1c[:, :], xt_s[:, ts(k % 2, 512)], tmp_a[:, :],
                 ALU.add, ["xt%d" % (k % 2), "tmpa"], "x1c")
            em.dma("gpsimd", x1_d[ts(k, 128), ts(c, 512)], x1c[:, :],
                   reads=["x1c"], writes=["x1out"])
            e_mm(em, ps_st[:, :], ones_s[:, 0:1], x1c[:, :],
                 ["x1c", "ones"], "st", start=(k == 0), stop=(k == ND - 1))
            e_act(em, sq_s[:, :], x1c[:, :], AF.Square,
                  ["x1c"], "sq")
            e_mm(em, ps_st2[:, :], ones_s[:, 0:1], sq_s[:, :], ["sq", "ones"],
                 "st2", start=(k == 0), stop=(k == ND - 1))
        ln_rows(em, ps_st[:, :], ps_st2[:, :], mu2_r[:, :], rs2_r[:, :],
                rowtmp[:, :], 1e-5, D, "st", "st2", "mu2", "rs2", "rt")
        e_bcast(em, ps_r[:, :], ones_s, mu2_r[:, :], "mu2", "psr")
        e_bcast(em, ps_r2[:, :], ones_s, rs2_r[:, :], "rs2", "psr2")
        for k in range(ND):
            em.dma("sync", x1c[:, :], x1_d[ts(k, 128), ts(c, 512)],
                   reads=["x1out"], writes=["x1c"])
            e_tt(em, tmp_a[:, :], x1c[:, :], ps_r[:, :], ALU.subtract,
                 ["x1c", "psr"], "tmpa")
            e_tt(em, tmp_a[:, :], tmp_a[:, :], ps_r2[:, :], ALU.mult,
                 ["tmpa", "psr2"], "tmpa")
            em.dma("gpsimd", xn1_d[ts(k, 128), ts(c, 512)], tmp_a[:, :],
                   reads=["tmpa"], writes=["xn1out"])

    em.finalize()
    return nc


def l0_in_maps(inputs):
    x = np.asarray(inputs["L0_attn_x"])
    w_qkv = np.ascontiguousarray(inputs["L0_attn_w_qkv"])
    b_qkv = np.asarray(inputs["L0_attn_b_qkv"])
    w_fc = np.ascontiguousarray(inputs["L0_attn_w_fc"])
    b_fc = np.asarray(inputs["L0_attn_b_fc"])
    gamma = np.asarray(inputs["L0_attn_gamma"])
    beta = np.asarray(inputs["L0_attn_beta"])
    gate_W = np.asarray(inputs["L0_ffn_gate_W"])
    expert_W = np.asarray(inputs["L0_ffn_expert_W"])
    out_W = np.asarray(inputs["L0_ffn_out_W"])

    common = {
        "w_qkv": w_qkv,
        "bqkv_row": b_qkv.reshape(1, -1),
        "bqkv_col": np.ascontiguousarray(b_qkv.reshape(12, 128).T),
        "w_fc": w_fc,
        "bfc_col": np.ascontiguousarray(b_fc.reshape(8, 128).T),
        "gam_col": np.ascontiguousarray(gamma.reshape(8, 128).T),
        "bet_col": np.ascontiguousarray(beta.reshape(8, 128).T),
        "gate_wT": np.ascontiguousarray(gate_W.T),
        "ew0T": np.ascontiguousarray(expert_W[0].T),
        "ew1T": np.ascontiguousarray(expert_W[1].T),
        "out_wT": np.ascontiguousarray(out_W.T),
        "iden": np.eye(128, dtype=np.float32),
    }
    maps = []
    for core in range(8):
        b, half = core // 2, core % 2
        xb = x[b]
        own = xb[half * T_OWN:(half + 1) * T_OWN]
        oth = xb[(1 - half) * T_OWN:(2 - half) * T_OWN]
        xf = np.ascontiguousarray(
            np.concatenate([own, oth], axis=0).T)
        m = dict(common)
        m["xf"] = xf
        maps.append(m)
    return maps


def run_l0(inputs):
    nc = build_l0()
    maps = l0_in_maps(inputs)
    res = run_bass_kernel_spmd(nc, maps, core_ids=list(range(8)))
    x1 = np.zeros((B, L, D), np.float32)
    xn1 = np.zeros((B, L, D), np.float32)
    for core in range(8):
        b, half = core // 2, core % 2
        sl = slice(half * T_OWN, (half + 1) * T_OWN)
        x1[b, sl] = res.results[core]["x1"].T
        xn1[b, sl] = res.results[core]["xn1"].T
    return x1, xn1


D_IN = 2048
DST = 16
TC = 512
NCT = D_IN // 128


def build_mamba():
    nc = bass.Bass("TRN2", target_bir_lowering=False)
    ND = D // 128

    NRT0 = 2 * D_IN // 128
    xn1f = nc.dram_tensor("xn1f", [D, L], F32, kind="ExternalInput").ap()
    in_wT = nc.dram_tensor("in_t", [128, NRT0 * ND * 128], F32,
                           kind="ExternalInput").ap()
    xprojT = nc.dram_tensor("xprojT", [D_IN, 96], F32, kind="ExternalInput").ap()
    dt_wT = nc.dram_tensor("dt_wT", [64, D_IN], F32, kind="ExternalInput").ap()
    conv_c_d = nc.dram_tensor("conv_c", [128, NCT * 4], F32, kind="ExternalInput").ap()
    convb_c_d = nc.dram_tensor("convb_c", [128, NCT], F32, kind="ExternalInput").ap()
    dtb_c_d = nc.dram_tensor("dtb_c", [128, NCT], F32, kind="ExternalInput").ap()
    alog_d = nc.dram_tensor("alog", [128, NCT * DST], F32, kind="ExternalInput").ap()
    dcol_d = nc.dram_tensor("dcol", [128, NCT], F32, kind="ExternalInput").ap()
    mout_wT = nc.dram_tensor("mout_t", [128, ND * NCT * 128], F32,
                             kind="ExternalInput").ap()
    outw_sT = nc.dram_tensor("outw_t", [128, ND * ND * 128], F32,
                             kind="ExternalInput").ap()
    gam_col_d = nc.dram_tensor("gam_col", [128, ND], F32, kind="ExternalInput").ap()
    bet_col_d = nc.dram_tensor("bet_col", [128, ND], F32, kind="ExternalInput").ap()
    esel_d = nc.dram_tensor("esel", [32, 32 * 128], F32, kind="ExternalInput").ap()
    part_d = nc.dram_tensor("partial", [D, L], F32, kind="ExternalOutput").ap()
    inw_b_d = nc.dram_tensor("inw_b", [128, NRT0 * ND * 128], BF16).ap()
    moutw_b_d = nc.dram_tensor("moutw_b", [128, ND * NCT * 128], BF16).ap()
    outw_b_d = nc.dram_tensor("outw_b", [128, ND * ND * 128], BF16).ap()

    em = EM(nc)

    ones_s = sbuf(nc, "ones_s", [128, 128])
    wtmp = sbuf(nc, "wtmp", [128, 512])
    wbtmp = sbuf(nc, "wbtmp", [128, 512], BF16)
    wstr = sbuf(nc, "wstr", [128, 2 * 2048], BF16)
    xproj_s = sbuf(nc, "xproj_s", [128, NCT * 96], BF16)
    dtw_s = sbuf(nc, "dtw_s", [64, D_IN], BF16)
    esel_s = sbuf(nc, "esel_s", [32, 32 * 128], BF16)
    conv_c = sbuf(nc, "conv_c_s", [128, NCT * 4])
    convb_c = sbuf(nc, "convb_c_s", [128, NCT])
    dtb_c = sbuf(nc, "dtb_c_s", [128, NCT])
    a_s = sbuf(nc, "a_s", [128, NCT * DST])
    dcol_s = sbuf(nc, "dcol_s", [128, NCT])
    gam_col = sbuf(nc, "gam_col_s", [128, ND])
    bet_col = sbuf(nc, "bet_col_s", [128, ND])
    xch = sbuf(nc, "xch", [128, ND * TC])
    hchb = sbuf(nc, "hchb", [128, ND * TC], BF16)
    xi_s = sbuf(nc, "xi_s", [128, NCT * (TC + 3)], BF16)
    z_s = sbuf(nc, "z_s", [128, NCT * TC], BF16)
    xcb_s = sbuf(nc, "xcb_s", [128, NCT * TC], BF16)
    ygb_s = sbuf(nc, "ygb_s", [128, NCT * TC], BF16)
    xdbl_s = sbuf(nc, "xdbl_s", [96, TC], BF16)
    xdbl_bc = sbuf(nc, "xdbl_bc", [32, TC], BF16)
    delta_s = sbuf(nc, "delta_s", [128, 2 * TC])
    du_s = sbuf(nc, "du_s", [128, 2 * TC])
    tsp_s = sbuf(nc, "tsp_s", [128, 2 * TC])
    av_s = sbuf(nc, "av_s", [128, 2 * TC])
    bv_s = sbuf(nc, "bv_s", [128, 2 * TC])
    hv_s = sbuf(nc, "hv_s", [128, 2 * TC])
    ty_s = sbuf(nc, "ty_s", [128, 2 * TC])
    yv_s = sbuf(nc, "yv_s", [128, TC])
    bc_c = sbuf(nc, "bc_c", [128, DST * TC], BF16)
    cc_c = sbuf(nc, "cc_c", [128, DST * TC], BF16)
    t1_s = sbuf(nc, "t1_s", [128, TC])
    t2_s = sbuf(nc, "t2_s", [128, TC])
    states = sbuf(nc, "states", [128, NCT * DST])
    p1_s = sbuf(nc, "p1_s", [128, ND * TC], BF16)
    mu_r = sbuf(nc, "mu_r", [1, TC])
    rs_r = sbuf(nc, "rs_r", [1, TC])
    rowtmp = sbuf(nc, "rowtmp", [1, TC])
    sq_s = sbuf(nc, "sq_s", [128, TC])

    ps_m = psum_t(nc, "ps_m", [128, 512])
    ps_x = psum_t(nc, "ps_x", [128, 512])
    ps_bc = psum_t(nc, "ps_bc", [128, 512])
    ps_cc = psum_t(nc, "ps_cc", [128, 512])
    ps_st = psum_t(nc, "ps_st", [1, 512])
    ps_st2 = psum_t(nc, "ps_st2", [1, 512])
    ps_r = psum_t(nc, "ps_r", [128, 512])
    ps_r2 = psum_t(nc, "ps_r2", [128, 512])

    em.emit("gpsimd", lambda g: g.memset(ones_s[:, :], 1.0), writes=["ones"])
    em.emit("gpsimd", lambda g: g.memset(xi_s[:, :], 0.0), writes=["xi"])
    em.dma("sync", conv_c[:, :], conv_c_d[:, :], writes=["convc"])
    em.dma("sync", convb_c[:, :], convb_c_d[:, :], writes=["convb"])
    em.dma("sync", dtb_c[:, :], dtb_c_d[:, :], writes=["dtb"])
    em.dma("sync", dcol_s[:, :], dcol_d[:, :], writes=["dcol"])
    em.dma("sync", gam_col[:, :], gam_col_d[:, :], writes=["gam"])
    em.dma("sync", bet_col[:, :], bet_col_d[:, :], writes=["bet"])
    em.dma("sync", a_s[:, :], alog_d[:, :], writes=["as"])
    em.emit("scalar", lambda s: s.activation(out=a_s[:, :], in_=a_s[:, :],
                                             func=AF.Exp),
            reads=["as"], writes=["as"])
    e_ts(em, a_s[:, :], a_s[:, :], -1.0, None, ALU.mult, None, ["as"], "as")
    for i in range(8):
        em.dma("sync", wtmp[0:32, 0:512], esel_d[:, ts(i, 512)], writes=["wtmp"])
        e_copy(em, esel_s[:, ts(i, 512)], wtmp[0:32, 0:512], ["wtmp"], "esel")
    for ct in range(NCT):
        em.dma("sync", wtmp[:, 0:96], xprojT[ts(ct, 128), :], writes=["wtmp"])
        e_copy(em, xproj_s[:, ts(ct, 96)], wtmp[:, 0:96], ["wtmp"], "xprj")
    for i in range(4):
        em.dma("sync", wtmp[0:64, 0:512], dt_wT[:, ts(i, 512)], writes=["wtmp"])
        e_copy(em, dtw_s[:, ts(i, 512)], wtmp[0:64, 0:512], ["wtmp"], "dtw")
    def cast_to_dram(src_ap, dst_ap, cols, tag):
        CW = 2048
        for o in range(0, cols, CW):
            par = (o // CW) % 2
            if par == 0:
                lbuf = xch[:, 0:CW]
                lkeys = ["xch0", "xch1", "xch2", "xch3"]
                stg, skey = xcb_s[:, 0:CW], "xcb"
            else:
                lbuf = xch[:, CW:2 * CW]
                lkeys = ["xch4", "xch5", "xch6", "xch7"]
                stg, skey = z_s[:, 0:CW], "z"
            em.dma(("sync", "scalar")[par], lbuf, src_ap[:, o:o + CW],
                   writes=lkeys)
            e_copy(em, stg, lbuf, lkeys, skey,
                   eng=("scalar" if par == 0 else "vector"))
            em.dma(("gpsimd", "sync")[par], dst_ap[:, o:o + CW],
                   stg, reads=[skey], writes=[tag])

    cast_to_dram(in_wT, inw_b_d, NRT0 * ND * 128, "inwb")
    cast_to_dram(mout_wT, moutw_b_d, ND * NCT * 128, "moutb")
    cast_to_dram(outw_sT, outw_b_d, ND * ND * 128, "outwb")

    NRT = 2 * D_IN // 128

    def ph_pre(c):
        csl = ts(c, TC)
        for k in range(ND):
            em.dma("sync", xch[:, ts(k, TC)], xn1f[ts(k, 128), csl],
                   writes=["xch%d" % k])
            e_mm(em, ps_st[:, :], ones_s[:, 0:1], xch[:, ts(k, TC)],
                 ["xch%d" % k, "ones"], "st", start=(k == 0), stop=(k == ND - 1))
            e_act(em, sq_s[:, :], xch[:, ts(k, TC)], AF.Square,
                  ["xch%d" % k], "sq")
            e_mm(em, ps_st2[:, :], ones_s[:, 0:1], sq_s[:, :], ["sq", "ones"],
                 "st2", start=(k == 0), stop=(k == ND - 1))
        ln_rows(em, ps_st[:, :], ps_st2[:, :], mu_r[:, :], rs_r[:, :],
                rowtmp[:, :], 1e-5, D, "st", "st2", "mu", "rs", "rt")
        e_bcast(em, ps_r[:, :], ones_s, mu_r[:, :], "mu", "psr")
        e_bcast(em, ps_r2[:, :], ones_s, rs_r[:, :], "rs", "psr2")
        for k in range(ND):
            e_tt(em, sq_s[:, :], xch[:, ts(k, TC)], ps_r[:, :], ALU.subtract,
                 ["xch%d" % k, "psr"], "sq")
            e_tt(em, sq_s[:, :], sq_s[:, :], ps_r2[:, :], ALU.mult,
                 ["sq", "psr2"], "sq")
            e_ts(em, hchb[:, ts(k, TC)], sq_s[:, :], gam_col[:, k:k + 1],
                 bet_col[:, k:k + 1], ALU.mult, ALU.add,
                 ["sq", "gam", "bet"], "hchb")
    def ph_mid(c):
        for rt in range(NRT):
            buf = rt % 2
            bank, bk = (ps_m, "psm") if buf == 0 else (ps_x, "psx")
            em.dma(("sync", "scalar")[buf],
                   wstr[:, buf * 2048: buf * 2048 + 1024],
                   inw_b_d[:, rt * 1024:(rt + 1) * 1024],
                   reads=["inwb"], writes=["wstr%d" % buf])
            for k in range(ND):
                e_mm(em, bank[:, :],
                     wstr[:, buf * 2048 + k * 128: buf * 2048 + (k + 1) * 128],
                     hchb[:, ts(k, TC)], ["wstr%d" % buf, "hchb"], bk,
                     start=(k == 0), stop=(k == ND - 1))
            if rt < NCT:
                ct = rt
                e_copy(em, xi_s[:, ct * (TC + 3): ct * (TC + 3) + 3],
                       xi_s[:, ct * (TC + 3) + TC: ct * (TC + 3) + TC + 3],
                       ["xi"], "xi", eng="vector")
                e_copy(em, xi_s[:, ct * (TC + 3) + 3: (ct + 1) * (TC + 3)],
                       bank[:, :], [bk, "xi"], "xi")
            else:
                ct = rt - NCT
                e_copy(em, z_s[:, ts(ct, TC)], bank[:, :], [bk], "z")
        for ct in range(NCT):
            base = ct * (TC + 3)
            e_ts(em, t1_s[:, :], xi_s[:, base: base + TC],
                 conv_c[:, ct * 4: ct * 4 + 1], None, ALU.mult, None,
                 ["xi", "convc"], "t1")
            for j in range(1, 4):
                e_stt(em, t1_s[:, :], xi_s[:, base + j: base + j + TC],
                      conv_c[:, ct * 4 + j: ct * 4 + j + 1], t1_s[:, :],
                      ALU.mult, ALU.add, ["xi", "convc", "t1"], "t1")
            e_act(em, xcb_s[:, ts(ct, TC)], t1_s[:, :], AF.Silu,
                  ["t1", "convb"], "xcb", bias=convb_c[:, ct:ct + 1])
        for ct in range(NCT):
            e_mm(em, ps_x[0:96, :], xproj_s[:, ts(ct, 96)], xcb_s[:, ts(ct, TC)],
                 ["xprj", "xcb"], "psx", start=(ct == 0), stop=(ct == NCT - 1))
        e_copy(em, xdbl_s[:, :], ps_x[0:96, :], ["psx"], "xdbl")
        em.dma("gpsimd", xdbl_bc[:, :], xdbl_s[64:96, :], reads=["xdbl"],
               writes=["xdblbc"])
        for s in range(DST):
            bank, bk = (ps_bc, "psbc") if s % 2 == 0 else (ps_cc, "pscc")
            e_mm(em, bank[:, :], esel_s[:, s * 128:(s + 1) * 128],
                 xdbl_bc[:, :], ["esel", "xdblbc"], bk, start=True, stop=True)
            e_copy(em, bc_c[:, ts(s, TC)], bank[:, :], [bk], "bcc",
                   eng=("scalar" if s % 2 == 0 else "vector"))
        for s in range(DST):
            bank, bk = (ps_bc, "psbc") if s % 2 == 0 else (ps_cc, "pscc")
            e_mm(em, bank[:, :], esel_s[:, (16 + s) * 128:(17 + s) * 128],
                 xdbl_bc[:, :], ["esel", "xdblbc"], bk, start=True, stop=True)
            e_copy(em, cc_c[:, ts(s, TC)], bank[:, :], [bk], "ccc",
                   eng=("scalar" if s % 2 == 0 else "vector"))
    def ph_scan(c):
        for ct in range(NCT):
            cp = ct % 2
            dsl = delta_s[:, ts(cp, TC)]
            dusl = du_s[:, ts(cp, TC)]
            dk, duk = "delta%d" % cp, "du%d" % cp
            bank, bk = (ps_m, "psm") if cp == 0 else (ps_x, "psx")
            tsa = tsp_s[:, ts(cp, TC)]
            tsk = "tsp%d" % cp
            e_mm(em, bank[:, :], dtw_s[:, ts(ct, 128)], xdbl_s[0:64, :],
                 ["dtw", "xdbl"], bk, start=True, stop=True)
            e_act(em, tsa, bank[:, :], AF.Abs, [bk, "dtb"], tsk,
                  bias=dtb_c[:, ct:ct + 1])
            e_act(em, tsa, tsa, AF.Exp, [tsk], tsk, scale=-1.0)
            e_act(em, tsa, tsa, AF.Ln, [tsk], tsk, bias=1.0)
            e_act(em, dsl, bank[:, :], AF.Relu, [bk, "dtb"], dk,
                  bias=dtb_c[:, ct:ct + 1])
            e_tt(em, dsl, dsl, tsa, ALU.add, [dk, tsk], dk)
            e_tt(em, dusl, dsl, xcb_s[:, ts(ct, TC)], ALU.mult,
                 [dk, "xcb"], duk)
            def emit_ab(s):
                sp = s % 2
                e_act(em, av_s[:, ts(sp, TC)], dsl, AF.Exp, [dk, "as"],
                      "av%d" % sp,
                      scale=a_s[:, ct * DST + s: ct * DST + s + 1])
                e_tt(em, bv_s[:, ts(sp, TC)], dusl, bc_c[:, ts(s, TC)],
                     ALU.mult, [duk, "bcc"], "bv%d" % sp)
            emit_ab(0)
            for s in range(DST):
                sp = s % 2
                av = av_s[:, ts(sp, TC)]
                bv = bv_s[:, ts(sp, TC)]
                hv = hv_s[:, ts(sp, TC)]
                avk, bvk, hvk = "av%d" % sp, "bv%d" % sp, "hv%d" % sp
                stcol = states[:, ct * DST + s: ct * DST + s + 1]
                stkey = "stt%d_%d" % (ct, s)
                if c == 0:
                    em.emit("vector",
                            lambda v, av=av, bv=bv, hv=hv:
                            v.tensor_tensor_scan(
                                out=hv, data0=av, data1=bv,
                                initial=0.0, op0=ALU.mult, op1=ALU.add),
                            reads=[avk, bvk], writes=[hvk])
                else:
                    em.emit("vector",
                            lambda v, av=av, bv=bv, hv=hv, stcol=stcol:
                            v.tensor_tensor_scan(
                                out=hv, data0=av, data1=bv,
                                initial=stcol, op0=ALU.mult, op1=ALU.add),
                            reads=[avk, bvk, stkey], writes=[hvk])
                e_copy(em, stcol, hv_s[:, sp * TC + TC - 1: sp * TC + TC],
                       [hvk], stkey)
                if s + 1 < DST:
                    emit_ab(s + 1)
                if s == 0:
                    e_tt(em, yv_s[:, :], hv, cc_c[:, ts(s, TC)], ALU.mult,
                         [hvk, "ccc"], "yv")
                else:
                    tyv = ty_s[:, ts(sp, TC)]
                    e_tt(em, tyv, hv, cc_c[:, ts(s, TC)], ALU.mult,
                         [hvk, "ccc"], "ty%d" % sp)
                    e_tt(em, yv_s[:, :], yv_s[:, :], tyv, ALU.add,
                         ["yv", "ty%d" % sp], "yv")
            e_stt(em, yv_s[:, :], xcb_s[:, ts(ct, TC)], dcol_s[:, ct:ct + 1],
                  yv_s[:, :], ALU.mult, ALU.add, ["xcb", "dcol", "yv"], "yv")
            e_act(em, t1_s[:, :], z_s[:, ts(ct, TC)], AF.Silu, ["z"], "t1")
            e_tt(em, ygb_s[:, ts(ct, TC)], yv_s[:, :], t1_s[:, :], ALU.mult,
                 ["yv", "t1"], "ygb")
    def ph_post(c):
        csl = ts(c, TC)
        for dt in range(ND):
            sl2 = dt % 2
            bank, bk = (ps_m, "psm") if sl2 == 0 else (ps_x, "psx")
            em.dma("sync", wstr[:, sl2 * 2048: sl2 * 2048 + 1024],
                   moutw_b_d[:, dt * 2048: dt * 2048 + 1024],
                   reads=["moutb"], writes=["wstr%d" % sl2])
            em.dma("scalar", wstr[:, sl2 * 2048 + 1024: (sl2 + 1) * 2048],
                   moutw_b_d[:, dt * 2048 + 1024:(dt + 1) * 2048],
                   reads=["moutb"], writes=["wstr%d" % sl2])
            for ct in range(NCT):
                e_mm(em, bank[:, :],
                     wstr[:, sl2 * 2048 + ct * 128:
                          sl2 * 2048 + (ct + 1) * 128],
                     ygb_s[:, ts(ct, TC)], ["wstr%d" % sl2, "ygb"], bk,
                     start=(ct == 0), stop=(ct == NCT - 1))
            e_copy(em, p1_s[:, ts(dt, TC)], bank[:, :], [bk], "p1",
                   eng=("scalar" if sl2 == 0 else "vector"))
        for dt in range(ND):
            sl2 = dt % 2
            bank, bk = (ps_m, "psm") if sl2 == 0 else (ps_x, "psx")
            em.dma(("sync", "scalar")[sl2],
                   wstr[:, sl2 * 2048: sl2 * 2048 + 1024],
                   outw_b_d[:, dt * 1024:(dt + 1) * 1024],
                   reads=["outwb"], writes=["wstr%d" % sl2])
            for j in range(ND):
                e_mm(em, bank[:, :],
                     wstr[:, sl2 * 2048 + j * 128: sl2 * 2048 + (j + 1) * 128],
                     p1_s[:, ts(j, TC)], ["wstr%d" % sl2, "p1"], bk,
                     start=(j == 0), stop=(j == ND - 1))
            obuf, okey = (t1_s, "t1") if sl2 == 0 else (t2_s, "t2")
            e_copy(em, obuf[:, :], bank[:, :], [bk], okey,
                   eng=("scalar" if sl2 == 0 else "vector"))
            em.dma(("gpsimd", "sync")[sl2], part_d[ts(dt, 128), csl],
                   obuf[:, :], reads=[okey], writes=["pout"])

    NCH = L // TC
    ph_pre(0)
    ph_mid(0)
    for c in range(NCH):
        if c + 1 < NCH:
            ph_pre(c + 1)
        ph_scan(c)
        if c + 1 < NCH:
            ph_mid(c + 1)
        ph_post(c)

    em.finalize()
    return nc


def _tilemaj(wmat, nr, nccols):
    return np.ascontiguousarray(
        np.float32(wmat).reshape(nr, 128, nccols, 128)
        .transpose(1, 2, 0, 3).reshape(128, -1))


def mamba_in_maps(inputs, xn1):
    esel = np.zeros((32, 32 * 128), np.float32)
    for i in range(32):
        esel[i, i * 128:(i + 1) * 128] = 1.0
    maps = []
    for core in range(8):
        b, dirb = core // 2, core % 2
        tag = "f" if dirb == 0 else "b"
        arr = xn1[b]
        if dirb == 1:
            arr = arr[::-1]
        m = {
            "xn1f": np.ascontiguousarray(arr.T),
            "in_t": _tilemaj(np.asarray(inputs["L1_attn_in_w_" + tag]).T,
                             D // 128, 2 * D_IN // 128),
            "xprojT": np.ascontiguousarray(
                np.asarray(inputs["L1_attn_xproj_w_" + tag]).T),
            "dt_wT": np.ascontiguousarray(
                np.asarray(inputs["L1_attn_dt_w_" + tag]).T),
            "conv_c": np.ascontiguousarray(
                np.asarray(inputs["L1_attn_conv_w_" + tag]).reshape(
                    D_IN, 4).reshape(NCT, 128, 4).transpose(1, 0, 2).reshape(
                        128, NCT * 4)),
            "convb_c": np.ascontiguousarray(
                np.asarray(inputs["L1_attn_conv_b_" + tag]).reshape(
                    NCT, 128).T),
            "dtb_c": np.ascontiguousarray(
                np.asarray(inputs["L1_attn_dt_b_" + tag]).reshape(NCT, 128).T),
            "alog": np.ascontiguousarray(
                np.asarray(inputs["L1_attn_A_log_" + tag]).reshape(
                    NCT, 128, DST).transpose(1, 0, 2).reshape(128, NCT * DST)),
            "dcol": np.ascontiguousarray(
                np.asarray(inputs["L1_attn_D_" + tag]).reshape(NCT, 128).T),
            "mout_t": _tilemaj(np.asarray(inputs["L1_attn_mout_w_" + tag]).T,
                               D_IN // 128, D // 128),
            "outw_t": _tilemaj(
                np.asarray(inputs["L1_attn_out_w"])[:,
                    dirb * D:(dirb + 1) * D].T, D // 128, D // 128),
            "gam_col": np.ascontiguousarray(
                np.asarray(inputs["L1_attn_gamma"]).reshape(8, 128).T),
            "bet_col": np.ascontiguousarray(
                np.asarray(inputs["L1_attn_beta"]).reshape(8, 128).T),
            "esel": esel,
        }
        maps.append(m)
    return maps


def run_mamba(inputs, xn1):
    nc = build_mamba()
    maps = mamba_in_maps(inputs, xn1)
    res = run_bass_kernel_spmd(nc, maps, core_ids=list(range(8)))
    pf = np.zeros((B, L, D), np.float32)
    pb = np.zeros((B, L, D), np.float32)
    for core in range(8):
        b, dirb = core // 2, core % 2
        p = res.results[core]["partial"].T
        if dirb == 1:
            p = p[::-1]
            pb[b] = p
        else:
            pf[b] = p
    return pf, pb


TS = 2560
NW = TS // 256
HEADS = 16
DH = 64
INNER = 1024
FFI = 2816


FFN_SKIP = set()


def build_ffn():
    nc = bass.Bass("TRN2", target_bir_lowering=False)
    ND = D // 128
    NI = INNER // 128
    NJC = FFI // 128

    xst_d = nc.dram_tensor("xst", [D, 1 + TS], F32, kind="ExternalInput").ap()
    wqk_d = nc.dram_tensor("wqk_t", [128, 2 * NI * ND * 128], F32,
                           kind="ExternalInput").ap()
    wv_d = nc.dram_tensor("wv_t", [128, 2 * ND * 512], F32,
                          kind="ExternalInput").ap()
    wo_d = nc.dram_tensor("wo_t", [128, ND * NI * 128], F32,
                          kind="ExternalInput").ap()
    w1a_d = nc.dram_tensor("w1a_t", [128, NJC * ND * 128], F32,
                           kind="ExternalInput").ap()
    w1g_d = nc.dram_tensor("w1g_t", [128, NJC * ND * 128], F32,
                           kind="ExternalInput").ap()
    w2_d = nc.dram_tensor("w2_t", [128, ND * NJC * 128], F32,
                          kind="ExternalInput").ap()
    b1a_d = nc.dram_tensor("b1a", [128, NJC], F32, kind="ExternalInput").ap()
    b1g_d = nc.dram_tensor("b1g", [128, NJC], F32, kind="ExternalInput").ap()
    b2_d = nc.dram_tensor("b2c", [128, ND], F32, kind="ExternalInput").ap()
    lng_d = nc.dram_tensor("lng", [128, ND], F32, kind="ExternalInput").ap()
    lnb_d = nc.dram_tensor("lnb", [128, ND], F32, kind="ExternalInput").ap()
    qs_d = nc.dram_tensor("qs", [128, 1], F32, kind="ExternalInput").ap()
    ks_d = nc.dram_tensor("ks", [128, 1], F32, kind="ExternalInput").ap()
    cs_d = nc.dram_tensor("cs_t", [128, TS], F32, kind="ExternalInput").ap()
    ssn_d = nc.dram_tensor("ssn_t", [128, TS], F32, kind="ExternalInput").ap()
    csk_d = nc.dram_tensor("csk_t", [128, TS], F32, kind="ExternalInput").ap()
    ssnk_d = nc.dram_tensor("ssnk_t", [128, TS], F32, kind="ExternalInput").ap()
    flag_d = nc.dram_tensor("flagrow", [1, NW * 2 * 384], F32,
                            kind="ExternalInput").ap()
    band_d = nc.dram_tensor("band", [128, 384], F32, kind="ExternalInput").ap()
    cmask_d = nc.dram_tensor("cmask", [1, TS], F32, kind="ExternalInput").ap()
    blk16_d = nc.dram_tensor("blk16", [128, NI * 16], F32,
                             kind="ExternalInput").ap()
    esel2_d = nc.dram_tensor("esel2", [16, NI * 128], F32,
                             kind="ExternalInput").ap()
    iden_d = nc.dram_tensor("iden", [128, 128], F32, kind="ExternalInput").ap()
    rperm_d = nc.dram_tensor("rperm", [128, 128], F32, kind="ExternalInput").ap()
    xout_d = nc.dram_tensor("xout", [D, T_OWN], F32, kind="ExternalOutput").ap()
    wqk_b = nc.dram_tensor("wqk_b", [128, 2 * NI * ND * 128], BF16).ap()
    wv_b = nc.dram_tensor("wv_b", [128, 2 * ND * 512], BF16).ap()
    wo_b = nc.dram_tensor("wo_b", [128, ND * NI * 128], BF16).ap()
    w1a_b = nc.dram_tensor("w1a_b", [128, NJC * ND * 128], BF16).ap()
    w1g_b = nc.dram_tensor("w1g_b", [128, NJC * ND * 128], BF16).ap()
    w2_b = nc.dram_tensor("w2_b", [128, ND * NJC * 128], BF16).ap()

    em = EM(nc)

    xs = sbuf(nc, "xs", [128, ND * (1 + TS)])
    wqkstr = sbuf(nc, "wqkstr", [128, 2 * 1024], BF16)
    wostr = sbuf(nc, "wostr", [128, 2 * NI * 128], BF16)
    wvstr = sbuf(nc, "wvstr", [128, 2 * 1024], BF16)
    ones_s = sbuf(nc, "ones_s", [128, 128])
    rperm_s = sbuf(nc, "rperm_s", [128, 128])
    eye_b = sbuf(nc, "eye_b", [128, 128], BF16)
    wtmp = sbuf(nc, "wtmp", [128, 512])
    w1str = sbuf(nc, "w1str", [128, 2048], BF16)
    tf1 = sbuf(nc, "tf1", [128, 2 * 512])
    tf2 = sbuf(nc, "tf2", [128, 2 * 512])
    xwb = sbuf(nc, "xwb", [128, ND * 256], BF16)
    qro = sbuf(nc, "qro", [128, NI * 256], BF16)
    kro = sbuf(nc, "kro", [128, NI * 512], BF16)
    vtm = sbuf(nc, "vtm", [128, 4 * INNER], BF16)
    q1_s = sbuf(nc, "q1_s", [128, 2 * 256])
    cs_s = sbuf(nc, "cs_s", [128, 256])
    ssn_s = sbuf(nc, "ssn_s", [128, 256])
    csk_s = sbuf(nc, "csk_s", [128, 256])
    ssnk_s = sbuf(nc, "ssnk_s", [128, 256])
    rr_s = sbuf(nc, "rr_s", [16, 512])
    rr_b = sbuf(nc, "rr_b", [16, 512], BF16)
    blk16_s = sbuf(nc, "blk16_s", [128, NI * 16])
    esel2_b = sbuf(nc, "esel2_b", [16, NI * 128], BF16)
    pexp_b = sbuf(nc, "pexp_b", [128, 4 * 384], BF16)
    pt_b = sbuf(nc, "pt_b", [128, 2 * 384], BF16)
    col_s = sbuf(nc, "col_s", [128, 16])
    attm = sbuf(nc, "attm", [128, INNER], BF16)
    attf = sbuf(nc, "attf", [128, NI * 256], BF16)
    band_b = sbuf(nc, "band_b", [128, 384], BF16)
    ones_bf = sbuf(nc, "ones_bf", [1, 128], BF16)
    flag_r = sbuf(nc, "flag_r", [1, 384])
    flag_b = sbuf(nc, "flag_b", [1, 384], BF16)
    cmask_r = sbuf(nc, "cmask_r", [1, 512])
    qs_c = sbuf(nc, "qs_c", [128, 2])
    b1a_c = sbuf(nc, "b1a_c", [128, NJC])
    b1g_c = sbuf(nc, "b1g_c", [128, NJC])
    b2_c = sbuf(nc, "b2_c", [128, ND])
    lng_c = sbuf(nc, "lng_c", [128, ND])
    lnb_c = sbuf(nc, "lnb_c", [128, ND])
    xnb = sbuf(nc, "xnb", [128, ND * 512], BF16)
    hgb = sbuf(nc, "hgb", [128, NJC * 512], BF16)
    fout = sbuf(nc, "fout", [128, ND * 512], BF16)
    mu_r = sbuf(nc, "mu_r", [1, 512])
    rs_r = sbuf(nc, "rs_r", [1, 512])
    rowtmp = sbuf(nc, "rowtmp", [1, 512])

    ps = [psum_t(nc, "psg%d" % i, [128, 512]) for i in range(6)]
    ps_r = psum_t(nc, "ps_r", [128, 512])
    ps_r2 = psum_t(nc, "ps_r2", [128, 512])
    ps_tb = ps[4][:, 256:320].bitcast(BF16)
    ps_tb2 = ps_r[:, 0:64].bitcast(BF16)
    PK = ["psg%d" % i for i in range(6)]

    em.emit("gpsimd", lambda g: g.memset(ones_s[:, :], 1.0), writes=["ones"])
    em.emit("gpsimd", lambda g: g.memset(kro[:, :], 0.0), writes=["kro"])
    em.emit("gpsimd", lambda g: g.memset(vtm[:, :], 0.0), writes=["vtm"])
    e_copy(em, ones_bf[:, :], ones_s[0:1, :], ["ones"], "onesb")
    em.dma("sync", wtmp[:, 0:384], band_d[:, :], writes=["wtmp0"])
    e_copy(em, band_b[:, :], wtmp[:, 0:384], ["wtmp0"], "bandb")
    em.dma("sync", blk16_s[:, :], blk16_d[:, :], writes=["blk16"])
    em.dma("sync", b1a_c[:, :], b1a_d[:, :], writes=["b1a"])
    em.dma("sync", b1g_c[:, :], b1g_d[:, :], writes=["b1g"])
    em.dma("sync", b2_c[:, :], b2_d[:, :], writes=["b2"])
    em.dma("sync", lng_c[:, :], lng_d[:, :], writes=["lng"])
    em.dma("sync", lnb_c[:, :], lnb_d[:, :], writes=["lnb"])
    for i in range(2):
        em.dma("sync", wtmp[0:16, 0:512], esel2_d[:, ts(i, 512)], writes=["wtmp0"])
        e_copy(em, esel2_b[:, ts(i, 512)], wtmp[0:16, 0:512], ["wtmp0"], "esel2")
    em.dma("sync", rperm_s[:, :], rperm_d[:, :], writes=["rperm"])
    em.dma("sync", wtmp[:, 0:128], iden_d[:, :], writes=["wtmp0"])
    e_copy(em, eye_b[:, :], wtmp[:, 0:128], ["wtmp0"], "eye")
    em.dma("sync", qs_c[:, 0:1], qs_d[:, :], writes=["qsc"])
    em.dma("sync", wtmp[:, 0:1], ks_d[:, :], writes=["wtmp0"])
    e_copy(em, qs_c[:, 1:2], wtmp[:, 0:1], ["wtmp0"], "qsc", eng="vector")
    e_ts(em, qs_c[:, 0:1], qs_c[:, 0:1], 8.0, None, ALU.mult, None,
         ["qsc"], "qsc")

    for k in range(ND):
        em.dma("sync", xs[:, k * (1 + TS):(k + 1) * (1 + TS)],
               xst_d[ts(k, 128), :], writes=["xs"])

    QKQ = ("sync", "scalar", "gpsimd", "sync")

    def cast_to_dram(src_ap, dst_ap, cols, tag):
        CW = 1024
        qi = 0
        for o in range(0, cols, CW):
            wd = min(CW, cols - o)
            par = qi % 2
            lbuf, lkeys = (tf1, ["tf10", "tf11"]) if par == 0 else \
                (tf2, ["tf20", "tf21"])
            stgb, skeys = (hgb, ["hgb0", "hgb1"]) if par == 0 else \
                (fout, ["fout"])
            stg = stgb[:, 0:wd]
            em.dma(QKQ[qi % 4], lbuf[:, 0:wd],
                   src_ap[:, o:o + wd], writes=lkeys)
            if qi % 2 == 0:
                em.emit("scalar", lambda s, stg=stg, lb=lbuf[:, 0:wd]:
                        s.copy(out=stg, in_=lb), reads=lkeys, writes=skeys)
            else:
                em.emit("vector", lambda v, stg=stg, lb=lbuf[:, 0:wd]:
                        v.tensor_copy(out=stg, in_=lb), reads=lkeys,
                        writes=skeys)
            em.dma(QKQ[(qi + 2) % 4], dst_ap[:, o:o + wd],
                   stg, reads=skeys, writes=[tag])
            qi += 1

    cast_to_dram(wqk_d, wqk_b, 2 * NI * ND * 128, "wqkb")
    cast_to_dram(wv_d, wv_b, 2 * ND * 512, "wvb")
    cast_to_dram(wo_d, wo_b, ND * NI * 128, "wob")
    cast_to_dram(w1a_d, w1a_b, NJC * ND * 128, "w1ab")
    cast_to_dram(w1g_d, w1g_b, NJC * ND * 128, "w1gb")
    cast_to_dram(w2_d, w2_b, ND * NJC * 128, "w2b")

    def xshift_ap(k, col0, width):
        base = k * (1 + TS)
        off = 1 + col0 - (1 if k >= ND // 2 else 0)
        return xs[:, base + off: base + off + width]

    def wo_residual(w):
        em.dma("sync", cmask_r[:, 0:256], cmask_d[:, w * 256:(w + 1) * 256],
               writes=["cmask"])
        e_bcast(em, ps_r2[:, 0:256], ones_s, cmask_r[:, 0:256], "cmask", "psr2")
        for dt in range(ND):
            pk = PK[dt % 2]
            wb3, wo3, wk3 = ((wostr, 0, ["wos0"]), (wostr, 1024, ["wos1"]),
                             (hgb, 0, ["hgb0", "hgb1"]))[dt % 3]
            em.dma(QKQ[dt % 3], wb3[:, wo3: wo3 + NI * 128],
                   wo_b[:, dt * NI * 128: (dt + 1) * NI * 128],
                   reads=["wob"], writes=wk3)
            for i in range(NI):
                e_mm(em, ps[dt % 2][:, 0:256],
                     wb3[:, wo3 + i * 128: wo3 + (i + 1) * 128],
                     attf[:, i * 256:(i + 1) * 256],
                     wk3 + ["attf"], pk,
                     start=(i == 0), stop=(i == NI - 1))
            par = dt % 2
            ta = tf1[:, par * 512: par * 512 + 256]
            e_copy(em, ta, ps[par][:, 0:256], [pk], "tf1%d" % par)
            e_tt(em, ta, ta, ps_r2[:, 0:256],
                 ALU.mult, ["tf1%d" % par, "psr2"], "tf1%d" % par)
            xsl = xs[:, dt * (1 + TS) + 1 + w * 256:
                     dt * (1 + TS) + 1 + (w + 1) * 256]
            e_tt(em, xsl, xsl, ta, ALU.add,
                 ["xs", "tf1%d" % par], "xs")

    for w in range(NW if "attn" not in FFN_SKIP else 0):
        for k in range(ND):
            e_copy(em, xwb[:, ts(k, 256)], xshift_ap(k, w * 256, 256),
                   ["xs"], "xwb", eng=("scalar" if k % 2 else "vector"))
        for jt in range(NI):
            e_copy(em, kro[:, jt * 512: jt * 512 + 256],
                   kro[:, jt * 512 + 256: (jt + 1) * 512], ["kro"], "kro",
                   eng="vector")
        for i in range(2):
            e_copy(em, vtm[:, ts(i, INNER)], vtm[:, ts(2 + i, INNER)],
                   ["vtm"], "vtm", eng="scalar")
        def vdma(idx):
            half, kg = idx // 4, idx % 4
            slot = idx % 2
            em.dma(QKQ[idx % 3], wvstr[:, ts(slot, 1024)],
                   wv_b[:, half * 4096 + kg * 1024:
                        half * 4096 + (kg + 1) * 1024],
                   reads=["wvb"], writes=["wvs%d" % slot])
        if "v" not in FFN_SKIP:
            vdma(0)
            vdma(1)
        for idx in range(8 if "v" not in FFN_SKIP else 0):
            half, kg = idx // 4, idx % 4
            slot = idx % 2
            vb = 2 - 2 * half
            for kk in range(2):
                k = kg * 2 + kk
                for tt in range(2):
                    e_mm(em, ps[vb + tt][:, :],
                         xwb[:, k * 256 + tt * 128: k * 256 + tt * 128 + 128],
                         wvstr[:, slot * 1024 + kk * 512:
                               slot * 1024 + kk * 512 + 512],
                         ["xwb", "wvs%d" % slot], PK[vb + tt],
                         start=(k == 0), stop=(k == ND - 1))
            if idx + 2 < 8:
                vdma(idx + 2)
            if kg == 3:
                for tt in range(2):
                    e_copy(em, vtm[:, (2 + tt) * INNER + half * 512:
                                   (2 + tt) * INNER + (half + 1) * 512],
                           ps[vb + tt][:, :], [PK[vb + tt]], "vtm",
                           eng=("scalar" if tt == 0 else "vector"))
        em.dma("sync", cs_s[:, :], cs_d[:, w * 256:(w + 1) * 256], writes=["cst"])
        em.dma("scalar", ssn_s[:, :], ssn_d[:, w * 256:(w + 1) * 256], writes=["sst"])
        em.dma("sync", csk_s[:, :], csk_d[:, w * 256:(w + 1) * 256], writes=["ckt"])
        em.dma("gpsimd", ssnk_s[:, :], ssnk_d[:, w * 256:(w + 1) * 256],
               writes=["skt"])
        for qk_i in range(2 if "qk" not in FFN_SKIP else 0):
            dkey = "qraw" if qk_i == 0 else "kro"
            def qk_tail(jt):
                par = jt % 2
                pk = PK[par]
                e_act(em, tf2[:, par * 512: par * 512 + 256],
                      ps[par][:, 0:256], AF.Square, [pk], "tf2%d" % par)
                e_mm(em, ps[4][0:16, 0:256], blk16_s[:, ts(jt, 16)],
                     tf2[:, par * 512: par * 512 + 256],
                     ["tf2%d" % par, "blk16"], "psg4",
                     start=(jt == 0), stop=(jt == NI - 1))
                dst = (qro[:, ts(jt, 256)] if qk_i == 0 else
                       kro[:, jt * 512 + 256:(jt + 1) * 512])
                e_copy(em, dst, ps[par][:, 0:256], [pk], dkey)
            pend = None
            for jt in range(NI):
                par = jt % 2
                pk = PK[par]
                qj = qk_i * NI + jt
                sl4 = (qk_i * NI + jt) % 4
                wb4, wo4, wk4 = ((wqkstr, 0, "wqks0"), (wqkstr, 1024, "wqks1"),
                                 (w1str, 0, "w1s0"), (w1str, 1024, "w1s1"))[sl4]
                em.dma(QKQ[jt % 3], wb4[:, wo4:wo4 + 1024],
                       wqk_b[:, qj * 1024:(qj + 1) * 1024],
                       reads=["wqkb"], writes=[wk4])
                for k in range(ND):
                    e_mm(em, ps[par][:, 0:256],
                         wb4[:, wo4 + k * 128: wo4 + (k + 1) * 128],
                         xwb[:, ts(k, 256)],
                         [wk4, "xwb"], pk,
                         start=(k == 0), stop=(k == ND - 1))
                if pend is not None:
                    qk_tail(pend)
                pend = jt
            qk_tail(pend)
            e_act(em, rr_s[:, ts(qk_i, 256)], ps[4][0:16, 0:256], AF.Sqrt,
                  ["psg4"], "rr")
            e_ts(em, rr_s[:, ts(qk_i, 256)], rr_s[:, ts(qk_i, 256)], 1e-12,
                 None, ALU.max, None, ["rr"], "rr")
            em.emit("vector",
                    lambda v, qk_i=qk_i: v.reciprocal(
                        out=rr_s[:, ts(qk_i, 256)],
                        in_=rr_s[:, ts(qk_i, 256)]),
                    reads=["rr"], writes=["rr"])
            e_copy(em, rr_b[:, ts(qk_i, 256)], rr_s[:, ts(qk_i, 256)],
                   ["rr"], "rrb")
        for qk_i in range(2 if "rot" not in FFN_SKIP else 0):
            srcbuf, dkey = (qro, "qraw") if qk_i == 0 else (kro, "kro")
            cst, ckey = (cs_s, "cst") if qk_i == 0 else (csk_s, "ckt")
            sst, skey2 = (ssn_s, "sst") if qk_i == 0 else (ssnk_s, "skt")
            for jt in range(NI):
                par = jt % 2
                po = par * 256
                ta = tf1[:, par * 512: par * 512 + 256]
                tb = tf2[:, par * 512 + 256: par * 512 + 512]
                sl = (ts(jt, 256) if qk_i == 0 else
                      slice(jt * 512 + 256, (jt + 1) * 512))
                e_mm(em, ps_r[:, po:po + 256], esel2_b[:, ts(jt, 128)],
                     rr_b[:, ts(qk_i, 256)], ["esel2", "rrb"], "psr%d" % par,
                     start=True, stop=True)
                e_stt(em, q1_s[:, po:po + 256], srcbuf[:, sl],
                      qs_c[:, qk_i:qk_i + 1], ps_r[:, po:po + 256],
                      ALU.mult, ALU.mult,
                      [dkey, "qsc", "psr%d" % par], "q1%d" % par)
                e_mm(em, ps[par][:, 256:512], rperm_s[:, :],
                     q1_s[:, po:po + 256], ["rperm", "q1%d" % par], PK[par],
                     start=True, stop=True)
                e_tt(em, ta, q1_s[:, po:po + 256], cst[:, :],
                     ALU.mult, ["q1%d" % par, ckey], "tf1%d" % par)
                e_tt(em, tb, ps[par][:, 256:512],
                     sst[:, :], ALU.mult, [PK[par], skey2], "tf2%d" % par)
                e_tt(em, srcbuf[:, sl], ta, tb, ALU.add,
                     ["tf1%d" % par, "tf2%d" % par], dkey)
        if w >= 1 and "wo" not in FFN_SKIP:
            wo_residual(w - 1)
        use_flag = w in (0, 2)
        for qc in range(2 if "core" not in FFN_SKIP else 0):
            if use_flag:
                em.dma("sync", flag_r[:, :],
                       flag_d[:, (w * 2 + qc) * 384:(w * 2 + qc + 1) * 384],
                       writes=["flag"])
                e_copy(em, flag_b[:, :], flag_r[:, :], ["flag"], "flagb")
            def stage_a(h):
                jt, sub = h // 2, h % 2
                pb4 = h % 4
                po4 = pb4 * 384
                rsl = slice(sub * 64, sub * 64 + 64)
                psim = ps[pb4]
                pks = PK[pb4]
                ck = "col%d" % pb4
                e_mm(em, psim[:, 0:384],
                     qro[rsl, jt * 256 + qc * 128: jt * 256 + qc * 128 + 128],
                     kro[rsl, jt * 512 + qc * 128: jt * 512 + qc * 128 + 384],
                     ["qraw", "kro"], pks, start=True, stop=False)
                if use_flag:
                    e_mm(em, psim[:, 0:384], ones_bf[0:1, :], flag_b[:, :],
                         ["onesb", "flagb"], pks, start=False, stop=False)
                e_mm(em, psim[:, 0:384], eye_b[:, :], band_b[:, :],
                     ["eye", "bandb"], pks, start=False, stop=True)
                e_red(em, col_s[:, pb4 * 4:pb4 * 4 + 1], psim[:, 0:384],
                      ALU.max, [pks], ck, negate=True)
                em.emit("scalar",
                        lambda s, po4=po4, pb4=pb4, psim=psim: s.activation(
                            out=pexp_b[:, po4:po4 + 384], in_=psim[:, 0:384],
                            func=AF.Exp, bias=col_s[:, pb4 * 4:pb4 * 4 + 1]),
                        reads=[pks, ck],
                        writes=["pexp%d" % pb4])
            def stage_b(h):
                par = h % 2
                pb4 = h % 4
                po4 = pb4 * 384
                po = par * 384
                ck = "col%d" % pb4
                e_red(em, col_s[:, pb4 * 4 + 1:pb4 * 4 + 2],
                      pexp_b[:, po4:po4 + 384], ALU.add, ["pexp%d" % pb4],
                      ck)
                em.emit("vector",
                        lambda v, pb4=pb4: v.reciprocal(
                            out=col_s[:, pb4 * 4 + 2:pb4 * 4 + 3],
                            in_=col_s[:, pb4 * 4 + 1:pb4 * 4 + 2]),
                        reads=[ck], writes=[ck])
                pvd = (ps[5], "psg5") if par == 0 else (ps_r2, "psr2")
                tpd0 = (ps_tb, ["pstb", "psg4"]) if par == 0 else \
                    (ps_tb2, ["pstb2", "psr0"])
                for i in range(3):
                    tpd = tpd0
                    em.emit("tensor",
                            lambda t, i=i, po4=po4, tpd=tpd: t.transpose(
                                out=tpd[0][:, :], in_=pexp_b[:, po4 + i * 128:
                                                            po4 + (i + 1) * 128],
                                identity=eye_b[:, :]),
                            reads=["pexp%d" % pb4, "eye"], writes=tpd[1])
                    e_copy(em, pt_b[:, po + i * 128: po + (i + 1) * 128],
                           tpd[0][:, :], tpd[1], "ptb%d" % par,
                           eng="vector")
                for i in range(3):
                    e_mm(em, pvd[0][:, 0:64],
                         pt_b[:, po + i * 128: po + (i + 1) * 128],
                         vtm[:, (qc + i) * INNER + h * 64:
                             (qc + i) * INNER + (h + 1) * 64],
                         ["ptb%d" % par, "vtm"], pvd[1],
                         start=(i == 0), stop=(i == 2))
                e_ts(em, attm[:, h * 64:(h + 1) * 64],
                     pvd[0][:, 0:64],
                     col_s[:, pb4 * 4 + 2:pb4 * 4 + 3], None, ALU.mult, None,
                     [pvd[1], ck], "attm_%d" % h)
            for h in range(4):
                stage_a(h)
            for h in range(HEADS):
                stage_b(h)
                if h + 4 < HEADS:
                    stage_a(h + 4)
            for i in range(NI):
                tpd = (ps_tb, ["pstb", "psg4"]) if i % 2 == 0 else \
                    (ps_tb2, ["pstb2", "psr0"])
                em.emit("tensor",
                        lambda t, i=i, tpd=tpd: t.transpose(
                            out=tpd[0][:, :], in_=attm[:, ts(i, 128)],
                            identity=eye_b[:, :]),
                        reads=["attm_%d" % (2 * i), "attm_%d" % (2 * i + 1),
                               "eye"], writes=tpd[1])
                e_copy(em, attf[:, i * 256 + qc * 128:
                               i * 256 + qc * 128 + 128],
                       tpd[0][:, :], tpd[1], "attf",
                       eng=("scalar" if i % 2 == 0 else "vector"))
    if "attn" not in FFN_SKIP and "wo" not in FFN_SKIP:
        wo_residual(NW - 1)

    CH = 512
    for cc in range(TS // CH if "ffn" not in FFN_SKIP else 0):
        c0 = cc * CH
        for k in range(ND):
            par = k % 2
            e_mm(em, ps[4][0:1, 0:CH], ones_s[:, 0:1],
                 xshift_ap(k, c0, CH),
                 ["xs", "ones"], "psg4", start=(k == 0),
                 stop=(k == ND - 1))
            e_act(em, tf2[:, par * CH:(par + 1) * CH],
                  xshift_ap(k, c0, CH), AF.Square,
                  ["xs"], "tf2%d" % par)
            e_mm(em, ps[5][0:1, 0:CH], ones_s[:, 0:1],
                 tf2[:, par * CH:(par + 1) * CH],
                 ["tf2%d" % par, "ones"], "psg5", start=(k == 0),
                 stop=(k == ND - 1))
        ln_rows(em, ps[4][0:1, 0:CH], ps[5][0:1, 0:CH], mu_r[:, :],
                rs_r[:, :], rowtmp[:, :], 1e-5, D, "psg4", "psg5", "mu", "rs",
                "rt")
        em.emit("tensor",
                lambda t: t.matmul(out=ps_r[:, 0:CH], lhsT=ones_s[0:1, :],
                                   rhs=mu_r[:, :], start=True, stop=True),
                reads=["mu", "ones"], writes=["psr0", "psr1"])
        e_bcast(em, ps_r2[:, 0:CH], ones_s, rs_r[:, :], "rs", "psr2")
        for k in range(ND):
            par = k % 2
            e_tt(em, tf1[:, par * CH:(par + 1) * CH],
                 xshift_ap(k, c0, CH), ps_r[:, 0:CH], ALU.subtract,
                 ["xs", "psr0", "psr1"], "tf1%d" % par)
            e_tt(em, tf1[:, par * CH:(par + 1) * CH],
                 tf1[:, par * CH:(par + 1) * CH], ps_r2[:, 0:CH],
                 ALU.mult, ["tf1%d" % par, "psr2"], "tf1%d" % par)
            e_ts(em, xnb[:, ts(k, CH)], tf1[:, par * CH:(par + 1) * CH],
                 lng_c[:, k:k + 1], lnb_c[:, k:k + 1], ALU.mult, ALU.add,
                 ["tf1%d" % par, "lng", "lnb"], "xnb")
        if cc >= 1:
            p0 = (cc - 1) * CH
            for dt in range(ND):
                xsl = xs[:, dt * (1 + TS) + 1 + p0:
                         dt * (1 + TS) + 1 + p0 + CH]
                e_tt(em, xsl, xsl, fout[:, ts(dt, CH)], ALU.add,
                     ["xs", "fout"], "xs")
        for jc in range(NJC if "w1" not in FFN_SKIP else 0):
            par = jc % 2
            wbuf = w1str if par == 0 else wqkstr
            ka = "w1s0" if par == 0 else "wqks0"
            kg2 = "w1s1" if par == 0 else "wqks1"
            em.dma(QKQ[jc % 3], wbuf[:, 0:1024],
                   w1a_b[:, jc * 1024:(jc + 1) * 1024],
                   reads=["w1ab"], writes=[ka])
            em.dma(QKQ[(jc + 1) % 3], wbuf[:, 1024:2048],
                   w1g_b[:, jc * 1024:(jc + 1) * 1024],
                   reads=["w1gb"], writes=[kg2])
            for k in range(ND):
                e_mm(em, ps[par][:, 0:CH],
                     wbuf[:, k * 128:(k + 1) * 128],
                     xnb[:, ts(k, CH)], [ka, "xnb"], PK[par],
                     start=(k == 0), stop=(k == ND - 1))
                e_mm(em, ps[2 + par][:, 0:CH],
                     wbuf[:, 1024 + k * 128: 1024 + (k + 1) * 128],
                     xnb[:, ts(k, CH)], [kg2, "xnb"], PK[2 + par],
                     start=(k == 0), stop=(k == ND - 1))
            e_act(em, tf1[:, par * CH:(par + 1) * CH], ps[par][:, 0:CH],
                  AF.Identity, [PK[par], "b1a"], "tf1%d" % par,
                  bias=b1a_c[:, jc:jc + 1])
            e_act(em, tf2[:, par * CH:(par + 1) * CH],
                  ps[2 + par][:, 0:CH], AF.Gelu, [PK[2 + par], "b1g"],
                  "tf2%d" % par, bias=b1g_c[:, jc:jc + 1])
            e_tt(em, hgb[:, ts(jc, CH)], tf1[:, par * CH:(par + 1) * CH],
                 tf2[:, par * CH:(par + 1) * CH], ALU.mult,
                 ["tf1%d" % par, "tf2%d" % par], "hgb%d" % par)
        em.dma("sync", cmask_r[:, 0:CH], cmask_d[:, c0:c0 + CH],
               writes=["cmask"])
        e_bcast(em, ps_r2[:, 0:CH], ones_s, cmask_r[:, 0:CH], "cmask", "psr2")
        for dt in range(ND if "w2" not in FFN_SKIP else 0):
            par = dt % 2
            wbuf, wkeys = (vtm, ["vtm"]) if par == 0 else (kro, ["kro"])
            em.dma(QKQ[dt % 3], wbuf[:, 0:NJC * 128],
                   w2_b[:, dt * NJC * 128:(dt + 1) * NJC * 128],
                   reads=["w2b"], writes=wkeys)
            for jc in range(NJC):
                e_mm(em, ps[4 + par][:, 0:CH],
                     wbuf[:, jc * 128:(jc + 1) * 128],
                     hgb[:, ts(jc, CH)],
                     wkeys + ["hgb0", "hgb1"], PK[4 + par],
                     start=(jc == 0), stop=(jc == NJC - 1))
            e_act(em, tf1[:, par * CH:(par + 1) * CH],
                  ps[4 + par][:, 0:CH], AF.Identity, [PK[4 + par], "b2"],
                  "tf1%d" % par, bias=b2_c[:, dt:dt + 1])
            e_tt(em, fout[:, ts(dt, CH)], tf1[:, par * CH:(par + 1) * CH],
                 ps_r2[:, 0:CH], ALU.mult, ["tf1%d" % par, "psr2"], "fout")
    p0 = TS - CH
    for dt in range(ND):
        xsl = xs[:, dt * (1 + TS) + 1 + p0: dt * (1 + TS) + 1 + p0 + CH]
        e_tt(em, xsl, xsl, fout[:, ts(dt, CH)], ALU.add, ["xs", "fout"], "xs")

    qi = 0
    for k in range(ND):
        for o in range(0, T_OWN, 512):
            em.dma(QKQ[qi % 3], xout_d[ts(k, 128), o:o + 512],
                   xs[:, k * (1 + TS) + 1 + 512 + o:
                      k * (1 + TS) + 1 + 512 + o + 512],
                   reads=["xs"], writes=["xo"])
            qi += 1

    em.finalize()
    return nc


def _rotary_tables():
    half = np.arange(0, DH, 2, dtype=np.float64)
    inv_freq = 1.0 / 10000.0 ** (half / DH)
    base_scale = (half + 0.4 * DH) / (1.4 * DH)

    def tables(positions):
        t = positions.astype(np.float64)
        freqs = t[:, None] * inv_freq[None, :]
        power = (t - L // 2) / (256 // 2)
        scale = base_scale[None, :] ** power[:, None]
        cos = np.cos(freqs)
        sin = np.sin(freqs)
        cs = (cos * scale).T
        ss = (sin * scale).T
        inv = 1.0 / scale
        ck = (cos * inv).T
        sk = (sin * inv).T
        def stack(c, s):
            blk = np.concatenate([c, c, c, c], axis=0)
            sgn = np.concatenate([-s, s, -s, s], axis=0)
            return np.float32(blk), np.float32(sgn)
        csq, ssnq = stack(cs, ss)
        csk2, ssnk2 = stack(ck, sk)
        return (np.ascontiguousarray(csq), np.ascontiguousarray(ssnq),
                np.ascontiguousarray(csk2), np.ascontiguousarray(ssnk2))

    return tables


_FFN_NC = None
_L0_NC = None
_MAMBA_NC = None
_COMB_NC = None


def _rperm128():
    r = np.zeros((128, 128), np.float32)
    for blk in range(2):
        b0 = blk * 64
        for i in range(32):
            r[b0 + 32 + i, b0 + i] = 1.0
            r[b0 + i, b0 + 32 + i] = 1.0
    return r


def ffn_in_maps(inputs, layer, xcur):
    NI = INNER // 128
    NJC = FFI // 128
    wqkv = np.ascontiguousarray(np.asarray(inputs["L1_ffn_Wqkv"])[layer])
    wo = np.ascontiguousarray(np.asarray(inputs["L1_ffn_Wo"])[layer])
    w1 = np.asarray(inputs["L1_ffn_W1"])[layer]
    w2 = np.asarray(inputs["L1_ffn_W2"])[layer]
    b1 = np.asarray(inputs["L1_ffn_b1"])[layer]
    b2 = np.asarray(inputs["L1_ffn_b2"])[layer]
    lng = np.asarray(inputs["L1_ffn_ln_g"])[layer]
    lnb = np.asarray(inputs["L1_ffn_ln_b"])[layer]
    qs = np.asarray(inputs["L1_ffn_q_scale"])[layer]
    ks = np.asarray(inputs["L1_ffn_k_scale"])[layer]
    FF0 = 2730
    ND = D // 128
    w1a = np.zeros((D, FFI), np.float32); w1a[:, :FF0] = w1[:, :FF0]
    w1g = np.zeros((D, FFI), np.float32); w1g[:, :FF0] = w1[:, FF0:]
    w2p = np.zeros((FFI, D), np.float32); w2p[:FF0] = w2
    b1a = np.zeros(FFI, np.float32); b1a[:FF0] = b1[:FF0]
    b1g = np.zeros(FFI, np.float32); b1g[:FF0] = b1[FF0:]
    def tilemaj(wmat, nr, nccols):
        return np.ascontiguousarray(
            np.float32(wmat).reshape(nr, 128, nccols, 128)
            .transpose(1, 2, 0, 3).reshape(128, -1))
    wqk_t = tilemaj(wqkv[:, :2 * INNER], ND, 2 * NI)
    wv_t = np.ascontiguousarray(
        np.float32(wqkv[:, 2 * INNER:]).reshape(ND, 128, 2, 512)
        .transpose(1, 2, 0, 3).reshape(128, -1))
    wo_t = tilemaj(wo, NI, ND)
    w1a_t = tilemaj(w1a, ND, NJC)
    w1g_t = tilemaj(w1g, ND, NJC)
    w2_t = tilemaj(w2p, NJC, ND)
    blk16 = np.zeros((128, NI * 16), np.float32)
    for jt in range(NI):
        blk16[0:64, jt * 16 + 2 * jt] = 1.0
        blk16[64:128, jt * 16 + 2 * jt + 1] = 1.0
    esel2 = np.zeros((16, NI * 128), np.float32)
    for jt in range(NI):
        esel2[2 * jt, jt * 128: jt * 128 + 64] = 1.0
        esel2[2 * jt + 1, jt * 128 + 64: (jt + 1) * 128] = 1.0
    band = np.full((128, 384), -1.0e38, np.float32)
    for i in range(128):
        band[i, i:i + 257] = 0.0
    tables = _rotary_tables()
    common = {
        "wqk_t": wqk_t, "wv_t": wv_t, "wo_t": wo_t,
        "w1a_t": w1a_t, "w1g_t": w1g_t, "w2_t": w2_t,
        "b1a": np.ascontiguousarray(b1a.reshape(NJC, 128).T),
        "b1g": np.ascontiguousarray(b1g.reshape(NJC, 128).T),
        "b2c": np.ascontiguousarray(b2.reshape(8, 128).T),
        "lng": np.ascontiguousarray(lng.reshape(8, 128).T),
        "lnb": np.ascontiguousarray(lnb.reshape(8, 128).T),
        "qs": np.ascontiguousarray(np.tile(qs, 2)[:, None]),
        "ks": np.ascontiguousarray(np.tile(ks, 2)[:, None]),
        "band": band, "blk16": blk16, "esel2": esel2,
        "iden": np.eye(128, dtype=np.float32),
        "rperm": _rperm128(),
    }
    maps = []
    for core in range(8):
        b, half = core // 2, core % 2
        start = half * T_OWN
        xb = xcur[b]
        xst = np.zeros((1 + TS, D), np.float32)
        lo = start - 512
        src_lo = max(lo, 0)
        xst[1 + (src_lo - lo): 1 + TS] = xb[src_lo: start + T_OWN]
        if lo - 1 >= 0:
            xst[0] = xb[lo - 1]
        pos = np.arange(lo, start + T_OWN)
        cs, ssn, csk, ssnk = tables(pos)
        flag = np.zeros((NW, 2, 384), np.float32)
        noprev = {0} | ({2} if half == 0 else set())
        for w in range(NW):
            if w in noprev:
                for qc in range(2):
                    n = 256 - qc * 128
                    flag[w, qc, 0:n] = -1.0e38
        cmask = np.ones((1, TS), np.float32)
        if half == 0:
            cmask[0, 0:512] = 0.0
        m = dict(common)
        m["xst"] = np.ascontiguousarray(xst.T)
        m["cs_t"] = cs
        m["ssn_t"] = ssn
        m["csk_t"] = csk
        m["ssnk_t"] = ssnk
        m["flagrow"] = flag.reshape(1, -1)
        m["cmask"] = cmask
        maps.append(m)
    return maps


def run_ffn_layer(inputs, layer, xcur):
    global _FFN_NC
    if _FFN_NC is None:
        _FFN_NC = build_ffn()
    maps = ffn_in_maps(inputs, layer, xcur)
    res = run_bass_kernel_spmd(_FFN_NC, maps, core_ids=list(range(8)))
    xnew = np.zeros((B, L, D), np.float32)
    for core in range(8):
        b, half = core // 2, core % 2
        xnew[b, half * T_OWN:(half + 1) * T_OWN] = res.results[core]["xout"].T
    return xnew


def build_combine():
    nc = bass.Bass("TRN2", target_bir_lowering=False)
    ND = D // 128
    x1_d = nc.dram_tensor("x1", [D, T_OWN], F32, kind="ExternalInput").ap()
    xn1_d = nc.dram_tensor("xn1", [D, T_OWN], F32, kind="ExternalInput").ap()
    pf_d = nc.dram_tensor("pf", [D, T_OWN], F32, kind="ExternalInput").ap()
    pb_d = nc.dram_tensor("pb", [D, T_OWN], F32, kind="ExternalInput").ap()
    g_d = nc.dram_tensor("g", [D, T_OWN], F32, kind="ExternalInput").ap()
    ob_d = nc.dram_tensor("obc", [128, ND], F32, kind="ExternalInput").ap()
    x2_d = nc.dram_tensor("x2", [D, T_OWN], F32, kind="ExternalOutput").ap()
    em = EM(nc)
    a_s = sbuf(nc, "a_s", [128, 2 * 512])
    p_s = sbuf(nc, "p_s", [128, 2 * 512])
    n_s = sbuf(nc, "n_s", [128, 2 * 512])
    g_s = sbuf(nc, "g_s", [128, 2 * 512])
    x_s = sbuf(nc, "x_s", [128, 2 * 512])
    c_s = sbuf(nc, "c_s", [128, 2 * 512])
    ob_c = sbuf(nc, "ob_c", [128, ND])
    em.dma("sync", ob_c[:, :], ob_d[:, :], writes=["ob"])
    qi = 0
    for k in range(ND):
        for o in range(0, T_OWN, 512):
            sl = (ts(k, 128), slice(o, o + 512))
            p = qi % 2
            qs3 = ("sync", "scalar", "gpsimd")
            aa = a_s[:, ts(p, 512)]
            em.dma(qs3[qi % 3], aa, pf_d[sl[0], sl[1]], writes=["a%d" % p])
            em.dma(qs3[(qi + 1) % 3], p_s[:, ts(p, 512)], pb_d[sl[0], sl[1]],
                   writes=["p%d" % p])
            em.dma(qs3[(qi + 2) % 3], n_s[:, ts(p, 512)], xn1_d[sl[0], sl[1]],
                   writes=["n%d" % p])
            em.dma(qs3[qi % 3], g_s[:, ts(p, 512)], g_d[sl[0], sl[1]],
                   writes=["g%d" % p])
            em.dma(qs3[(qi + 1) % 3], x_s[:, ts(p, 512)], x1_d[sl[0], sl[1]],
                   writes=["x%d" % p])
            e_tt(em, aa, aa, p_s[:, ts(p, 512)], ALU.add,
                 ["a%d" % p, "p%d" % p], "a%d" % p)
            e_tt(em, aa, aa, n_s[:, ts(p, 512)], ALU.add,
                 ["a%d" % p, "n%d" % p], "a%d" % p)
            e_ts(em, aa, aa, ob_c[:, k:k + 1], None, ALU.add,
                 None, ["a%d" % p, "ob"], "a%d" % p)
            e_act(em, c_s[:, ts(p, 512)], g_s[:, ts(p, 512)], AF.Silu,
                  ["g%d" % p], "c%d" % p)
            e_tt(em, aa, aa, c_s[:, ts(p, 512)], ALU.mult,
                 ["a%d" % p, "c%d" % p], "a%d" % p)
            e_tt(em, aa, aa, x_s[:, ts(p, 512)], ALU.add,
                 ["a%d" % p, "x%d" % p], "a%d" % p)
            em.dma(qs3[(qi + 2) % 3], x2_d[sl[0], sl[1]], aa, reads=["a%d" % p],
                   writes=["out"])
            qi += 1
    em.finalize()
    return nc


def run_combine(inputs, x1, xn1, pf, pb, g):
    global _COMB_NC
    if _COMB_NC is None:
        _COMB_NC = build_combine()
    ob = np.asarray(inputs["L1_attn_out_b"])
    obc = np.ascontiguousarray(ob.reshape(8, 128).T)
    maps = []
    for core in range(8):
        b, half = core // 2, core % 2
        sl = slice(half * T_OWN, (half + 1) * T_OWN)
        maps.append({
            "x1": np.ascontiguousarray(x1[b, sl].T),
            "xn1": np.ascontiguousarray(xn1[b, sl].T),
            "pf": np.ascontiguousarray(pf[b, sl].T),
            "pb": np.ascontiguousarray(pb[b, sl].T),
            "g": np.ascontiguousarray(g[b, sl].T),
            "obc": obc,
        })
    res = run_bass_kernel_spmd(_COMB_NC, maps, core_ids=list(range(8)))
    x2 = np.zeros((B, L, D), np.float32)
    for core in range(8):
        b, half = core // 2, core % 2
        x2[b, half * T_OWN:(half + 1) * T_OWN] = res.results[core]["x2"].T
    return x2


def kernel(**inputs):
    x1, xn1 = run_l0(inputs)
    pf, pb = run_mamba(inputs, xn1)
    xcur = xn1
    for layer in range(6):
        xcur = run_ffn_layer(inputs, layer, xcur)
    x2 = run_combine(inputs, x1, xn1, pf, pb, xcur)
    return x2
